# version 01
# baseline (speedup 1.0000x reference)
import contextlib
import numpy as np
import ml_dtypes
import concourse.bass as bass
import concourse.mybir as mybir
from concourse.bass_utils import run_bass_kernel_spmd
from concourse.alu_op_type import AluOpType as Alu

F32 = mybir.dt.float32
BF16 = mybir.dt.bfloat16
AF = mybir.ActivationFunctionType
AX = mybir.AxisListType.X

MAGIC = 1.5 * 2.0 ** 23
PI = float(np.pi)
LR = 0.001
EPS = 1e-5
NB, TL, D, DI, FF = 8, 512, 1024, 256, 3072
CG = 2.0 / (4096.0 * 1024.0)
NCORE = 8

STAGE = 4
DBG = False
DBG_STEPS = None

AF_ELE = 18688
AB_ELE = 67840


class Ctx:
    def __init__(self, nc, es):
        self.nc, self.es, self.sem, self.cnt = nc, es, {}, {}

    def S(self, name):
        self.sem[name] = self.es.enter_context(self.nc.semaphore(name))
        self.cnt[name] = 0

    def inc(self, instr, name, n=1):
        self.cnt[name] += n
        instr.then_inc(self.sem[name], n)
        return self.cnt[name]

    def val(self, name):
        return self.cnt[name]

    def drain(self, eng, instr, name="vv"):
        v = self.inc(instr, name, 1)
        eng.wait_ge(self.sem[name], v)

    def wait(self, eng, name, target=None):
        if target is None:
            target = self.cnt[name]
        if target > 0:
            eng.wait_ge(self.sem[name], target)


class Arena:
    def __init__(self, t, cap):
        self.t, self.cap, self.off = t, cap, 0

    def take(self, *shape):
        n = int(np.prod(shape))
        ap = self.t[:, self.off:self.off + n]
        assert self.off + n <= self.cap, f"arena overflow {self.off}+{n}>{self.cap}"
        self.off += n
        if len(shape) > 1:
            names = " ".join(f"d{i}" for i in range(len(shape)))
            ap = ap.rearrange(f"p ({names}) -> p {names}",
                              **{f"d{i}": s for i, s in enumerate(shape)})
        return ap

    def at(self, off, *shape):
        n = int(np.prod(shape))
        ap = self.t[:, off:off + n]
        if len(shape) > 1:
            names = " ".join(f"d{i}" for i in range(len(shape)))
            ap = ap.rearrange(f"p ({names}) -> p {names}",
                              **{f"d{i}": s for i, s in enumerate(shape)})
        return ap

    def reset(self, off=0):
        self.off = off


def build_program(stage=STAGE, dbg=None, dbg_steps=None):
    dbg = DBG if dbg is None else dbg
    nsteps = NB if (DBG_STEPS if dbg_steps is None else dbg_steps) is None else (DBG_STEPS if dbg_steps is None else dbg_steps)
    nc = bass.Bass("TRN2", target_bir_lowering=False)

    def din(name, shape):
        return nc.dram_tensor(name, shape, F32, kind="ExternalInput").ap()

    x_in = din("x_in", [NB * TL, D])
    tkT, tvT, tqT = din("tkT", [D, D]), din("tvT", [D, D]), din("tqT", [D, D])
    w1Ti, w2Ti, w2ni = din("w1T", [D, DI]), din("w2T", [DI, D]), din("w2n", [D, DI])
    b1i, b2i = din("b1i", [1, DI]), din("b2i", [1, D])
    wqT, wkT = din("wqT", [D, D]), din("wkT", [D, D])
    wvT, woT = din("wvT", [D, D]), din("woT", [D, D])
    rfTi = din("rfT", [64, 256])
    wgate, wup, wdown = din("wgate", [FF, D]), din("wup", [FF, D]), din("wdown", [D, FF])

    y_out = nc.dram_tensor("y", [NB * TL, D], F32, kind="ExternalOutput").ap()
    dbg_out = nc.dram_tensor("dbg", [128, 65536], F32, kind="ExternalOutput").ap() if dbg else None

    xres = nc.dram_tensor("xres", [NB * TL, D], F32).ap()
    wgT_dq = nc.dram_tensor("wgT_dq", [24, 8, 128, 128], BF16).ap()
    wuT_dq = nc.dram_tensor("wuT_dq", [24, 8, 128, 128], BF16).ap()
    wdT_dq = nc.dram_tensor("wdT_dq", [24, 128, D], BF16).ap()
    gbuf_in = nc.dram_tensor("gbuf_in", [1028, 512], F32).ap()
    gbuf_out = nc.dram_tensor("gbuf_out", [1028, 512], F32).ap()
    up_s = nc.dram_tensor("up_s", [24, 128, NB * TL], BF16).ap()
    gate_s = nc.dram_tensor("gate_s", [24, 128, NB * TL], BF16).ap()
    g2_s = nc.dram_tensor("g2_s", [24, 128, NB * TL], BF16).ap()
    t4_s = nc.dram_tensor("t4_s", [24, 128, NB * TL], BF16).ap()
    mxi = [nc.dram_tensor(f"mxi{k}", [1, 1], F32).ap() for k in range(10)]
    mxo = [nc.dram_tensor(f"mxo{k}", [1, 1], F32).ap() for k in range(10)]
    kv_in = nc.dram_tensor("kv_in", [NB * 128, 16 * 264], F32).ap()
    kv_out = nc.dram_tensor("kv_out", [NB * 128, 16 * 264], F32).ap()
    xnT_sp = nc.dram_tensor("xnT_sp", [NB, 128, 8 * TL], BF16).ap()

    es = contextlib.ExitStack()
    c = Ctx(nc, es)

    def sbuf(name, shape, dtype):
        return es.enter_context(nc.sbuf_tensor(name, shape, dtype))

    ident_b = sbuf("ident_b", [128, 128], BF16)
    ones_b = sbuf("ones_b", [128, 512], BF16)
    consts = sbuf("consts", [128, 8], F32)
    colw = sbuf("colw", [128, 32], F32)

    arena_f_t = sbuf("arena_f", [128, AF_ELE], F32)
    arena_b_t = sbuf("arena_b", [128, AB_ELE], BF16)
    arf = Arena(arena_f_t, AF_ELE)
    arb = Arena(arena_b_t, AB_ELE)

    pb = [es.enter_context(nc.psum_tensor(f"pb{i}", [128, 512], F32)) for i in range(4)]
    pt = [es.enter_context(nc.psum_tensor(f"pt{i}", [128, 8, 128], BF16)) for i in range(2)]
    pm = es.enter_context(nc.psum_tensor("pm", [128, 512], F32))
    pm2 = es.enter_context(nc.psum_tensor("pm2", [128, 512], F32))

    for s in ["ld", "cst", "st", "mm", "ev", "vv", "pp", "cc"]:
        c.S(s)

    blk = es.enter_context(nc.Block())

    def barrier():
        t_ = {k: c.val(k) for k in ("mm", "ev", "vv", "st", "ld", "cst", "pp", "cc")}

        @blk.tensor
        def _(t):
            for k in ("ev", "vv", "pp", "st", "ld", "cst"):
                c.wait(t, k, t_[k])

        @blk.vector
        def _(v):
            for k in ("mm", "ev", "pp", "st", "ld", "cst"):
                c.wait(v, k, t_[k])

        @blk.scalar
        def _(sc):
            for k in ("mm", "vv", "pp", "st", "ld", "cst"):
                c.wait(sc, k, t_[k])

        @blk.gpsimd
        def _(g):
            for k in ("mm", "ev", "vv", "ld", "cc"):
                c.wait(g, k, t_[k])

        @blk.sync
        def _(sy):
            for k in ("mm", "ev", "vv", "pp", "st", "cst", "cc"):
                c.wait(sy, k, t_[k])

    xs = arf.take(4, D)
    scr_b = arf.take(2048)
    gst3 = scr_b[0:1, :]
    a_f = arf.take(2, TL)
    w1T_f = arf.take(8, DI)
    w2T_f = arf.take(2, D)
    w2n_f = arf.take(8, DI)
    b1_f = arf.take(DI)[0:1, :]
    b2_f = arf.take(D)[0:1, :]
    gst1 = arf.take(8, DI)
    gst2 = arf.take(2, D)
    gr1, gr2, gr3 = gst1, gst2, gst3
    assert arf.off <= AF_ELE, arf.off

    tkT_b = arb.take(8, D)
    tvT_b = arb.take(8, D)
    tqT_b = arb.take(8, D)
    w1T_b = arb.take(8, DI)
    w2T_b = arb.take(2, D)
    w2n_b = arb.take(8, DI)
    b1_b = arb.take(DI)[0:1, :]
    b2_b = arb.take(D)[0:1, :]
    xn_tm = arb.take(4, D)
    G_tm = xn_tm
    xnT = arb.take(8, TL)
    z_fm = arb.take(8, TL)
    z_tm = arb.take(4, D)
    y_fm = arb.take(8, TL)
    q_fm = arb.take(8, TL)
    _goff = arb.off
    G_fm = arb.take(8, TL)
    gr2b = arb.at(_goff, 2, 1024)
    h_fm = arb.take(2, TL)
    h2_fm = h_fm
    h_tm = arb.take(4, DI)
    gp_b = arb.take(2, TL)
    da_fm = arb.take(2, TL)
    da_tm = arb.take(4, DI)
    gst4 = arb.take(8, DI)
    G_fm_off = None
    print("attn_a arena use: f32", arf.off, "bf16", arb.off)
    assert arb.off <= AB_ELE, arb.off

    def phase_setup():
        @blk.gpsimd
        def _(g):
            g.memset(ident_b[:, :], 0.0)
            g.affine_select(out=ident_b[:, :], in_=ident_b[:, :], compare_op=Alu.not_equal,
                            fill=1.0, base=0, pattern=[[-1, 128]], channel_multiplier=1)
            g.memset(ones_b[:, :], 1.0)
            g.memset(consts[:, 0:1], MAGIC)
            g.memset(consts[:, 1:2], PI / 2)
            i = g.memset(consts[:, 2:3], 65536.0)
            c.inc(i, "pp", 1)
            for dst, src in ((tkT_b, tkT), (tvT_b, tvT), (tqT_b, tqT),
                             (w1T_b, w1Ti), (w2T_b, w2Ti), (w2n_b, w2ni)):
                c.inc(g.dma_start(out=dst[:, :, :], in_=src.rearrange("(k p) o -> p k o", p=128)),
                      "cst", 16)
            c.inc(g.dma_start(out=b1_b, in_=b1i), "cst", 16)
            c.inc(g.dma_start(out=b2_b, in_=b2i), "cst", 16)

        @blk.sync
        def _(s):
            c.inc(s.dma_start(out=w1T_f[:, :, :], in_=w1Ti.rearrange("(k p) o -> p k o", p=128)), "ld", 16)
            c.inc(s.dma_start(out=w2T_f[:, :, :], in_=w2Ti.rearrange("(k p) o -> p k o", p=128)), "ld", 16)
            c.inc(s.dma_start(out=w2n_f[:, :, :], in_=w2ni.rearrange("(k p) o -> p k o", p=128)), "ld", 16)
            c.inc(s.dma_start(out=b1_f, in_=b1i), "ld", 16)
            c.inc(s.dma_start(out=b2_f, in_=b2i), "ld", 16)
            c.inc(s.dma_start(out=xs[:, :, :],
                              in_=x_in[0:TL, :].rearrange("(t p) o -> p t o", p=128)), "ld", 16)

    def transpose_group(dst, srcs, par, pre_ev=None):
        @blk.tensor
        def _(t):
            if pre_ev is not None:
                c.wait(t, "ev", pre_ev)
            c.wait(t, "vv", c.val("vv"))
            i = None
            for k, s_ap in enumerate(srcs):
                i = t.transpose(pt[par][:, k, :], s_ap, ident_b[:, :])
            c.inc(i, "mm", 1)

        @blk.vector
        def _(v):
            c.wait(v, "mm")
            i = v.tensor_copy(dst, pt[par][:, 0:len(srcs), :])
            c.inc(i, "vv", 1)

    def phase_attn_a():
        ld_xs = c.val("ld")

        @blk.tensor
        def _(t):
            c.wait(t, "pp")
            c.wait(t, "cst")
            c.wait(t, "ld")

        for s in range(nsteps):
            for tc in range(4):
                @blk.vector
                def _(v, s=s, tc=tc, ld_xs=ld_xs):
                    if tc == 0:
                        c.wait(v, "ld", ld_xs)
                        if s == 0:
                            c.wait(v, "cst")
                    c.drain(v, v.tensor_reduce(colw[:, 0:1], xs[:, tc, :], AX, Alu.add))
                    c.drain(v, v.tensor_scalar(colw[:, 1:2], colw[:, 0:1], -1.0 / D, None, Alu.mult))
                    i = v.tensor_scalar(scr_b[:, 0:D], xs[:, tc, :], colw[:, 1:2], None, Alu.add)
                    c.inc(i, "vv", 1)

                @blk.scalar
                def _(sc, s=s, tc=tc):
                    if s == 0 and tc == 0:
                        c.wait(sc, "cst")
                    c.wait(sc, "vv")
                    c.inc(sc.activation(a_f[:, :, :].rearrange("p a b -> p (a b)"),
                                        scr_b[:, 0:D], AF.Square), "ev", 1)

                @blk.vector
                def _(v):
                    c.wait(v, "ev")
                    c.drain(v, v.tensor_reduce(colw[:, 2:3], a_f[:, :, :].rearrange("p a b -> p (a b)"), AX, Alu.add))
                    i = v.tensor_scalar(colw[:, 2:3], colw[:, 2:3], 1.0 / D, EPS, Alu.mult, Alu.add)
                    c.inc(i, "vv", 1)

                @blk.scalar
                def _(sc):
                    c.wait(sc, "vv")
                    c.inc(sc.activation(colw[:, 3:4], colw[:, 2:3], AF.Sqrt), "ev", 1)

                @blk.vector
                def _(v, tc=tc):
                    c.wait(v, "ev")
                    c.drain(v, v.reciprocal(colw[:, 3:4], colw[:, 3:4]))
                    i = v.tensor_scalar(xn_tm[:, tc, :], scr_b[:, 0:D], colw[:, 3:4], None, Alu.mult)
                    c.inc(i, "vv", 1)

                transpose_group(xnT[:, 0:8, tc * 128:(tc + 1) * 128],
                                [xn_tm[:, tc, kc * 128:(kc + 1) * 128] for kc in range(8)],
                                par=tc % 2)
            xnT_vv = c.val("vv")

            ev0 = c.val("ev")
            for pi_, (wT, dst) in enumerate(((tkT_b, z_fm), (tvT_b, y_fm), (tqT_b, q_fm))):
                for oc in range(8):
                    n = pi_ * 8 + oc

                    @blk.tensor
                    def _(t, wT=wT, oc=oc, n=n, ev0=ev0, xnT_vv=xnT_vv):
                        if n == 0:
                            c.wait(t, "vv", xnT_vv)
                        if n >= 4:
                            c.wait(t, "ev", ev0 + n - 3)
                        bank = pb[n % 4]
                        i = None
                        for kc in range(8):
                            i = t.matmul(bank[:, :], wT[:, kc, oc * 128:(oc + 1) * 128],
                                         xnT[:, kc, :], start=(kc == 0), stop=(kc == 7))
                        c.inc(i, "mm", 1)

                    @blk.scalar
                    def _(sc, dst=dst, oc=oc, n=n):
                        c.wait(sc, "mm")
                        c.inc(sc.activation(dst[:, oc, :], pb[n % 4][:, :], AF.Copy), "ev", 1)
            proj_ev = c.val("ev")

            for oc in range(8):
                transpose_group(z_tm[:, 0:4, oc * 128:(oc + 1) * 128],
                                [z_fm[:, oc, ts * 128:(ts + 1) * 128] for ts in range(4)],
                                par=oc % 2, pre_ev=(proj_ev - 16 if oc == 0 else None))

            @blk.vector
            def _(v, proj_ev=proj_ev):
                c.wait(v, "ev", proj_ev)
                i = v.tensor_tensor(y_fm[:, :, :], z_fm[:, :, :], y_fm[:, :, :], Alu.subtract)
                c.inc(i, "vv", 1)
            zy_vv = c.val("vv")

            for ic in range(2):
                @blk.tensor
                def _(t, ic=ic, proj_ev=proj_ev):
                    if ic == 0:
                        c.wait(t, "ev", proj_ev - 16)
                    bank = pb[ic]
                    for kc in range(8):
                        t.matmul(bank[:, :], w1T_b[:, kc, ic * 128:(ic + 1) * 128],
                                 z_fm[:, kc, :], start=(kc == 0), stop=False)
                    i = t.matmul(bank[:, :], b1_b[0:1, ic * 128:(ic + 1) * 128],
                                 ones_b[0:1, :], start=False, stop=True)
                    c.inc(i, "mm", 1)

                @blk.scalar
                def _(sc, ic=ic):
                    c.wait(sc, "mm")
                    c.inc(sc.activation(a_f[:, ic, :], pb[ic][:, :], AF.Copy), "ev", 1)

            @blk.scalar
            def _(sc):
                sc.activation(h_fm[:, :, :], a_f[:, :, :], AF.Gelu)
                c.inc(sc.activation(gp_b[:, :, :], a_f[:, :, :], AF.Derivative_Gelu), "ev", 1)
            h_ev = c.val("ev")

            for ic in range(2):
                transpose_group(h_tm[:, 0:4, ic * 128:(ic + 1) * 128],
                                [h_fm[:, ic, ts * 128:(ts + 1) * 128] for ts in range(4)],
                                par=ic, pre_ev=(h_ev if ic == 0 else None))

            vv_r0 = c.val("vv")
            for oc in range(8):
                @blk.tensor
                def _(t, oc=oc, h_ev=h_ev, vv_r0=vv_r0):
                    if oc == 0:
                        c.wait(t, "ev", h_ev)
                    if oc >= 4:
                        c.wait(t, "vv", vv_r0 + 2 * (oc - 4) + 2)
                    bank = pb[oc % 4]
                    for kc in range(2):
                        t.matmul(bank[:, :], w2T_b[:, kc, oc * 128:(oc + 1) * 128],
                                 h_fm[:, kc, :], start=(kc == 0), stop=False)
                    i = t.matmul(bank[:, :], b2_b[0:1, oc * 128:(oc + 1) * 128],
                                 ones_b[0:1, :], start=False, stop=True)
                    c.inc(i, "mm", 1)

                @blk.vector
                def _(v, oc=oc, zy_vv=zy_vv):
                    c.wait(v, "mm")
                    if oc == 0:
                        c.wait(v, "vv", zy_vv)
                    v.tensor_tensor(scr_b[:, 0:TL], pb[oc % 4][:, :], y_fm[:, oc, :], Alu.add)
                    i = v.tensor_scalar(G_fm[:, oc, :], scr_b[:, 0:TL], CG, None, Alu.mult)
                    c.inc(i, "vv", 2)

            for oc in range(8):
                transpose_group(G_tm[:, 0:4, oc * 128:(oc + 1) * 128],
                                [G_fm[:, oc, ts * 128:(ts + 1) * 128] for ts in range(4)],
                                par=oc % 2)
            G_vv = c.val("vv")

            for ic in range(2):
                @blk.tensor
                def _(t, ic=ic, G_vv=G_vv):
                    if ic == 0:
                        c.wait(t, "vv", G_vv)
                    bank = pb[ic]
                    i = None
                    for kc in range(8):
                        i = t.matmul(bank[:, :], w2n_b[:, kc, ic * 128:(ic + 1) * 128],
                                     G_fm[:, kc, :], start=(kc == 0), stop=(kc == 7))
                    c.inc(i, "mm", 1)

                @blk.vector
                def _(v, ic=ic):
                    c.wait(v, "mm")
                    i = v.tensor_tensor(da_fm[:, ic, :], pb[ic][:, :], gp_b[:, ic, :], Alu.mult)
                    c.inc(i, "vv", 1)

            for ic in range(2):
                transpose_group(da_tm[:, 0:4, ic * 128:(ic + 1) * 128],
                                [da_fm[:, ic, ts * 128:(ts + 1) * 128] for ts in range(4)],
                                par=ic)
            da_vv = c.val("vv")

            gev0 = c.val("ev")
            for os_ in range(8):
                @blk.tensor
                def _(t, os_=os_, da_vv=da_vv, gev0=gev0):
                    if os_ == 0:
                        c.wait(t, "vv", da_vv)
                    if os_ >= 4:
                        c.wait(t, "ev", gev0 + os_ - 3)
                    bank = pb[os_ % 4]
                    i = None
                    for ts in range(4):
                        i = t.matmul(bank[:, 0:DI], z_tm[:, ts, os_ * 128:(os_ + 1) * 128],
                                     da_tm[:, ts, :], start=(ts == 0), stop=(ts == 3))
                    c.inc(i, "mm", 1)

                @blk.scalar
                def _(sc, os_=os_):
                    c.wait(sc, "mm")
                    c.inc(sc.activation(gst1[:, os_, :], pb[os_ % 4][:, 0:DI], AF.Copy), "ev", 1)

            gev1 = c.val("ev")
            for u in range(4):
                is_, cc_ = u % 2, u // 2

                @blk.tensor
                def _(t, is_=is_, cc_=cc_, u=u, gev1=gev1):
                    c.wait(t, "ev", gev1 - 3 + u)
                    bank = pb[u % 4]
                    i = None
                    for ts in range(4):
                        i = t.matmul(bank[:, :], h_tm[:, ts, is_ * 128:(is_ + 1) * 128],
                                     G_tm[:, ts, cc_ * 512:(cc_ + 1) * 512],
                                     start=(ts == 0), stop=(ts == 3))
                    c.inc(i, "mm", 1)

                @blk.scalar
                def _(sc, is_=is_, cc_=cc_, u=u):
                    c.wait(sc, "mm")
                    c.inc(sc.activation(gst2[:, is_, cc_ * 512:(cc_ + 1) * 512],
                                        pb[u % 4][:, :], AF.Copy), "ev", 1)

            @blk.tensor
            def _(t):
                i = None
                for ts in range(4):
                    i = t.matmul(pm[0:1, 0:DI], ones_b[:, 0:1], da_tm[:, ts, :],
                                 start=(ts == 0), stop=(ts == 3))
                c.inc(i, "mm", 1)

            @blk.scalar
            def _(sc):
                c.wait(sc, "mm")
                c.inc(sc.activation(gst3[0:1, 0:DI], pm[0:1, 0:DI], AF.Copy), "ev", 1)

            for cc_ in range(2):
                @blk.tensor
                def _(t, cc_=cc_):
                    c.wait(t, "ev", c.val("ev"))
                    i = None
                    for ts in range(4):
                        i = t.matmul(pm2[0:1, :], ones_b[:, 0:1], G_tm[:, ts, cc_ * 512:(cc_ + 1) * 512],
                                     start=(ts == 0), stop=(ts == 3))
                    c.inc(i, "mm", 1)

                @blk.scalar
                def _(sc, cc_=cc_):
                    c.wait(sc, "mm")
                    c.inc(sc.activation(gst3[0:1, 1024 + cc_ * 512:1024 + (cc_ + 1) * 512],
                                        pm2[0:1, :], AF.Copy), "ev", 1)
            grads_ev = c.val("ev")

            @blk.gpsimd
            def _(g, grads_ev=grads_ev):
                c.wait(g, "ev", grads_ev)
                c.inc(g.dma_start(
                    out=gbuf_in[0:512, :].rearrange("(os ph) (pl i) -> (ph pl) os i", os=8, ph=64, pl=2),
                    in_=gst1[:, :, :]), "st", 16)
                c.inc(g.dma_start(
                    out=gbuf_in[512:1024, :].rearrange("(i p two) cc -> p i (two cc)", i=2, p=128, two=2),
                    in_=gst2[:, :, :]), "st", 16)
                c.inc(g.dma_start(out=gbuf_in[1024:1028, :], in_=gst3[0:1, :]), "st", 16)
                c.wait(g, "st")
                i = g.collective_compute("AllReduce", Alu.add, replica_groups=[list(range(NCORE))],
                                         ins=[gbuf_in.opt()], outs=[gbuf_out.opt()])
                c.inc(i, "cc", 1)

            @blk.sync
            def _(sy):
                c.wait(sy, "cc")
                c.inc(sy.dma_start(
                    out=gr1[:, :, :],
                    in_=gbuf_out[0:512, :].rearrange("(os ph) (pl i) -> (ph pl) os i", os=8, ph=64, pl=2)), "ld", 16)
                c.inc(sy.dma_start(
                    out=gr2[:, :, :],
                    in_=gbuf_out[512:1024, :].rearrange("(i p two) cc -> p i (two cc)", i=2, p=128, two=2)), "ld", 16)
                c.inc(sy.dma_start(out=gr3[0:1, :], in_=gbuf_out[1024:1028, :]), "ld", 16)
            gr_ld = c.val("ld")

            @blk.vector
            def _(v, gr_ld=gr_ld):
                c.wait(v, "ld", gr_ld)
                v.scalar_tensor_tensor(w1T_f[:, :, :], gr1[:, :, :], -LR, w1T_f[:, :, :], Alu.mult, Alu.add)
                v.scalar_tensor_tensor(w2T_f[:, :, :], gr2[:, :, :], -LR, w2T_f[:, :, :], Alu.mult, Alu.add)
                v.scalar_tensor_tensor(b1_f[0:1, :], gr3[0:1, 0:DI], -LR, b1_f[0:1, :], Alu.mult, Alu.add)
                i = v.scalar_tensor_tensor(b2_f[0:1, :], gr3[0:1, 1024:2048], -LR, b2_f[0:1, :], Alu.mult, Alu.add)
                c.inc(i, "vv", 1)

            @blk.scalar
            def _(sc, gr_ld=gr_ld):
                c.wait(sc, "ld", gr_ld)
                c.inc(sc.activation(gr2b[:, :, :], gr2[:, :, :], AF.Copy), "ev", 1)
            g2b_ev = c.val("ev")

            for is_ in range(2):
                @blk.tensor
                def _(t, is_=is_, g2b_ev=g2b_ev):
                    if is_ == 0:
                        c.wait(t, "ev", g2b_ev)
                    else:
                        c.wait(t, "vv", c.val("vv"))
                    i = None
                    for u in range(8):
                        i = t.transpose(pt[is_][:, u, :],
                                        gr2b[:, is_, u * 128:(u + 1) * 128], ident_b[:, :])
                    c.inc(i, "mm", 1)

                @blk.vector
                def _(v, is_=is_):
                    c.wait(v, "mm")
                    i = v.tensor_copy(gst4[:, 0:8, is_ * 128:(is_ + 1) * 128], pt[is_][:, :, :])
                    c.inc(i, "vv", 1)

            @blk.vector
            def _(v):
                i = v.scalar_tensor_tensor(w2n_f[:, :, :], gst4[:, :, :], -LR, w2n_f[:, :, :], Alu.mult, Alu.add)
                c.inc(i, "vv", 1)
            updv2 = c.val("vv")

            @blk.scalar
            def _(sc, updv2=updv2):
                c.wait(sc, "vv", updv2)
                sc.activation(w1T_b[:, :, :], w1T_f[:, :, :], AF.Copy)
                sc.activation(w2T_b[:, :, :], w2T_f[:, :, :], AF.Copy)
                sc.activation(b1_b[0:1, :], b1_f[0:1, :], AF.Copy)
                sc.activation(b2_b[0:1, :], b2_f[0:1, :], AF.Copy)
                c.inc(sc.activation(w2n_b[:, :, :], w2n_f[:, :, :], AF.Copy), "ev", 1)
            params_ev = c.val("ev")

            for ic in range(2):
                @blk.tensor
                def _(t, ic=ic, params_ev=params_ev):
                    if ic == 0:
                        c.wait(t, "ev", params_ev)
                    bank = pb[ic]
                    for kc in range(8):
                        t.matmul(bank[:, :], w1T_b[:, kc, ic * 128:(ic + 1) * 128],
                                 q_fm[:, kc, :], start=(kc == 0), stop=False)
                    i = t.matmul(bank[:, :], b1_b[0:1, ic * 128:(ic + 1) * 128],
                                 ones_b[0:1, :], start=False, stop=True)
                    c.inc(i, "mm", 1)

                @blk.scalar
                def _(sc, ic=ic):
                    c.wait(sc, "mm")
                    c.inc(sc.activation(h2_fm[:, ic, :], pb[ic][:, :], AF.Gelu), "ev", 1)
            h2_ev = c.val("ev")

            ovv0 = c.val("vv")
            for n in range(8):
                tc, occ = divmod(n, 2)

                @blk.tensor
                def _(t, tc=tc, occ=occ, n=n, h2_ev=h2_ev, ovv0=ovv0):
                    if n == 0:
                        c.wait(t, "ev", h2_ev)
                    if n >= 4:
                        c.wait(t, "vv", ovv0 + n - 3)
                    bank = pb[n % 4]
                    for ic in range(2):
                        t.matmul(bank[:, :], h2_fm[:, ic, tc * 128:(tc + 1) * 128],
                                 w2T_b[:, ic, occ * 512:(occ + 1) * 512],
                                 start=(ic == 0), stop=False)
                    t.matmul(bank[:, :], ones_b[0:1, 0:128],
                             b2_b[0:1, occ * 512:(occ + 1) * 512], start=False, stop=False)
                    i = None
                    for u in range(4):
                        i = t.matmul(bank[:, u * 128:(u + 1) * 128],
                                     q_fm[:, occ * 4 + u, tc * 128:(tc + 1) * 128],
                                     ident_b[:, :], start=False, stop=(u == 3))
                    c.inc(i, "mm", 1)

                @blk.vector
                def _(v, tc=tc, occ=occ, n=n):
                    c.wait(v, "mm")
                    i = v.tensor_tensor(xs[:, tc, occ * 512:(occ + 1) * 512],
                                        pb[n % 4][:, :],
                                        xs[:, tc, occ * 512:(occ + 1) * 512], Alu.add)
                    c.inc(i, "vv", 1)

            @blk.gpsimd
            def _(g, s=s):
                c.wait(g, "vv")
                c.inc(g.dma_start(
                    out=xres[s * TL:(s + 1) * TL, :].rearrange("(t p) o -> p t o", p=128),
                    in_=xs[:, :, :]), "st", 16)

            barrier()

            if s + 1 < nsteps:
                @blk.sync
                def _(sy, s=s):
                    c.inc(sy.dma_start(
                        out=xs[:, :, :],
                        in_=x_in[(s + 1) * TL:(s + 2) * TL, :].rearrange("(t p) o -> p t o", p=128)),
                        "ld", 16)
                ld_xs = c.val("ld")

    def phase_dequant():
        arf.reset()
        arb.reset()
        wchunk = arf.take(2, FF)
        sblk = arf.take(96)
        isblk = arf.take(96)
        sfull = arf.take(FF)
        isfull = arf.take(FF)
        tq_ = arf.take(FF)
        assert arf.off <= AF_ELE, arf.off
        dqo = arb.take(2, FF)
        dqst = arb.take(2, 8, 128)

        jobs = ([("g", oc, wgate[oc * 128:(oc + 1) * 128, :], D) for oc in range(24)]
                + [("u", oc, wup[oc * 128:(oc + 1) * 128, :], D) for oc in range(24)]
                + [("d", oc, wdown[oc * 128:(oc + 1) * 128, :], FF) for oc in range(8)])
        dq_vv = {}
        st_done = {}

        for j, (kind, oc, src_, w) in enumerate(jobs):
            nblk = w // 32
            nct = w // 128

            @blk.sync
            def _(sy, j=j, src_=src_, w=w):
                if j >= 2:
                    c.wait(sy, "vv", dq_vv[j - 2])
                c.inc(sy.dma_start(out=wchunk[:, j % 2, 0:w], in_=src_), "ld", 16)
            ldt = c.val("ld")

            @blk.vector
            def _(v, j=j, w=w, nblk=nblk, ldt=ldt):
                c.wait(v, "ld", ldt)
                c.drain(v, v.tensor_reduce(
                    sblk[:, 0:nblk].rearrange("p (b o) -> p b o", o=1),
                    wchunk[:, j % 2, 0:w].rearrange("p (b q) -> p b q", q=32),
                    AX, Alu.max, apply_absolute_value=True))
                c.drain(v, v.tensor_scalar(sblk[:, 0:nblk], sblk[:, 0:nblk],
                                           1.0 / 7.0, 1e-12, Alu.mult, Alu.max))
                c.drain(v, v.reciprocal(isblk[:, 0:nblk], sblk[:, 0:nblk]))
                v.tensor_copy(sfull[:, 0:w].rearrange("p (b q) -> p b q", q=32),
                              sblk[:, 0:nblk].rearrange("p (b o) -> p b o", o=1).broadcast_to([128, nblk, 32]))
                c.drain(v, v.tensor_copy(isfull[:, 0:w].rearrange("p (b q) -> p b q", q=32),
                                         isblk[:, 0:nblk].rearrange("p (b o) -> p b o", o=1).broadcast_to([128, nblk, 32])))
                v.tensor_tensor(tq_[:, 0:w], wchunk[:, j % 2, 0:w], isfull[:, 0:w], Alu.mult)
                v.tensor_scalar(tq_[:, 0:w], tq_[:, 0:w], MAGIC, MAGIC, Alu.add, Alu.subtract)
                i = v.tensor_tensor(dqo[:, j % 2, 0:w], tq_[:, 0:w], sfull[:, 0:w], Alu.mult)
                dq_vv[j] = c.inc(i, "vv", 1)

            for grp in range((nct + 7) // 8):
                n = min(8, nct - grp * 8)

                @blk.tensor
                def _(t, j=j, grp=grp, n=n):
                    c.wait(t, "vv", c.val("vv"))
                    if j >= 2 and grp == 0:
                        c.wait(t, "st", st_done[j - 2])
                    i = None
                    for u in range(n):
                        i = t.transpose(pt[j % 2][:, u, :],
                                        dqo[:, j % 2, (grp * 8 + u) * 128:(grp * 8 + u + 1) * 128],
                                        ident_b[:, :])
                    c.inc(i, "mm", 1)

                @blk.vector
                def _(v, j=j, grp=grp, n=n):
                    c.wait(v, "mm")
                    if grp >= 1:
                        c.wait(v, "st", st_done[j])
                    i = v.tensor_copy(dqst[:, j % 2, 0:n, :], pt[j % 2][:, 0:n, :])
                    c.inc(i, "vv", 1)

                @blk.gpsimd
                def _(g, j=j, kind=kind, oc=oc, grp=grp, n=n):
                    c.wait(g, "vv")
                    if kind == "g":
                        dst = wgT_dq[oc].rearrange("k p f -> p k f")
                    elif kind == "u":
                        dst = wuT_dq[oc].rearrange("k p f -> p k f")
                    else:
                        ic0 = grp * 8
                        dst = wdT_dq[ic0:ic0 + n, :, oc * 128:(oc + 1) * 128].rearrange("i p f -> p i f")
                    st_done[j] = c.inc(g.dma_start(out=dst, in_=dqst[:, j % 2, 0:n, :]), "st", 16)

        barrier()

    def phase_ffn(src, dst, mxb):
        arf.reset()
        arb.reset()
        p3_vv, p4_vv, p5_vv, p5_st, p5_mm, p1_mm = [], [], [], [], [], []
        xtile = arf.take(2, D)
        xc_s = arf.take(D)
        sq_s = arf.take(D)
        t3a = arf.take(4096)
        t1q = t3a
        t3b = arf.take(4096)
        ost = arf.take(2, 2, D)
        xr4 = arf.at(0, 2, 2, D)
        assert arf.off <= AF_ELE, arf.off
        xnT_f = arb.take(8, 4096)
        wslab = arb.take(2, 8, 128)
        evslab = arb.take(2, 4096)
        xnstg = arb.take(D)
        mcastb = arb.take(128)[:, 0:1]
        assert arb.off <= AB_ELE, arb.off
        ldg = arb.at(0, 2, 4096)
        ldu = arb.at(8192, 2, 4096)
        g2sl = arb.at(16384, 2, 4096)
        t4sl = arb.at(24576, 2, 4096)
        wdT_all = arb.at(0, 24, D)
        oqin = arb.at(24576, 2, 24, 256)
        oqb = arb.at(36864, 24, 256)

        @blk.vector
        def _(v):
            v.memset(colw[:, 24:27], 0.0)
            i = v.memset(colw[:, 27:28], 1.0)
            c.inc(i, "vv", 1)

        p1_vv = []
        for tc in range(32):
            @blk.sync
            def _(sy, tc=tc):
                if tc >= 2:
                    c.wait(sy, "vv", p1_vv[tc - 2])
                c.inc(sy.dma_start(out=xtile[:, tc % 2, :],
                                   in_=src[tc * 128:(tc + 1) * 128, :]), "ld", 16)
            ldt = c.val("ld")

            @blk.vector
            def _(v, tc=tc, ldt=ldt):
                c.wait(v, "ld", ldt)
                c.drain(v, v.tensor_reduce(colw[:, 0:1], xtile[:, tc % 2, :], AX, Alu.add))
                c.drain(v, v.tensor_scalar(colw[:, 1:2], colw[:, 0:1], -1.0 / D, None, Alu.mult))
                i = v.tensor_scalar(xc_s[:, :], xtile[:, tc % 2, :], colw[:, 1:2], None, Alu.add)
                c.inc(i, "vv", 1)

            @blk.scalar
            def _(sc):
                c.wait(sc, "vv")
                c.inc(sc.activation(sq_s[:, :], xc_s[:, :], AF.Square), "ev", 1)

            @blk.vector
            def _(v):
                c.wait(v, "ev")
                c.drain(v, v.tensor_reduce(colw[:, 2:3], sq_s[:, :], AX, Alu.add))
                i = v.tensor_scalar(colw[:, 2:3], colw[:, 2:3], 1.0 / D, EPS, Alu.mult, Alu.add)
                c.inc(i, "vv", 1)

            @blk.scalar
            def _(sc):
                c.wait(sc, "vv")
                c.inc(sc.activation(colw[:, 3:4], colw[:, 2:3], AF.Sqrt), "ev", 1)

            @blk.vector
            def _(v, tc=tc):
                c.wait(v, "ev")
                if tc >= 1:
                    c.wait(v, "mm", p1_mm[tc - 1])
                c.drain(v, v.reciprocal(colw[:, 3:4], colw[:, 3:4]))
                c.drain(v, v.tensor_scalar(xnstg[:, :], xc_s[:, :], colw[:, 3:4], None, Alu.mult))
                c.drain(v, v.reduce_max(colw[:, 8:9], xnstg[:, :], AX, apply_absolute_value=True))
                i = v.tensor_tensor(colw[:, 24:25], colw[:, 24:25], colw[:, 8:9], Alu.max)
                p1_vv.append(c.inc(i, "vv", 1))

            @blk.tensor
            def _(t, tc=tc):
                c.wait(t, "vv", c.val("vv"))
                i = None
                for kc in range(8):
                    i = t.transpose(pt[tc % 2][:, kc, :], xnstg[:, kc * 128:(kc + 1) * 128], ident_b[:, :])
                p1_mm.append(c.inc(i, "mm", 1))

            @blk.vector
            def _(v, tc=tc):
                c.wait(v, "mm")
                i = v.tensor_copy(xnT_f[:, 0:8, tc * 128:(tc + 1) * 128], pt[tc % 2][:, :, :])
                c.inc(i, "vv", 1)

        def gmax(acc_col, k, out_col):
            @blk.vector
            def _(v):
                c.wait(v, "vv", c.val("vv"))
                i = v.tensor_copy(mcastb, colw[:, acc_col:acc_col + 1])
                c.inc(i, "vv", 1)

            @blk.tensor
            def _(t):
                c.wait(t, "vv", c.val("vv"))
                c.wait(t, "ev", c.val("ev"))
                i = t.matmul(pm[0:1, 0:128], mcastb, ident_b[:, :], start=True, stop=True)
                c.inc(i, "mm", 1)

            @blk.vector
            def _(v):
                c.wait(v, "mm")
                i = v.reduce_max(colw[0:1, 8:9], pm[0:1, 0:128], AX)
                c.inc(i, "vv", 1)

            @blk.gpsimd
            def _(g, k=k):
                c.wait(g, "vv")
                c.inc(g.dma_start(out=mxi[k], in_=colw[0:1, 8:9]), "st", 16)
                c.wait(g, "st")
                i = g.collective_compute("AllReduce", Alu.max, replica_groups=[list(range(NCORE))],
                                         ins=[mxi[k].opt()], outs=[mxo[k].opt()])
                c.inc(i, "cc", 1)

            @blk.sync
            def _(sy, k=k, out_col=out_col):
                c.wait(sy, "cc")
                c.inc(sy.dma_start(out=colw[:, out_col:out_col + 1],
                                   in_=mxo[k][0:1, 0:1].broadcast_to([128, 1])), "ld", 16)
            return c.val("ld")

        def scale_ops(ldt, s_col, inv_col=None, extra=None):
            @blk.vector
            def _(v):
                c.wait(v, "ld", ldt)
                c.drain(v, v.tensor_scalar(colw[:, s_col:s_col + 1], colw[:, s_col:s_col + 1],
                                           1.0 / 32767.0, 1e-12, Alu.mult, Alu.max))
                if inv_col is not None:
                    c.drain(v, v.reciprocal(colw[:, inv_col:inv_col + 1], colw[:, s_col:s_col + 1]))
                if extra is not None:
                    extra(v)
                i = v.engine_nop()
                c.inc(i, "pp", 1)
            return c.val("pp")

        pp_s0 = scale_ops(gmax(24, mxb + 0, 10), 10, 11)
        barrier()

        for kc in range(8):
            @blk.scalar
            def _(sc, kc=kc):
                if kc > 0:
                    c.wait(sc, "vv", c.val("vv"))
                c.inc(sc.activation(t1q[:, :], xnT_f[:, kc, :], AF.Identity,
                                    bias=consts[:, 0:1], scale=colw[:, 11:12]), "ev", 1)

            @blk.vector
            def _(v, kc=kc):
                c.wait(v, "ev")
                i = v.tensor_scalar(xnT_f[:, kc, :], t1q[:, :], MAGIC, colw[:, 10:11],
                                    Alu.subtract, Alu.mult)
                c.inc(i, "vv", 1)
        barrier()

        @blk.vector
        def _(v):
            i = v.memset(colw[:, 24:26], 0.0)
            c.inc(i, "vv", 1)
        barrier()

        for widx, (cache, spill, acc_col) in enumerate(((wgT_dq, gate_s, 24), (wuT_dq, up_s, 25))):
            mm_sp = {}
            ev_b0 = c.val("ev")
            st_b0 = c.val("st")
            for i_ in range(24):
                @blk.sync
                def _(sy, i_=i_, cache=cache):
                    if i_ >= 2:
                        c.wait(sy, "mm", mm_sp[i_ - 2])
                    c.inc(sy.dma_start(out=wslab[:, i_ % 2, :, :],
                                       in_=cache[i_].rearrange("k p f -> p k f")), "ld", 16)
                ldt = c.val("ld")

                @blk.tensor
                def _(t, i_=i_, ldt=ldt, ev_b0=ev_b0):
                    c.wait(t, "ld", ldt)
                    i2 = None
                    for tch in range(8):
                        n = i_ * 8 + tch
                        if n >= 4:
                            c.wait(t, "ev", ev_b0 + n - 3)
                        bank = pb[n % 4]
                        for kc in range(8):
                            i2 = t.matmul(bank[:, :], wslab[:, i_ % 2, kc, :],
                                          xnT_f[:, kc, tch * 512:(tch + 1) * 512],
                                          start=(kc == 0), stop=(kc == 7))
                        c.inc(i2, "mm", 1)
                mm_sp[i_] = c.val("mm")

                @blk.scalar
                def _(sc, i_=i_, st_b0=st_b0):
                    for tch in range(8):
                        c.wait(sc, "mm", mm_sp[i_] - 7 + tch)
                        if i_ >= 2 and tch == 0:
                            c.wait(sc, "st", st_b0 + 16 * (i_ - 1))
                        n = i_ * 8 + tch
                        c.inc(sc.activation(evslab[:, i_ % 2, tch * 512:(tch + 1) * 512],
                                            pb[n % 4][:, :], AF.Copy), "ev", 1)

                @blk.vector
                def _(v, i_=i_, acc_col=acc_col):
                    for tch in range(8):
                        c.wait(v, "ev")
                        c.drain(v, v.reduce_max(colw[:, 8:9],
                                                evslab[:, i_ % 2, tch * 512:(tch + 1) * 512],
                                                AX, apply_absolute_value=True))
                        i2 = v.tensor_tensor(colw[:, acc_col:acc_col + 1],
                                             colw[:, acc_col:acc_col + 1], colw[:, 8:9], Alu.max)
                        c.inc(i2, "vv", 1)

                @blk.gpsimd
                def _(g, i_=i_, spill=spill):
                    c.wait(g, "ev", c.val("ev"))
                    c.inc(g.dma_start(out=spill[i_], in_=evslab[:, i_ % 2, :]), "st", 16)
            barrier()
            if widx == 0:
                def ex2(v):
                    v.tensor_scalar(colw[:, 20:21], colw[:, 14:15], -MAGIC, None, Alu.mult)
                    v.tensor_scalar(colw[:, 21:22], colw[:, 14:15], 1.0 / 65536.0, None, Alu.mult)
                scale_ops(gmax(24, mxb + 1, 14), 14, 15, extra=ex2)
            else:
                def ex1(g):
                    pass
                scale_ops(gmax(25, mxb + 2, 12), 12, 13)
        barrier()

        @blk.vector
        def _(v):
            i = v.memset(colw[:, 24:25], 0.0)
            c.inc(i, "vv", 1)
        for i_ in range(24):
            @blk.sync
            def _(sy, i_=i_):
                if i_ >= 2:
                    c.wait(sy, "vv", p3_vv[i_ - 2])
                c.inc(sy.dma_start(out=ldg[:, i_ % 2, :], in_=gate_s[i_]), "ld", 16)
            ldt = c.val("ld")

            @blk.scalar
            def _(sc, i_=i_, ldt=ldt):
                c.wait(sc, "ld", ldt)
                if i_ >= 1:
                    c.wait(sc, "vv", c.val("vv"))
                sc.activation(t3a[:, :], ldg[:, i_ % 2, :], AF.Identity,
                              bias=consts[:, 0:1], scale=colw[:, 15:16])
                sc.activation(t3b[:, :], t3a[:, :], AF.Sigmoid,
                              bias=colw[:, 20:21], scale=colw[:, 14:15])
                i2 = sc.activation(t3b[:, :], t3b[:, :], AF.Identity,
                                   bias=consts[:, 0:1], scale=consts[:, 2:3])
                c.inc(i2, "ev", 1)

            @blk.vector
            def _(v, i_=i_):
                c.wait(v, "ev")
                v.tensor_scalar(t3b[:, :], t3b[:, :], MAGIC, 65535.0, Alu.subtract, Alu.min)
                v.tensor_scalar(t3b[:, :], t3b[:, :], colw[:, 21:22], None, Alu.mult)
                v.scalar_tensor_tensor(g2sl[:, i_ % 2, :], t3a[:, :], MAGIC, t3b[:, :],
                                       Alu.subtract, Alu.mult)
                c.drain(v, v.reduce_max(colw[:, 8:9], g2sl[:, i_ % 2, :], AX,
                                        apply_absolute_value=True))
                i2 = v.tensor_tensor(colw[:, 24:25], colw[:, 24:25], colw[:, 8:9], Alu.max)
                p3_vv.append(c.inc(i2, "vv", 1))

            @blk.gpsimd
            def _(g, i_=i_):
                c.wait(g, "vv")
                c.inc(g.dma_start(out=g2_s[i_], in_=g2sl[:, i_ % 2, :]), "st", 16)
        barrier()

        def ex3(v):
            v.tensor_tensor(colw[:, 17:18], colw[:, 12:13], colw[:, 16:17], Alu.mult)
        scale_ops(gmax(24, mxb + 3, 16), 16, None, extra=ex3)
        @blk.vector
        def _(v):
            c.wait(v, "pp", c.val("pp"))
            c.drain(v, v.reciprocal(colw[:, 22:23], colw[:, 16:17]))
            i = v.engine_nop()
            c.inc(i, "pp", 1)
        barrier()

        @blk.vector
        def _(v):
            i = v.memset(colw[:, 24:25], 0.0)
            c.inc(i, "vv", 1)
        for i_ in range(24):
            @blk.sync
            def _(sy, i_=i_):
                if i_ >= 2:
                    c.wait(sy, "vv", p4_vv[i_ - 2])
                c.inc(sy.dma_start(out=ldg[:, i_ % 2, :], in_=g2_s[i_]), "ld", 16)
                c.inc(sy.dma_start(out=ldu[:, i_ % 2, :], in_=up_s[i_]), "ld", 16)
            ldt = c.val("ld")

            @blk.scalar
            def _(sc, i_=i_, ldt=ldt):
                c.wait(sc, "ld", ldt)
                if i_ >= 1:
                    c.wait(sc, "vv", c.val("vv"))
                sc.activation(t3a[:, :], ldu[:, i_ % 2, :], AF.Identity,
                              bias=consts[:, 0:1], scale=colw[:, 13:14])
                i2 = sc.activation(t3b[:, :], ldg[:, i_ % 2, :], AF.Identity,
                                   bias=consts[:, 0:1], scale=colw[:, 22:23])
                c.inc(i2, "ev", 1)

            @blk.vector
            def _(v, i_=i_):
                c.wait(v, "ev")
                v.tensor_scalar(t3a[:, :], t3a[:, :], MAGIC, colw[:, 17:18],
                                Alu.subtract, Alu.mult)
                v.scalar_tensor_tensor(t4sl[:, i_ % 2, :], t3b[:, :], MAGIC, t3a[:, :],
                                       Alu.subtract, Alu.mult)
                c.drain(v, v.reduce_max(colw[:, 8:9], t4sl[:, i_ % 2, :], AX,
                                        apply_absolute_value=True))
                i2 = v.tensor_tensor(colw[:, 24:25], colw[:, 24:25], colw[:, 8:9], Alu.max)
                p4_vv.append(c.inc(i2, "vv", 1))

            @blk.gpsimd
            def _(g, i_=i_):
                c.wait(g, "vv")
                c.inc(g.dma_start(out=t4_s[i_], in_=t4sl[:, i_ % 2, :]), "st", 16)
        barrier()
        scale_ops(gmax(24, mxb + 4, 18), 18, 19)
        barrier()

        @blk.sync
        def _(sy):
            c.inc(sy.dma_start(out=wdT_all[:, :, :],
                               in_=wdT_dq.rearrange("i p o -> p i o")), "ld", 16)
        wd_ld = c.val("ld")

        for tb in range(16):
            @blk.sync
            def _(sy, tb=tb):
                if tb >= 2:
                    c.wait(sy, "vv", p5_vv[tb - 2])
                c.inc(sy.dma_start(out=oqin[:, tb % 2, :, :],
                                   in_=t4_s[:, :, tb * 256:(tb + 1) * 256].rearrange("i p t -> p i t")), "ld", 16)
                c.inc(sy.dma_start(out=xr4[:, tb % 2, :, :],
                                   in_=src[tb * 256:(tb + 1) * 256, :].rearrange("(t p) o -> p t o", p=128)), "ld", 16)
            ldt = c.val("ld")

            @blk.scalar
            def _(sc, tb=tb, ldt=ldt):
                c.wait(sc, "ld", ldt)
                if tb >= 1:
                    c.wait(sc, "vv", c.val("vv"))
                i2 = sc.activation(t3a[:, 0:4096].rearrange("p (a b) -> p a b", a=16)
                                   if False else t3a[:, :], oqin[:, tb % 2, :, :].rearrange("p a b -> p (a b)")[:, 0:4096],
                                   AF.Identity, bias=consts[:, 0:1], scale=colw[:, 19:20])
                i3 = sc.activation(t3b[:, 0:2048], oqin[:, tb % 2, :, :].rearrange("p a b -> p (a b)")[:, 4096:6144],
                                   AF.Identity, bias=consts[:, 0:1], scale=colw[:, 19:20])
                c.inc(i3, "ev", 1)

            @blk.vector
            def _(v, tb=tb):
                c.wait(v, "ev")
                if tb >= 1:
                    c.wait(v, "mm", p5_mm[tb - 1])
                if tb >= 2:
                    c.wait(v, "st", p5_st[tb - 2])
                v.tensor_scalar(oqb[:, :, :].rearrange("p a b -> p (a b)")[:, 0:4096],
                                t3a[:, :], MAGIC, None, Alu.subtract)
                i2 = v.tensor_scalar(oqb[:, :, :].rearrange("p a b -> p (a b)")[:, 4096:6144],
                                     t3b[:, 0:2048], MAGIC, None, Alu.subtract)
                c.inc(i2, "vv", 1)
            oq_vv = c.val("vv")

            for u in range(4):
                ts2, occ = divmod(u, 2)

                @blk.tensor
                def _(t, tb=tb, ts2=ts2, occ=occ, u=u, oq_vv=oq_vv, wd_ld=wd_ld):
                    if tb == 0 and u == 0:
                        c.wait(t, "ld", wd_ld)
                    c.wait(t, "vv", oq_vv)
                    if tb >= 1:
                        c.wait(t, "vv", c.val("vv"))
                    bank = pb[u]
                    i2 = None
                    for ic in range(24):
                        i2 = t.matmul(bank[:, :], oqb[:, ic, ts2 * 128:(ts2 + 1) * 128],
                                      wdT_all[:, ic, occ * 512:(occ + 1) * 512],
                                      start=(ic == 0), stop=(ic == 23))
                    mv = c.inc(i2, "mm", 1)
                    if u == 3:
                        p5_mm.append(mv)

                @blk.vector
                def _(v, tb=tb, ts2=ts2, occ=occ, u=u):
                    c.wait(v, "mm")
                    i2 = v.scalar_tensor_tensor(ost[:, tb % 2, ts2, occ * 512:(occ + 1) * 512],
                                                pb[u][:, :], colw[:, 18:19],
                                                xr4[:, tb % 2, ts2, occ * 512:(occ + 1) * 512],
                                                Alu.mult, Alu.add)
                    if u == 3:
                        p5_vv.append(c.inc(i2, "vv", 1))
                    else:
                        c.inc(i2, "vv", 1)

            @blk.gpsimd
            def _(g, tb=tb):
                c.wait(g, "vv")
                p5_st.append(c.inc(g.dma_start(
                    out=dst[tb * 256:(tb + 1) * 256, :].rearrange("(t p) o -> p t o", p=128),
                    in_=ost[:, tb % 2, :, :]), "st", 16))
        barrier()


    def phase_attn_b():
        TWO_PI = 2.0 * PI

        def emit_ln(xsrc_tile, xnstg_ap, tc_guard_mm=None):
            @blk.vector
            def _(v):
                c.drain(v, v.tensor_reduce(colw[:, 0:1], xsrc_tile, AX, Alu.add))
                c.drain(v, v.tensor_scalar(colw[:, 1:2], colw[:, 0:1], -1.0 / D, None, Alu.mult))
                i = v.tensor_scalar(xc_s2[:, :], xsrc_tile, colw[:, 1:2], None, Alu.add)
                c.inc(i, "vv", 1)

            @blk.scalar
            def _(sc):
                c.wait(sc, "vv")
                c.inc(sc.activation(sq_s2[:, :], xc_s2[:, :], AF.Square), "ev", 1)

            @blk.vector
            def _(v):
                c.wait(v, "ev")
                c.drain(v, v.tensor_reduce(colw[:, 2:3], sq_s2[:, :], AX, Alu.add))
                i = v.tensor_scalar(colw[:, 2:3], colw[:, 2:3], 1.0 / D, EPS, Alu.mult, Alu.add)
                c.inc(i, "vv", 1)

            @blk.scalar
            def _(sc):
                c.wait(sc, "vv")
                c.inc(sc.activation(colw[:, 3:4], colw[:, 2:3], AF.Sqrt), "ev", 1)

            @blk.vector
            def _(v, tc_guard_mm=tc_guard_mm):
                c.wait(v, "ev")
                if tc_guard_mm is not None:
                    c.wait(v, "mm", tc_guard_mm)
                c.drain(v, v.reciprocal(colw[:, 3:4], colw[:, 3:4]))
                i = v.tensor_scalar(xnstg_ap, xc_s2[:, :], colw[:, 3:4], None, Alu.mult)
                c.inc(i, "vv", 1)

        def emit_trig(src_f32, cos_dst, sin_dst):
            @blk.vector
            def _(v):
                c.wait(v, "ev", c.val("ev"))
                v.tensor_scalar(u1_s[:, :], src_f32, 1.0 / TWO_PI, MAGIC, Alu.mult, Alu.add)
                v.tensor_scalar(u1_s[:, :], u1_s[:, :], MAGIC, TWO_PI, Alu.subtract, Alu.mult)
                v.scalar_tensor_tensor(u1_s[:, :], u1_s[:, :], -1.0, src_f32, Alu.mult, Alu.add)
                v.tensor_scalar(u2_s[:, :], src_f32, PI / 2, 1.0 / TWO_PI, Alu.add, Alu.mult)
                v.tensor_scalar(u2_s[:, :], u2_s[:, :], MAGIC, MAGIC, Alu.add, Alu.subtract)
                v.tensor_scalar(u2_s[:, :], u2_s[:, :], TWO_PI, None, Alu.mult)
                i = v.scalar_tensor_tensor(u2_s[:, :], u2_s[:, :], -1.0, src_f32, Alu.mult, Alu.add)
                c.inc(i, "vv", 1)

            @blk.scalar
            def _(sc):
                c.wait(sc, "vv")
                sc.activation(sin_dst, u1_s[:, :], AF.Sin)
                i = sc.activation(cos_dst, u2_s[:, :], AF.Sin, bias=consts[:, 1:2])
                c.inc(i, "ev", 1)

        arf.reset()
        arb.reset()
        x2til = arf.take(4, D)
        xc_s2 = arf.take(D)
        sq_s2 = arf.take(D)
        projh = arf.take(4, 256)
        sqK = arf.take(4, 64)
        u1_s = arf.take(D)
        u2_s = arf.take(D)
        kvstg = arf.take(16, 264)
        assert arf.off <= AF_ELE, arf.off
        wkTb = arb.take(8, D)
        wvTb = arb.take(8, D)
        xnT_ab = arb.take(8, TL)
        K_fm = arb.take(8, TL)
        V_tm = arb.take(4, D)
        xnstg2 = arb.take(D)
        Kfcs = arb.take(4, 512)
        Vsb = arb.take(4, 66)
        rfIb2 = arb.take(2, 320)
        assert arb.off <= AB_ELE, arb.off

        @blk.gpsimd
        def _(g):
            c.inc(g.dma_start(out=wkTb[:, :, :], in_=wkT.rearrange("(k p) o -> p k o", p=128)), "cst", 16)
            c.inc(g.dma_start(out=wvTb[:, :, :], in_=wvT.rearrange("(k p) o -> p k o", p=128)), "cst", 16)
            i = g.memset(rfIb2[:, :, :], 0.0)
            c.inc(i, "pp", 1)
            g.wait_ge(c.sem["pp"], c.cnt["pp"])
            c.inc(g.dma_start(out=rfIb2[0:64, 0, 0:256], in_=rfTi), "cst", 16)
            c.inc(g.dma_start(out=rfIb2[64:128, 1, 0:256], in_=rfTi), "cst", 16)
        cst_k = c.val("cst")

        @blk.vector
        def _(v):
            c.wait(v, "cst", cst_k)
            c.wait(v, "pp", c.val("pp"))
            v.tensor_copy(rfIb2[0:64, 0, 256:320], ident_b[0:64, 0:64])
            i = v.tensor_copy(rfIb2[64:128, 1, 256:320], ident_b[64:128, 64:128])
            c.inc(i, "vv", 1)
        barrier()

        for b in range(NB):
            @blk.sync
            def _(sy, b=b):
                c.inc(sy.dma_start(out=x2til[:, :, :],
                                   in_=xres[b * TL:(b + 1) * TL, :].rearrange("(t p) o -> p t o", p=128)),
                      "ld", 16)
            ldt = c.val("ld")

            @blk.vector
            def _(v, ldt=ldt):
                c.wait(v, "ld", ldt)

            for tc in range(4):
                tgm = None if tc == 0 else c.val("mm")
                emit_ln(x2til[:, tc, :], xnstg2[:, :], tc_guard_mm=tgm)

                @blk.tensor
                def _(t, tc=tc):
                    c.wait(t, "vv", c.val("vv"))
                    i = None
                    for kc in range(8):
                        i = t.transpose(pt[tc % 2][:, kc, :], xnstg2[:, kc * 128:(kc + 1) * 128], ident_b[:, :])
                    c.inc(i, "mm", 1)

                @blk.vector
                def _(v, tc=tc):
                    c.wait(v, "mm")
                    i = v.tensor_copy(xnT_ab[:, 0:8, tc * 128:(tc + 1) * 128], pt[tc % 2][:, :, :])
                    c.inc(i, "vv", 1)
            xn_vv = c.val("vv")

            @blk.gpsimd
            def _(g, b=b):
                c.wait(g, "vv", xn_vv)
                c.inc(g.dma_start(out=xnT_sp[b], in_=xnT_ab[:, :, :].rearrange("p a b -> p (a b)")), "st", 16)

            ev_k0 = c.val("ev")
            for oc in range(8):
                @blk.tensor
                def _(t, oc=oc, ev_k0=ev_k0):
                    if oc >= 4:
                        c.wait(t, "ev", ev_k0 + oc - 3)
                    bank = pb[oc % 4]
                    i = None
                    for kc in range(8):
                        i = t.matmul(bank[:, :], wkTb[:, kc, oc * 128:(oc + 1) * 128],
                                     xnT_ab[:, kc, :], start=(kc == 0), stop=(kc == 7))
                    c.inc(i, "mm", 1)

                @blk.scalar
                def _(sc, oc=oc):
                    c.wait(sc, "mm")
                    c.inc(sc.activation(K_fm[:, oc, :], pb[oc % 4][:, :], AF.Copy), "ev", 1)

            ev_v0 = c.val("ev")
            for u in range(8):
                ts_, occ = divmod(u, 2)

                @blk.tensor
                def _(t, ts_=ts_, occ=occ, u=u, ev_v0=ev_v0):
                    if u >= 4:
                        c.wait(t, "ev", ev_v0 + u - 3)
                    bank = pb[u % 4]
                    i = None
                    for kc in range(8):
                        i = t.matmul(bank[:, :], xnT_ab[:, kc, ts_ * 128:(ts_ + 1) * 128],
                                     wvTb[:, kc, occ * 512:(occ + 1) * 512],
                                     start=(kc == 0), stop=(kc == 7))
                    c.inc(i, "mm", 1)

                @blk.scalar
                def _(sc, ts_=ts_, occ=occ, u=u):
                    c.wait(sc, "mm")
                    c.inc(sc.activation(V_tm[:, ts_, occ * 512:(occ + 1) * 512],
                                        pb[u % 4][:, :], AF.Copy), "ev", 1)
            kv_ready_ev = c.val("ev")

            @blk.vector
            def _(v):
                i = v.memset(Vsb[:, :, 65:66], 0.0)
                c.inc(i, "vv", 1)

            for h in range(16):
                pbase = (h % 2) * 64
                oc_h = h // 2

                for ts_ in range(4):
                    @blk.tensor
                    def _(t, h=h, ts_=ts_, pbase=pbase, oc_h=oc_h, kv_ready_ev=kv_ready_ev):
                        if h == 0 and ts_ == 0:
                            c.wait(t, "ev", kv_ready_ev)
                            c.wait(t, "vv", c.val("vv"))
                        c.wait(t, "ev", c.val("ev"))
                        i = t.matmul(pm[:, 0:320],
                                     K_fm[:, oc_h, ts_ * 128:(ts_ + 1) * 128],
                                     rfIb2[:, h % 2, :], start=True, stop=True)
                        c.inc(i, "mm", 1)

                    @blk.scalar
                    def _(sc, h=h, ts_=ts_):
                        c.wait(sc, "mm")
                        sc.activation(projh[:, ts_, :], pm[:, 0:256], AF.Copy)
                        c.inc(sc.activation(sqK[:, ts_, :], pm[:, 256:320], AF.Square), "ev", 1)

                @blk.vector
                def _(v, h=h):
                    c.wait(v, "ev")
                    c.drain(v, v.tensor_reduce(colw[:, 4:8].rearrange("p (a o) -> p a o", o=1),
                                               sqK[:, :, :], AX, Alu.add))
                    i = v.engine_nop()
                    c.inc(i, "vv", 1)

                @blk.scalar
                def _(sc):
                    c.wait(sc, "vv")
                    c.inc(sc.activation(colw[:, 4:8], colw[:, 4:8], AF.Exp, scale=-0.5), "ev", 1)

                @blk.vector
                def _(v):
                    c.wait(v, "ev")
                    i = v.tensor_scalar(colw[:, 4:8], colw[:, 4:8], 1.0 / 16.0, None, Alu.mult)
                    c.inc(i, "vv", 1)
                ksc_vv = c.val("vv")

                emit_trig(projh[:, :, :].rearrange("p a b -> p (a b)"),
                          Kfcs[:, :, 0:256], Kfcs[:, :, 256:512])
                trig_ev = c.val("ev")

                @blk.vector
                def _(v, h=h, ksc_vv=ksc_vv):
                    c.wait(v, "vv", ksc_vv)
                    c.wait(v, "mm", c.val("mm"))
                    for ts_ in range(4):
                        v.tensor_scalar(Vsb[:, ts_, 0:64], V_tm[:, ts_, h * 64:(h + 1) * 64],
                                        colw[:, 4 + ts_:5 + ts_], None, Alu.mult)
                    i = v.tensor_copy(Vsb[:, :, 64:65],
                                      colw[:, 4:8].rearrange("p (a o) -> p a o", o=1))
                    c.inc(i, "vv", 1)
                vs_vv = c.val("vv")

                @blk.tensor
                def _(t, h=h, vs_vv=vs_vv, trig_ev=trig_ev):
                    c.wait(t, "vv", vs_vv)
                    c.wait(t, "ev", trig_ev)
                    i = None
                    for fs in range(4):
                        for ts_ in range(4):
                            i = t.matmul(pm2[:, fs * 66:(fs + 1) * 66],
                                         Kfcs[:, ts_, fs * 128:(fs + 1) * 128],
                                         Vsb[:, ts_, :], start=(ts_ == 0), stop=(ts_ == 3))
                    c.inc(i, "mm", 1)

                @blk.scalar
                def _(sc, h=h):
                    c.wait(sc, "mm")
                    c.inc(sc.activation(kvstg[:, h, :], pm2[:, 0:264], AF.Copy), "ev", 1)

            @blk.gpsimd
            def _(g, b=b):
                c.wait(g, "ev", c.val("ev"))
                c.inc(g.dma_start(out=kv_in[b * 128:(b + 1) * 128, :], in_=kvstg[:, :, :]), "st", 16)
            barrier()

        @blk.gpsimd
        def _(g):
            c.wait(g, "st", c.val("st"))
            i = g.collective_compute("AllReduce", Alu.add, replica_groups=[list(range(NCORE))],
                                     ins=[kv_in.opt()], outs=[kv_out.opt()])
            c.inc(i, "cc", 1)
        barrier()

        if dbg:
            global dbg5, dbg5s
            dbg5 = nc.dram_tensor("dbg5", [128, 128], F32, kind="ExternalOutput").ap()
            dbg5s_t = es.enter_context(nc.sbuf_tensor("dbg5s", [128, 128], F32))
            dbg5s = dbg5s_t[:, :]
        arf.reset()
        arb.reset()
        projq = arf.take(2, TL)
        sqQ = arf.take(4, 64)
        u1_s = arf.take(D)
        u2_s = arf.take(D)
        kvb = arf.take(16, 264)
        xr2 = arf.take(4, D)
        ost2 = arf.take(4, D)
        xc_s2 = arf.take(D)
        sq_s2 = arf.take(D)
        assert arf.off <= AF_ELE, arf.off
        wqTb = arb.take(8, D)
        woTb = arb.take(8, D)
        xnT_ab = arb.take(8, TL)
        Q_fm = arb.take(8, TL)
        Qfcs = arb.take(4, TL)
        attT = arb.take(8, TL)
        kvbb = arb.take(16, 264)
        attstg2 = arb.take(4, 128)
        rfIb2 = arb.take(2, 320)
        assert arb.off <= AB_ELE, arb.off

        @blk.gpsimd
        def _(g):
            c.inc(g.dma_start(out=wqTb[:, :, :], in_=wqT.rearrange("(k p) o -> p k o", p=128)), "cst", 16)
            c.inc(g.dma_start(out=woTb[:, :, :], in_=woT.rearrange("(k p) o -> p k o", p=128)), "cst", 16)
            i = g.memset(rfIb2[:, :, :], 0.0)
            c.inc(i, "pp", 1)
            g.wait_ge(c.sem["pp"], c.cnt["pp"])
            c.inc(g.dma_start(out=rfIb2[0:64, 0, 0:256], in_=rfTi), "cst", 16)
            c.inc(g.dma_start(out=rfIb2[64:128, 1, 0:256], in_=rfTi), "cst", 16)
        cst_q = c.val("cst")

        @blk.vector
        def _(v):
            c.wait(v, "cst", cst_q)
            c.wait(v, "pp", c.val("pp"))
            v.tensor_copy(rfIb2[0:64, 0, 256:320], ident_b[0:64, 0:64])
            i = v.tensor_copy(rfIb2[64:128, 1, 256:320], ident_b[64:128, 64:128])
            c.inc(i, "vv", 1)
        barrier()

        @blk.tensor
        def _(t, cst_q=cst_q):
            c.wait(t, "cst", cst_q)

        for b in range(NB):
            @blk.sync
            def _(sy, b=b):
                c.inc(sy.dma_start(out=xnT_ab[:, :, :].rearrange("p a b -> p (a b)"), in_=xnT_sp[b]), "ld", 16)
                c.inc(sy.dma_start(out=kvb[:, :, :].rearrange("p a b -> p (a b)"),
                                   in_=kv_out[b * 128:(b + 1) * 128, :]), "ld", 16)
                c.inc(sy.dma_start(out=xr2[:, :, :],
                                   in_=xres[b * TL:(b + 1) * TL, :].rearrange("(t p) o -> p t o", p=128)), "ld", 16)
            ldt = c.val("ld")

            @blk.vector
            def _(v, ldt=ldt):
                c.wait(v, "ld", ldt)
                i = v.tensor_copy(kvbb[:, :, :], kvb[:, :, :])
                c.inc(i, "vv", 1)
            kvb_vv = c.val("vv")

            ev_q0 = c.val("ev")
            for oc in range(8):
                @blk.tensor
                def _(t, oc=oc, ev_q0=ev_q0, ldt=ldt):
                    if oc == 0:
                        c.wait(t, "ld", ldt)
                    if oc >= 4:
                        c.wait(t, "ev", ev_q0 + oc - 3)
                    bank = pb[oc % 4]
                    i = None
                    for kc in range(8):
                        i = t.matmul(bank[:, :], wqTb[:, kc, oc * 128:(oc + 1) * 128],
                                     xnT_ab[:, kc, :], start=(kc == 0), stop=(kc == 7))
                    c.inc(i, "mm", 1)

                @blk.scalar
                def _(sc, oc=oc):
                    c.wait(sc, "mm")
                    c.inc(sc.activation(Q_fm[:, oc, :], pb[oc % 4][:, :], AF.Copy), "ev", 1)

            for h in range(16):
                pbase = (h % 2) * 64
                oc_h = h // 2

                for ms in range(2):
                    @blk.tensor
                    def _(t, h=h, ms=ms, pbase=pbase, oc_h=oc_h):
                        c.wait(t, "ev", c.val("ev"))
                        i = t.matmul(pm[:, 0:TL], rfIb2[:, h % 2, ms * 128:(ms + 1) * 128],
                                     Q_fm[:, oc_h, :], start=True, stop=True)
                        c.inc(i, "mm", 1)

                    @blk.scalar
                    def _(sc, ms=ms):
                        c.wait(sc, "mm")
                        c.inc(sc.activation(projq[:, ms, :], pm[:, 0:TL], AF.Copy), "ev", 1)

                @blk.tensor
                def _(t, h=h, pbase=pbase, oc_h=oc_h):
                    c.wait(t, "ev", c.val("ev"))
                    i = None
                    for ts_ in range(4):
                        i = t.matmul(pm2[:, ts_ * 64:(ts_ + 1) * 64],
                                     Q_fm[:, oc_h, ts_ * 128:(ts_ + 1) * 128],
                                     ident_b[:, pbase:pbase + 64], start=True, stop=True)
                    c.inc(i, "mm", 1)

                @blk.scalar
                def _(sc):
                    c.wait(sc, "mm")
                    c.inc(sc.activation(sqQ[:, :, :],
                                        pm2[:, 0:256].rearrange("p (a b) -> p a b", a=4),
                                        AF.Square), "ev", 1)

                @blk.vector
                def _(v):
                    c.wait(v, "ev")
                    c.drain(v, v.tensor_reduce(colw[:, 4:8].rearrange("p (a o) -> p a o", o=1),
                                               sqQ[:, :, :], AX, Alu.add))
                    i = v.engine_nop()
                    c.inc(i, "vv", 1)

                @blk.scalar
                def _(sc):
                    c.wait(sc, "vv")
                    c.inc(sc.activation(colw[:, 4:8], colw[:, 4:8], AF.Exp, scale=-0.5), "ev", 1)

                @blk.vector
                def _(v):
                    c.wait(v, "ev")
                    c.drain(v, v.tensor_scalar(colw[:, 4:8], colw[:, 4:8], 1.0 / 16.0, None, Alu.mult))
                    i = v.engine_nop()
                    c.inc(i, "vv", 1)
                qsc_vv = c.val("vv")

                if dbg and b == 0 and h == 13:
                    dbg4 = nc.dram_tensor("dbg4", [128, 6144], F32, kind="ExternalOutput").ap()

                    @blk.gpsimd
                    def _(g):
                        c.wait(g, "vv", c.val("vv"))
                        c.inc(g.dma_start(out=dbg4[:, 0:256],
                                          in_=sqQ[:, :, :].rearrange("p a b -> p (a b)")), "st", 16)
                        c.inc(g.dma_start(out=dbg4[:, 256:260], in_=colw[:, 4:8]), "st", 16)
                        c.inc(g.dma_start(out=dbg4[:, 512:1024],
                                          in_=Q_fm[:, 6, :]), "st", 16)

                emit_trig(projq[:, :, :].rearrange("p a b -> p (a b)"),
                          Qfcs[:, 0:2, :].rearrange("p a b -> p (a b)"),
                          Qfcs[:, 2:4, :].rearrange("p a b -> p (a b)"))
                trig_ev = c.val("ev")

                if dbg and b == 0 and h == 13:
                    @blk.gpsimd
                    def _(g):
                        c.wait(g, "ev", trig_ev)
                        c.inc(g.dma_start(out=dbg4[:, 1024:2048],
                                          in_=projq[:, :, :].rearrange("p a b -> p (a b)")), "st", 16)
                        c.inc(g.dma_start(out=dbg4[:, 2048:4096],
                                          in_=Qfcs[:, :, :].rearrange("p a b -> p (a b)")), "st", 16)
                        c.inc(g.dma_start(out=dbg4[:, 4096:4736],
                                          in_=rfIb2[:, :, :].rearrange("p a b -> p (a b)")), "st", 16)
                        c.inc(g.dma_start(out=dbg4[:, 4736:5248],
                                          in_=Q_fm[:, 6, :]), "st", 16)

                for ts_ in range(4):
                    @blk.tensor
                    def _(t, h=h, ts_=ts_, trig_ev=trig_ev, kvb_vv=kvb_vv):
                        if ts_ == 0:
                            c.wait(t, "ev", trig_ev)
                            c.wait(t, "vv", kvb_vv)
                        c.wait(t, "vv", c.val("vv"))
                        bank = pb[ts_]
                        i = None
                        for fc in range(4):
                            i = t.matmul(bank[:, 0:66], Qfcs[:, fc, ts_ * 128:(ts_ + 1) * 128],
                                         kvbb[:, h, fc * 66:(fc + 1) * 66],
                                         start=(fc == 0), stop=(fc == 3))
                        c.inc(i, "mm", 1)

                    @blk.vector
                    def _(v, h=h, ts_=ts_, pbase=pbase):
                        c.wait(v, "mm")
                        c.drain(v, v.tensor_tensor(colw[:, 9:10], colw[:, 4 + ts_:5 + ts_],
                                                   pb[ts_][:, 64:65], Alu.mult))
                        c.drain(v, v.tensor_scalar(colw[:, 9:10], colw[:, 9:10], 1e-6, None, Alu.max))
                        c.drain(v, v.reciprocal(colw[:, 9:10], colw[:, 9:10]))
                        c.drain(v, v.tensor_tensor(colw[:, 9:10], colw[:, 4 + ts_:5 + ts_],
                                                   colw[:, 9:10], Alu.mult))
                        if dbg and b == 0 and h == 13 and ts_ == 3:
                            v.tensor_copy(dbg5s[:, 0:65], pb[ts_][:, 0:65])
                            v.tensor_copy(dbg5s[:, 65:66], colw[:, 9:10])
                            v.tensor_copy(dbg5s[:, 66:67], colw[:, 7:8])
                        i = v.tensor_scalar(attstg2[:, ts_, pbase:pbase + 64], pb[ts_][:, 0:64],
                                            colw[:, 9:10], None, Alu.mult)
                        c.inc(i, "vv", 1)

                if h % 2 == 1:
                    @blk.tensor
                    def _(t, oc_h=oc_h):
                        c.wait(t, "vv", c.val("vv"))
                        i = None
                        for ts_ in range(4):
                            i = t.transpose(pt[1][:, ts_, :], attstg2[:, ts_, :], ident_b[:, :])
                        c.inc(i, "mm", 1)

                    @blk.vector
                    def _(v, oc_h=oc_h):
                        c.wait(v, "mm")
                        i = v.tensor_copy(attT[:, oc_h, :], pt[1][:, 0:4, :])
                        c.inc(i, "vv", 1)
            att_vv = c.val("vv")

            if dbg and b == 0:
                @blk.gpsimd
                def _(g):
                    c.wait(g, "vv", att_vv)
                    c.inc(g.dma_start(out=dbg5, in_=dbg5s), "st", 16)

                dbg3 = nc.dram_tensor("dbg3", [128, 4096], F32, kind="ExternalOutput").ap()

                @blk.gpsimd
                def _(g):
                    c.wait(g, "vv", att_vv)
                    c.inc(g.dma_start(out=dbg3, in_=attT[:, :, :].rearrange("p a b -> p (a b)")), "st", 16)

            ev_o0 = c.val("ev")
            for u in range(8):
                ts_, occ = divmod(u, 2)

                @blk.tensor
                def _(t, ts_=ts_, occ=occ, u=u, att_vv=att_vv, ev_o0=ev_o0):
                    if u == 0:
                        c.wait(t, "vv", att_vv)
                    if u >= 4:
                        c.wait(t, "vv", c.val("vv"))
                    bank = pb[u % 4]
                    i = None
                    for cc in range(8):
                        i = t.matmul(bank[:, :], attT[:, cc, ts_ * 128:(ts_ + 1) * 128],
                                     woTb[:, cc, occ * 512:(occ + 1) * 512],
                                     start=(cc == 0), stop=(cc == 7))
                    c.inc(i, "mm", 1)

                @blk.vector
                def _(v, ts_=ts_, occ=occ, u=u):
                    c.wait(v, "mm")
                    i = v.tensor_tensor(ost2[:, ts_, occ * 512:(occ + 1) * 512],
                                        pb[u % 4][:, :],
                                        xr2[:, ts_, occ * 512:(occ + 1) * 512], Alu.add)
                    c.inc(i, "vv", 1)

            @blk.gpsimd
            def _(g, b=b):
                c.wait(g, "vv", c.val("vv"))
                c.inc(g.dma_start(
                    out=xres[b * TL:(b + 1) * TL, :].rearrange("(t p) o -> p t o", p=128),
                    in_=ost2[:, :, :]), "st", 16)
            barrier()

    phase_setup()
    phase_attn_a()
    if stage >= 2:
        phase_dequant()
        phase_ffn(xres, y_out if stage == 2 else xres, 0)
    if stage >= 3:
        phase_attn_b()
        if stage == 3:
            @blk.gpsimd
            def _(g):
                c.wait(g, "st", c.val("st"))
                i = g.dma_start(out=y_out, in_=xres)
                c.inc(i, "st", 16)
                c.wait(g, "st")
    if stage >= 4:
        phase_ffn(xres, y_out, 5)

    if dbg and stage >= 3:
        kvtap = arf.at(0, 16, 264)

        @blk.sync
        def _(sy):
            c.wait(sy, "st", c.val("st"))
            c.inc(sy.dma_start(out=kvtap[:, :, :].rearrange("p a b -> p (a b)"),
                               in_=kv_out[0:128, :]), "ld", 16)

        @blk.gpsimd
        def _(g):
            c.wait(g, "ld", c.val("ld"))

    if dbg and stage >= 2:
        global DBG_MAP2
        DBG_MAP2 = {}
        dbg2 = nc.dram_tensor("dbg2", [128, 40960], F32, kind="ExternalOutput").ap()

        @blk.gpsimd
        def _(g):
            off = 0
            taps3 = ([("kv0", arf.at(0, 4224), 4224),
                      ("attT", arb.at(26624, 4096), 4096),
                      ("qfcs", arb.at(24576, 2048), 2048)]
                     if stage >= 3 else [])
            for name, ap_, n in taps3 + [("colw", colw[:, :], 32),
                                 ("gate0", gate_s[0], 4096), ("up0", up_s[0], 4096),
                                 ("g20", g2_s[0], 4096), ("t40", t4_s[0], 4096),
                                 ("wg0", wgT_dq[0, 0], 128), ("wg1", wgT_dq[0, 1], 128),
                                 ("wd0", wdT_dq[0], 1024)]:
                DBG_MAP2[name] = (off, n)
                c.inc(g.dma_start(out=dbg2[:, off:off + n], in_=ap_), "st", 16)
                off += n
            c.wait(g, "st")

    if dbg:
        taps = [("xnT", xnT, BF16, 4096), ("z_fm", z_fm, BF16, 4096),
                ("z_tm", z_tm, BF16, 4096), ("q_fm", q_fm, BF16, 4096),
                ("G_fm", G_fm, BF16, 4096), ("G_tm", G_tm, BF16, 4096),
                ("h_tm", h_tm, BF16, 1024), ("gp_b", gp_b, BF16, 1024),
                ("da_fm", da_fm, BF16, 1024), ("da_tm", da_tm, BF16, 1024),
                ("h2_fm", h_fm, BF16, 1024), ("a_f", a_f, F32, 1024),
                ("gst1", gst1, F32, 2048), ("gst2", gst2, F32, 2048),
                ("zy", y_fm, BF16, 4096), ("xs", xs, F32, 4096),
                ("scrb", scr_b, F32, 2048), ("w1T_b", w1T_b, BF16, 2048),
                ("w2T_b", w2T_b, BF16, 2048), ("w2n_b", w2n_b, BF16, 2048)]
        off = [0]
        global DBG_MAP
        DBG_MAP = {}

        @blk.gpsimd
        def _(g):
            for name, ap, dt_, n in taps:
                flat = ap
                if len(flat.shape) == 3:
                    flat = flat.rearrange("p a b -> p (a b)")
                rows = flat.shape[0]
                DBG_MAP[name] = (off[0], n, rows)
                i = g.dma_start(out=dbg_out[0:rows, off[0]:off[0] + n], in_=flat)
                c.inc(i, "st", 16)
                off[0] += n
            c.wait(g, "st")

    if stage == 1:
        @blk.gpsimd
        def _(g):
            c.wait(g, "st", c.val("st"))
            i = g.dma_start(out=y_out, in_=xres)
            c.inc(i, "st", 16)
            c.wait(g, "st")

    es.close()
    return nc


_CACHE = {}


def _get_prog(stage=None):
    key = STAGE if stage is None else stage
    if key not in _CACHE:
        _CACHE[key] = build_program(key)
    return _CACHE[key]


def make_in_maps(inputs):
    x = np.asarray(inputs["L0_attn_a_in_seq"], dtype=np.float32)
    T = lambda a: np.ascontiguousarray(np.asarray(a, dtype=np.float32).T)
    C = lambda a: np.ascontiguousarray(np.asarray(a, dtype=np.float32))
    shared = {
        "tkT": T(inputs["L0_attn_a_t_k"]), "tvT": T(inputs["L0_attn_a_t_v"]),
        "tqT": T(inputs["L0_attn_a_t_q"]),
        "w1T": T(inputs["L0_attn_a_W1"]), "w2T": T(inputs["L0_attn_a_W2"]),
        "w2n": C(inputs["L0_attn_a_W2"]),
        "b1i": C(inputs["L0_attn_a_b1"]).reshape(1, DI),
        "b2i": C(inputs["L0_attn_a_b2"]).reshape(1, D),
        "wqT": T(inputs["L0_attn_b_wq"]), "wkT": T(inputs["L0_attn_b_wk"]),
        "wvT": T(inputs["L0_attn_b_wv"]), "woT": T(inputs["L0_attn_b_wo"]),
        "rfT": T(inputs["L0_attn_b_rf"]),
        "wgate": C(inputs["L0_ffn_w_gate"]), "wup": C(inputs["L0_ffn_w_up"]),
        "wdown": C(inputs["L0_ffn_w_down"]),
    }
    in_maps = []
    for cid in range(NCORE):
        m = dict(shared)
        m["x_in"] = np.ascontiguousarray(x[:, cid * TL:(cid + 1) * TL, :].reshape(NB * TL, D))
        in_maps.append(m)
    return in_maps


def kernel(**inputs):
    in_maps = make_in_maps(inputs)
    nc = _get_prog()
    res = run_bass_kernel_spmd(nc, in_maps, core_ids=list(range(NCORE)))
    out = np.empty((NB, 4096, D), dtype=np.float32)
    for cid in range(NCORE):
        out[:, cid * TL:(cid + 1) * TL, :] = np.asarray(res.results[cid]["y"]).reshape(NB, TL, D)
    return out


# version 02
# speedup vs baseline: 1.1657x; 1.1657x over previous
import contextlib
import numpy as np
import ml_dtypes
import concourse.bass as bass
import concourse.mybir as mybir
from concourse.bass_utils import run_bass_kernel_spmd
from concourse.alu_op_type import AluOpType as Alu

F32 = mybir.dt.float32
BF16 = mybir.dt.bfloat16
AF = mybir.ActivationFunctionType
AX = mybir.AxisListType.X

MAGIC = 1.5 * 2.0 ** 23
PI = float(np.pi)
LR = 0.001
EPS = 1e-5
NB, TL, D, DI, FF = 8, 512, 1024, 256, 3072
CG = 2.0 / (4096.0 * 1024.0)
NCORE = 8

STAGE = 4
DBG = False
DBG_STEPS = None

AF_ELE = 18688
AB_ELE = 67840


class Ctx:
    def __init__(self, nc, es):
        self.nc, self.es, self.sem, self.cnt = nc, es, {}, {}

    def S(self, name):
        self.sem[name] = self.es.enter_context(self.nc.semaphore(name))
        self.cnt[name] = 0

    def inc(self, instr, name, n=1):
        self.cnt[name] += n
        instr.then_inc(self.sem[name], n)
        return self.cnt[name]

    def val(self, name):
        return self.cnt[name]

    def drain(self, eng, instr, name="vv"):
        v = self.inc(instr, name, 1)
        eng.wait_ge(self.sem[name], v)

    def wait(self, eng, name, target=None):
        if target is None:
            target = self.cnt[name]
        if target > 0:
            eng.wait_ge(self.sem[name], target)


class Arena:
    def __init__(self, t, cap):
        self.t, self.cap, self.off = t, cap, 0

    def take(self, *shape):
        n = int(np.prod(shape))
        ap = self.t[:, self.off:self.off + n]
        assert self.off + n <= self.cap, f"arena overflow {self.off}+{n}>{self.cap}"
        self.off += n
        if len(shape) > 1:
            names = " ".join(f"d{i}" for i in range(len(shape)))
            ap = ap.rearrange(f"p ({names}) -> p {names}",
                              **{f"d{i}": s for i, s in enumerate(shape)})
        return ap

    def at(self, off, *shape):
        n = int(np.prod(shape))
        ap = self.t[:, off:off + n]
        if len(shape) > 1:
            names = " ".join(f"d{i}" for i in range(len(shape)))
            ap = ap.rearrange(f"p ({names}) -> p {names}",
                              **{f"d{i}": s for i, s in enumerate(shape)})
        return ap

    def reset(self, off=0):
        self.off = off


def build_program(stage=STAGE, dbg=None, dbg_steps=None):
    dbg = DBG if dbg is None else dbg
    nsteps = NB if (DBG_STEPS if dbg_steps is None else dbg_steps) is None else (DBG_STEPS if dbg_steps is None else dbg_steps)
    nc = bass.Bass("TRN2", target_bir_lowering=False)

    def din(name, shape):
        return nc.dram_tensor(name, shape, F32, kind="ExternalInput").ap()

    x_in = din("x_in", [NB * TL, D])
    tkT, tvT, tqT = din("tkT", [D, D]), din("tvT", [D, D]), din("tqT", [D, D])
    w1Ti, w2Ti, w2ni = din("w1T", [D, DI]), din("w2T", [DI, D]), din("w2n", [D, DI])
    b1i, b2i = din("b1i", [1, DI]), din("b2i", [1, D])
    wqT, wkT = din("wqT", [D, D]), din("wkT", [D, D])
    wvT, woT = din("wvT", [D, D]), din("woT", [D, D])
    rfTi = din("rfT", [64, 256])
    wgate, wup, wdown = din("wgate", [FF, D]), din("wup", [FF, D]), din("wdown", [D, FF])

    y_out = nc.dram_tensor("y", [NB * TL, D], F32, kind="ExternalOutput").ap()
    dbg_out = nc.dram_tensor("dbg", [128, 65536], F32, kind="ExternalOutput").ap() if dbg else None

    xres = nc.dram_tensor("xres", [NB * TL, D], F32).ap()
    wgT_dq = nc.dram_tensor("wgT_dq", [24, 8, 128, 128], BF16).ap()
    wuT_dq = nc.dram_tensor("wuT_dq", [24, 8, 128, 128], BF16).ap()
    wdT_dq = nc.dram_tensor("wdT_dq", [24, 128, D], BF16).ap()
    gbuf_in = nc.dram_tensor("gbuf_in", [1028, 512], F32).ap()
    gbuf_out = nc.dram_tensor("gbuf_out", [1028, 512], F32).ap()
    up_s = nc.dram_tensor("up_s", [24, 128, NB * TL], BF16).ap()
    gate_s = nc.dram_tensor("gate_s", [24, 128, NB * TL], BF16).ap()
    g2_s = nc.dram_tensor("g2_s", [24, 128, NB * TL], BF16).ap()
    t4_s = nc.dram_tensor("t4_s", [24, 128, NB * TL], BF16).ap()
    mxi = [nc.dram_tensor(f"mxi{k}", [1, 1], F32).ap() for k in range(10)]
    mxo = [nc.dram_tensor(f"mxo{k}", [1, 1], F32).ap() for k in range(10)]
    kv_in = nc.dram_tensor("kv_in", [NB * 128, 16 * 264], F32).ap()
    kv_out = nc.dram_tensor("kv_out", [NB * 128, 16 * 264], F32).ap()
    xnT_sp = nc.dram_tensor("xnT_sp", [NB, 128, 8 * TL], BF16).ap()

    es = contextlib.ExitStack()
    c = Ctx(nc, es)

    def sbuf(name, shape, dtype):
        return es.enter_context(nc.sbuf_tensor(name, shape, dtype))

    ident_b = sbuf("ident_b", [128, 128], BF16)
    ones_b = sbuf("ones_b", [128, 512], BF16)
    consts = sbuf("consts", [128, 8], F32)
    colw = sbuf("colw", [128, 32], F32)

    arena_f_t = sbuf("arena_f", [128, AF_ELE], F32)
    arena_b_t = sbuf("arena_b", [128, AB_ELE], BF16)
    arf = Arena(arena_f_t, AF_ELE)
    arb = Arena(arena_b_t, AB_ELE)

    pb = [es.enter_context(nc.psum_tensor(f"pb{i}", [128, 512], F32)) for i in range(4)]
    pt = [es.enter_context(nc.psum_tensor(f"pt{i}", [128, 8, 128], BF16)) for i in range(2)]
    pm = es.enter_context(nc.psum_tensor("pm", [128, 512], F32))
    pm2 = es.enter_context(nc.psum_tensor("pm2", [128, 512], F32))

    for s in ["ld", "cst", "st", "mm", "ev", "vv", "pp", "cc"]:
        c.S(s)

    blk = es.enter_context(nc.Block())

    def barrier():
        t_ = {k: c.val(k) for k in ("mm", "ev", "vv", "st", "ld", "cst", "pp", "cc")}

        @blk.tensor
        def _(t):
            for k in ("ev", "vv", "pp", "st", "ld", "cst"):
                c.wait(t, k, t_[k])

        @blk.vector
        def _(v):
            for k in ("mm", "ev", "pp", "st", "ld", "cst"):
                c.wait(v, k, t_[k])

        @blk.scalar
        def _(sc):
            for k in ("mm", "vv", "pp", "st", "ld", "cst"):
                c.wait(sc, k, t_[k])

        @blk.gpsimd
        def _(g):
            for k in ("mm", "ev", "vv", "ld", "cc"):
                c.wait(g, k, t_[k])

        @blk.sync
        def _(sy):
            for k in ("mm", "ev", "vv", "pp", "st", "cst", "cc"):
                c.wait(sy, k, t_[k])

    xs = arf.take(4, D)
    scr_b = arf.take(2048)
    gst3 = scr_b[0:1, :]
    a_f = arf.take(2, TL)
    w1T_f = arf.take(8, DI)
    w2T_f = arf.take(2, D)
    w2n_f = arf.take(8, DI)
    b1_f = arf.take(DI)[0:1, :]
    b2_f = arf.take(D)[0:1, :]
    gst1 = arf.take(8, DI)
    gst2 = arf.take(2, D)
    gr1, gr2, gr3 = gst1, gst2, gst3
    assert arf.off <= AF_ELE, arf.off

    tkT_b = arb.take(8, D)
    tvT_b = arb.take(8, D)
    tqT_b = arb.take(8, D)
    w1T_b = arb.take(8, DI)
    w2T_b = arb.take(2, D)
    w2n_b = arb.take(8, DI)
    b1_b = arb.take(DI)[0:1, :]
    b2_b = arb.take(D)[0:1, :]
    xn_tm = arb.take(4, D)
    G_tm = xn_tm
    xnT = arb.take(8, TL)
    z_fm = arb.take(8, TL)
    z_tm = arb.take(4, D)
    y_fm = arb.take(8, TL)
    q_fm = arb.take(8, TL)
    _goff = arb.off
    G_fm = arb.take(8, TL)
    gr2b = arb.at(_goff, 2, 1024)
    h_fm = arb.take(2, TL)
    h2_fm = h_fm
    h_tm = arb.take(4, DI)
    gp_b = arb.take(2, TL)
    da_fm = arb.take(2, TL)
    da_tm = arb.take(4, DI)
    gst4 = arb.take(8, DI)
    G_fm_off = None
    print("attn_a arena use: f32", arf.off, "bf16", arb.off)
    assert arb.off <= AB_ELE, arb.off

    def phase_setup():
        @blk.gpsimd
        def _(g):
            g.memset(ident_b[:, :], 0.0)
            g.affine_select(out=ident_b[:, :], in_=ident_b[:, :], compare_op=Alu.not_equal,
                            fill=1.0, base=0, pattern=[[-1, 128]], channel_multiplier=1)
            g.memset(ones_b[:, :], 1.0)
            g.memset(consts[:, 0:1], MAGIC)
            g.memset(consts[:, 1:2], PI / 2)
            i = g.memset(consts[:, 2:3], 65536.0)
            c.inc(i, "pp", 1)
            for dst, src in ((tkT_b, tkT), (tvT_b, tvT), (tqT_b, tqT),
                             (w1T_b, w1Ti), (w2T_b, w2Ti), (w2n_b, w2ni)):
                c.inc(g.dma_start(out=dst[:, :, :], in_=src.rearrange("(k p) o -> p k o", p=128)),
                      "cst", 16)
            c.inc(g.dma_start(out=b1_b, in_=b1i), "cst", 16)
            c.inc(g.dma_start(out=b2_b, in_=b2i), "cst", 16)

        @blk.sync
        def _(s):
            c.inc(s.dma_start(out=w1T_f[:, :, :], in_=w1Ti.rearrange("(k p) o -> p k o", p=128)), "ld", 16)
            c.inc(s.dma_start(out=w2T_f[:, :, :], in_=w2Ti.rearrange("(k p) o -> p k o", p=128)), "ld", 16)
            c.inc(s.dma_start(out=w2n_f[:, :, :], in_=w2ni.rearrange("(k p) o -> p k o", p=128)), "ld", 16)
            c.inc(s.dma_start(out=b1_f, in_=b1i), "ld", 16)
            c.inc(s.dma_start(out=b2_f, in_=b2i), "ld", 16)
            c.inc(s.dma_start(out=xs[:, :, :],
                              in_=x_in[0:TL, :].rearrange("(t p) o -> p t o", p=128)), "ld", 16)

    def transpose_group(dst, srcs, par, pre_ev=None):
        @blk.tensor
        def _(t):
            if pre_ev is not None:
                c.wait(t, "ev", pre_ev)
            c.wait(t, "vv", c.val("vv"))
            i = None
            for k, s_ap in enumerate(srcs):
                i = t.transpose(pt[par][:, k, :], s_ap, ident_b[:, :])
            c.inc(i, "mm", 1)

        @blk.vector
        def _(v):
            c.wait(v, "mm")
            i = v.tensor_copy(dst, pt[par][:, 0:len(srcs), :])
            c.inc(i, "vv", 1)

    def phase_attn_a():
        ld_xs = c.val("ld")

        @blk.tensor
        def _(t):
            c.wait(t, "pp")
            c.wait(t, "cst")
            c.wait(t, "ld")

        for s in range(nsteps):
            for tc in range(4):
                @blk.vector
                def _(v, s=s, tc=tc, ld_xs=ld_xs):
                    if tc == 0:
                        c.wait(v, "ld", ld_xs)
                        if s == 0:
                            c.wait(v, "cst")
                    c.drain(v, v.tensor_reduce(colw[:, 0:1], xs[:, tc, :], AX, Alu.add))
                    c.drain(v, v.tensor_scalar(colw[:, 1:2], colw[:, 0:1], -1.0 / D, None, Alu.mult))
                    i = v.tensor_scalar(scr_b[:, 0:D], xs[:, tc, :], colw[:, 1:2], None, Alu.add)
                    c.inc(i, "vv", 1)

                @blk.scalar
                def _(sc, s=s, tc=tc):
                    if s == 0 and tc == 0:
                        c.wait(sc, "cst")
                    c.wait(sc, "vv")
                    c.inc(sc.activation(a_f[:, :, :].rearrange("p a b -> p (a b)"),
                                        scr_b[:, 0:D], AF.Square), "ev", 1)

                @blk.vector
                def _(v):
                    c.wait(v, "ev")
                    c.drain(v, v.tensor_reduce(colw[:, 2:3], a_f[:, :, :].rearrange("p a b -> p (a b)"), AX, Alu.add))
                    i = v.tensor_scalar(colw[:, 2:3], colw[:, 2:3], 1.0 / D, EPS, Alu.mult, Alu.add)
                    c.inc(i, "vv", 1)

                @blk.scalar
                def _(sc):
                    c.wait(sc, "vv")
                    c.inc(sc.activation(colw[:, 3:4], colw[:, 2:3], AF.Sqrt), "ev", 1)

                @blk.vector
                def _(v, tc=tc):
                    c.wait(v, "ev")
                    c.drain(v, v.reciprocal(colw[:, 3:4], colw[:, 3:4]))
                    i = v.tensor_scalar(xn_tm[:, tc, :], scr_b[:, 0:D], colw[:, 3:4], None, Alu.mult)
                    c.inc(i, "vv", 1)

                transpose_group(xnT[:, 0:8, tc * 128:(tc + 1) * 128],
                                [xn_tm[:, tc, kc * 128:(kc + 1) * 128] for kc in range(8)],
                                par=tc % 2)
            xnT_vv = c.val("vv")

            ev0 = c.val("ev")
            for pi_, (wT, dst) in enumerate(((tkT_b, z_fm), (tvT_b, y_fm), (tqT_b, q_fm))):
                for oc in range(8):
                    n = pi_ * 8 + oc

                    @blk.tensor
                    def _(t, wT=wT, oc=oc, n=n, ev0=ev0, xnT_vv=xnT_vv):
                        if n == 0:
                            c.wait(t, "vv", xnT_vv)
                        if n >= 4:
                            c.wait(t, "ev", ev0 + n - 3)
                        bank = pb[n % 4]
                        i = None
                        for kc in range(8):
                            i = t.matmul(bank[:, :], wT[:, kc, oc * 128:(oc + 1) * 128],
                                         xnT[:, kc, :], start=(kc == 0), stop=(kc == 7))
                        c.inc(i, "mm", 1)

                    @blk.scalar
                    def _(sc, dst=dst, oc=oc, n=n):
                        c.wait(sc, "mm")
                        c.inc(sc.activation(dst[:, oc, :], pb[n % 4][:, :], AF.Copy), "ev", 1)
            proj_ev = c.val("ev")

            for oc in range(8):
                transpose_group(z_tm[:, 0:4, oc * 128:(oc + 1) * 128],
                                [z_fm[:, oc, ts * 128:(ts + 1) * 128] for ts in range(4)],
                                par=oc % 2, pre_ev=(proj_ev - 16 if oc == 0 else None))

            @blk.vector
            def _(v, proj_ev=proj_ev):
                c.wait(v, "ev", proj_ev)
                i = v.tensor_tensor(y_fm[:, :, :], z_fm[:, :, :], y_fm[:, :, :], Alu.subtract)
                c.inc(i, "vv", 1)
            zy_vv = c.val("vv")

            for ic in range(2):
                @blk.tensor
                def _(t, ic=ic, proj_ev=proj_ev):
                    if ic == 0:
                        c.wait(t, "ev", proj_ev - 16)
                    bank = pb[ic]
                    for kc in range(8):
                        t.matmul(bank[:, :], w1T_b[:, kc, ic * 128:(ic + 1) * 128],
                                 z_fm[:, kc, :], start=(kc == 0), stop=False)
                    i = t.matmul(bank[:, :], b1_b[0:1, ic * 128:(ic + 1) * 128],
                                 ones_b[0:1, :], start=False, stop=True)
                    c.inc(i, "mm", 1)

                @blk.scalar
                def _(sc, ic=ic):
                    c.wait(sc, "mm")
                    c.inc(sc.activation(a_f[:, ic, :], pb[ic][:, :], AF.Copy), "ev", 1)

            @blk.scalar
            def _(sc):
                sc.activation(h_fm[:, :, :], a_f[:, :, :], AF.Gelu)
                c.inc(sc.activation(gp_b[:, :, :], a_f[:, :, :], AF.Derivative_Gelu), "ev", 1)
            h_ev = c.val("ev")

            for ic in range(2):
                transpose_group(h_tm[:, 0:4, ic * 128:(ic + 1) * 128],
                                [h_fm[:, ic, ts * 128:(ts + 1) * 128] for ts in range(4)],
                                par=ic, pre_ev=(h_ev if ic == 0 else None))

            vv_r0 = c.val("vv")
            for oc in range(8):
                @blk.tensor
                def _(t, oc=oc, h_ev=h_ev, vv_r0=vv_r0):
                    if oc == 0:
                        c.wait(t, "ev", h_ev)
                    if oc >= 4:
                        c.wait(t, "vv", vv_r0 + 2 * (oc - 4) + 2)
                    bank = pb[oc % 4]
                    for kc in range(2):
                        t.matmul(bank[:, :], w2T_b[:, kc, oc * 128:(oc + 1) * 128],
                                 h_fm[:, kc, :], start=(kc == 0), stop=False)
                    i = t.matmul(bank[:, :], b2_b[0:1, oc * 128:(oc + 1) * 128],
                                 ones_b[0:1, :], start=False, stop=True)
                    c.inc(i, "mm", 1)

                @blk.vector
                def _(v, oc=oc, zy_vv=zy_vv):
                    c.wait(v, "mm")
                    if oc == 0:
                        c.wait(v, "vv", zy_vv)
                    v.tensor_tensor(scr_b[:, 0:TL], pb[oc % 4][:, :], y_fm[:, oc, :], Alu.add)
                    i = v.tensor_scalar(G_fm[:, oc, :], scr_b[:, 0:TL], CG, None, Alu.mult)
                    c.inc(i, "vv", 2)

            for oc in range(8):
                transpose_group(G_tm[:, 0:4, oc * 128:(oc + 1) * 128],
                                [G_fm[:, oc, ts * 128:(ts + 1) * 128] for ts in range(4)],
                                par=oc % 2)
            G_vv = c.val("vv")

            for ic in range(2):
                @blk.tensor
                def _(t, ic=ic, G_vv=G_vv):
                    if ic == 0:
                        c.wait(t, "vv", G_vv)
                    bank = pb[ic]
                    i = None
                    for kc in range(8):
                        i = t.matmul(bank[:, :], w2n_b[:, kc, ic * 128:(ic + 1) * 128],
                                     G_fm[:, kc, :], start=(kc == 0), stop=(kc == 7))
                    c.inc(i, "mm", 1)

                @blk.vector
                def _(v, ic=ic):
                    c.wait(v, "mm")
                    i = v.tensor_tensor(da_fm[:, ic, :], pb[ic][:, :], gp_b[:, ic, :], Alu.mult)
                    c.inc(i, "vv", 1)

            for ic in range(2):
                transpose_group(da_tm[:, 0:4, ic * 128:(ic + 1) * 128],
                                [da_fm[:, ic, ts * 128:(ts + 1) * 128] for ts in range(4)],
                                par=ic)
            da_vv = c.val("vv")

            gev0 = c.val("ev")
            for os_ in range(8):
                @blk.tensor
                def _(t, os_=os_, da_vv=da_vv, gev0=gev0):
                    if os_ == 0:
                        c.wait(t, "vv", da_vv)
                    if os_ >= 4:
                        c.wait(t, "ev", gev0 + os_ - 3)
                    bank = pb[os_ % 4]
                    i = None
                    for ts in range(4):
                        i = t.matmul(bank[:, 0:DI], z_tm[:, ts, os_ * 128:(os_ + 1) * 128],
                                     da_tm[:, ts, :], start=(ts == 0), stop=(ts == 3))
                    c.inc(i, "mm", 1)

                @blk.scalar
                def _(sc, os_=os_):
                    c.wait(sc, "mm")
                    c.inc(sc.activation(gst1[:, os_, :], pb[os_ % 4][:, 0:DI], AF.Copy), "ev", 1)

            gev1 = c.val("ev")
            for u in range(4):
                is_, cc_ = u % 2, u // 2

                @blk.tensor
                def _(t, is_=is_, cc_=cc_, u=u, gev1=gev1):
                    c.wait(t, "ev", gev1 - 3 + u)
                    bank = pb[u % 4]
                    i = None
                    for ts in range(4):
                        i = t.matmul(bank[:, :], h_tm[:, ts, is_ * 128:(is_ + 1) * 128],
                                     G_tm[:, ts, cc_ * 512:(cc_ + 1) * 512],
                                     start=(ts == 0), stop=(ts == 3))
                    c.inc(i, "mm", 1)

                @blk.scalar
                def _(sc, is_=is_, cc_=cc_, u=u):
                    c.wait(sc, "mm")
                    c.inc(sc.activation(gst2[:, is_, cc_ * 512:(cc_ + 1) * 512],
                                        pb[u % 4][:, :], AF.Copy), "ev", 1)

            @blk.tensor
            def _(t):
                i = None
                for ts in range(4):
                    i = t.matmul(pm[0:1, 0:DI], ones_b[:, 0:1], da_tm[:, ts, :],
                                 start=(ts == 0), stop=(ts == 3))
                c.inc(i, "mm", 1)

            @blk.scalar
            def _(sc):
                c.wait(sc, "mm")
                c.inc(sc.activation(gst3[0:1, 0:DI], pm[0:1, 0:DI], AF.Copy), "ev", 1)

            for cc_ in range(2):
                @blk.tensor
                def _(t, cc_=cc_):
                    c.wait(t, "ev", c.val("ev"))
                    i = None
                    for ts in range(4):
                        i = t.matmul(pm2[0:1, :], ones_b[:, 0:1], G_tm[:, ts, cc_ * 512:(cc_ + 1) * 512],
                                     start=(ts == 0), stop=(ts == 3))
                    c.inc(i, "mm", 1)

                @blk.scalar
                def _(sc, cc_=cc_):
                    c.wait(sc, "mm")
                    c.inc(sc.activation(gst3[0:1, 1024 + cc_ * 512:1024 + (cc_ + 1) * 512],
                                        pm2[0:1, :], AF.Copy), "ev", 1)
            grads_ev = c.val("ev")

            @blk.gpsimd
            def _(g, grads_ev=grads_ev):
                c.wait(g, "ev", grads_ev)
                c.inc(g.dma_start(
                    out=gbuf_in[0:512, :].rearrange("(os ph) (pl i) -> (ph pl) os i", os=8, ph=64, pl=2),
                    in_=gst1[:, :, :]), "st", 16)
                c.inc(g.dma_start(
                    out=gbuf_in[512:1024, :].rearrange("(i p two) cc -> p i (two cc)", i=2, p=128, two=2),
                    in_=gst2[:, :, :]), "st", 16)
                c.inc(g.dma_start(out=gbuf_in[1024:1028, :], in_=gst3[0:1, :]), "st", 16)
                c.wait(g, "st")
                i = g.collective_compute("AllReduce", Alu.add, replica_groups=[list(range(NCORE))],
                                         ins=[gbuf_in.opt()], outs=[gbuf_out.opt()])
                c.inc(i, "cc", 1)

            @blk.sync
            def _(sy):
                c.wait(sy, "cc")
                c.inc(sy.dma_start(
                    out=gr1[:, :, :],
                    in_=gbuf_out[0:512, :].rearrange("(os ph) (pl i) -> (ph pl) os i", os=8, ph=64, pl=2)), "ld", 16)
                c.inc(sy.dma_start(
                    out=gr2[:, :, :],
                    in_=gbuf_out[512:1024, :].rearrange("(i p two) cc -> p i (two cc)", i=2, p=128, two=2)), "ld", 16)
                c.inc(sy.dma_start(out=gr3[0:1, :], in_=gbuf_out[1024:1028, :]), "ld", 16)
            gr_ld = c.val("ld")

            @blk.vector
            def _(v, gr_ld=gr_ld):
                c.wait(v, "ld", gr_ld)
                v.scalar_tensor_tensor(w1T_f[:, :, :], gr1[:, :, :], -LR, w1T_f[:, :, :], Alu.mult, Alu.add)
                v.scalar_tensor_tensor(w2T_f[:, :, :], gr2[:, :, :], -LR, w2T_f[:, :, :], Alu.mult, Alu.add)
                v.scalar_tensor_tensor(b1_f[0:1, :], gr3[0:1, 0:DI], -LR, b1_f[0:1, :], Alu.mult, Alu.add)
                i = v.scalar_tensor_tensor(b2_f[0:1, :], gr3[0:1, 1024:2048], -LR, b2_f[0:1, :], Alu.mult, Alu.add)
                c.inc(i, "vv", 1)

            @blk.scalar
            def _(sc, gr_ld=gr_ld):
                c.wait(sc, "ld", gr_ld)
                c.inc(sc.activation(gr2b[:, :, :], gr2[:, :, :], AF.Copy), "ev", 1)
            g2b_ev = c.val("ev")

            for is_ in range(2):
                @blk.tensor
                def _(t, is_=is_, g2b_ev=g2b_ev):
                    if is_ == 0:
                        c.wait(t, "ev", g2b_ev)
                    else:
                        c.wait(t, "vv", c.val("vv"))
                    i = None
                    for u in range(8):
                        i = t.transpose(pt[is_][:, u, :],
                                        gr2b[:, is_, u * 128:(u + 1) * 128], ident_b[:, :])
                    c.inc(i, "mm", 1)

                @blk.vector
                def _(v, is_=is_):
                    c.wait(v, "mm")
                    i = v.tensor_copy(gst4[:, 0:8, is_ * 128:(is_ + 1) * 128], pt[is_][:, :, :])
                    c.inc(i, "vv", 1)

            @blk.vector
            def _(v):
                i = v.scalar_tensor_tensor(w2n_f[:, :, :], gst4[:, :, :], -LR, w2n_f[:, :, :], Alu.mult, Alu.add)
                c.inc(i, "vv", 1)
            updv2 = c.val("vv")

            @blk.scalar
            def _(sc, updv2=updv2):
                c.wait(sc, "vv", updv2)
                sc.activation(w1T_b[:, :, :], w1T_f[:, :, :], AF.Copy)
                sc.activation(w2T_b[:, :, :], w2T_f[:, :, :], AF.Copy)
                sc.activation(b1_b[0:1, :], b1_f[0:1, :], AF.Copy)
                sc.activation(b2_b[0:1, :], b2_f[0:1, :], AF.Copy)
                c.inc(sc.activation(w2n_b[:, :, :], w2n_f[:, :, :], AF.Copy), "ev", 1)
            params_ev = c.val("ev")

            for ic in range(2):
                @blk.tensor
                def _(t, ic=ic, params_ev=params_ev):
                    if ic == 0:
                        c.wait(t, "ev", params_ev)
                    bank = pb[ic]
                    for kc in range(8):
                        t.matmul(bank[:, :], w1T_b[:, kc, ic * 128:(ic + 1) * 128],
                                 q_fm[:, kc, :], start=(kc == 0), stop=False)
                    i = t.matmul(bank[:, :], b1_b[0:1, ic * 128:(ic + 1) * 128],
                                 ones_b[0:1, :], start=False, stop=True)
                    c.inc(i, "mm", 1)

                @blk.scalar
                def _(sc, ic=ic):
                    c.wait(sc, "mm")
                    c.inc(sc.activation(h2_fm[:, ic, :], pb[ic][:, :], AF.Gelu), "ev", 1)
            h2_ev = c.val("ev")

            ovv0 = c.val("vv")
            for n in range(8):
                tc, occ = divmod(n, 2)

                @blk.tensor
                def _(t, tc=tc, occ=occ, n=n, h2_ev=h2_ev, ovv0=ovv0):
                    if n == 0:
                        c.wait(t, "ev", h2_ev)
                    if n >= 4:
                        c.wait(t, "vv", ovv0 + n - 3)
                    bank = pb[n % 4]
                    for ic in range(2):
                        t.matmul(bank[:, :], h2_fm[:, ic, tc * 128:(tc + 1) * 128],
                                 w2T_b[:, ic, occ * 512:(occ + 1) * 512],
                                 start=(ic == 0), stop=False)
                    t.matmul(bank[:, :], ones_b[0:1, 0:128],
                             b2_b[0:1, occ * 512:(occ + 1) * 512], start=False, stop=False)
                    i = None
                    for u in range(4):
                        i = t.matmul(bank[:, u * 128:(u + 1) * 128],
                                     q_fm[:, occ * 4 + u, tc * 128:(tc + 1) * 128],
                                     ident_b[:, :], start=False, stop=(u == 3))
                    c.inc(i, "mm", 1)

                @blk.vector
                def _(v, tc=tc, occ=occ, n=n):
                    c.wait(v, "mm")
                    i = v.tensor_tensor(xs[:, tc, occ * 512:(occ + 1) * 512],
                                        pb[n % 4][:, :],
                                        xs[:, tc, occ * 512:(occ + 1) * 512], Alu.add)
                    c.inc(i, "vv", 1)

            @blk.gpsimd
            def _(g, s=s):
                c.wait(g, "vv")
                c.inc(g.dma_start(
                    out=xres[s * TL:(s + 1) * TL, :].rearrange("(t p) o -> p t o", p=128),
                    in_=xs[:, :, :]), "st", 16)

            barrier()

            if s + 1 < nsteps:
                @blk.sync
                def _(sy, s=s):
                    c.inc(sy.dma_start(
                        out=xs[:, :, :],
                        in_=x_in[(s + 1) * TL:(s + 2) * TL, :].rearrange("(t p) o -> p t o", p=128)),
                        "ld", 16)
                ld_xs = c.val("ld")

    def phase_dequant():
        arf.reset()
        arb.reset()
        wchunk = arf.take(2, FF)
        sblk = arf.take(96)
        isblk = arf.take(96)
        sfull = arf.take(FF)
        isfull = arf.take(FF)
        tq_ = arf.take(FF)
        assert arf.off <= AF_ELE, arf.off
        dqo = arb.take(2, FF)
        dqst = arb.take(2, 8, 128)

        jobs = ([("g", oc, wgate[oc * 128:(oc + 1) * 128, :], D) for oc in range(24)]
                + [("u", oc, wup[oc * 128:(oc + 1) * 128, :], D) for oc in range(24)]
                + [("d", oc, wdown[oc * 128:(oc + 1) * 128, :], FF) for oc in range(8)])
        dq_vv = {}
        st_done = {}

        for j, (kind, oc, src_, w) in enumerate(jobs):
            nblk = w // 32
            nct = w // 128

            @blk.sync
            def _(sy, j=j, src_=src_, w=w):
                if j >= 2:
                    c.wait(sy, "vv", dq_vv[j - 2])
                c.inc(sy.dma_start(out=wchunk[:, j % 2, 0:w], in_=src_), "ld", 16)
            ldt = c.val("ld")

            @blk.vector
            def _(v, j=j, w=w, nblk=nblk, ldt=ldt):
                c.wait(v, "ld", ldt)
                c.drain(v, v.tensor_reduce(
                    sblk[:, 0:nblk].rearrange("p (b o) -> p b o", o=1),
                    wchunk[:, j % 2, 0:w].rearrange("p (b q) -> p b q", q=32),
                    AX, Alu.max, apply_absolute_value=True))
                c.drain(v, v.tensor_scalar(sblk[:, 0:nblk], sblk[:, 0:nblk],
                                           1.0 / 7.0, 1e-12, Alu.mult, Alu.max))
                c.drain(v, v.reciprocal(isblk[:, 0:nblk], sblk[:, 0:nblk]))
                v.tensor_copy(sfull[:, 0:w].rearrange("p (b q) -> p b q", q=32),
                              sblk[:, 0:nblk].rearrange("p (b o) -> p b o", o=1).broadcast_to([128, nblk, 32]))
                c.drain(v, v.tensor_copy(isfull[:, 0:w].rearrange("p (b q) -> p b q", q=32),
                                         isblk[:, 0:nblk].rearrange("p (b o) -> p b o", o=1).broadcast_to([128, nblk, 32])))
                v.tensor_tensor(tq_[:, 0:w], wchunk[:, j % 2, 0:w], isfull[:, 0:w], Alu.mult)
                v.tensor_scalar(tq_[:, 0:w], tq_[:, 0:w], MAGIC, MAGIC, Alu.add, Alu.subtract)
                i = v.tensor_tensor(dqo[:, j % 2, 0:w], tq_[:, 0:w], sfull[:, 0:w], Alu.mult)
                dq_vv[j] = c.inc(i, "vv", 1)

            for grp in range((nct + 7) // 8):
                n = min(8, nct - grp * 8)

                @blk.tensor
                def _(t, j=j, grp=grp, n=n):
                    c.wait(t, "vv", c.val("vv"))
                    if j >= 2 and grp == 0:
                        c.wait(t, "st", st_done[j - 2])
                    i = None
                    for u in range(n):
                        i = t.transpose(pt[j % 2][:, u, :],
                                        dqo[:, j % 2, (grp * 8 + u) * 128:(grp * 8 + u + 1) * 128],
                                        ident_b[:, :])
                    c.inc(i, "mm", 1)

                @blk.vector
                def _(v, j=j, grp=grp, n=n):
                    c.wait(v, "mm")
                    if grp >= 1:
                        c.wait(v, "st", st_done[j])
                    i = v.tensor_copy(dqst[:, j % 2, 0:n, :], pt[j % 2][:, 0:n, :])
                    c.inc(i, "vv", 1)

                @blk.gpsimd
                def _(g, j=j, kind=kind, oc=oc, grp=grp, n=n):
                    c.wait(g, "vv")
                    if kind == "g":
                        dst = wgT_dq[oc].rearrange("k p f -> p k f")
                    elif kind == "u":
                        dst = wuT_dq[oc].rearrange("k p f -> p k f")
                    else:
                        ic0 = grp * 8
                        dst = wdT_dq[ic0:ic0 + n, :, oc * 128:(oc + 1) * 128].rearrange("i p f -> p i f")
                    st_done[j] = c.inc(g.dma_start(out=dst, in_=dqst[:, j % 2, 0:n, :]), "st", 16)

        barrier()

    def phase_ffn(src, dst, mxb):
        arf.reset()
        arb.reset()
        p3_vv, p4_vv, p5_vv, p5_st, p5_mm, p1_mm = [], [], [], [], [], []
        xtile = arf.take(2, D)
        xc_s = arf.take(D)
        sq_s = arf.take(D)
        t3a = arf.take(4096)
        t1q = t3a
        t3b = arf.take(4096)
        ost = arf.take(2, 2, D)
        xr4 = arf.at(0, 2, 2, D)
        assert arf.off <= AF_ELE, arf.off
        xnT_f = arb.take(8, 4096)
        wslab = arb.take(2, 8, 128)
        evslab = arb.take(2, 4096)
        xnstg = arb.take(D)
        mcastb = arb.take(128)[:, 0:1]
        assert arb.off <= AB_ELE, arb.off
        ldg = arb.at(0, 2, 4096)
        ldu = arb.at(8192, 2, 4096)
        g2sl = arb.at(16384, 2, 4096)
        t4sl = arb.at(24576, 2, 4096)
        wdT_all = arb.at(0, 24, D)
        oqin = arb.at(24576, 2, 24, 256)
        oqb = arb.at(36864, 24, 256)

        @blk.vector
        def _(v):
            v.memset(colw[:, 24:27], 0.0)
            i = v.memset(colw[:, 27:28], 1.0)
            c.inc(i, "vv", 1)

        p1_vv = []
        for tc in range(32):
            @blk.sync
            def _(sy, tc=tc):
                if tc >= 2:
                    c.wait(sy, "vv", p1_vv[tc - 2])
                c.inc(sy.dma_start(out=xtile[:, tc % 2, :],
                                   in_=src[tc * 128:(tc + 1) * 128, :]), "ld", 16)
            ldt = c.val("ld")

            @blk.vector
            def _(v, tc=tc, ldt=ldt):
                c.wait(v, "ld", ldt)
                c.drain(v, v.tensor_reduce(colw[:, 0:1], xtile[:, tc % 2, :], AX, Alu.add))
                c.drain(v, v.tensor_scalar(colw[:, 1:2], colw[:, 0:1], -1.0 / D, None, Alu.mult))
                i = v.tensor_scalar(xc_s[:, :], xtile[:, tc % 2, :], colw[:, 1:2], None, Alu.add)
                c.inc(i, "vv", 1)

            @blk.scalar
            def _(sc):
                c.wait(sc, "vv")
                c.inc(sc.activation(sq_s[:, :], xc_s[:, :], AF.Square), "ev", 1)

            @blk.vector
            def _(v):
                c.wait(v, "ev")
                c.drain(v, v.tensor_reduce(colw[:, 2:3], sq_s[:, :], AX, Alu.add))
                i = v.tensor_scalar(colw[:, 2:3], colw[:, 2:3], 1.0 / D, EPS, Alu.mult, Alu.add)
                c.inc(i, "vv", 1)

            @blk.scalar
            def _(sc):
                c.wait(sc, "vv")
                c.inc(sc.activation(colw[:, 3:4], colw[:, 2:3], AF.Sqrt), "ev", 1)

            @blk.vector
            def _(v, tc=tc):
                c.wait(v, "ev")
                if tc >= 1:
                    c.wait(v, "mm", p1_mm[tc - 1])
                c.drain(v, v.reciprocal(colw[:, 3:4], colw[:, 3:4]))
                c.drain(v, v.tensor_scalar(xnstg[:, :], xc_s[:, :], colw[:, 3:4], None, Alu.mult))
                c.drain(v, v.reduce_max(colw[:, 8:9], xnstg[:, :], AX, apply_absolute_value=True))
                i = v.tensor_tensor(colw[:, 24:25], colw[:, 24:25], colw[:, 8:9], Alu.max)
                p1_vv.append(c.inc(i, "vv", 1))

            @blk.tensor
            def _(t, tc=tc):
                c.wait(t, "vv", c.val("vv"))
                i = None
                for kc in range(8):
                    i = t.transpose(pt[tc % 2][:, kc, :], xnstg[:, kc * 128:(kc + 1) * 128], ident_b[:, :])
                p1_mm.append(c.inc(i, "mm", 1))

            @blk.vector
            def _(v, tc=tc):
                c.wait(v, "mm")
                i = v.tensor_copy(xnT_f[:, 0:8, tc * 128:(tc + 1) * 128], pt[tc % 2][:, :, :])
                c.inc(i, "vv", 1)

        def gmax(acc_col, k, out_col):
            @blk.vector
            def _(v):
                c.wait(v, "vv", c.val("vv"))
                i = v.tensor_copy(mcastb, colw[:, acc_col:acc_col + 1])
                c.inc(i, "vv", 1)

            @blk.tensor
            def _(t):
                c.wait(t, "vv", c.val("vv"))
                c.wait(t, "ev", c.val("ev"))
                i = t.matmul(pm[0:1, 0:128], mcastb, ident_b[:, :], start=True, stop=True)
                c.inc(i, "mm", 1)

            @blk.vector
            def _(v):
                c.wait(v, "mm")
                i = v.reduce_max(colw[0:1, 8:9], pm[0:1, 0:128], AX)
                c.inc(i, "vv", 1)

            @blk.gpsimd
            def _(g, k=k):
                c.wait(g, "vv")
                c.inc(g.dma_start(out=mxi[k], in_=colw[0:1, 8:9]), "st", 16)
                c.wait(g, "st")
                i = g.collective_compute("AllReduce", Alu.max, replica_groups=[list(range(NCORE))],
                                         ins=[mxi[k].opt()], outs=[mxo[k].opt()])
                c.inc(i, "cc", 1)

            @blk.sync
            def _(sy, k=k, out_col=out_col):
                c.wait(sy, "cc")
                c.inc(sy.dma_start(out=colw[:, out_col:out_col + 1],
                                   in_=mxo[k][0:1, 0:1].broadcast_to([128, 1])), "ld", 16)
            return c.val("ld")

        def scale_ops(ldt, s_col, inv_col=None, extra=None):
            @blk.vector
            def _(v):
                c.wait(v, "ld", ldt)
                c.drain(v, v.tensor_scalar(colw[:, s_col:s_col + 1], colw[:, s_col:s_col + 1],
                                           1.0 / 32767.0, 1e-12, Alu.mult, Alu.max))
                if inv_col is not None:
                    c.drain(v, v.reciprocal(colw[:, inv_col:inv_col + 1], colw[:, s_col:s_col + 1]))
                if extra is not None:
                    extra(v)
                i = v.engine_nop()
                c.inc(i, "pp", 1)
            return c.val("pp")

        pp_s0 = scale_ops(gmax(24, mxb + 0, 10), 10, 11)
        barrier()

        for kc in range(8):
            @blk.scalar
            def _(sc, kc=kc):
                if kc > 0:
                    c.wait(sc, "vv", c.val("vv"))
                c.inc(sc.activation(t1q[:, :], xnT_f[:, kc, :], AF.Identity,
                                    bias=consts[:, 0:1], scale=colw[:, 11:12]), "ev", 1)

            @blk.vector
            def _(v, kc=kc):
                c.wait(v, "ev")
                i = v.tensor_scalar(xnT_f[:, kc, :], t1q[:, :], MAGIC, colw[:, 10:11],
                                    Alu.subtract, Alu.mult)
                c.inc(i, "vv", 1)
        barrier()

        @blk.vector
        def _(v):
            i = v.memset(colw[:, 24:26], 0.0)
            c.inc(i, "vv", 1)
        barrier()

        for widx, (cache, spill, acc_col) in enumerate(((wgT_dq, gate_s, 24), (wuT_dq, up_s, 25))):
            mm_sp = {}
            ev_b0 = c.val("ev")
            st_b0 = c.val("st")
            for i_ in range(24):
                @blk.sync
                def _(sy, i_=i_, cache=cache):
                    if i_ >= 2:
                        c.wait(sy, "mm", mm_sp[i_ - 2])
                    c.inc(sy.dma_start(out=wslab[:, i_ % 2, :, :],
                                       in_=cache[i_].rearrange("k p f -> p k f")), "ld", 16)
                ldt = c.val("ld")

                @blk.tensor
                def _(t, i_=i_, ldt=ldt, ev_b0=ev_b0):
                    c.wait(t, "ld", ldt)
                    i2 = None
                    for tch in range(8):
                        n = i_ * 8 + tch
                        if n >= 4:
                            c.wait(t, "ev", ev_b0 + n - 3)
                        bank = pb[n % 4]
                        for kc in range(8):
                            i2 = t.matmul(bank[:, :], wslab[:, i_ % 2, kc, :],
                                          xnT_f[:, kc, tch * 512:(tch + 1) * 512],
                                          start=(kc == 0), stop=(kc == 7))
                        c.inc(i2, "mm", 1)
                mm_sp[i_] = c.val("mm")

                @blk.scalar
                def _(sc, i_=i_, st_b0=st_b0):
                    for tch in range(8):
                        c.wait(sc, "mm", mm_sp[i_] - 7 + tch)
                        if i_ >= 2 and tch == 0:
                            c.wait(sc, "st", st_b0 + 16 * (i_ - 1))
                        n = i_ * 8 + tch
                        c.inc(sc.activation(evslab[:, i_ % 2, tch * 512:(tch + 1) * 512],
                                            pb[n % 4][:, :], AF.Copy), "ev", 1)

                @blk.vector
                def _(v, i_=i_, acc_col=acc_col):
                    for tch in range(8):
                        c.wait(v, "ev")
                        c.drain(v, v.reduce_max(colw[:, 8:9],
                                                evslab[:, i_ % 2, tch * 512:(tch + 1) * 512],
                                                AX, apply_absolute_value=True))
                        i2 = v.tensor_tensor(colw[:, acc_col:acc_col + 1],
                                             colw[:, acc_col:acc_col + 1], colw[:, 8:9], Alu.max)
                        c.inc(i2, "vv", 1)

                @blk.gpsimd
                def _(g, i_=i_, spill=spill):
                    c.wait(g, "ev", c.val("ev"))
                    c.inc(g.dma_start(out=spill[i_], in_=evslab[:, i_ % 2, :]), "st", 16)
            barrier()
            if widx == 0:
                def ex2(v):
                    v.tensor_scalar(colw[:, 20:21], colw[:, 14:15], -MAGIC, None, Alu.mult)
                    v.tensor_scalar(colw[:, 21:22], colw[:, 14:15], 1.0 / 65536.0, None, Alu.mult)
                scale_ops(gmax(24, mxb + 1, 14), 14, 15, extra=ex2)
            else:
                def ex1(g):
                    pass
                scale_ops(gmax(25, mxb + 2, 12), 12, 13)
        barrier()

        @blk.vector
        def _(v):
            i = v.memset(colw[:, 24:25], 0.0)
            c.inc(i, "vv", 1)
        for i_ in range(24):
            @blk.sync
            def _(sy, i_=i_):
                if i_ >= 2:
                    c.wait(sy, "vv", p3_vv[i_ - 2])
                c.inc(sy.dma_start(out=ldg[:, i_ % 2, :], in_=gate_s[i_]), "ld", 16)
            ldt = c.val("ld")

            @blk.scalar
            def _(sc, i_=i_, ldt=ldt):
                c.wait(sc, "ld", ldt)
                if i_ >= 1:
                    c.wait(sc, "vv", c.val("vv"))
                sc.activation(t3a[:, :], ldg[:, i_ % 2, :], AF.Identity,
                              bias=consts[:, 0:1], scale=colw[:, 15:16])
                sc.activation(t3b[:, :], t3a[:, :], AF.Sigmoid,
                              bias=colw[:, 20:21], scale=colw[:, 14:15])
                i2 = sc.activation(t3b[:, :], t3b[:, :], AF.Identity,
                                   bias=consts[:, 0:1], scale=consts[:, 2:3])
                c.inc(i2, "ev", 1)

            @blk.vector
            def _(v, i_=i_):
                c.wait(v, "ev")
                v.tensor_scalar(t3b[:, :], t3b[:, :], MAGIC, 65535.0, Alu.subtract, Alu.min)
                v.tensor_scalar(t3b[:, :], t3b[:, :], colw[:, 21:22], None, Alu.mult)
                v.scalar_tensor_tensor(g2sl[:, i_ % 2, :], t3a[:, :], MAGIC, t3b[:, :],
                                       Alu.subtract, Alu.mult)
                c.drain(v, v.reduce_max(colw[:, 8:9], g2sl[:, i_ % 2, :], AX,
                                        apply_absolute_value=True))
                i2 = v.tensor_tensor(colw[:, 24:25], colw[:, 24:25], colw[:, 8:9], Alu.max)
                p3_vv.append(c.inc(i2, "vv", 1))

            @blk.gpsimd
            def _(g, i_=i_):
                c.wait(g, "vv")
                c.inc(g.dma_start(out=g2_s[i_], in_=g2sl[:, i_ % 2, :]), "st", 16)
        barrier()

        def ex3(v):
            v.tensor_tensor(colw[:, 17:18], colw[:, 12:13], colw[:, 16:17], Alu.mult)
        scale_ops(gmax(24, mxb + 3, 16), 16, None, extra=ex3)
        @blk.vector
        def _(v):
            c.wait(v, "pp", c.val("pp"))
            c.drain(v, v.reciprocal(colw[:, 22:23], colw[:, 16:17]))
            i = v.engine_nop()
            c.inc(i, "pp", 1)
        barrier()

        @blk.vector
        def _(v):
            i = v.memset(colw[:, 24:25], 0.0)
            c.inc(i, "vv", 1)
        for i_ in range(24):
            @blk.sync
            def _(sy, i_=i_):
                if i_ >= 2:
                    c.wait(sy, "vv", p4_vv[i_ - 2])
                c.inc(sy.dma_start(out=ldg[:, i_ % 2, :], in_=g2_s[i_]), "ld", 16)
                c.inc(sy.dma_start(out=ldu[:, i_ % 2, :], in_=up_s[i_]), "ld", 16)
            ldt = c.val("ld")

            @blk.scalar
            def _(sc, i_=i_, ldt=ldt):
                c.wait(sc, "ld", ldt)
                if i_ >= 1:
                    c.wait(sc, "vv", c.val("vv"))
                sc.activation(t3a[:, :], ldu[:, i_ % 2, :], AF.Identity,
                              bias=consts[:, 0:1], scale=colw[:, 13:14])
                i2 = sc.activation(t3b[:, :], ldg[:, i_ % 2, :], AF.Identity,
                                   bias=consts[:, 0:1], scale=colw[:, 22:23])
                c.inc(i2, "ev", 1)

            @blk.vector
            def _(v, i_=i_):
                c.wait(v, "ev")
                v.tensor_scalar(t3a[:, :], t3a[:, :], MAGIC, colw[:, 17:18],
                                Alu.subtract, Alu.mult)
                v.scalar_tensor_tensor(t4sl[:, i_ % 2, :], t3b[:, :], MAGIC, t3a[:, :],
                                       Alu.subtract, Alu.mult)
                c.drain(v, v.reduce_max(colw[:, 8:9], t4sl[:, i_ % 2, :], AX,
                                        apply_absolute_value=True))
                i2 = v.tensor_tensor(colw[:, 24:25], colw[:, 24:25], colw[:, 8:9], Alu.max)
                p4_vv.append(c.inc(i2, "vv", 1))

            @blk.gpsimd
            def _(g, i_=i_):
                c.wait(g, "vv")
                c.inc(g.dma_start(out=t4_s[i_], in_=t4sl[:, i_ % 2, :]), "st", 16)
        barrier()
        scale_ops(gmax(24, mxb + 4, 18), 18, 19)
        barrier()

        @blk.sync
        def _(sy):
            c.inc(sy.dma_start(out=wdT_all[:, :, :],
                               in_=wdT_dq.rearrange("i p o -> p i o")), "ld", 16)
        wd_ld = c.val("ld")

        for tb in range(16):
            @blk.sync
            def _(sy, tb=tb):
                if tb >= 2:
                    c.wait(sy, "vv", p5_vv[tb - 2])
                c.inc(sy.dma_start(out=oqin[:, tb % 2, :, :],
                                   in_=t4_s[:, :, tb * 256:(tb + 1) * 256].rearrange("i p t -> p i t")), "ld", 16)
                c.inc(sy.dma_start(out=xr4[:, tb % 2, :, :],
                                   in_=src[tb * 256:(tb + 1) * 256, :].rearrange("(t p) o -> p t o", p=128)), "ld", 16)
            ldt = c.val("ld")

            @blk.scalar
            def _(sc, tb=tb, ldt=ldt):
                c.wait(sc, "ld", ldt)
                if tb >= 1:
                    c.wait(sc, "vv", c.val("vv"))
                i2 = sc.activation(t3a[:, 0:4096].rearrange("p (a b) -> p a b", a=16)
                                   if False else t3a[:, :], oqin[:, tb % 2, :, :].rearrange("p a b -> p (a b)")[:, 0:4096],
                                   AF.Identity, bias=consts[:, 0:1], scale=colw[:, 19:20])
                i3 = sc.activation(t3b[:, 0:2048], oqin[:, tb % 2, :, :].rearrange("p a b -> p (a b)")[:, 4096:6144],
                                   AF.Identity, bias=consts[:, 0:1], scale=colw[:, 19:20])
                c.inc(i3, "ev", 1)

            @blk.vector
            def _(v, tb=tb):
                c.wait(v, "ev")
                if tb >= 1:
                    c.wait(v, "mm", p5_mm[tb - 1])
                if tb >= 2:
                    c.wait(v, "st", p5_st[tb - 2])
                v.tensor_scalar(oqb[:, :, :].rearrange("p a b -> p (a b)")[:, 0:4096],
                                t3a[:, :], MAGIC, None, Alu.subtract)
                i2 = v.tensor_scalar(oqb[:, :, :].rearrange("p a b -> p (a b)")[:, 4096:6144],
                                     t3b[:, 0:2048], MAGIC, None, Alu.subtract)
                c.inc(i2, "vv", 1)
            oq_vv = c.val("vv")

            for u in range(4):
                ts2, occ = divmod(u, 2)

                @blk.tensor
                def _(t, tb=tb, ts2=ts2, occ=occ, u=u, oq_vv=oq_vv, wd_ld=wd_ld):
                    if tb == 0 and u == 0:
                        c.wait(t, "ld", wd_ld)
                    c.wait(t, "vv", oq_vv)
                    if tb >= 1:
                        c.wait(t, "vv", c.val("vv"))
                    bank = pb[u]
                    i2 = None
                    for ic in range(24):
                        i2 = t.matmul(bank[:, :], oqb[:, ic, ts2 * 128:(ts2 + 1) * 128],
                                      wdT_all[:, ic, occ * 512:(occ + 1) * 512],
                                      start=(ic == 0), stop=(ic == 23))
                    mv = c.inc(i2, "mm", 1)
                    if u == 3:
                        p5_mm.append(mv)

                @blk.vector
                def _(v, tb=tb, ts2=ts2, occ=occ, u=u):
                    c.wait(v, "mm")
                    i2 = v.scalar_tensor_tensor(ost[:, tb % 2, ts2, occ * 512:(occ + 1) * 512],
                                                pb[u][:, :], colw[:, 18:19],
                                                xr4[:, tb % 2, ts2, occ * 512:(occ + 1) * 512],
                                                Alu.mult, Alu.add)
                    if u == 3:
                        p5_vv.append(c.inc(i2, "vv", 1))
                    else:
                        c.inc(i2, "vv", 1)

            @blk.gpsimd
            def _(g, tb=tb):
                c.wait(g, "vv")
                p5_st.append(c.inc(g.dma_start(
                    out=dst[tb * 256:(tb + 1) * 256, :].rearrange("(t p) o -> p t o", p=128),
                    in_=ost[:, tb % 2, :, :]), "st", 16))
        barrier()


    def phase_attn_b():
        TWO_PI = 2.0 * PI

        def emit_ln2(xsrc_tile, xnstg_ap, xc_s2, sq_s2, tc_guard_mm=None):
            @blk.vector
            def _(v):
                c.drain(v, v.tensor_reduce(colw[:, 0:1], xsrc_tile, AX, Alu.add))
                c.drain(v, v.tensor_scalar(colw[:, 1:2], colw[:, 0:1], -1.0 / D, None, Alu.mult))
                i = v.tensor_scalar(xc_s2[:, :], xsrc_tile, colw[:, 1:2], None, Alu.add)
                c.inc(i, "vv", 1)

            @blk.scalar
            def _(sc):
                c.wait(sc, "vv")
                c.inc(sc.activation(sq_s2[:, :], xc_s2[:, :], AF.Square), "ev", 1)

            @blk.vector
            def _(v):
                c.wait(v, "ev")
                c.drain(v, v.tensor_reduce(colw[:, 2:3], sq_s2[:, :], AX, Alu.add))
                i = v.tensor_scalar(colw[:, 2:3], colw[:, 2:3], 1.0 / D, EPS, Alu.mult, Alu.add)
                c.inc(i, "vv", 1)

            @blk.scalar
            def _(sc):
                c.wait(sc, "vv")
                c.inc(sc.activation(colw[:, 3:4], colw[:, 2:3], AF.Sqrt), "ev", 1)

            @blk.vector
            def _(v, tc_guard_mm=tc_guard_mm):
                c.wait(v, "ev")
                if tc_guard_mm is not None:
                    c.wait(v, "mm", tc_guard_mm)
                c.drain(v, v.reciprocal(colw[:, 3:4], colw[:, 3:4]))
                i = v.tensor_scalar(xnstg_ap, xc_s2[:, :], colw[:, 3:4], None, Alu.mult)
                c.inc(i, "vv", 1)

        def build_rfp(rfp, tag):
            @blk.gpsimd
            def _(g):
                i = g.memset(rfp[:, :], 0.0)
                c.inc(i, "pp", 1)
                g.wait_ge(c.sem["pp"], c.cnt["pp"])
                c.inc(g.dma_start(out=rfp[0:64, 0:256], in_=rfTi), "cst", 16)
                c.inc(g.dma_start(out=rfp[64:128, 256:512], in_=rfTi), "cst", 16)

        def emit_trig_pair(src_f32, cos_dst, sin_dst, u1, u2, pre_ev, pre_mm):
            @blk.vector
            def _(v):
                c.wait(v, "ev", pre_ev)
                v.tensor_scalar(u1[:, :], src_f32, 1.0 / TWO_PI, MAGIC, Alu.mult, Alu.add)
                v.tensor_scalar(u1[:, :], u1[:, :], MAGIC, TWO_PI, Alu.subtract, Alu.mult)
                v.scalar_tensor_tensor(u1[:, :], u1[:, :], -1.0, src_f32, Alu.mult, Alu.add)
                v.tensor_scalar(u2[:, :], src_f32, PI / 2, 1.0 / TWO_PI, Alu.add, Alu.mult)
                v.tensor_scalar(u2[:, :], u2[:, :], MAGIC, MAGIC, Alu.add, Alu.subtract)
                v.tensor_scalar(u2[:, :], u2[:, :], TWO_PI, None, Alu.mult)
                i = v.scalar_tensor_tensor(u2[:, :], u2[:, :], -1.0, src_f32, Alu.mult, Alu.add)
                c.inc(i, "vv", 1)

            @blk.scalar
            def _(sc):
                c.wait(sc, "vv")
                c.wait(sc, "mm", pre_mm)
                sc.activation(sin_dst, u1[:, :], AF.Sin)
                i = sc.activation(cos_dst, u2[:, :], AF.Sin, bias=consts[:, 1:2])
                c.inc(i, "ev", 1)

        arf.reset()
        arb.reset()
        x2til = arf.take(4, D)
        xc_s2 = arf.take(D)
        sq_s2 = arf.take(D)
        projp = arf.take(2, 4, 512)
        sqKp = arf.at(4096, 4, 128)
        u1_s = arf.take(2048)
        u2_s = arf.take(2048)
        kvstg = arf.take(16, 264)
        assert arf.off <= AF_ELE, arf.off
        wkTb = arb.take(8, D)
        wvTb = arb.take(8, D)
        xnT_ab = arb.take(8, TL)
        K_fm = arb.take(8, TL)
        V_tm = arb.take(4, D)
        xnstg2 = arb.take(D)
        Kfcsp = arb.take(2, 4, 1024)
        Vsbp = arb.take(2, 4, 2, 66)
        rfp = arb.take(512)
        assert arb.off <= AB_ELE, arb.off

        @blk.gpsimd
        def _(g):
            c.inc(g.dma_start(out=wkTb[:, :, :], in_=wkT.rearrange("(k p) o -> p k o", p=128)), "cst", 16)
            c.inc(g.dma_start(out=wvTb[:, :, :], in_=wvT.rearrange("(k p) o -> p k o", p=128)), "cst", 16)
        build_rfp(rfp, "s1")
        barrier()

        ck = {}
        for b in range(NB):
            @blk.sync
            def _(sy, b=b):
                c.inc(sy.dma_start(out=x2til[:, :, :],
                                   in_=xres[b * TL:(b + 1) * TL, :].rearrange("(t p) o -> p t o", p=128)),
                      "ld", 16)
            ldt = c.val("ld")

            @blk.vector
            def _(v, ldt=ldt):
                c.wait(v, "ld", ldt)

            for tc in range(4):
                tgm = None if (b == 0 and tc == 0) else c.val("mm")
                emit_ln2(x2til[:, tc, :], xnstg2[:, :], xc_s2, sq_s2, tc_guard_mm=tgm)

                @blk.tensor
                def _(t, tc=tc):
                    c.wait(t, "vv", c.val("vv"))
                    i = None
                    for kc in range(8):
                        i = t.transpose(pt[tc % 2][:, kc, :], xnstg2[:, kc * 128:(kc + 1) * 128], ident_b[:, :])
                    c.inc(i, "mm", 1)

                @blk.vector
                def _(v, tc=tc):
                    c.wait(v, "mm")
                    i = v.tensor_copy(xnT_ab[:, 0:8, tc * 128:(tc + 1) * 128], pt[tc % 2][:, :, :])
                    c.inc(i, "vv", 1)
            xn_vv = c.val("vv")

            @blk.gpsimd
            def _(g, b=b, xn_vv=xn_vv):
                c.wait(g, "vv", xn_vv)
                c.inc(g.dma_start(out=xnT_sp[b], in_=xnT_ab[:, :, :].rearrange("p a b -> p (a b)")), "st", 16)

            ev_k0 = c.val("ev")
            for oc in range(8):
                @blk.tensor
                def _(t, oc=oc, ev_k0=ev_k0):
                    if oc >= 4:
                        c.wait(t, "ev", ev_k0 + oc - 3)
                    bank = pb[oc % 4]
                    i = None
                    for kc in range(8):
                        i = t.matmul(bank[:, :], wkTb[:, kc, oc * 128:(oc + 1) * 128],
                                     xnT_ab[:, kc, :], start=(kc == 0), stop=(kc == 7))
                    c.inc(i, "mm", 1)

                @blk.scalar
                def _(sc, oc=oc):
                    c.wait(sc, "mm")
                    c.inc(sc.activation(K_fm[:, oc, :], pb[oc % 4][:, :], AF.Copy), "ev", 1)

            ev_v0 = c.val("ev")
            for u in range(8):
                ts_, occ = divmod(u, 2)

                @blk.tensor
                def _(t, ts_=ts_, occ=occ, u=u, ev_v0=ev_v0):
                    if u >= 4:
                        c.wait(t, "ev", ev_v0 + u - 3)
                    bank = pb[u % 4]
                    i = None
                    for kc in range(8):
                        i = t.matmul(bank[:, :], xnT_ab[:, kc, ts_ * 128:(ts_ + 1) * 128],
                                     wvTb[:, kc, occ * 512:(occ + 1) * 512],
                                     start=(kc == 0), stop=(kc == 7))
                    c.inc(i, "mm", 1)

                @blk.scalar
                def _(sc, ts_=ts_, occ=occ):
                    c.wait(sc, "mm")
                    c.inc(sc.activation(V_tm[:, ts_, occ * 512:(occ + 1) * 512],
                                        pb[u % 4][:, :], AF.Copy), "ev", 1)
            prep_ev = c.val("ev")

            for pr in range(8):
                gi = b * 8 + pr
                par = gi % 2

                @blk.tensor
                def _(t, pr=pr, gi=gi, prep_ev=prep_ev):
                    if pr == 0:
                        c.wait(t, "ev", prep_ev)
                        c.wait(t, "vv", c.val("vv"))
                    if gi >= 1:
                        c.wait(t, "ev", ck[("prev", gi - 1)])
                    if gi >= 1:
                        c.wait(t, "ev", ck[("kvev", gi - 1)])
                    i = None
                    for ts_ in range(4):
                        t.matmul(pb[ts_][:, 0:512], K_fm[:, pr, ts_ * 128:(ts_ + 1) * 128],
                                 rfp[:, :], start=True, stop=True)
                        i = t.matmul(pm[:, ts_ * 128:(ts_ + 1) * 128],
                                     K_fm[:, pr, ts_ * 128:(ts_ + 1) * 128],
                                     ident_b[:, :], start=True, stop=True)
                    ck[("prmm", gi)] = c.inc(i, "mm", 1)

                @blk.scalar
                def _(sc, gi=gi, par=par):
                    c.wait(sc, "mm", ck[("prmm", gi)])
                    if gi >= 2:
                        c.wait(sc, "vv", ck[("trgv", gi - 2)])
                    for ts_ in range(4):
                        sc.activation(projp[:, par, ts_, :], pb[ts_][:, 0:512], AF.Copy)
                    i = sc.activation(sqKp[:, :, :].rearrange("p a b -> p (a b)"), pm[:, :], AF.Square)
                    ck[("prev", gi)] = c.inc(i, "ev", 1)

                @blk.vector
                def _(v, gi=gi, par=par):
                    c.wait(v, "ev", ck[("prev", gi)])
                    i = None
                    for hh in range(2):
                        i = v.tensor_reduce(
                            colw[:, 4 + 12 * par + 4 * hh:8 + 12 * par + 4 * hh].rearrange("p (g o) -> p g o", o=1),
                            sqKp[:, :, hh * 64:(hh + 1) * 64], AX, Alu.add)
                    ck[("nsqv", gi)] = c.inc(i, "vv", 1)

                @blk.scalar
                def _(sc, gi=gi, par=par):
                    c.wait(sc, "vv", ck[("nsqv", gi)])
                    i = sc.activation(colw[:, 4 + 12 * par:12 + 12 * par],
                                      colw[:, 4 + 12 * par:12 + 12 * par], AF.Exp, scale=-0.5)
                    ck[("expe", gi)] = c.inc(i, "ev", 1)

                @blk.vector
                def _(v, gi=gi, par=par):
                    c.wait(v, "ev", ck[("expe", gi)])
                    c.drain(v, v.tensor_scalar(colw[:, 4 + 12 * par:12 + 12 * par],
                                               colw[:, 4 + 12 * par:12 + 12 * par],
                                               1.0 / 16.0, None, Alu.mult))
                    i = v.engine_nop()
                    ck[("kscv", gi)] = c.inc(i, "vv", 1)

                emit_trig_pair(projp[:, par, :, :].rearrange("p a b -> p (a b)"),
                               Kfcsp[:, par, :, :].rearrange("p a (h x) -> p a h x", h=2)[:, :, :, 0:256],
                               Kfcsp[:, par, :, :].rearrange("p a (h x) -> p a h x", h=2)[:, :, :, 256:512],
                               u1_s, u2_s,
                               pre_ev=ck[("prev", gi)],
                               pre_mm=ck.get(("kvmm", gi - 2), 0))
                ck[("trgv", gi)] = c.val("vv")
                ck[("trge", gi)] = c.val("ev")

                @blk.vector
                def _(v, pr=pr, gi=gi, par=par):
                    c.wait(v, "vv", ck[("kscv", gi)])
                    if gi >= 2:
                        c.wait(v, "mm", ck[("kvmm", gi - 2)])
                    v.memset(Vsbp[:, par, :, :, 65:66], 0.0)
                    for ts_ in range(4):
                        for hh in range(2):
                            v.tensor_scalar(Vsbp[:, par, ts_, hh, 0:64],
                                            V_tm[:, ts_, (pr * 2 + hh) * 64:(pr * 2 + hh + 1) * 64],
                                            colw[:, 4 + 12 * par + 4 * hh + ts_:5 + 12 * par + 4 * hh + ts_],
                                            None, Alu.mult)
                    for hh in range(2):
                        i = v.tensor_copy(Vsbp[:, par, :, hh, 64:65],
                                          colw[:, 4 + 12 * par + 4 * hh:8 + 12 * par + 4 * hh]
                                          .rearrange("p (g o) -> p g o", o=1))
                    ck[("vsv", gi)] = c.inc(i, "vv", 1)

                @blk.tensor
                def _(t, pr=pr, gi=gi, par=par):
                    c.wait(t, "ev", ck[("trge", gi)])
                    c.wait(t, "vv", ck[("vsv", gi)])
                    if gi >= 1:
                        c.wait(t, "ev", ck[("kvev", gi - 1)])
                    i = None
                    for hh in range(2):
                        bank = pm2 if hh == 0 else pb[0]
                        for fs in range(4):
                            for ts_ in range(4):
                                i = t.matmul(bank[:, fs * 66:(fs + 1) * 66],
                                             Kfcsp[:, par, ts_, hh * 512 + fs * 128:hh * 512 + (fs + 1) * 128],
                                             Vsbp[:, par, ts_, hh, :], start=(ts_ == 0), stop=(ts_ == 3))
                    ck[("kvmm", gi)] = c.inc(i, "mm", 1)

                @blk.scalar
                def _(sc, pr=pr, gi=gi):
                    c.wait(sc, "mm", ck[("kvmm", gi)])
                    sc.activation(kvstg[:, pr * 2, :], pm2[:, 0:264], AF.Copy)
                    i = sc.activation(kvstg[:, pr * 2 + 1, :], pb[0][:, 0:264], AF.Copy)
                    ck[("kvev", gi)] = c.inc(i, "ev", 1)

            @blk.gpsimd
            def _(g, b=b):
                c.wait(g, "ev", c.val("ev"))
                c.inc(g.dma_start(out=kv_in[b * 128:(b + 1) * 128, :], in_=kvstg[:, :, :]), "st", 16)
            barrier()

        @blk.gpsimd
        def _(g):
            c.wait(g, "st", c.val("st"))
            i = g.collective_compute("AllReduce", Alu.add, replica_groups=[list(range(NCORE))],
                                     ins=[kv_in.opt()], outs=[kv_out.opt()])
            c.inc(i, "cc", 1)
        barrier()

        arf.reset()
        arb.reset()
        projq = arf.take(2, 4, 512)
        u1_s = arf.take(2048)
        u2_s = arf.take(2048)
        xr2 = arf.take(4, D)
        ost2 = arf.take(4, D)
        sqQp = arf.at(arf.off - 4096, 4, 128)
        szz = arf.take(2, 264)
        assert arf.off <= AF_ELE, arf.off
        wqTb = arb.take(8, D)
        woTb = arb.take(8, D)
        xnT_ab = arb.take(8, TL)
        Q_fm = arb.take(8, TL)
        Qfcsp = arb.take(2, 8, 512)
        attT = arb.take(8, TL)
        kvbb = arb.take(16, 264)
        attstg2 = arb.take(4, 128)
        rfp = arb.take(512)
        assert arb.off <= AB_ELE, arb.off

        @blk.gpsimd
        def _(g):
            c.inc(g.dma_start(out=wqTb[:, :, :], in_=wqT.rearrange("(k p) o -> p k o", p=128)), "cst", 16)
            c.inc(g.dma_start(out=woTb[:, :, :], in_=woT.rearrange("(k p) o -> p k o", p=128)), "cst", 16)
        build_rfp(rfp, "s2")
        barrier()

        for b in range(NB):
            @blk.sync
            def _(sy, b=b):
                c.inc(sy.dma_start(out=xnT_ab[:, :, :].rearrange("p a b -> p (a b)"), in_=xnT_sp[b]), "ld", 16)
                c.inc(sy.dma_start(out=xr2[:, :, :],
                                   in_=xres[b * TL:(b + 1) * TL, :].rearrange("(t p) o -> p t o", p=128)), "ld", 16)
            ldt = c.val("ld")

            @blk.gpsimd
            def _(g, b=b):
                c.inc(g.dma_start(out=kvbb[:, :, :].rearrange("p a b -> p (a b)"),
                                  in_=kv_out[b * 128:(b + 1) * 128, :]), "cst", 16)
            kv_ld = c.val("cst")

            ev_q0 = c.val("ev")
            for oc in range(8):
                @blk.tensor
                def _(t, oc=oc, ev_q0=ev_q0, ldt=ldt):
                    if oc == 0:
                        c.wait(t, "ld", ldt)
                    if oc >= 4:
                        c.wait(t, "ev", ev_q0 + oc - 3)
                    bank = pb[oc % 4]
                    i = None
                    for kc in range(8):
                        i = t.matmul(bank[:, :], wqTb[:, kc, oc * 128:(oc + 1) * 128],
                                     xnT_ab[:, kc, :], start=(kc == 0), stop=(kc == 7))
                    c.inc(i, "mm", 1)

                @blk.scalar
                def _(sc, oc=oc):
                    c.wait(sc, "mm")
                    c.inc(sc.activation(Q_fm[:, oc, :], pb[oc % 4][:, :], AF.Copy), "ev", 1)
            prep_ev = c.val("ev")

            if dbg and b == 0:
                dbg7 = nc.dram_tensor("dbg7", [128, 8192], F32, kind="ExternalOutput").ap()

                @blk.gpsimd
                def _(g, prep_ev=prep_ev):
                    c.wait(g, "ev", prep_ev)
                    c.inc(g.dma_start(out=dbg7[:, 0:4096], in_=Q_fm[:, :, :].rearrange("p a b -> p (a b)")), "st", 16)
                    c.inc(g.dma_start(out=dbg7[:, 4096:8192], in_=xnT_ab[:, :, :].rearrange("p a b -> p (a b)")), "st", 16)

            for pr in range(8):
                gi = 64 + b * 8 + pr
                par = gi % 2

                @blk.tensor
                def _(t, pr=pr, gi=gi, par=par, prep_ev=prep_ev):
                    if pr == 0:
                        c.wait(t, "ev", prep_ev)
                        c.wait(t, "vv", c.val("vv"))
                        c.wait(t, "cst", kv_ld)
                    if gi >= 65:
                        c.wait(t, "ev", ck[("prev", gi - 1)])
                        c.wait(t, "ev", ck.get(("qze", gi - 1), 0))
                    i = None
                    for blk_ in range(4):
                        t.matmul(pb[blk_][:, 0:512], rfp[:, blk_ * 128:(blk_ + 1) * 128],
                                 Q_fm[:, pr, :], start=True, stop=True)
                        i = t.matmul(pm[:, blk_ * 128:(blk_ + 1) * 128],
                                     K_fm[:, pr, blk_ * 128:(blk_ + 1) * 128] if False
                                     else Q_fm[:, pr, blk_ * 128:(blk_ + 1) * 128],
                                     ident_b[:, :], start=True, stop=True)
                    ck[("prmm", gi)] = c.inc(i, "mm", 1)

                @blk.scalar
                def _(sc, gi=gi, par=par):
                    c.wait(sc, "mm", ck[("prmm", gi)])
                    if gi >= 66:
                        c.wait(sc, "vv", ck[("trgv", gi - 2)])
                    for blk_ in range(4):
                        sc.activation(projq[:, par, blk_, :], pb[blk_][:, 0:512], AF.Copy)
                    i = sc.activation(sqQp[:, :, :].rearrange("p a b -> p (a b)"), pm[:, :], AF.Square)
                    ck[("prev", gi)] = c.inc(i, "ev", 1)

                @blk.vector
                def _(v, gi=gi, par=par):
                    c.wait(v, "ev", ck[("prev", gi)])
                    i = None
                    for hh in range(2):
                        i = v.tensor_reduce(
                            colw[:, 4 + 12 * par + 4 * hh:8 + 12 * par + 4 * hh].rearrange("p (g o) -> p g o", o=1),
                            sqQp[:, :, hh * 64:(hh + 1) * 64], AX, Alu.add)
                    ck[("nsqv", gi)] = c.inc(i, "vv", 1)

                @blk.scalar
                def _(sc, gi=gi, par=par):
                    c.wait(sc, "vv", ck[("nsqv", gi)])
                    i = sc.activation(colw[:, 4 + 12 * par:12 + 12 * par],
                                      colw[:, 4 + 12 * par:12 + 12 * par], AF.Exp, scale=-0.5)
                    ck[("expe", gi)] = c.inc(i, "ev", 1)

                @blk.vector
                def _(v, gi=gi, par=par):
                    c.wait(v, "ev", ck[("expe", gi)])
                    c.drain(v, v.tensor_scalar(colw[:, 4 + 12 * par:12 + 12 * par],
                                               colw[:, 4 + 12 * par:12 + 12 * par],
                                               1.0 / 16.0, None, Alu.mult))
                    i = v.engine_nop()
                    ck[("kscv", gi)] = c.inc(i, "vv", 1)

                emit_trig_pair(projq[:, par, :, :].rearrange("p a b -> p (a b)"),
                               Qfcsp[:, par, :, :].rearrange("p (h f) x -> p h f x", h=2)[:, :, 0:2, :],
                               Qfcsp[:, par, :, :].rearrange("p (h f) x -> p h f x", h=2)[:, :, 2:4, :],
                               u1_s, u2_s,
                               pre_ev=ck[("prev", gi)],
                               pre_mm=ck.get(("qkmm", gi - 2), 0))
                ck[("trgv", gi)] = c.val("vv")
                ck[("trge", gi)] = c.val("ev")

                @blk.tensor
                def _(t, pr=pr, gi=gi, par=par):
                    c.wait(t, "ev", ck[("trge", gi)])
                    if gi >= 65:
                        c.wait(t, "ev", ck.get(("qze", gi - 1), 0))
                    i = None
                    for hh in range(2):
                        bank = pm2 if hh == 0 else pb[0]
                        h = pr * 2 + hh
                        for ts_ in range(4):
                            for fc in range(4):
                                i = t.matmul(bank[:, ts_ * 66:ts_ * 66 + 66],
                                             Qfcsp[:, par, hh * 4 + fc, ts_ * 128:(ts_ + 1) * 128],
                                             kvbb[:, h, fc * 66:(fc + 1) * 66],
                                             start=(fc == 0), stop=(fc == 3))
                    ck[("qkmm", gi)] = c.inc(i, "mm", 1)

                @blk.scalar
                def _(sc, gi=gi):
                    c.wait(sc, "mm", ck[("qkmm", gi)])
                    if gi >= 65:
                        c.wait(sc, "vv", ck[("epiv", gi - 1)])
                    sc.activation(szz[:, 0, :], pm2[:, 0:264], AF.Copy)
                    i = sc.activation(szz[:, 1, :], pb[0][:, 0:264], AF.Copy)
                    ck[("qze", gi)] = c.inc(i, "ev", 1)

                @blk.vector
                def _(v, pr=pr, gi=gi, par=par):
                    c.wait(v, "ev", ck[("qze", gi)])
                    c.wait(v, "vv", ck[("kscv", gi)])
                    for hh in range(2):
                        qv3 = colw[:, 4 + 12 * par + 4 * hh:8 + 12 * par + 4 * hh].rearrange("p (g o) -> p g o", o=1)
                        zz = colw[:, 24:28] if par == 0 else colw[:, 28:32]
                        zz3 = zz.rearrange("p (g o) -> p g o", o=1)
                        c.drain(v, v.tensor_tensor(zz3, qv3,
                                        szz[:, hh, :].rearrange("p (w c) -> p w c", c=66)[:, :, 64:65],
                                        Alu.mult))
                        c.drain(v, v.tensor_scalar(zz, zz, 1e-6, None, Alu.max))
                        c.drain(v, v.reciprocal(zz, zz))
                        c.drain(v, v.tensor_tensor(zz3, qv3, zz3, Alu.mult))
                        for ts_ in range(4):
                            i = v.tensor_scalar(attstg2[:, ts_, hh * 64:(hh + 1) * 64],
                                                szz[:, hh, ts_ * 66:ts_ * 66 + 64],
                                                zz[:, ts_:ts_ + 1], None, Alu.mult)
                    ck[("epiv", gi)] = c.inc(i, "vv", 1)

                if dbg and gi == 64:
                    dbg8 = nc.dram_tensor("dbg8", [128, 2048], F32, kind="ExternalOutput").ap()

                    @blk.gpsimd
                    def _(g, gi=gi):
                        c.wait(g, "vv", ck[("epiv", gi)])
                        c.inc(g.dma_start(out=dbg8[:, 0:512], in_=attstg2[:, :, :].rearrange("p a b -> p (a b)")), "st", 16)
                        c.inc(g.dma_start(out=dbg8[:, 512:1040], in_=szz[:, :, :].rearrange("p a b -> p (a b)")), "st", 16)
                        c.inc(g.dma_start(out=dbg8[:, 1040:1072], in_=colw[:, 0:32]), "st", 16)
                        c.inc(g.dma_start(out=dbg8[:, 1072:1584], in_=sqQp[:, :, :].rearrange("p a b -> p (a b)")), "st", 16)

                @blk.tensor
                def _(t, pr=pr, gi=gi):
                    c.wait(t, "vv", ck[("epiv", gi)])
                    i = None
                    for ts_ in range(4):
                        i = t.transpose(pt[1][:, ts_, :], attstg2[:, ts_, :], ident_b[:, :])
                    ck[("attm", gi)] = c.inc(i, "mm", 1)

                @blk.vector
                def _(v, pr=pr, gi=gi):
                    c.wait(v, "mm", ck[("attm", gi)])
                    i = v.tensor_copy(attT[:, pr, :], pt[1][:, 0:4, :])
                    ck[("attv", gi)] = c.inc(i, "vv", 1)
            att_vv = c.val("vv")

            if dbg and b == 0:
                dbg6 = nc.dram_tensor("dbg6", [128, 16384], F32, kind="ExternalOutput").ap()

                @blk.gpsimd
                def _(g):
                    c.wait(g, "vv", att_vv)
                    c.inc(g.dma_start(out=dbg6[:, 0:4096], in_=attT[:, :, :].rearrange("p a b -> p (a b)")), "st", 16)
                    c.inc(g.dma_start(out=dbg6[:, 4096:12288], in_=Qfcsp[:, :, :, :].rearrange("p a b c -> p (a b c)")), "st", 16)
                    c.inc(g.dma_start(out=dbg6[:, 12288:16384], in_=projq[:, :, :, :].rearrange("p a b c -> p (a b c)")), "st", 16)

            for u in range(8):
                ts_, occ = divmod(u, 2)

                @blk.tensor
                def _(t, ts_=ts_, occ=occ, u=u, att_vv=att_vv):
                    if u == 0:
                        c.wait(t, "vv", att_vv)
                    if u >= 4:
                        c.wait(t, "vv", c.val("vv"))
                    bank = pb[u % 4]
                    i = None
                    for cc in range(8):
                        i = t.matmul(bank[:, :], attT[:, cc, ts_ * 128:(ts_ + 1) * 128],
                                     woTb[:, cc, occ * 512:(occ + 1) * 512],
                                     start=(cc == 0), stop=(cc == 7))
                    c.inc(i, "mm", 1)

                @blk.vector
                def _(v, ts_=ts_, occ=occ, u=u):
                    c.wait(v, "mm")
                    i = v.tensor_tensor(ost2[:, ts_, occ * 512:(occ + 1) * 512],
                                        pb[u % 4][:, :],
                                        xr2[:, ts_, occ * 512:(occ + 1) * 512], Alu.add)
                    c.inc(i, "vv", 1)

            @blk.gpsimd
            def _(g, b=b):
                c.wait(g, "vv", c.val("vv"))
                c.inc(g.dma_start(
                    out=xres[b * TL:(b + 1) * TL, :].rearrange("(t p) o -> p t o", p=128),
                    in_=ost2[:, :, :]), "st", 16)
            barrier()

    phase_setup()
    phase_attn_a()
    if stage >= 2:
        phase_dequant()
        phase_ffn(xres, y_out if stage == 2 else xres, 0)
    if stage >= 3:
        phase_attn_b()
        if stage == 3:
            @blk.gpsimd
            def _(g):
                c.wait(g, "st", c.val("st"))
                i = g.dma_start(out=y_out, in_=xres)
                c.inc(i, "st", 16)
                c.wait(g, "st")
    if stage >= 4:
        phase_ffn(xres, y_out, 5)

    if dbg and stage >= 3:
        kvtap = arf.at(0, 16, 264)

        @blk.sync
        def _(sy):
            c.wait(sy, "st", c.val("st"))
            c.inc(sy.dma_start(out=kvtap[:, :, :].rearrange("p a b -> p (a b)"),
                               in_=kv_out[0:128, :]), "ld", 16)

        @blk.gpsimd
        def _(g):
            c.wait(g, "ld", c.val("ld"))

    if dbg and stage >= 2:
        global DBG_MAP2
        DBG_MAP2 = {}
        dbg2 = nc.dram_tensor("dbg2", [128, 40960], F32, kind="ExternalOutput").ap()

        @blk.gpsimd
        def _(g):
            off = 0
            taps3 = ([("kv0", arf.at(0, 4224), 4224),
                      ("attT", arb.at(26624, 4096), 4096),
                      ("qfcs", arb.at(24576, 2048), 2048)]
                     if stage >= 3 else [])
            for name, ap_, n in taps3 + [("colw", colw[:, :], 32),
                                 ("gate0", gate_s[0], 4096), ("up0", up_s[0], 4096),
                                 ("g20", g2_s[0], 4096), ("t40", t4_s[0], 4096),
                                 ("wg0", wgT_dq[0, 0], 128), ("wg1", wgT_dq[0, 1], 128),
                                 ("wd0", wdT_dq[0], 1024)]:
                DBG_MAP2[name] = (off, n)
                c.inc(g.dma_start(out=dbg2[:, off:off + n], in_=ap_), "st", 16)
                off += n
            c.wait(g, "st")

    if dbg:
        taps = [("xnT", xnT, BF16, 4096), ("z_fm", z_fm, BF16, 4096),
                ("z_tm", z_tm, BF16, 4096), ("q_fm", q_fm, BF16, 4096),
                ("G_fm", G_fm, BF16, 4096), ("G_tm", G_tm, BF16, 4096),
                ("h_tm", h_tm, BF16, 1024), ("gp_b", gp_b, BF16, 1024),
                ("da_fm", da_fm, BF16, 1024), ("da_tm", da_tm, BF16, 1024),
                ("h2_fm", h_fm, BF16, 1024), ("a_f", a_f, F32, 1024),
                ("gst1", gst1, F32, 2048), ("gst2", gst2, F32, 2048),
                ("zy", y_fm, BF16, 4096), ("xs", xs, F32, 4096),
                ("scrb", scr_b, F32, 2048), ("w1T_b", w1T_b, BF16, 2048),
                ("w2T_b", w2T_b, BF16, 2048), ("w2n_b", w2n_b, BF16, 2048)]
        off = [0]
        global DBG_MAP
        DBG_MAP = {}

        @blk.gpsimd
        def _(g):
            for name, ap, dt_, n in taps:
                flat = ap
                if len(flat.shape) == 3:
                    flat = flat.rearrange("p a b -> p (a b)")
                rows = flat.shape[0]
                DBG_MAP[name] = (off[0], n, rows)
                i = g.dma_start(out=dbg_out[0:rows, off[0]:off[0] + n], in_=flat)
                c.inc(i, "st", 16)
                off[0] += n
            c.wait(g, "st")

    if stage == 1:
        @blk.gpsimd
        def _(g):
            c.wait(g, "st", c.val("st"))
            i = g.dma_start(out=y_out, in_=xres)
            c.inc(i, "st", 16)
            c.wait(g, "st")

    es.close()
    return nc


_CACHE = {}


def _get_prog(stage=None):
    key = STAGE if stage is None else stage
    if key not in _CACHE:
        _CACHE[key] = build_program(key)
    return _CACHE[key]


def make_in_maps(inputs):
    x = np.asarray(inputs["L0_attn_a_in_seq"], dtype=np.float32)
    T = lambda a: np.ascontiguousarray(np.asarray(a, dtype=np.float32).T)
    C = lambda a: np.ascontiguousarray(np.asarray(a, dtype=np.float32))
    shared = {
        "tkT": T(inputs["L0_attn_a_t_k"]), "tvT": T(inputs["L0_attn_a_t_v"]),
        "tqT": T(inputs["L0_attn_a_t_q"]),
        "w1T": T(inputs["L0_attn_a_W1"]), "w2T": T(inputs["L0_attn_a_W2"]),
        "w2n": C(inputs["L0_attn_a_W2"]),
        "b1i": C(inputs["L0_attn_a_b1"]).reshape(1, DI),
        "b2i": C(inputs["L0_attn_a_b2"]).reshape(1, D),
        "wqT": T(inputs["L0_attn_b_wq"]), "wkT": T(inputs["L0_attn_b_wk"]),
        "wvT": T(inputs["L0_attn_b_wv"]), "woT": T(inputs["L0_attn_b_wo"]),
        "rfT": T(inputs["L0_attn_b_rf"]),
        "wgate": C(inputs["L0_ffn_w_gate"]), "wup": C(inputs["L0_ffn_w_up"]),
        "wdown": C(inputs["L0_ffn_w_down"]),
    }
    in_maps = []
    for cid in range(NCORE):
        m = dict(shared)
        m["x_in"] = np.ascontiguousarray(x[:, cid * TL:(cid + 1) * TL, :].reshape(NB * TL, D))
        in_maps.append(m)
    return in_maps


def kernel(**inputs):
    in_maps = make_in_maps(inputs)
    nc = _get_prog()
    res = run_bass_kernel_spmd(nc, in_maps, core_ids=list(range(NCORE)))
    out = np.empty((NB, 4096, D), dtype=np.float32)
    for cid in range(NCORE):
        out[:, cid * TL:(cid + 1) * TL, :] = np.asarray(res.results[cid]["y"]).reshape(NB, TL, D)
    return out


# version 03
# speedup vs baseline: 1.1761x; 1.0089x over previous
import contextlib
import numpy as np
import ml_dtypes
import concourse.bass as bass
import concourse.mybir as mybir
from concourse.bass_utils import run_bass_kernel_spmd
from concourse.alu_op_type import AluOpType as Alu

F32 = mybir.dt.float32
BF16 = mybir.dt.bfloat16
AF = mybir.ActivationFunctionType
AX = mybir.AxisListType.X

MAGIC = 1.5 * 2.0 ** 23
PI = float(np.pi)
LR = 0.001
EPS = 1e-5
NB, TL, D, DI, FF = 8, 512, 1024, 256, 3072
CG = 2.0 / (4096.0 * 1024.0)
NCORE = 8

STAGE = 4
DBG = False
DBG_STEPS = None

AF_ELE = 18688
AB_ELE = 67840


class Ctx:
    def __init__(self, nc, es):
        self.nc, self.es, self.sem, self.cnt = nc, es, {}, {}

    def S(self, name):
        self.sem[name] = self.es.enter_context(self.nc.semaphore(name))
        self.cnt[name] = 0

    def inc(self, instr, name, n=1):
        self.cnt[name] += n
        instr.then_inc(self.sem[name], n)
        return self.cnt[name]

    def val(self, name):
        return self.cnt[name]

    def drain(self, eng, instr, name="vv"):
        v = self.inc(instr, name, 1)
        eng.wait_ge(self.sem[name], v)

    def wait(self, eng, name, target=None):
        if target is None:
            target = self.cnt[name]
        if target > 0:
            eng.wait_ge(self.sem[name], target)


class Arena:
    def __init__(self, t, cap):
        self.t, self.cap, self.off = t, cap, 0

    def take(self, *shape):
        n = int(np.prod(shape))
        ap = self.t[:, self.off:self.off + n]
        assert self.off + n <= self.cap, f"arena overflow {self.off}+{n}>{self.cap}"
        self.off += n
        if len(shape) > 1:
            names = " ".join(f"d{i}" for i in range(len(shape)))
            ap = ap.rearrange(f"p ({names}) -> p {names}",
                              **{f"d{i}": s for i, s in enumerate(shape)})
        return ap

    def at(self, off, *shape):
        n = int(np.prod(shape))
        ap = self.t[:, off:off + n]
        if len(shape) > 1:
            names = " ".join(f"d{i}" for i in range(len(shape)))
            ap = ap.rearrange(f"p ({names}) -> p {names}",
                              **{f"d{i}": s for i, s in enumerate(shape)})
        return ap

    def reset(self, off=0):
        self.off = off


def build_program(stage=STAGE, dbg=None, dbg_steps=None):
    dbg = DBG if dbg is None else dbg
    nsteps = NB if (DBG_STEPS if dbg_steps is None else dbg_steps) is None else (DBG_STEPS if dbg_steps is None else dbg_steps)
    nc = bass.Bass("TRN2", target_bir_lowering=False)

    def din(name, shape):
        return nc.dram_tensor(name, shape, F32, kind="ExternalInput").ap()

    x_in = din("x_in", [NB * TL, D])
    tkT, tvT, tqT = din("tkT", [D, D]), din("tvT", [D, D]), din("tqT", [D, D])
    w1Ti, w2Ti, w2ni = din("w1T", [D, DI]), din("w2T", [DI, D]), din("w2n", [D, DI])
    b1i, b2i = din("b1i", [1, DI]), din("b2i", [1, D])
    wqT, wkT = din("wqT", [D, D]), din("wkT", [D, D])
    wvT, woT = din("wvT", [D, D]), din("woT", [D, D])
    rfTi = din("rfT", [64, 256])
    wgate, wup, wdown = din("wgate", [FF, D]), din("wup", [FF, D]), din("wdown", [D, FF])

    y_out = nc.dram_tensor("y", [NB * TL, D], F32, kind="ExternalOutput").ap()
    dbg_out = nc.dram_tensor("dbg", [128, 65536], F32, kind="ExternalOutput").ap() if dbg else None

    xres = nc.dram_tensor("xres", [NB * TL, D], F32).ap()
    wgT_dq = nc.dram_tensor("wgT_dq", [24, 8, 128, 128], BF16).ap()
    wuT_dq = nc.dram_tensor("wuT_dq", [24, 8, 128, 128], BF16).ap()
    wdT_dq = nc.dram_tensor("wdT_dq", [24, 128, D], BF16).ap()
    gbuf_in = nc.dram_tensor("gbuf_in", [1028, 512], F32).ap()
    gbuf_out = nc.dram_tensor("gbuf_out", [1028, 512], F32).ap()
    up_s = nc.dram_tensor("up_s", [24, 128, NB * TL], BF16).ap()
    gate_s = nc.dram_tensor("gate_s", [24, 128, NB * TL], BF16).ap()
    g2_s = nc.dram_tensor("g2_s", [24, 128, NB * TL], BF16).ap()
    t4_s = nc.dram_tensor("t4_s", [24, 128, NB * TL], BF16).ap()
    mxi = [nc.dram_tensor(f"mxi{k}", [1, 1], F32).ap() for k in range(10)]
    mxo = [nc.dram_tensor(f"mxo{k}", [1, 1], F32).ap() for k in range(10)]
    kv_in = nc.dram_tensor("kv_in", [NB * 128, 16 * 264], F32).ap()
    kv_out = nc.dram_tensor("kv_out", [NB * 128, 16 * 264], F32).ap()
    xnT_sp = nc.dram_tensor("xnT_sp", [NB, 128, 8 * TL], BF16).ap()

    es = contextlib.ExitStack()
    c = Ctx(nc, es)

    def sbuf(name, shape, dtype):
        return es.enter_context(nc.sbuf_tensor(name, shape, dtype))

    ident_b = sbuf("ident_b", [128, 128], BF16)
    ones_b = sbuf("ones_b", [128, 512], BF16)
    consts = sbuf("consts", [128, 8], F32)
    colw = sbuf("colw", [128, 32], F32)

    arena_f_t = sbuf("arena_f", [128, AF_ELE], F32)
    arena_b_t = sbuf("arena_b", [128, AB_ELE], BF16)
    arf = Arena(arena_f_t, AF_ELE)
    arb = Arena(arena_b_t, AB_ELE)

    pb = [es.enter_context(nc.psum_tensor(f"pb{i}", [128, 512], F32)) for i in range(4)]
    pt = [es.enter_context(nc.psum_tensor(f"pt{i}", [128, 8, 128], BF16)) for i in range(2)]
    pm = es.enter_context(nc.psum_tensor("pm", [128, 512], F32))
    pm2 = es.enter_context(nc.psum_tensor("pm2", [128, 512], F32))

    for s in ["ld", "cst", "st", "mm", "ev", "vv", "pp", "cc"]:
        c.S(s)

    blk = es.enter_context(nc.Block())

    def barrier():
        t_ = {k: c.val(k) for k in ("mm", "ev", "vv", "st", "ld", "cst", "pp", "cc")}

        @blk.tensor
        def _(t):
            for k in ("ev", "vv", "pp", "st", "ld", "cst"):
                c.wait(t, k, t_[k])

        @blk.vector
        def _(v):
            for k in ("mm", "ev", "pp", "st", "ld", "cst"):
                c.wait(v, k, t_[k])

        @blk.scalar
        def _(sc):
            for k in ("mm", "vv", "pp", "st", "ld", "cst"):
                c.wait(sc, k, t_[k])

        @blk.gpsimd
        def _(g):
            for k in ("mm", "ev", "vv", "ld", "cc"):
                c.wait(g, k, t_[k])

        @blk.sync
        def _(sy):
            for k in ("mm", "ev", "vv", "pp", "st", "cst", "cc"):
                c.wait(sy, k, t_[k])

    xs = arf.take(4, D)
    scr_b = arf.take(2048)
    gst3 = scr_b[0:1, :]
    a_f = arf.take(2, TL)
    w1T_f = arf.take(8, DI)
    w2T_f = arf.take(2, D)
    w2n_f = arf.take(8, DI)
    b1_f = arf.take(DI)[0:1, :]
    b2_f = arf.take(D)[0:1, :]
    gst1 = arf.take(8, DI)
    gst2 = arf.take(2, D)
    gr1, gr2, gr3 = gst1, gst2, gst3
    assert arf.off <= AF_ELE, arf.off

    tkT_b = arb.take(8, D)
    tvT_b = arb.take(8, D)
    tqT_b = arb.take(8, D)
    w1T_b = arb.take(8, DI)
    w2T_b = arb.take(2, D)
    w2n_b = arb.take(8, DI)
    b1_b = arb.take(DI)[0:1, :]
    b2_b = arb.take(D)[0:1, :]
    xn_tm = arb.take(4, D)
    G_tm = xn_tm
    xnT = arb.take(8, TL)
    z_fm = arb.take(8, TL)
    z_tm = arb.take(4, D)
    y_fm = arb.take(8, TL)
    q_fm = arb.take(8, TL)
    _goff = arb.off
    G_fm = arb.take(8, TL)
    gr2b = arb.at(_goff, 2, 1024)
    h_fm = arb.take(2, TL)
    h2_fm = h_fm
    h_tm = arb.take(4, DI)
    gp_b = arb.take(2, TL)
    da_fm = arb.take(2, TL)
    da_tm = arb.take(4, DI)
    gst4 = arb.take(8, DI)
    G_fm_off = None
    print("attn_a arena use: f32", arf.off, "bf16", arb.off)
    assert arb.off <= AB_ELE, arb.off

    def phase_setup():
        @blk.gpsimd
        def _(g):
            g.memset(ident_b[:, :], 0.0)
            g.affine_select(out=ident_b[:, :], in_=ident_b[:, :], compare_op=Alu.not_equal,
                            fill=1.0, base=0, pattern=[[-1, 128]], channel_multiplier=1)
            g.memset(ones_b[:, :], 1.0)
            g.memset(consts[:, 0:1], MAGIC)
            g.memset(consts[:, 1:2], PI / 2)
            i = g.memset(consts[:, 2:3], 65536.0)
            c.inc(i, "pp", 1)
            for dst, src in ((tkT_b, tkT), (tvT_b, tvT), (tqT_b, tqT),
                             (w1T_b, w1Ti), (w2T_b, w2Ti), (w2n_b, w2ni)):
                c.inc(g.dma_start(out=dst[:, :, :], in_=src.rearrange("(k p) o -> p k o", p=128)),
                      "cst", 16)
            c.inc(g.dma_start(out=b1_b, in_=b1i), "cst", 16)
            c.inc(g.dma_start(out=b2_b, in_=b2i), "cst", 16)

        @blk.sync
        def _(s):
            c.inc(s.dma_start(out=w1T_f[:, :, :], in_=w1Ti.rearrange("(k p) o -> p k o", p=128)), "ld", 16)
            c.inc(s.dma_start(out=w2T_f[:, :, :], in_=w2Ti.rearrange("(k p) o -> p k o", p=128)), "ld", 16)
            c.inc(s.dma_start(out=w2n_f[:, :, :], in_=w2ni.rearrange("(k p) o -> p k o", p=128)), "ld", 16)
            c.inc(s.dma_start(out=b1_f, in_=b1i), "ld", 16)
            c.inc(s.dma_start(out=b2_f, in_=b2i), "ld", 16)
            c.inc(s.dma_start(out=xs[:, :, :],
                              in_=x_in[0:TL, :].rearrange("(t p) o -> p t o", p=128)), "ld", 16)

    def transpose_group(dst, srcs, par, pre_ev=None):
        @blk.tensor
        def _(t):
            if pre_ev is not None:
                c.wait(t, "ev", pre_ev)
            c.wait(t, "vv", c.val("vv"))
            i = None
            for k, s_ap in enumerate(srcs):
                i = t.transpose(pt[par][:, k, :], s_ap, ident_b[:, :])
            c.inc(i, "mm", 1)

        @blk.vector
        def _(v):
            c.wait(v, "mm")
            i = v.tensor_copy(dst, pt[par][:, 0:len(srcs), :])
            c.inc(i, "vv", 1)

    def phase_attn_a():
        ld_xs = c.val("ld")

        @blk.tensor
        def _(t):
            c.wait(t, "pp")
            c.wait(t, "cst")
            c.wait(t, "ld")

        for s in range(nsteps):
            for tc in range(4):
                @blk.vector
                def _(v, s=s, tc=tc, ld_xs=ld_xs):
                    if tc == 0:
                        c.wait(v, "ld", ld_xs)
                        if s == 0:
                            c.wait(v, "cst")
                    c.drain(v, v.tensor_reduce(colw[:, 0:1], xs[:, tc, :], AX, Alu.add))
                    c.drain(v, v.tensor_scalar(colw[:, 1:2], colw[:, 0:1], -1.0 / D, None, Alu.mult))
                    i = v.tensor_scalar(scr_b[:, 0:D], xs[:, tc, :], colw[:, 1:2], None, Alu.add)
                    c.inc(i, "vv", 1)

                @blk.scalar
                def _(sc, s=s, tc=tc):
                    if s == 0 and tc == 0:
                        c.wait(sc, "cst")
                    c.wait(sc, "vv")
                    c.inc(sc.activation(a_f[:, :, :].rearrange("p a b -> p (a b)"),
                                        scr_b[:, 0:D], AF.Square), "ev", 1)

                @blk.vector
                def _(v):
                    c.wait(v, "ev")
                    c.drain(v, v.tensor_reduce(colw[:, 2:3], a_f[:, :, :].rearrange("p a b -> p (a b)"), AX, Alu.add))
                    i = v.tensor_scalar(colw[:, 2:3], colw[:, 2:3], 1.0 / D, EPS, Alu.mult, Alu.add)
                    c.inc(i, "vv", 1)

                @blk.scalar
                def _(sc):
                    c.wait(sc, "vv")
                    c.inc(sc.activation(colw[:, 3:4], colw[:, 2:3], AF.Sqrt), "ev", 1)

                @blk.vector
                def _(v, tc=tc):
                    c.wait(v, "ev")
                    c.drain(v, v.reciprocal(colw[:, 3:4], colw[:, 3:4]))
                    i = v.tensor_scalar(xn_tm[:, tc, :], scr_b[:, 0:D], colw[:, 3:4], None, Alu.mult)
                    c.inc(i, "vv", 1)

                transpose_group(xnT[:, 0:8, tc * 128:(tc + 1) * 128],
                                [xn_tm[:, tc, kc * 128:(kc + 1) * 128] for kc in range(8)],
                                par=tc % 2)
            xnT_vv = c.val("vv")

            ev0 = c.val("ev")
            for pi_, (wT, dst) in enumerate(((tkT_b, z_fm), (tvT_b, y_fm), (tqT_b, q_fm))):
                for oc in range(8):
                    n = pi_ * 8 + oc

                    @blk.tensor
                    def _(t, wT=wT, oc=oc, n=n, ev0=ev0, xnT_vv=xnT_vv):
                        if n == 0:
                            c.wait(t, "vv", xnT_vv)
                        if n >= 4:
                            c.wait(t, "ev", ev0 + n - 3)
                        bank = pb[n % 4]
                        i = None
                        for kc in range(8):
                            i = t.matmul(bank[:, :], wT[:, kc, oc * 128:(oc + 1) * 128],
                                         xnT[:, kc, :], start=(kc == 0), stop=(kc == 7))
                        c.inc(i, "mm", 1)

                    @blk.scalar
                    def _(sc, dst=dst, oc=oc, n=n):
                        c.wait(sc, "mm")
                        c.inc(sc.activation(dst[:, oc, :], pb[n % 4][:, :], AF.Copy), "ev", 1)
            proj_ev = c.val("ev")

            for oc in range(8):
                transpose_group(z_tm[:, 0:4, oc * 128:(oc + 1) * 128],
                                [z_fm[:, oc, ts * 128:(ts + 1) * 128] for ts in range(4)],
                                par=oc % 2, pre_ev=(proj_ev - 16 if oc == 0 else None))

            @blk.vector
            def _(v, proj_ev=proj_ev):
                c.wait(v, "ev", proj_ev)
                i = v.tensor_tensor(y_fm[:, :, :], z_fm[:, :, :], y_fm[:, :, :], Alu.subtract)
                c.inc(i, "vv", 1)
            zy_vv = c.val("vv")

            for ic in range(2):
                @blk.tensor
                def _(t, ic=ic, proj_ev=proj_ev):
                    if ic == 0:
                        c.wait(t, "ev", proj_ev - 16)
                    bank = pb[ic]
                    for kc in range(8):
                        t.matmul(bank[:, :], w1T_b[:, kc, ic * 128:(ic + 1) * 128],
                                 z_fm[:, kc, :], start=(kc == 0), stop=False)
                    i = t.matmul(bank[:, :], b1_b[0:1, ic * 128:(ic + 1) * 128],
                                 ones_b[0:1, :], start=False, stop=True)
                    c.inc(i, "mm", 1)

                @blk.scalar
                def _(sc, ic=ic):
                    c.wait(sc, "mm")
                    c.inc(sc.activation(a_f[:, ic, :], pb[ic][:, :], AF.Copy), "ev", 1)

            @blk.scalar
            def _(sc):
                sc.activation(h_fm[:, :, :], a_f[:, :, :], AF.Gelu)
                c.inc(sc.activation(gp_b[:, :, :], a_f[:, :, :], AF.Derivative_Gelu), "ev", 1)
            h_ev = c.val("ev")

            for ic in range(2):
                transpose_group(h_tm[:, 0:4, ic * 128:(ic + 1) * 128],
                                [h_fm[:, ic, ts * 128:(ts + 1) * 128] for ts in range(4)],
                                par=ic, pre_ev=(h_ev if ic == 0 else None))

            vv_r0 = c.val("vv")
            for oc in range(8):
                @blk.tensor
                def _(t, oc=oc, h_ev=h_ev, vv_r0=vv_r0):
                    if oc == 0:
                        c.wait(t, "ev", h_ev)
                    if oc >= 4:
                        c.wait(t, "vv", vv_r0 + 2 * (oc - 4) + 2)
                    bank = pb[oc % 4]
                    for kc in range(2):
                        t.matmul(bank[:, :], w2T_b[:, kc, oc * 128:(oc + 1) * 128],
                                 h_fm[:, kc, :], start=(kc == 0), stop=False)
                    i = t.matmul(bank[:, :], b2_b[0:1, oc * 128:(oc + 1) * 128],
                                 ones_b[0:1, :], start=False, stop=True)
                    c.inc(i, "mm", 1)

                @blk.vector
                def _(v, oc=oc, zy_vv=zy_vv):
                    c.wait(v, "mm")
                    if oc == 0:
                        c.wait(v, "vv", zy_vv)
                    v.tensor_tensor(scr_b[:, 0:TL], pb[oc % 4][:, :], y_fm[:, oc, :], Alu.add)
                    i = v.tensor_scalar(G_fm[:, oc, :], scr_b[:, 0:TL], CG, None, Alu.mult)
                    c.inc(i, "vv", 2)

            for oc in range(8):
                transpose_group(G_tm[:, 0:4, oc * 128:(oc + 1) * 128],
                                [G_fm[:, oc, ts * 128:(ts + 1) * 128] for ts in range(4)],
                                par=oc % 2)
            G_vv = c.val("vv")

            for ic in range(2):
                @blk.tensor
                def _(t, ic=ic, G_vv=G_vv):
                    if ic == 0:
                        c.wait(t, "vv", G_vv)
                    bank = pb[ic]
                    i = None
                    for kc in range(8):
                        i = t.matmul(bank[:, :], w2n_b[:, kc, ic * 128:(ic + 1) * 128],
                                     G_fm[:, kc, :], start=(kc == 0), stop=(kc == 7))
                    c.inc(i, "mm", 1)

                @blk.vector
                def _(v, ic=ic):
                    c.wait(v, "mm")
                    i = v.tensor_tensor(da_fm[:, ic, :], pb[ic][:, :], gp_b[:, ic, :], Alu.mult)
                    c.inc(i, "vv", 1)

            for ic in range(2):
                transpose_group(da_tm[:, 0:4, ic * 128:(ic + 1) * 128],
                                [da_fm[:, ic, ts * 128:(ts + 1) * 128] for ts in range(4)],
                                par=ic)
            da_vv = c.val("vv")

            gev0 = c.val("ev")
            for os_ in range(8):
                @blk.tensor
                def _(t, os_=os_, da_vv=da_vv, gev0=gev0):
                    if os_ == 0:
                        c.wait(t, "vv", da_vv)
                    if os_ >= 4:
                        c.wait(t, "ev", gev0 + os_ - 3)
                    bank = pb[os_ % 4]
                    i = None
                    for ts in range(4):
                        i = t.matmul(bank[:, 0:DI], z_tm[:, ts, os_ * 128:(os_ + 1) * 128],
                                     da_tm[:, ts, :], start=(ts == 0), stop=(ts == 3))
                    c.inc(i, "mm", 1)

                @blk.scalar
                def _(sc, os_=os_):
                    c.wait(sc, "mm")
                    c.inc(sc.activation(gst1[:, os_, :], pb[os_ % 4][:, 0:DI], AF.Copy), "ev", 1)

            gev1 = c.val("ev")
            for u in range(4):
                is_, cc_ = u % 2, u // 2

                @blk.tensor
                def _(t, is_=is_, cc_=cc_, u=u, gev1=gev1):
                    c.wait(t, "ev", gev1 - 3 + u)
                    bank = pb[u % 4]
                    i = None
                    for ts in range(4):
                        i = t.matmul(bank[:, :], h_tm[:, ts, is_ * 128:(is_ + 1) * 128],
                                     G_tm[:, ts, cc_ * 512:(cc_ + 1) * 512],
                                     start=(ts == 0), stop=(ts == 3))
                    c.inc(i, "mm", 1)

                @blk.scalar
                def _(sc, is_=is_, cc_=cc_, u=u):
                    c.wait(sc, "mm")
                    c.inc(sc.activation(gst2[:, is_, cc_ * 512:(cc_ + 1) * 512],
                                        pb[u % 4][:, :], AF.Copy), "ev", 1)

            @blk.tensor
            def _(t):
                i = None
                for ts in range(4):
                    i = t.matmul(pm[0:1, 0:DI], ones_b[:, 0:1], da_tm[:, ts, :],
                                 start=(ts == 0), stop=(ts == 3))
                c.inc(i, "mm", 1)

            @blk.scalar
            def _(sc):
                c.wait(sc, "mm")
                c.inc(sc.activation(gst3[0:1, 0:DI], pm[0:1, 0:DI], AF.Copy), "ev", 1)

            for cc_ in range(2):
                @blk.tensor
                def _(t, cc_=cc_):
                    c.wait(t, "ev", c.val("ev"))
                    i = None
                    for ts in range(4):
                        i = t.matmul(pm2[0:1, :], ones_b[:, 0:1], G_tm[:, ts, cc_ * 512:(cc_ + 1) * 512],
                                     start=(ts == 0), stop=(ts == 3))
                    c.inc(i, "mm", 1)

                @blk.scalar
                def _(sc, cc_=cc_):
                    c.wait(sc, "mm")
                    c.inc(sc.activation(gst3[0:1, 1024 + cc_ * 512:1024 + (cc_ + 1) * 512],
                                        pm2[0:1, :], AF.Copy), "ev", 1)
            grads_ev = c.val("ev")

            @blk.gpsimd
            def _(g, grads_ev=grads_ev):
                c.wait(g, "ev", grads_ev)
                c.inc(g.dma_start(
                    out=gbuf_in[0:512, :].rearrange("(os ph) (pl i) -> (ph pl) os i", os=8, ph=64, pl=2),
                    in_=gst1[:, :, :]), "st", 16)
                c.inc(g.dma_start(
                    out=gbuf_in[512:1024, :].rearrange("(i p two) cc -> p i (two cc)", i=2, p=128, two=2),
                    in_=gst2[:, :, :]), "st", 16)
                c.inc(g.dma_start(out=gbuf_in[1024:1028, :], in_=gst3[0:1, :]), "st", 16)
                c.wait(g, "st")
                i = g.collective_compute("AllReduce", Alu.add, replica_groups=[list(range(NCORE))],
                                         ins=[gbuf_in.opt()], outs=[gbuf_out.opt()])
                c.inc(i, "cc", 1)

            @blk.sync
            def _(sy):
                c.wait(sy, "cc")
                c.inc(sy.dma_start(
                    out=gr1[:, :, :],
                    in_=gbuf_out[0:512, :].rearrange("(os ph) (pl i) -> (ph pl) os i", os=8, ph=64, pl=2)), "ld", 16)
                c.inc(sy.dma_start(
                    out=gr2[:, :, :],
                    in_=gbuf_out[512:1024, :].rearrange("(i p two) cc -> p i (two cc)", i=2, p=128, two=2)), "ld", 16)
                c.inc(sy.dma_start(out=gr3[0:1, :], in_=gbuf_out[1024:1028, :]), "ld", 16)
            gr_ld = c.val("ld")

            @blk.vector
            def _(v, gr_ld=gr_ld):
                c.wait(v, "ld", gr_ld)
                v.scalar_tensor_tensor(w1T_f[:, :, :], gr1[:, :, :], -LR, w1T_f[:, :, :], Alu.mult, Alu.add)
                v.scalar_tensor_tensor(w2T_f[:, :, :], gr2[:, :, :], -LR, w2T_f[:, :, :], Alu.mult, Alu.add)
                v.scalar_tensor_tensor(b1_f[0:1, :], gr3[0:1, 0:DI], -LR, b1_f[0:1, :], Alu.mult, Alu.add)
                i = v.scalar_tensor_tensor(b2_f[0:1, :], gr3[0:1, 1024:2048], -LR, b2_f[0:1, :], Alu.mult, Alu.add)
                c.inc(i, "vv", 1)

            @blk.scalar
            def _(sc, gr_ld=gr_ld):
                c.wait(sc, "ld", gr_ld)
                c.inc(sc.activation(gr2b[:, :, :], gr2[:, :, :], AF.Copy), "ev", 1)
            g2b_ev = c.val("ev")

            for is_ in range(2):
                @blk.tensor
                def _(t, is_=is_, g2b_ev=g2b_ev):
                    if is_ == 0:
                        c.wait(t, "ev", g2b_ev)
                    else:
                        c.wait(t, "vv", c.val("vv"))
                    i = None
                    for u in range(8):
                        i = t.transpose(pt[is_][:, u, :],
                                        gr2b[:, is_, u * 128:(u + 1) * 128], ident_b[:, :])
                    c.inc(i, "mm", 1)

                @blk.vector
                def _(v, is_=is_):
                    c.wait(v, "mm")
                    i = v.tensor_copy(gst4[:, 0:8, is_ * 128:(is_ + 1) * 128], pt[is_][:, :, :])
                    c.inc(i, "vv", 1)

            @blk.vector
            def _(v):
                i = v.scalar_tensor_tensor(w2n_f[:, :, :], gst4[:, :, :], -LR, w2n_f[:, :, :], Alu.mult, Alu.add)
                c.inc(i, "vv", 1)
            updv2 = c.val("vv")

            @blk.scalar
            def _(sc, updv2=updv2):
                c.wait(sc, "vv", updv2)
                sc.activation(w1T_b[:, :, :], w1T_f[:, :, :], AF.Copy)
                sc.activation(w2T_b[:, :, :], w2T_f[:, :, :], AF.Copy)
                sc.activation(b1_b[0:1, :], b1_f[0:1, :], AF.Copy)
                sc.activation(b2_b[0:1, :], b2_f[0:1, :], AF.Copy)
                c.inc(sc.activation(w2n_b[:, :, :], w2n_f[:, :, :], AF.Copy), "ev", 1)
            params_ev = c.val("ev")

            for ic in range(2):
                @blk.tensor
                def _(t, ic=ic, params_ev=params_ev):
                    if ic == 0:
                        c.wait(t, "ev", params_ev)
                    bank = pb[ic]
                    for kc in range(8):
                        t.matmul(bank[:, :], w1T_b[:, kc, ic * 128:(ic + 1) * 128],
                                 q_fm[:, kc, :], start=(kc == 0), stop=False)
                    i = t.matmul(bank[:, :], b1_b[0:1, ic * 128:(ic + 1) * 128],
                                 ones_b[0:1, :], start=False, stop=True)
                    c.inc(i, "mm", 1)

                @blk.scalar
                def _(sc, ic=ic):
                    c.wait(sc, "mm")
                    c.inc(sc.activation(h2_fm[:, ic, :], pb[ic][:, :], AF.Gelu), "ev", 1)
            h2_ev = c.val("ev")

            ovv0 = c.val("vv")
            for n in range(8):
                tc, occ = divmod(n, 2)

                @blk.tensor
                def _(t, tc=tc, occ=occ, n=n, h2_ev=h2_ev, ovv0=ovv0):
                    if n == 0:
                        c.wait(t, "ev", h2_ev)
                    if n >= 4:
                        c.wait(t, "vv", ovv0 + n - 3)
                    bank = pb[n % 4]
                    for ic in range(2):
                        t.matmul(bank[:, :], h2_fm[:, ic, tc * 128:(tc + 1) * 128],
                                 w2T_b[:, ic, occ * 512:(occ + 1) * 512],
                                 start=(ic == 0), stop=False)
                    t.matmul(bank[:, :], ones_b[0:1, 0:128],
                             b2_b[0:1, occ * 512:(occ + 1) * 512], start=False, stop=False)
                    i = None
                    for u in range(4):
                        i = t.matmul(bank[:, u * 128:(u + 1) * 128],
                                     q_fm[:, occ * 4 + u, tc * 128:(tc + 1) * 128],
                                     ident_b[:, :], start=False, stop=(u == 3))
                    c.inc(i, "mm", 1)

                @blk.vector
                def _(v, tc=tc, occ=occ, n=n):
                    c.wait(v, "mm")
                    i = v.tensor_tensor(xs[:, tc, occ * 512:(occ + 1) * 512],
                                        pb[n % 4][:, :],
                                        xs[:, tc, occ * 512:(occ + 1) * 512], Alu.add)
                    c.inc(i, "vv", 1)

            @blk.gpsimd
            def _(g, s=s):
                c.wait(g, "vv")
                c.inc(g.dma_start(
                    out=xres[s * TL:(s + 1) * TL, :].rearrange("(t p) o -> p t o", p=128),
                    in_=xs[:, :, :]), "st", 16)

            barrier()

            if s + 1 < nsteps:
                @blk.sync
                def _(sy, s=s):
                    c.inc(sy.dma_start(
                        out=xs[:, :, :],
                        in_=x_in[(s + 1) * TL:(s + 2) * TL, :].rearrange("(t p) o -> p t o", p=128)),
                        "ld", 16)
                ld_xs = c.val("ld")

    def phase_dequant():
        arf.reset()
        arb.reset()
        wchunk = arf.take(2, FF)
        sblk = arf.take(96)
        isblk = arf.take(96)
        sfull = arf.take(FF)
        isfull = arf.take(FF)
        tq_ = arf.take(FF)
        assert arf.off <= AF_ELE, arf.off
        dqo = arb.take(2, FF)
        dqst = arb.take(2, 8, 128)

        jobs = ([("g", oc, wgate[oc * 128:(oc + 1) * 128, :], D) for oc in range(24)]
                + [("u", oc, wup[oc * 128:(oc + 1) * 128, :], D) for oc in range(24)]
                + [("d", oc, wdown[oc * 128:(oc + 1) * 128, :], FF) for oc in range(8)])
        dq_vv = {}
        st_done = {}

        for j, (kind, oc, src_, w) in enumerate(jobs):
            nblk = w // 32
            nct = w // 128

            @blk.sync
            def _(sy, j=j, src_=src_, w=w):
                if j >= 2:
                    c.wait(sy, "vv", dq_vv[j - 2])
                c.inc(sy.dma_start(out=wchunk[:, j % 2, 0:w], in_=src_), "ld", 16)
            ldt = c.val("ld")

            @blk.vector
            def _(v, j=j, w=w, nblk=nblk, ldt=ldt):
                c.wait(v, "ld", ldt)
                c.drain(v, v.tensor_reduce(
                    sblk[:, 0:nblk].rearrange("p (b o) -> p b o", o=1),
                    wchunk[:, j % 2, 0:w].rearrange("p (b q) -> p b q", q=32),
                    AX, Alu.max, apply_absolute_value=True))
                c.drain(v, v.tensor_scalar(sblk[:, 0:nblk], sblk[:, 0:nblk],
                                           1.0 / 7.0, 1e-12, Alu.mult, Alu.max))
                c.drain(v, v.reciprocal(isblk[:, 0:nblk], sblk[:, 0:nblk]))
                v.tensor_copy(sfull[:, 0:w].rearrange("p (b q) -> p b q", q=32),
                              sblk[:, 0:nblk].rearrange("p (b o) -> p b o", o=1).broadcast_to([128, nblk, 32]))
                c.drain(v, v.tensor_copy(isfull[:, 0:w].rearrange("p (b q) -> p b q", q=32),
                                         isblk[:, 0:nblk].rearrange("p (b o) -> p b o", o=1).broadcast_to([128, nblk, 32])))
                v.tensor_tensor(tq_[:, 0:w], wchunk[:, j % 2, 0:w], isfull[:, 0:w], Alu.mult)
                v.tensor_scalar(tq_[:, 0:w], tq_[:, 0:w], MAGIC, MAGIC, Alu.add, Alu.subtract)
                i = v.tensor_tensor(dqo[:, j % 2, 0:w], tq_[:, 0:w], sfull[:, 0:w], Alu.mult)
                dq_vv[j] = c.inc(i, "vv", 1)

            for grp in range((nct + 7) // 8):
                n = min(8, nct - grp * 8)

                @blk.tensor
                def _(t, j=j, grp=grp, n=n):
                    c.wait(t, "vv", c.val("vv"))
                    if j >= 2 and grp == 0:
                        c.wait(t, "st", st_done[j - 2])
                    i = None
                    for u in range(n):
                        i = t.transpose(pt[j % 2][:, u, :],
                                        dqo[:, j % 2, (grp * 8 + u) * 128:(grp * 8 + u + 1) * 128],
                                        ident_b[:, :])
                    c.inc(i, "mm", 1)

                @blk.vector
                def _(v, j=j, grp=grp, n=n):
                    c.wait(v, "mm")
                    if grp >= 1:
                        c.wait(v, "st", st_done[j])
                    i = v.tensor_copy(dqst[:, j % 2, 0:n, :], pt[j % 2][:, 0:n, :])
                    c.inc(i, "vv", 1)

                @blk.gpsimd
                def _(g, j=j, kind=kind, oc=oc, grp=grp, n=n):
                    c.wait(g, "vv")
                    if kind == "g":
                        dst = wgT_dq[oc].rearrange("k p f -> p k f")
                    elif kind == "u":
                        dst = wuT_dq[oc].rearrange("k p f -> p k f")
                    else:
                        ic0 = grp * 8
                        dst = wdT_dq[ic0:ic0 + n, :, oc * 128:(oc + 1) * 128].rearrange("i p f -> p i f")
                    st_done[j] = c.inc(g.dma_start(out=dst, in_=dqst[:, j % 2, 0:n, :]), "st", 16)

        barrier()

    def phase_ffn(src, dst, mxb):
        arf.reset()
        arb.reset()
        p3_vv, p4_vv, p5_vv, p5_st, p5_mm, p1_mm = [], [], [], [], [], []
        xtile = arf.take(2, D)
        xc_s = arf.take(D)
        sq_s = arf.take(D)
        t3a = arf.take(4096)
        t1q = t3a
        t3b = arf.take(4096)
        ost = arf.take(2, 2, D)
        xr4 = arf.at(0, 2, 2, D)
        assert arf.off <= AF_ELE, arf.off
        xnT_f = arb.take(8, 4096)
        wslab = arb.take(2, 8, 128)
        evslab = arb.take(2, 4096)
        xnstg = arb.take(D)
        mcastb = arb.take(128)[:, 0:1]
        assert arb.off <= AB_ELE, arb.off
        ldg = arb.at(0, 2, 4096)
        ldu = arb.at(8192, 2, 4096)
        g2sl = arb.at(16384, 2, 4096)
        t4sl = arb.at(24576, 2, 4096)
        wdT_all = arb.at(0, 24, D)
        oqin = arb.at(24576, 2, 24, 256)
        oqb = arb.at(36864, 24, 256)

        @blk.vector
        def _(v):
            v.memset(colw[:, 24:27], 0.0)
            i = v.memset(colw[:, 27:28], 1.0)
            c.inc(i, "vv", 1)

        p1_vv = []
        for tc in range(32):
            @blk.sync
            def _(sy, tc=tc):
                if tc >= 2:
                    c.wait(sy, "vv", p1_vv[tc - 2])
                c.inc(sy.dma_start(out=xtile[:, tc % 2, :],
                                   in_=src[tc * 128:(tc + 1) * 128, :]), "ld", 16)
            ldt = c.val("ld")

            @blk.vector
            def _(v, tc=tc, ldt=ldt):
                c.wait(v, "ld", ldt)
                c.drain(v, v.tensor_reduce(colw[:, 0:1], xtile[:, tc % 2, :], AX, Alu.add))
                c.drain(v, v.tensor_scalar(colw[:, 1:2], colw[:, 0:1], -1.0 / D, None, Alu.mult))
                i = v.tensor_scalar(xc_s[:, :], xtile[:, tc % 2, :], colw[:, 1:2], None, Alu.add)
                c.inc(i, "vv", 1)

            @blk.scalar
            def _(sc):
                c.wait(sc, "vv")
                c.inc(sc.activation(sq_s[:, :], xc_s[:, :], AF.Square), "ev", 1)

            @blk.vector
            def _(v):
                c.wait(v, "ev")
                c.drain(v, v.tensor_reduce(colw[:, 2:3], sq_s[:, :], AX, Alu.add))
                i = v.tensor_scalar(colw[:, 2:3], colw[:, 2:3], 1.0 / D, EPS, Alu.mult, Alu.add)
                c.inc(i, "vv", 1)

            @blk.scalar
            def _(sc):
                c.wait(sc, "vv")
                c.inc(sc.activation(colw[:, 3:4], colw[:, 2:3], AF.Sqrt), "ev", 1)

            @blk.vector
            def _(v, tc=tc):
                c.wait(v, "ev")
                if tc >= 1:
                    c.wait(v, "mm", p1_mm[tc - 1])
                c.drain(v, v.reciprocal(colw[:, 3:4], colw[:, 3:4]))
                c.drain(v, v.tensor_scalar(xnstg[:, :], xc_s[:, :], colw[:, 3:4], None, Alu.mult))
                c.drain(v, v.reduce_max(colw[:, 8:9], xnstg[:, :], AX, apply_absolute_value=True))
                i = v.tensor_tensor(colw[:, 24:25], colw[:, 24:25], colw[:, 8:9], Alu.max)
                p1_vv.append(c.inc(i, "vv", 1))

            @blk.tensor
            def _(t, tc=tc):
                c.wait(t, "vv", c.val("vv"))
                i = None
                for kc in range(8):
                    i = t.transpose(pt[tc % 2][:, kc, :], xnstg[:, kc * 128:(kc + 1) * 128], ident_b[:, :])
                p1_mm.append(c.inc(i, "mm", 1))

            @blk.vector
            def _(v, tc=tc):
                c.wait(v, "mm")
                i = v.tensor_copy(xnT_f[:, 0:8, tc * 128:(tc + 1) * 128], pt[tc % 2][:, :, :])
                c.inc(i, "vv", 1)

        def gmax(acc_col, k, out_col):
            @blk.vector
            def _(v):
                c.wait(v, "vv", c.val("vv"))
                i = v.tensor_copy(mcastb, colw[:, acc_col:acc_col + 1])
                c.inc(i, "vv", 1)

            @blk.tensor
            def _(t):
                c.wait(t, "vv", c.val("vv"))
                c.wait(t, "ev", c.val("ev"))
                i = t.matmul(pm[0:1, 0:128], mcastb, ident_b[:, :], start=True, stop=True)
                c.inc(i, "mm", 1)

            @blk.vector
            def _(v):
                c.wait(v, "mm")
                i = v.reduce_max(colw[0:1, 8:9], pm[0:1, 0:128], AX)
                c.inc(i, "vv", 1)

            @blk.gpsimd
            def _(g, k=k):
                c.wait(g, "vv")
                c.inc(g.dma_start(out=mxi[k], in_=colw[0:1, 8:9]), "st", 16)
                c.wait(g, "st")
                i = g.collective_compute("AllReduce", Alu.max, replica_groups=[list(range(NCORE))],
                                         ins=[mxi[k].opt()], outs=[mxo[k].opt()])
                c.inc(i, "cc", 1)

            @blk.sync
            def _(sy, k=k, out_col=out_col):
                c.wait(sy, "cc")
                c.inc(sy.dma_start(out=colw[:, out_col:out_col + 1],
                                   in_=mxo[k][0:1, 0:1].broadcast_to([128, 1])), "ld", 16)
            return c.val("ld")

        def scale_ops(ldt, s_col, inv_col=None, extra=None):
            @blk.vector
            def _(v):
                c.wait(v, "ld", ldt)
                c.drain(v, v.tensor_scalar(colw[:, s_col:s_col + 1], colw[:, s_col:s_col + 1],
                                           1.0 / 32767.0, 1e-12, Alu.mult, Alu.max))
                if inv_col is not None:
                    c.drain(v, v.reciprocal(colw[:, inv_col:inv_col + 1], colw[:, s_col:s_col + 1]))
                if extra is not None:
                    extra(v)
                i = v.engine_nop()
                c.inc(i, "pp", 1)
            return c.val("pp")

        pp_s0 = scale_ops(gmax(24, mxb + 0, 10), 10, 11)
        barrier()

        for kc in range(8):
            @blk.scalar
            def _(sc, kc=kc):
                if kc > 0:
                    c.wait(sc, "vv", c.val("vv"))
                c.inc(sc.activation(t1q[:, :], xnT_f[:, kc, :], AF.Identity,
                                    bias=consts[:, 0:1], scale=colw[:, 11:12]), "ev", 1)

            @blk.vector
            def _(v, kc=kc):
                c.wait(v, "ev")
                i = v.tensor_scalar(xnT_f[:, kc, :], t1q[:, :], MAGIC, colw[:, 10:11],
                                    Alu.subtract, Alu.mult)
                c.inc(i, "vv", 1)
        barrier()

        @blk.vector
        def _(v):
            i = v.memset(colw[:, 24:26], 0.0)
            c.inc(i, "vv", 1)
        barrier()

        for widx, (cache, spill, acc_col) in enumerate(((wgT_dq, gate_s, 24), (wuT_dq, up_s, 25))):
            mm_sp = {}
            ev_b0 = c.val("ev")
            st_b0 = c.val("st")
            for i_ in range(24):
                @blk.sync
                def _(sy, i_=i_, cache=cache):
                    if i_ >= 2:
                        c.wait(sy, "mm", mm_sp[i_ - 2])
                    c.inc(sy.dma_start(out=wslab[:, i_ % 2, :, :],
                                       in_=cache[i_].rearrange("k p f -> p k f")), "ld", 16)
                ldt = c.val("ld")

                @blk.tensor
                def _(t, i_=i_, ldt=ldt, ev_b0=ev_b0):
                    c.wait(t, "ld", ldt)
                    i2 = None
                    for tch in range(8):
                        n = i_ * 8 + tch
                        if n >= 4:
                            c.wait(t, "ev", ev_b0 + n - 3)
                        bank = pb[n % 4]
                        for kc in range(8):
                            i2 = t.matmul(bank[:, :], wslab[:, i_ % 2, kc, :],
                                          xnT_f[:, kc, tch * 512:(tch + 1) * 512],
                                          start=(kc == 0), stop=(kc == 7))
                        c.inc(i2, "mm", 1)
                mm_sp[i_] = c.val("mm")

                @blk.scalar
                def _(sc, i_=i_, st_b0=st_b0):
                    for tch in range(8):
                        c.wait(sc, "mm", mm_sp[i_] - 7 + tch)
                        if i_ >= 2 and tch == 0:
                            c.wait(sc, "st", st_b0 + 16 * (i_ - 1))
                        n = i_ * 8 + tch
                        c.inc(sc.activation(evslab[:, i_ % 2, tch * 512:(tch + 1) * 512],
                                            pb[n % 4][:, :], AF.Copy), "ev", 1)

                @blk.vector
                def _(v, i_=i_, acc_col=acc_col):
                    for tch in range(8):
                        c.wait(v, "ev")
                        c.drain(v, v.reduce_max(colw[:, 8:9],
                                                evslab[:, i_ % 2, tch * 512:(tch + 1) * 512],
                                                AX, apply_absolute_value=True))
                        i2 = v.tensor_tensor(colw[:, acc_col:acc_col + 1],
                                             colw[:, acc_col:acc_col + 1], colw[:, 8:9], Alu.max)
                        c.inc(i2, "vv", 1)

                @blk.gpsimd
                def _(g, i_=i_, spill=spill):
                    c.wait(g, "ev", c.val("ev"))
                    c.inc(g.dma_start(out=spill[i_], in_=evslab[:, i_ % 2, :]), "st", 16)
            barrier()
            if widx == 0:
                def ex2(v):
                    v.tensor_scalar(colw[:, 20:21], colw[:, 14:15], -MAGIC, None, Alu.mult)
                    v.tensor_scalar(colw[:, 21:22], colw[:, 14:15], 1.0 / 65536.0, None, Alu.mult)
                scale_ops(gmax(24, mxb + 1, 14), 14, 15, extra=ex2)
            else:
                def ex1(g):
                    pass
                scale_ops(gmax(25, mxb + 2, 12), 12, 13)
        barrier()

        @blk.vector
        def _(v):
            i = v.memset(colw[:, 24:25], 0.0)
            c.inc(i, "vv", 1)
        for i_ in range(24):
            @blk.sync
            def _(sy, i_=i_):
                if i_ >= 2:
                    c.wait(sy, "vv", p3_vv[i_ - 2])
                c.inc(sy.dma_start(out=ldg[:, i_ % 2, :], in_=gate_s[i_]), "ld", 16)
            ldt = c.val("ld")

            @blk.scalar
            def _(sc, i_=i_, ldt=ldt):
                c.wait(sc, "ld", ldt)
                if i_ >= 1:
                    c.wait(sc, "vv", c.val("vv"))
                sc.activation(t3a[:, :], ldg[:, i_ % 2, :], AF.Identity,
                              bias=consts[:, 0:1], scale=colw[:, 15:16])
                sc.activation(t3b[:, :], t3a[:, :], AF.Sigmoid,
                              bias=colw[:, 20:21], scale=colw[:, 14:15])
                i2 = sc.activation(t3b[:, :], t3b[:, :], AF.Identity,
                                   bias=consts[:, 0:1], scale=consts[:, 2:3])
                c.inc(i2, "ev", 1)

            @blk.vector
            def _(v, i_=i_):
                c.wait(v, "ev")
                v.tensor_scalar(t3b[:, :], t3b[:, :], MAGIC, 65535.0, Alu.subtract, Alu.min)
                v.tensor_scalar(t3b[:, :], t3b[:, :], colw[:, 21:22], None, Alu.mult)
                v.scalar_tensor_tensor(g2sl[:, i_ % 2, :], t3a[:, :], MAGIC, t3b[:, :],
                                       Alu.subtract, Alu.mult)
                c.drain(v, v.reduce_max(colw[:, 8:9], g2sl[:, i_ % 2, :], AX,
                                        apply_absolute_value=True))
                i2 = v.tensor_tensor(colw[:, 24:25], colw[:, 24:25], colw[:, 8:9], Alu.max)
                p3_vv.append(c.inc(i2, "vv", 1))

            @blk.gpsimd
            def _(g, i_=i_):
                c.wait(g, "vv")
                c.inc(g.dma_start(out=g2_s[i_], in_=g2sl[:, i_ % 2, :]), "st", 16)
        barrier()

        def ex3(v):
            v.tensor_tensor(colw[:, 17:18], colw[:, 12:13], colw[:, 16:17], Alu.mult)
        scale_ops(gmax(24, mxb + 3, 16), 16, None, extra=ex3)
        @blk.vector
        def _(v):
            c.wait(v, "pp", c.val("pp"))
            c.drain(v, v.reciprocal(colw[:, 22:23], colw[:, 16:17]))
            i = v.engine_nop()
            c.inc(i, "pp", 1)
        barrier()

        @blk.vector
        def _(v):
            i = v.memset(colw[:, 24:25], 0.0)
            c.inc(i, "vv", 1)
        for i_ in range(24):
            @blk.sync
            def _(sy, i_=i_):
                if i_ >= 2:
                    c.wait(sy, "vv", p4_vv[i_ - 2])
                c.inc(sy.dma_start(out=ldg[:, i_ % 2, :], in_=g2_s[i_]), "ld", 16)
                c.inc(sy.dma_start(out=ldu[:, i_ % 2, :], in_=up_s[i_]), "ld", 16)
            ldt = c.val("ld")

            @blk.scalar
            def _(sc, i_=i_, ldt=ldt):
                c.wait(sc, "ld", ldt)
                if i_ >= 1:
                    c.wait(sc, "vv", c.val("vv"))
                sc.activation(t3a[:, :], ldu[:, i_ % 2, :], AF.Identity,
                              bias=consts[:, 0:1], scale=colw[:, 13:14])
                i2 = sc.activation(t3b[:, :], ldg[:, i_ % 2, :], AF.Identity,
                                   bias=consts[:, 0:1], scale=colw[:, 22:23])
                c.inc(i2, "ev", 1)

            @blk.vector
            def _(v, i_=i_):
                c.wait(v, "ev")
                v.tensor_scalar(t3a[:, :], t3a[:, :], MAGIC, colw[:, 17:18],
                                Alu.subtract, Alu.mult)
                v.scalar_tensor_tensor(t4sl[:, i_ % 2, :], t3b[:, :], MAGIC, t3a[:, :],
                                       Alu.subtract, Alu.mult)
                c.drain(v, v.reduce_max(colw[:, 8:9], t4sl[:, i_ % 2, :], AX,
                                        apply_absolute_value=True))
                i2 = v.tensor_tensor(colw[:, 24:25], colw[:, 24:25], colw[:, 8:9], Alu.max)
                p4_vv.append(c.inc(i2, "vv", 1))

            @blk.gpsimd
            def _(g, i_=i_):
                c.wait(g, "vv")
                c.inc(g.dma_start(out=t4_s[i_], in_=t4sl[:, i_ % 2, :]), "st", 16)
        barrier()
        scale_ops(gmax(24, mxb + 4, 18), 18, 19)
        barrier()

        @blk.sync
        def _(sy):
            c.inc(sy.dma_start(out=wdT_all[:, :, :],
                               in_=wdT_dq.rearrange("i p o -> p i o")), "ld", 16)
        wd_ld = c.val("ld")

        for tb in range(16):
            @blk.sync
            def _(sy, tb=tb):
                if tb >= 2:
                    c.wait(sy, "vv", p5_vv[tb - 2])
                c.inc(sy.dma_start(out=oqin[:, tb % 2, :, :],
                                   in_=t4_s[:, :, tb * 256:(tb + 1) * 256].rearrange("i p t -> p i t")), "ld", 16)
                c.inc(sy.dma_start(out=xr4[:, tb % 2, :, :],
                                   in_=src[tb * 256:(tb + 1) * 256, :].rearrange("(t p) o -> p t o", p=128)), "ld", 16)
            ldt = c.val("ld")

            @blk.scalar
            def _(sc, tb=tb, ldt=ldt):
                c.wait(sc, "ld", ldt)
                if tb >= 1:
                    c.wait(sc, "vv", c.val("vv"))
                i2 = sc.activation(t3a[:, 0:4096].rearrange("p (a b) -> p a b", a=16)
                                   if False else t3a[:, :], oqin[:, tb % 2, :, :].rearrange("p a b -> p (a b)")[:, 0:4096],
                                   AF.Identity, bias=consts[:, 0:1], scale=colw[:, 19:20])
                i3 = sc.activation(t3b[:, 0:2048], oqin[:, tb % 2, :, :].rearrange("p a b -> p (a b)")[:, 4096:6144],
                                   AF.Identity, bias=consts[:, 0:1], scale=colw[:, 19:20])
                c.inc(i3, "ev", 1)

            @blk.vector
            def _(v, tb=tb):
                c.wait(v, "ev")
                if tb >= 1:
                    c.wait(v, "mm", p5_mm[tb - 1])
                if tb >= 2:
                    c.wait(v, "st", p5_st[tb - 2])
                v.tensor_scalar(oqb[:, :, :].rearrange("p a b -> p (a b)")[:, 0:4096],
                                t3a[:, :], MAGIC, None, Alu.subtract)
                i2 = v.tensor_scalar(oqb[:, :, :].rearrange("p a b -> p (a b)")[:, 4096:6144],
                                     t3b[:, 0:2048], MAGIC, None, Alu.subtract)
                c.inc(i2, "vv", 1)
            oq_vv = c.val("vv")

            for u in range(4):
                ts2, occ = divmod(u, 2)

                @blk.tensor
                def _(t, tb=tb, ts2=ts2, occ=occ, u=u, oq_vv=oq_vv, wd_ld=wd_ld):
                    if tb == 0 and u == 0:
                        c.wait(t, "ld", wd_ld)
                    c.wait(t, "vv", oq_vv)
                    if tb >= 1:
                        c.wait(t, "vv", c.val("vv"))
                    bank = pb[u]
                    i2 = None
                    for ic in range(24):
                        i2 = t.matmul(bank[:, :], oqb[:, ic, ts2 * 128:(ts2 + 1) * 128],
                                      wdT_all[:, ic, occ * 512:(occ + 1) * 512],
                                      start=(ic == 0), stop=(ic == 23))
                    mv = c.inc(i2, "mm", 1)
                    if u == 3:
                        p5_mm.append(mv)

                @blk.vector
                def _(v, tb=tb, ts2=ts2, occ=occ, u=u):
                    c.wait(v, "mm")
                    i2 = v.scalar_tensor_tensor(ost[:, tb % 2, ts2, occ * 512:(occ + 1) * 512],
                                                pb[u][:, :], colw[:, 18:19],
                                                xr4[:, tb % 2, ts2, occ * 512:(occ + 1) * 512],
                                                Alu.mult, Alu.add)
                    if u == 3:
                        p5_vv.append(c.inc(i2, "vv", 1))
                    else:
                        c.inc(i2, "vv", 1)

            @blk.gpsimd
            def _(g, tb=tb):
                c.wait(g, "vv")
                p5_st.append(c.inc(g.dma_start(
                    out=dst[tb * 256:(tb + 1) * 256, :].rearrange("(t p) o -> p t o", p=128),
                    in_=ost[:, tb % 2, :, :]), "st", 16))
        barrier()


    def phase_attn_b():
        TWO_PI = 2.0 * PI

        def emit_ln2(xsrc_tile, xnstg_ap, xc_s2, sq_s2, tc_guard_mm=None):
            @blk.vector
            def _(v):
                c.drain(v, v.tensor_reduce(colw[:, 0:1], xsrc_tile, AX, Alu.add))
                c.drain(v, v.tensor_scalar(colw[:, 1:2], colw[:, 0:1], -1.0 / D, None, Alu.mult))
                i = v.tensor_scalar(xc_s2[:, :], xsrc_tile, colw[:, 1:2], None, Alu.add)
                c.inc(i, "vv", 1)

            @blk.scalar
            def _(sc):
                c.wait(sc, "vv")
                c.inc(sc.activation(sq_s2[:, :], xc_s2[:, :], AF.Square), "ev", 1)

            @blk.vector
            def _(v):
                c.wait(v, "ev")
                c.drain(v, v.tensor_reduce(colw[:, 2:3], sq_s2[:, :], AX, Alu.add))
                i = v.tensor_scalar(colw[:, 2:3], colw[:, 2:3], 1.0 / D, EPS, Alu.mult, Alu.add)
                c.inc(i, "vv", 1)

            @blk.scalar
            def _(sc):
                c.wait(sc, "vv")
                c.inc(sc.activation(colw[:, 3:4], colw[:, 2:3], AF.Sqrt), "ev", 1)

            @blk.vector
            def _(v, tc_guard_mm=tc_guard_mm):
                c.wait(v, "ev")
                if tc_guard_mm is not None:
                    c.wait(v, "mm", tc_guard_mm)
                c.drain(v, v.reciprocal(colw[:, 3:4], colw[:, 3:4]))
                i = v.tensor_scalar(xnstg_ap, xc_s2[:, :], colw[:, 3:4], None, Alu.mult)
                c.inc(i, "vv", 1)

        def build_rfp(rfp, tag):
            @blk.gpsimd
            def _(g):
                i = g.memset(rfp[:, :], 0.0)
                c.inc(i, "pp", 1)
                g.wait_ge(c.sem["pp"], c.cnt["pp"])
                c.inc(g.dma_start(out=rfp[0:64, 0:256], in_=rfTi), "cst", 16)
                c.inc(g.dma_start(out=rfp[64:128, 256:512], in_=rfTi), "cst", 16)

        def emit_trig_pair(src_f32, cos_dst, sin_dst, u1, u2, pre_ev, pre_mm):
            @blk.vector
            def _(v):
                c.wait(v, "ev", pre_ev)
                v.tensor_scalar(u1[:, :], src_f32, 1.0 / TWO_PI, MAGIC, Alu.mult, Alu.add)
                v.tensor_scalar(u1[:, :], u1[:, :], MAGIC, TWO_PI, Alu.subtract, Alu.mult)
                v.scalar_tensor_tensor(u1[:, :], u1[:, :], -1.0, src_f32, Alu.mult, Alu.add)
                v.tensor_scalar(u2[:, :], src_f32, PI / 2, 1.0 / TWO_PI, Alu.add, Alu.mult)
                v.tensor_scalar(u2[:, :], u2[:, :], MAGIC, MAGIC, Alu.add, Alu.subtract)
                v.tensor_scalar(u2[:, :], u2[:, :], TWO_PI, None, Alu.mult)
                i = v.scalar_tensor_tensor(u2[:, :], u2[:, :], -1.0, src_f32, Alu.mult, Alu.add)
                c.inc(i, "vv", 1)

            @blk.scalar
            def _(sc):
                c.wait(sc, "vv")
                c.wait(sc, "mm", pre_mm)
                sc.activation(sin_dst, u1[:, :], AF.Sin)
                i = sc.activation(cos_dst, u2[:, :], AF.Sin, bias=consts[:, 1:2])
                c.inc(i, "ev", 1)

        arf.reset()
        arb.reset()
        x2til = arf.take(4, D)
        xc_s2 = arf.take(D)
        sq_s2 = arf.take(D)
        projp = arf.take(2, 4, 512)
        sqKp = arf.at(4096, 4, 128)
        u1_s = arf.take(2048)
        u2_s = arf.take(2048)
        kvstg = arf.take(16, 264)
        assert arf.off <= AF_ELE, arf.off
        wkTb = arb.take(8, D)
        wvTb = arb.take(8, D)
        xnT_ab = arb.take(8, TL)
        K_fm = arb.take(8, TL)
        V_tm = arb.take(4, D)
        xnstg2 = arb.take(D)
        Kfcsp = arb.take(2, 4, 1024)
        Vsbp = arb.take(2, 4, 2, 66)
        rfp = arb.take(512)
        assert arb.off <= AB_ELE, arb.off

        @blk.gpsimd
        def _(g):
            c.inc(g.dma_start(out=wkTb[:, :, :], in_=wkT.rearrange("(k p) o -> p k o", p=128)), "cst", 16)
            c.inc(g.dma_start(out=wvTb[:, :, :], in_=wvT.rearrange("(k p) o -> p k o", p=128)), "cst", 16)
        build_rfp(rfp, "s1")
        barrier()

        ck = {}
        for b in range(NB):
            @blk.sync
            def _(sy, b=b):
                c.inc(sy.dma_start(out=x2til[:, :, :],
                                   in_=xres[b * TL:(b + 1) * TL, :].rearrange("(t p) o -> p t o", p=128)),
                      "ld", 16)
            ldt = c.val("ld")

            @blk.vector
            def _(v, ldt=ldt):
                c.wait(v, "ld", ldt)

            for tc in range(4):
                tgm = None if (b == 0 and tc == 0) else c.val("mm")
                emit_ln2(x2til[:, tc, :], xnstg2[:, :], xc_s2, sq_s2, tc_guard_mm=tgm)

                @blk.tensor
                def _(t, tc=tc):
                    c.wait(t, "vv", c.val("vv"))
                    i = None
                    for kc in range(8):
                        i = t.transpose(pt[tc % 2][:, kc, :], xnstg2[:, kc * 128:(kc + 1) * 128], ident_b[:, :])
                    c.inc(i, "mm", 1)

                @blk.vector
                def _(v, tc=tc):
                    c.wait(v, "mm")
                    i = v.tensor_copy(xnT_ab[:, 0:8, tc * 128:(tc + 1) * 128], pt[tc % 2][:, :, :])
                    c.inc(i, "vv", 1)
            xn_vv = c.val("vv")

            @blk.gpsimd
            def _(g, b=b, xn_vv=xn_vv):
                c.wait(g, "vv", xn_vv)
                c.inc(g.dma_start(out=xnT_sp[b], in_=xnT_ab[:, :, :].rearrange("p a b -> p (a b)")), "st", 16)

            ev_k0 = c.val("ev")
            for oc in range(8):
                @blk.tensor
                def _(t, oc=oc, ev_k0=ev_k0):
                    if oc >= 4:
                        c.wait(t, "ev", ev_k0 + oc - 3)
                    bank = pb[oc % 4]
                    i = None
                    for kc in range(8):
                        i = t.matmul(bank[:, :], wkTb[:, kc, oc * 128:(oc + 1) * 128],
                                     xnT_ab[:, kc, :], start=(kc == 0), stop=(kc == 7))
                    c.inc(i, "mm", 1)

                @blk.scalar
                def _(sc, oc=oc):
                    c.wait(sc, "mm")
                    c.inc(sc.activation(K_fm[:, oc, :], pb[oc % 4][:, :], AF.Copy), "ev", 1)

            ev_v0 = c.val("ev")
            for u in range(8):
                ts_, occ = divmod(u, 2)

                @blk.tensor
                def _(t, ts_=ts_, occ=occ, u=u, ev_v0=ev_v0):
                    if u >= 4:
                        c.wait(t, "ev", ev_v0 + u - 3)
                    bank = pb[u % 4]
                    i = None
                    for kc in range(8):
                        i = t.matmul(bank[:, :], xnT_ab[:, kc, ts_ * 128:(ts_ + 1) * 128],
                                     wvTb[:, kc, occ * 512:(occ + 1) * 512],
                                     start=(kc == 0), stop=(kc == 7))
                    c.inc(i, "mm", 1)

                @blk.scalar
                def _(sc, ts_=ts_, occ=occ):
                    c.wait(sc, "mm")
                    c.inc(sc.activation(V_tm[:, ts_, occ * 512:(occ + 1) * 512],
                                        pb[u % 4][:, :], AF.Copy), "ev", 1)
            prep_ev = c.val("ev")

            def s1_proj(pr, gi, par):
                @blk.tensor
                def _(t):
                    if pr == 0:
                        c.wait(t, "ev", prep_ev)
                        c.wait(t, "vv", c.val("vv"))
                    if gi >= 1:
                        c.wait(t, "ev", ck[("prev", gi - 1)])
                        c.wait(t, "ev", ck[("kvev", gi - 1)])
                    i = None
                    for ts_ in range(4):
                        t.matmul(pb[ts_][:, 0:512], K_fm[:, pr, ts_ * 128:(ts_ + 1) * 128],
                                 rfp[:, :], start=True, stop=True)
                        i = t.matmul(pm[:, ts_ * 128:(ts_ + 1) * 128],
                                     K_fm[:, pr, ts_ * 128:(ts_ + 1) * 128],
                                     ident_b[:, :], start=True, stop=True)
                    ck[("prmm", gi)] = c.inc(i, "mm", 1)

            def s1_evac(pr, gi, par):
                @blk.scalar
                def _(sc):
                    c.wait(sc, "mm", ck[("prmm", gi)])
                    if gi >= 2:
                        c.wait(sc, "vv", ck[("trgv", gi - 2)])
                    for ts_ in range(4):
                        sc.activation(projp[:, par, ts_, :], pb[ts_][:, 0:512], AF.Copy)
                    i = sc.activation(sqKp[:, :, :].rearrange("p a b -> p (a b)"), pm[:, :], AF.Square)
                    ck[("prev", gi)] = c.inc(i, "ev", 1)

            def s1_nsqtrig(pr, gi, par):
                @blk.vector
                def _(v):
                    c.wait(v, "ev", ck[("prev", gi)])
                    if gi >= 1:
                        c.wait(v, "ev", ck[("trge", gi - 1)])
                    for hh in range(2):
                        v.tensor_reduce(
                            colw[:, 4 + 12 * par + 4 * hh:8 + 12 * par + 4 * hh].rearrange("p (g o) -> p g o", o=1),
                            sqKp[:, :, hh * 64:(hh + 1) * 64], AX, Alu.add)
                    src_ = projp[:, par, :, :].rearrange("p a b -> p (a b)")
                    v.tensor_scalar(u1_s[:, :], src_, 1.0 / TWO_PI, MAGIC, Alu.mult, Alu.add)
                    v.tensor_scalar(u1_s[:, :], u1_s[:, :], MAGIC, TWO_PI, Alu.subtract, Alu.mult)
                    v.scalar_tensor_tensor(u1_s[:, :], u1_s[:, :], -1.0, src_, Alu.mult, Alu.add)
                    v.tensor_scalar(u2_s[:, :], src_, PI / 2, 1.0 / TWO_PI, Alu.add, Alu.mult)
                    v.tensor_scalar(u2_s[:, :], u2_s[:, :], MAGIC, MAGIC, Alu.add, Alu.subtract)
                    v.tensor_scalar(u2_s[:, :], u2_s[:, :], TWO_PI, None, Alu.mult)
                    i = v.scalar_tensor_tensor(u2_s[:, :], u2_s[:, :], -1.0, src_, Alu.mult, Alu.add)
                    ck[("trgv", gi)] = c.inc(i, "vv", 1)
                    ck[("nsqv", gi)] = ck[("trgv", gi)]

            def s1_exp(pr, gi, par):
                @blk.scalar
                def _(sc):
                    c.wait(sc, "vv", ck[("nsqv", gi)])
                    i = sc.activation(colw[:, 4 + 12 * par:12 + 12 * par],
                                      colw[:, 4 + 12 * par:12 + 12 * par], AF.Exp, scale=-0.5)
                    ck[("expe", gi)] = c.inc(i, "ev", 1)

            def s1_kscvs(pr, gi, par):
                @blk.vector
                def _(v):
                    c.wait(v, "ev", ck[("expe", gi)])
                    if gi >= 2:
                        c.wait(v, "mm", ck[("kvmm", gi - 2)])
                    c.drain(v, v.tensor_scalar(colw[:, 4 + 12 * par:12 + 12 * par],
                                               colw[:, 4 + 12 * par:12 + 12 * par],
                                               1.0 / 16.0, None, Alu.mult))
                    v.memset(Vsbp[:, par, :, :, 65:66], 0.0)
                    for ts_ in range(4):
                        for hh in range(2):
                            v.tensor_scalar(Vsbp[:, par, ts_, hh, 0:64],
                                            V_tm[:, ts_, (pr * 2 + hh) * 64:(pr * 2 + hh + 1) * 64],
                                            colw[:, 4 + 12 * par + 4 * hh + ts_:5 + 12 * par + 4 * hh + ts_],
                                            None, Alu.mult)
                    i = None
                    for hh in range(2):
                        i = v.tensor_copy(Vsbp[:, par, :, hh, 64:65],
                                          colw[:, 4 + 12 * par + 4 * hh:8 + 12 * par + 4 * hh]
                                          .rearrange("p (g o) -> p g o", o=1))
                    ck[("vsv", gi)] = c.inc(i, "vv", 1)

            def s1_sin(pr, gi, par):
                @blk.scalar
                def _(sc):
                    c.wait(sc, "vv", ck[("trgv", gi)])
                    if gi >= 2:
                        c.wait(sc, "mm", ck[("kvmm", gi - 2)])
                    sc.activation(Kfcsp[:, par, :, :].rearrange("p a (h x) -> p a h x", h=2)[:, :, :, 256:512],
                                  u1_s[:, :], AF.Sin)
                    i = sc.activation(Kfcsp[:, par, :, :].rearrange("p a (h x) -> p a h x", h=2)[:, :, :, 0:256],
                                      u2_s[:, :], AF.Sin, bias=consts[:, 1:2])
                    ck[("trge", gi)] = c.inc(i, "ev", 1)

            def s1_kv(pr, gi, par):
                @blk.tensor
                def _(t):
                    c.wait(t, "ev", ck[("trge", gi)])
                    c.wait(t, "vv", ck[("vsv", gi)])
                    if gi >= 1:
                        c.wait(t, "ev", ck[("kvev", gi - 1)])
                    i = None
                    for hh in range(2):
                        bank = pm2 if hh == 0 else pb[0]
                        for fs in range(4):
                            for ts_ in range(4):
                                i = t.matmul(bank[:, fs * 66:(fs + 1) * 66],
                                             Kfcsp[:, par, ts_, hh * 512 + fs * 128:hh * 512 + (fs + 1) * 128],
                                             Vsbp[:, par, ts_, hh, :], start=(ts_ == 0), stop=(ts_ == 3))
                    ck[("kvmm", gi)] = c.inc(i, "mm", 1)

            def s1_kvev(pr, gi, par):
                @blk.scalar
                def _(sc):
                    c.wait(sc, "mm", ck[("kvmm", gi)])
                    sc.activation(kvstg[:, pr * 2, :], pm2[:, 0:264], AF.Copy)
                    i = sc.activation(kvstg[:, pr * 2 + 1, :], pb[0][:, 0:264], AF.Copy)
                    ck[("kvev", gi)] = c.inc(i, "ev", 1)

            late1 = [s1_exp, s1_kscvs, s1_sin, s1_kv, s1_kvev]
            early1 = [s1_proj, s1_evac, s1_nsqtrig]
            for it in range(9):
                if it >= 1:
                    pr = it - 1
                    for f in late1:
                        f(pr, b * 8 + pr, (b * 8 + pr) % 2)
                if it < 8:
                    pr = it
                    for f in early1:
                        f(pr, b * 8 + pr, (b * 8 + pr) % 2)

            @blk.gpsimd
            def _(g, b=b):
                c.wait(g, "ev", c.val("ev"))
                c.inc(g.dma_start(out=kv_in[b * 128:(b + 1) * 128, :], in_=kvstg[:, :, :]), "st", 16)
            barrier()

        @blk.gpsimd
        def _(g):
            c.wait(g, "st", c.val("st"))
            i = g.collective_compute("AllReduce", Alu.add, replica_groups=[list(range(NCORE))],
                                     ins=[kv_in.opt()], outs=[kv_out.opt()])
            c.inc(i, "cc", 1)
        barrier()

        arf.reset()
        arb.reset()
        projq = arf.take(2, 4, 512)
        u1_s = arf.take(2048)
        u2_s = arf.take(2048)
        xr2 = arf.take(4, D)
        ost2 = arf.take(4, D)
        sqQp = arf.at(arf.off - 4096, 4, 128)
        szz = arf.take(2, 264)
        assert arf.off <= AF_ELE, arf.off
        wqTb = arb.take(8, D)
        woTb = arb.take(8, D)
        xnT_ab = arb.take(8, TL)
        Q_fm = arb.take(8, TL)
        Qfcsp = arb.take(2, 8, 512)
        attT = arb.take(8, TL)
        kvbb = arb.take(16, 264)
        attstg2 = arb.take(4, 128)
        rfp = arb.take(512)
        assert arb.off <= AB_ELE, arb.off

        @blk.gpsimd
        def _(g):
            c.inc(g.dma_start(out=wqTb[:, :, :], in_=wqT.rearrange("(k p) o -> p k o", p=128)), "cst", 16)
            c.inc(g.dma_start(out=woTb[:, :, :], in_=woT.rearrange("(k p) o -> p k o", p=128)), "cst", 16)
        build_rfp(rfp, "s2")
        barrier()

        for b in range(NB):
            @blk.sync
            def _(sy, b=b):
                c.inc(sy.dma_start(out=xnT_ab[:, :, :].rearrange("p a b -> p (a b)"), in_=xnT_sp[b]), "ld", 16)
                c.inc(sy.dma_start(out=xr2[:, :, :],
                                   in_=xres[b * TL:(b + 1) * TL, :].rearrange("(t p) o -> p t o", p=128)), "ld", 16)
            ldt = c.val("ld")

            @blk.gpsimd
            def _(g, b=b):
                c.inc(g.dma_start(out=kvbb[:, :, :].rearrange("p a b -> p (a b)"),
                                  in_=kv_out[b * 128:(b + 1) * 128, :]), "cst", 16)
            kv_ld = c.val("cst")

            ev_q0 = c.val("ev")
            for oc in range(8):
                @blk.tensor
                def _(t, oc=oc, ev_q0=ev_q0, ldt=ldt):
                    if oc == 0:
                        c.wait(t, "ld", ldt)
                    if oc >= 4:
                        c.wait(t, "ev", ev_q0 + oc - 3)
                    bank = pb[oc % 4]
                    i = None
                    for kc in range(8):
                        i = t.matmul(bank[:, :], wqTb[:, kc, oc * 128:(oc + 1) * 128],
                                     xnT_ab[:, kc, :], start=(kc == 0), stop=(kc == 7))
                    c.inc(i, "mm", 1)

                @blk.scalar
                def _(sc, oc=oc):
                    c.wait(sc, "mm")
                    c.inc(sc.activation(Q_fm[:, oc, :], pb[oc % 4][:, :], AF.Copy), "ev", 1)
            prep_ev = c.val("ev")

            if dbg and b == 0:
                dbg7 = nc.dram_tensor("dbg7", [128, 8192], F32, kind="ExternalOutput").ap()

                @blk.gpsimd
                def _(g, prep_ev=prep_ev):
                    c.wait(g, "ev", prep_ev)
                    c.inc(g.dma_start(out=dbg7[:, 0:4096], in_=Q_fm[:, :, :].rearrange("p a b -> p (a b)")), "st", 16)
                    c.inc(g.dma_start(out=dbg7[:, 4096:8192], in_=xnT_ab[:, :, :].rearrange("p a b -> p (a b)")), "st", 16)

            def s2_proj(pr, gi, par):
                @blk.tensor
                def _(t):
                    if pr == 0:
                        c.wait(t, "ev", prep_ev)
                        c.wait(t, "vv", c.val("vv"))
                        c.wait(t, "cst", kv_ld)
                    if gi >= 65:
                        c.wait(t, "ev", ck[("prev", gi - 1)])
                        c.wait(t, "ev", ck.get(("qze", gi - 1), 0))
                    i = None
                    for blk_ in range(4):
                        t.matmul(pb[blk_][:, 0:512], rfp[:, blk_ * 128:(blk_ + 1) * 128],
                                 Q_fm[:, pr, :], start=True, stop=True)
                        i = t.matmul(pm[:, blk_ * 128:(blk_ + 1) * 128],
                                     Q_fm[:, pr, blk_ * 128:(blk_ + 1) * 128],
                                     ident_b[:, :], start=True, stop=True)
                    ck[("prmm", gi)] = c.inc(i, "mm", 1)

            def s2_evac(pr, gi, par):
                @blk.scalar
                def _(sc):
                    c.wait(sc, "mm", ck[("prmm", gi)])
                    if gi >= 66:
                        c.wait(sc, "vv", ck[("trgv", gi - 2)])
                    for blk_ in range(4):
                        sc.activation(projq[:, par, blk_, :], pb[blk_][:, 0:512], AF.Copy)
                    i = sc.activation(sqQp[:, :, :].rearrange("p a b -> p (a b)"), pm[:, :], AF.Square)
                    ck[("prev", gi)] = c.inc(i, "ev", 1)

            def s2_nsqtrig(pr, gi, par):
                @blk.vector
                def _(v):
                    c.wait(v, "ev", ck[("prev", gi)])
                    if gi >= 65:
                        c.wait(v, "ev", ck[("trge", gi - 1)])
                    for hh in range(2):
                        v.tensor_reduce(
                            colw[:, 4 + 12 * par + 4 * hh:8 + 12 * par + 4 * hh].rearrange("p (g o) -> p g o", o=1),
                            sqQp[:, :, hh * 64:(hh + 1) * 64], AX, Alu.add)
                    src_ = projq[:, par, :, :].rearrange("p a b -> p (a b)")
                    v.tensor_scalar(u1_s[:, :], src_, 1.0 / TWO_PI, MAGIC, Alu.mult, Alu.add)
                    v.tensor_scalar(u1_s[:, :], u1_s[:, :], MAGIC, TWO_PI, Alu.subtract, Alu.mult)
                    v.scalar_tensor_tensor(u1_s[:, :], u1_s[:, :], -1.0, src_, Alu.mult, Alu.add)
                    v.tensor_scalar(u2_s[:, :], src_, PI / 2, 1.0 / TWO_PI, Alu.add, Alu.mult)
                    v.tensor_scalar(u2_s[:, :], u2_s[:, :], MAGIC, MAGIC, Alu.add, Alu.subtract)
                    v.tensor_scalar(u2_s[:, :], u2_s[:, :], TWO_PI, None, Alu.mult)
                    i = v.scalar_tensor_tensor(u2_s[:, :], u2_s[:, :], -1.0, src_, Alu.mult, Alu.add)
                    ck[("trgv", gi)] = c.inc(i, "vv", 1)
                    ck[("nsqv", gi)] = ck[("trgv", gi)]

            def s2_exp(pr, gi, par):
                @blk.scalar
                def _(sc):
                    c.wait(sc, "vv", ck[("nsqv", gi)])
                    i = sc.activation(colw[:, 4 + 12 * par:12 + 12 * par],
                                      colw[:, 4 + 12 * par:12 + 12 * par], AF.Exp, scale=-0.5)
                    ck[("expe", gi)] = c.inc(i, "ev", 1)

            def s2_ksc(pr, gi, par):
                @blk.vector
                def _(v):
                    c.wait(v, "ev", ck[("expe", gi)])
                    c.drain(v, v.tensor_scalar(colw[:, 4 + 12 * par:12 + 12 * par],
                                               colw[:, 4 + 12 * par:12 + 12 * par],
                                               1.0 / 16.0, None, Alu.mult))
                    i = v.engine_nop()
                    ck[("kscv", gi)] = c.inc(i, "vv", 1)

            def s2_sin(pr, gi, par):
                @blk.scalar
                def _(sc):
                    c.wait(sc, "vv", ck[("trgv", gi)])
                    if gi >= 66:
                        c.wait(sc, "mm", ck[("qkmm", gi - 2)])
                    sc.activation(Qfcsp[:, par, :, :].rearrange("p (h f) x -> p h f x", h=2)[:, :, 2:4, :],
                                  u1_s[:, :], AF.Sin)
                    i = sc.activation(Qfcsp[:, par, :, :].rearrange("p (h f) x -> p h f x", h=2)[:, :, 0:2, :],
                                      u2_s[:, :], AF.Sin, bias=consts[:, 1:2])
                    ck[("trge", gi)] = c.inc(i, "ev", 1)

            def s2_qkvz(pr, gi, par):
                @blk.tensor
                def _(t):
                    c.wait(t, "ev", ck[("trge", gi)])
                    if gi >= 65:
                        c.wait(t, "ev", ck.get(("qze", gi - 1), 0))
                    i = None
                    for hh in range(2):
                        bank = pm2 if hh == 0 else pb[0]
                        h = pr * 2 + hh
                        for ts_ in range(4):
                            for fc in range(4):
                                i = t.matmul(bank[:, ts_ * 66:ts_ * 66 + 66],
                                             Qfcsp[:, par, hh * 4 + fc, ts_ * 128:(ts_ + 1) * 128],
                                             kvbb[:, h, fc * 66:(fc + 1) * 66],
                                             start=(fc == 0), stop=(fc == 3))
                    ck[("qkmm", gi)] = c.inc(i, "mm", 1)

            def s2_qze(pr, gi, par):
                @blk.scalar
                def _(sc):
                    c.wait(sc, "mm", ck[("qkmm", gi)])
                    if gi >= 65:
                        c.wait(sc, "vv", ck[("epiv", gi - 1)])
                    sc.activation(szz[:, 0, :], pm2[:, 0:264], AF.Copy)
                    i = sc.activation(szz[:, 1, :], pb[0][:, 0:264], AF.Copy)
                    ck[("qze", gi)] = c.inc(i, "ev", 1)

            def s2_epi(pr, gi, par):
                @blk.vector
                def _(v):
                    c.wait(v, "ev", ck[("qze", gi)])
                    c.wait(v, "vv", ck[("kscv", gi)])
                    if gi >= 65:
                        c.wait(v, "mm", ck[("attm", gi - 1)])
                    i = None
                    for hh in range(2):
                        qv3 = colw[:, 4 + 12 * par + 4 * hh:8 + 12 * par + 4 * hh].rearrange("p (g o) -> p g o", o=1)
                        zz = colw[:, 24:28] if par == 0 else colw[:, 28:32]
                        zz3 = zz.rearrange("p (g o) -> p g o", o=1)
                        c.drain(v, v.tensor_tensor(zz3, qv3,
                                        szz[:, hh, :].rearrange("p (w c) -> p w c", c=66)[:, :, 64:65],
                                        Alu.mult))
                        c.drain(v, v.tensor_scalar(zz, zz, 1e-6, None, Alu.max))
                        c.drain(v, v.reciprocal(zz, zz))
                        c.drain(v, v.tensor_tensor(zz3, qv3, zz3, Alu.mult))
                        for ts_ in range(4):
                            i = v.tensor_scalar(attstg2[:, ts_, hh * 64:(hh + 1) * 64],
                                                szz[:, hh, ts_ * 66:ts_ * 66 + 64],
                                                zz[:, ts_:ts_ + 1], None, Alu.mult)
                    ck[("epiv", gi)] = c.inc(i, "vv", 1)

            def s2_atttr(pr, gi, par):
                @blk.tensor
                def _(t):
                    c.wait(t, "vv", ck[("epiv", gi)])
                    i = None
                    for ts_ in range(4):
                        i = t.transpose(pt[1][:, ts_, :], attstg2[:, ts_, :], ident_b[:, :])
                    ck[("attm", gi)] = c.inc(i, "mm", 1)

            def s2_attcp(pr, gi, par):
                @blk.vector
                def _(v):
                    c.wait(v, "mm", ck[("attm", gi)])
                    i = v.tensor_copy(attT[:, pr, :], pt[1][:, 0:4, :])
                    ck[("attv", gi)] = c.inc(i, "vv", 1)

            late2 = [s2_exp, s2_ksc, s2_sin, s2_qkvz, s2_qze, s2_epi, s2_atttr, s2_attcp]
            early2 = [s2_proj, s2_evac, s2_nsqtrig]
            for it in range(9):
                if it >= 1:
                    pr = it - 1
                    for f in late2:
                        f(pr, 64 + b * 8 + pr, (64 + b * 8 + pr) % 2)
                if it < 8:
                    pr = it
                    for f in early2:
                        f(pr, 64 + b * 8 + pr, (64 + b * 8 + pr) % 2)

            att_vv = c.val("vv")

            if dbg and b == 0:
                dbg6 = nc.dram_tensor("dbg6", [128, 16384], F32, kind="ExternalOutput").ap()

                @blk.gpsimd
                def _(g):
                    c.wait(g, "vv", att_vv)
                    c.inc(g.dma_start(out=dbg6[:, 0:4096], in_=attT[:, :, :].rearrange("p a b -> p (a b)")), "st", 16)
                    c.inc(g.dma_start(out=dbg6[:, 4096:12288], in_=Qfcsp[:, :, :, :].rearrange("p a b c -> p (a b c)")), "st", 16)
                    c.inc(g.dma_start(out=dbg6[:, 12288:16384], in_=projq[:, :, :, :].rearrange("p a b c -> p (a b c)")), "st", 16)

            for u in range(8):
                ts_, occ = divmod(u, 2)

                @blk.tensor
                def _(t, ts_=ts_, occ=occ, u=u, att_vv=att_vv):
                    if u == 0:
                        c.wait(t, "vv", att_vv)
                    if u >= 4:
                        c.wait(t, "vv", c.val("vv"))
                    bank = pb[u % 4]
                    i = None
                    for cc in range(8):
                        i = t.matmul(bank[:, :], attT[:, cc, ts_ * 128:(ts_ + 1) * 128],
                                     woTb[:, cc, occ * 512:(occ + 1) * 512],
                                     start=(cc == 0), stop=(cc == 7))
                    c.inc(i, "mm", 1)

                @blk.vector
                def _(v, ts_=ts_, occ=occ, u=u):
                    c.wait(v, "mm")
                    i = v.tensor_tensor(ost2[:, ts_, occ * 512:(occ + 1) * 512],
                                        pb[u % 4][:, :],
                                        xr2[:, ts_, occ * 512:(occ + 1) * 512], Alu.add)
                    c.inc(i, "vv", 1)

            @blk.gpsimd
            def _(g, b=b):
                c.wait(g, "vv", c.val("vv"))
                c.inc(g.dma_start(
                    out=xres[b * TL:(b + 1) * TL, :].rearrange("(t p) o -> p t o", p=128),
                    in_=ost2[:, :, :]), "st", 16)
            barrier()

    phase_setup()
    phase_attn_a()
    if stage >= 2:
        phase_dequant()
        phase_ffn(xres, y_out if stage == 2 else xres, 0)
    if stage >= 3:
        phase_attn_b()
        if stage == 3:
            @blk.gpsimd
            def _(g):
                c.wait(g, "st", c.val("st"))
                i = g.dma_start(out=y_out, in_=xres)
                c.inc(i, "st", 16)
                c.wait(g, "st")
    if stage >= 4:
        phase_ffn(xres, y_out, 5)

    if dbg and stage >= 3:
        kvtap = arf.at(0, 16, 264)

        @blk.sync
        def _(sy):
            c.wait(sy, "st", c.val("st"))
            c.inc(sy.dma_start(out=kvtap[:, :, :].rearrange("p a b -> p (a b)"),
                               in_=kv_out[0:128, :]), "ld", 16)

        @blk.gpsimd
        def _(g):
            c.wait(g, "ld", c.val("ld"))

    if dbg and stage >= 2:
        global DBG_MAP2
        DBG_MAP2 = {}
        dbg2 = nc.dram_tensor("dbg2", [128, 40960], F32, kind="ExternalOutput").ap()

        @blk.gpsimd
        def _(g):
            off = 0
            taps3 = ([("kv0", arf.at(0, 4224), 4224),
                      ("attT", arb.at(26624, 4096), 4096),
                      ("qfcs", arb.at(24576, 2048), 2048)]
                     if stage >= 3 else [])
            for name, ap_, n in taps3 + [("colw", colw[:, :], 32),
                                 ("gate0", gate_s[0], 4096), ("up0", up_s[0], 4096),
                                 ("g20", g2_s[0], 4096), ("t40", t4_s[0], 4096),
                                 ("wg0", wgT_dq[0, 0], 128), ("wg1", wgT_dq[0, 1], 128),
                                 ("wd0", wdT_dq[0], 1024)]:
                DBG_MAP2[name] = (off, n)
                c.inc(g.dma_start(out=dbg2[:, off:off + n], in_=ap_), "st", 16)
                off += n
            c.wait(g, "st")

    if dbg:
        taps = [("xnT", xnT, BF16, 4096), ("z_fm", z_fm, BF16, 4096),
                ("z_tm", z_tm, BF16, 4096), ("q_fm", q_fm, BF16, 4096),
                ("G_fm", G_fm, BF16, 4096), ("G_tm", G_tm, BF16, 4096),
                ("h_tm", h_tm, BF16, 1024), ("gp_b", gp_b, BF16, 1024),
                ("da_fm", da_fm, BF16, 1024), ("da_tm", da_tm, BF16, 1024),
                ("h2_fm", h_fm, BF16, 1024), ("a_f", a_f, F32, 1024),
                ("gst1", gst1, F32, 2048), ("gst2", gst2, F32, 2048),
                ("zy", y_fm, BF16, 4096), ("xs", xs, F32, 4096),
                ("scrb", scr_b, F32, 2048), ("w1T_b", w1T_b, BF16, 2048),
                ("w2T_b", w2T_b, BF16, 2048), ("w2n_b", w2n_b, BF16, 2048)]
        off = [0]
        global DBG_MAP
        DBG_MAP = {}

        @blk.gpsimd
        def _(g):
            for name, ap, dt_, n in taps:
                flat = ap
                if len(flat.shape) == 3:
                    flat = flat.rearrange("p a b -> p (a b)")
                rows = flat.shape[0]
                DBG_MAP[name] = (off[0], n, rows)
                i = g.dma_start(out=dbg_out[0:rows, off[0]:off[0] + n], in_=flat)
                c.inc(i, "st", 16)
                off[0] += n
            c.wait(g, "st")

    if stage == 1:
        @blk.gpsimd
        def _(g):
            c.wait(g, "st", c.val("st"))
            i = g.dma_start(out=y_out, in_=xres)
            c.inc(i, "st", 16)
            c.wait(g, "st")

    es.close()
    return nc


_CACHE = {}


def _get_prog(stage=None):
    key = STAGE if stage is None else stage
    if key not in _CACHE:
        _CACHE[key] = build_program(key)
    return _CACHE[key]


def make_in_maps(inputs):
    x = np.asarray(inputs["L0_attn_a_in_seq"], dtype=np.float32)
    T = lambda a: np.ascontiguousarray(np.asarray(a, dtype=np.float32).T)
    C = lambda a: np.ascontiguousarray(np.asarray(a, dtype=np.float32))
    shared = {
        "tkT": T(inputs["L0_attn_a_t_k"]), "tvT": T(inputs["L0_attn_a_t_v"]),
        "tqT": T(inputs["L0_attn_a_t_q"]),
        "w1T": T(inputs["L0_attn_a_W1"]), "w2T": T(inputs["L0_attn_a_W2"]),
        "w2n": C(inputs["L0_attn_a_W2"]),
        "b1i": C(inputs["L0_attn_a_b1"]).reshape(1, DI),
        "b2i": C(inputs["L0_attn_a_b2"]).reshape(1, D),
        "wqT": T(inputs["L0_attn_b_wq"]), "wkT": T(inputs["L0_attn_b_wk"]),
        "wvT": T(inputs["L0_attn_b_wv"]), "woT": T(inputs["L0_attn_b_wo"]),
        "rfT": T(inputs["L0_attn_b_rf"]),
        "wgate": C(inputs["L0_ffn_w_gate"]), "wup": C(inputs["L0_ffn_w_up"]),
        "wdown": C(inputs["L0_ffn_w_down"]),
    }
    in_maps = []
    for cid in range(NCORE):
        m = dict(shared)
        m["x_in"] = np.ascontiguousarray(x[:, cid * TL:(cid + 1) * TL, :].reshape(NB * TL, D))
        in_maps.append(m)
    return in_maps


def kernel(**inputs):
    in_maps = make_in_maps(inputs)
    nc = _get_prog()
    res = run_bass_kernel_spmd(nc, in_maps, core_ids=list(range(NCORE)))
    out = np.empty((NB, 4096, D), dtype=np.float32)
    for cid in range(NCORE):
        out[:, cid * TL:(cid + 1) * TL, :] = np.asarray(res.results[cid]["y"]).reshape(NB, TL, D)
    return out


# version 04
# speedup vs baseline: 1.2139x; 1.0322x over previous
import contextlib
import numpy as np
import ml_dtypes
import concourse.bass as bass
import concourse.mybir as mybir
from concourse.bass_utils import run_bass_kernel_spmd
from concourse.alu_op_type import AluOpType as Alu

F32 = mybir.dt.float32
BF16 = mybir.dt.bfloat16
AF = mybir.ActivationFunctionType
AX = mybir.AxisListType.X

MAGIC = 1.5 * 2.0 ** 23
PI = float(np.pi)
LR = 0.001
EPS = 1e-5
NB, TL, D, DI, FF = 8, 512, 1024, 256, 3072
CG = 2.0 / (4096.0 * 1024.0)
NCORE = 8

STAGE = 4
DBG = False
DBG_STEPS = None

AF_ELE = 18688
AB_ELE = 67840


class Ctx:
    def __init__(self, nc, es):
        self.nc, self.es, self.sem, self.cnt = nc, es, {}, {}

    def S(self, name):
        self.sem[name] = self.es.enter_context(self.nc.semaphore(name))
        self.cnt[name] = 0

    def inc(self, instr, name, n=1):
        self.cnt[name] += n
        instr.then_inc(self.sem[name], n)
        return self.cnt[name]

    def val(self, name):
        return self.cnt[name]

    def drain(self, eng, instr, name="vv"):
        v = self.inc(instr, name, 1)
        eng.wait_ge(self.sem[name], v)

    def wait(self, eng, name, target=None):
        if target is None:
            target = self.cnt[name]
        if target > 0:
            eng.wait_ge(self.sem[name], target)


class Arena:
    def __init__(self, t, cap):
        self.t, self.cap, self.off = t, cap, 0

    def take(self, *shape):
        n = int(np.prod(shape))
        ap = self.t[:, self.off:self.off + n]
        assert self.off + n <= self.cap, f"arena overflow {self.off}+{n}>{self.cap}"
        self.off += n
        if len(shape) > 1:
            names = " ".join(f"d{i}" for i in range(len(shape)))
            ap = ap.rearrange(f"p ({names}) -> p {names}",
                              **{f"d{i}": s for i, s in enumerate(shape)})
        return ap

    def at(self, off, *shape):
        n = int(np.prod(shape))
        ap = self.t[:, off:off + n]
        if len(shape) > 1:
            names = " ".join(f"d{i}" for i in range(len(shape)))
            ap = ap.rearrange(f"p ({names}) -> p {names}",
                              **{f"d{i}": s for i, s in enumerate(shape)})
        return ap

    def reset(self, off=0):
        self.off = off


def build_program(stage=STAGE, dbg=None, dbg_steps=None):
    dbg = DBG if dbg is None else dbg
    nsteps = NB if (DBG_STEPS if dbg_steps is None else dbg_steps) is None else (DBG_STEPS if dbg_steps is None else dbg_steps)
    nc = bass.Bass("TRN2", target_bir_lowering=False)

    def din(name, shape):
        return nc.dram_tensor(name, shape, F32, kind="ExternalInput").ap()

    x_in = din("x_in", [NB * TL, D])
    tkT, tvT, tqT = din("tkT", [D, D]), din("tvT", [D, D]), din("tqT", [D, D])
    w1Ti, w2Ti, w2ni = din("w1T", [D, DI]), din("w2T", [DI, D]), din("w2n", [D, DI])
    b1i, b2i = din("b1i", [1, DI]), din("b2i", [1, D])
    wqT, wkT = din("wqT", [D, D]), din("wkT", [D, D])
    wvT, woT = din("wvT", [D, D]), din("woT", [D, D])
    rfTi = din("rfT", [64, 256])
    wgate, wup, wdown = din("wgate", [FF, D]), din("wup", [FF, D]), din("wdown", [D, FF])

    y_out = nc.dram_tensor("y", [NB * TL, D], F32, kind="ExternalOutput").ap()
    dbg_out = nc.dram_tensor("dbg", [128, 65536], F32, kind="ExternalOutput").ap() if dbg else None

    xres = nc.dram_tensor("xres", [NB * TL, D], F32).ap()
    wgT_dq = nc.dram_tensor("wgT_dq", [24, 8, 128, 128], BF16).ap()
    wuT_dq = nc.dram_tensor("wuT_dq", [24, 8, 128, 128], BF16).ap()
    wdT_dq = nc.dram_tensor("wdT_dq", [24, 128, D], BF16).ap()
    gbuf_in = nc.dram_tensor("gbuf_in", [1028, 512], F32).ap()
    gbuf_out = nc.dram_tensor("gbuf_out", [1028, 512], F32).ap()
    up_s = nc.dram_tensor("up_s", [24, 128, NB * TL], BF16).ap()
    gate_s = nc.dram_tensor("gate_s", [24, 128, NB * TL], BF16).ap()
    g2_s = nc.dram_tensor("g2_s", [24, 128, NB * TL], BF16).ap()
    t4_s = nc.dram_tensor("t4_s", [24, 128, NB * TL], BF16).ap()
    mxi = [nc.dram_tensor(f"mxi{k}", [1, 1], F32).ap() for k in range(10)]
    mxo = [nc.dram_tensor(f"mxo{k}", [1, 1], F32).ap() for k in range(10)]
    kv_in = nc.dram_tensor("kv_in", [NB * 128, 16 * 264], F32).ap()
    kv_out = nc.dram_tensor("kv_out", [NB * 128, 16 * 264], F32).ap()
    xnT_sp = nc.dram_tensor("xnT_sp", [NB, 128, 8 * TL], BF16).ap()

    es = contextlib.ExitStack()
    c = Ctx(nc, es)

    def sbuf(name, shape, dtype):
        return es.enter_context(nc.sbuf_tensor(name, shape, dtype))

    ident_b = sbuf("ident_b", [128, 128], BF16)
    ones_b = sbuf("ones_b", [128, 512], BF16)
    consts = sbuf("consts", [128, 8], F32)
    colw = sbuf("colw", [128, 48], F32)

    arena_f_t = sbuf("arena_f", [128, AF_ELE], F32)
    arena_b_t = sbuf("arena_b", [128, AB_ELE], BF16)
    arf = Arena(arena_f_t, AF_ELE)
    arb = Arena(arena_b_t, AB_ELE)

    pb = [es.enter_context(nc.psum_tensor(f"pb{i}", [128, 512], F32)) for i in range(4)]
    pt = [es.enter_context(nc.psum_tensor(f"pt{i}", [128, 8, 128], BF16)) for i in range(2)]
    pm = es.enter_context(nc.psum_tensor("pm", [128, 512], F32))
    pm2 = es.enter_context(nc.psum_tensor("pm2", [128, 512], F32))

    for s in ["ld", "cst", "st", "mm", "ev", "vv", "pp", "cc"]:
        c.S(s)

    blk = es.enter_context(nc.Block())

    def barrier():
        t_ = {k: c.val(k) for k in ("mm", "ev", "vv", "st", "ld", "cst", "pp", "cc")}

        @blk.tensor
        def _(t):
            for k in ("ev", "vv", "pp", "st", "ld", "cst"):
                c.wait(t, k, t_[k])

        @blk.vector
        def _(v):
            for k in ("mm", "ev", "pp", "st", "ld", "cst"):
                c.wait(v, k, t_[k])

        @blk.scalar
        def _(sc):
            for k in ("mm", "vv", "pp", "st", "ld", "cst"):
                c.wait(sc, k, t_[k])

        @blk.gpsimd
        def _(g):
            for k in ("mm", "ev", "vv", "ld", "cc"):
                c.wait(g, k, t_[k])

        @blk.sync
        def _(sy):
            for k in ("mm", "ev", "vv", "pp", "st", "cst", "cc"):
                c.wait(sy, k, t_[k])

    xs = arf.take(4, D)
    scr_b = arf.take(2048)
    gst3 = scr_b[0:1, :]
    a_f = arf.take(2, TL)
    w1T_f = arf.take(8, DI)
    w2T_f = arf.take(2, D)
    w2n_f = arf.take(8, DI)
    b1_f = arf.take(DI)[0:1, :]
    b2_f = arf.take(D)[0:1, :]
    gst1 = arf.take(8, DI)
    gst2 = arf.take(2, D)
    gr1, gr2, gr3 = gst1, gst2, gst3
    assert arf.off <= AF_ELE, arf.off

    tkT_b = arb.take(8, D)
    tvT_b = arb.take(8, D)
    tqT_b = arb.take(8, D)
    w1T_b = arb.take(8, DI)
    w2T_b = arb.take(2, D)
    w2n_b = arb.take(8, DI)
    b1_b = arb.take(DI)[0:1, :]
    b2_b = arb.take(D)[0:1, :]
    xn_tm = arb.take(4, D)
    G_tm = xn_tm
    xnT = arb.take(8, TL)
    z_fm = arb.take(8, TL)
    z_tm = arb.take(4, D)
    y_fm = arb.take(8, TL)
    q_fm = arb.take(8, TL)
    _goff = arb.off
    G_fm = arb.take(8, TL)
    gr2b = arb.at(_goff, 2, 1024)
    h_fm = arb.take(2, TL)
    h2_fm = h_fm
    h_tm = arb.take(4, DI)
    gp_b = arb.take(2, TL)
    da_fm = arb.take(2, TL)
    da_tm = arb.take(4, DI)
    gst4 = arb.take(8, DI)
    G_fm_off = None
    print("attn_a arena use: f32", arf.off, "bf16", arb.off)
    assert arb.off <= AB_ELE, arb.off

    def phase_setup():
        @blk.gpsimd
        def _(g):
            g.memset(ident_b[:, :], 0.0)
            g.affine_select(out=ident_b[:, :], in_=ident_b[:, :], compare_op=Alu.not_equal,
                            fill=1.0, base=0, pattern=[[-1, 128]], channel_multiplier=1)
            g.memset(ones_b[:, :], 1.0)
            g.memset(consts[:, 0:1], MAGIC)
            g.memset(consts[:, 1:2], PI / 2)
            i = g.memset(consts[:, 2:3], 65536.0)
            c.inc(i, "pp", 1)
            for dst, src in ((tkT_b, tkT), (tvT_b, tvT), (tqT_b, tqT),
                             (w1T_b, w1Ti), (w2T_b, w2Ti), (w2n_b, w2ni)):
                c.inc(g.dma_start(out=dst[:, :, :], in_=src.rearrange("(k p) o -> p k o", p=128)),
                      "cst", 16)
            c.inc(g.dma_start(out=b1_b, in_=b1i), "cst", 16)
            c.inc(g.dma_start(out=b2_b, in_=b2i), "cst", 16)

        @blk.sync
        def _(s):
            c.inc(s.dma_start(out=w1T_f[:, :, :], in_=w1Ti.rearrange("(k p) o -> p k o", p=128)), "ld", 16)
            c.inc(s.dma_start(out=w2T_f[:, :, :], in_=w2Ti.rearrange("(k p) o -> p k o", p=128)), "ld", 16)
            c.inc(s.dma_start(out=w2n_f[:, :, :], in_=w2ni.rearrange("(k p) o -> p k o", p=128)), "ld", 16)
            c.inc(s.dma_start(out=b1_f, in_=b1i), "ld", 16)
            c.inc(s.dma_start(out=b2_f, in_=b2i), "ld", 16)
            c.inc(s.dma_start(out=xs[:, :, :],
                              in_=x_in[0:TL, :].rearrange("(t p) o -> p t o", p=128)), "ld", 16)

    def transpose_group(dst, srcs, par, pre_ev=None):
        @blk.tensor
        def _(t):
            if pre_ev is not None:
                c.wait(t, "ev", pre_ev)
            c.wait(t, "vv", c.val("vv"))
            i = None
            for k, s_ap in enumerate(srcs):
                i = t.transpose(pt[par][:, k, :], s_ap, ident_b[:, :])
            c.inc(i, "mm", 1)

        @blk.vector
        def _(v):
            c.wait(v, "mm")
            i = v.tensor_copy(dst, pt[par][:, 0:len(srcs), :])
            c.inc(i, "vv", 1)

    def phase_attn_a():
        ld_xs = c.val("ld")

        @blk.tensor
        def _(t):
            c.wait(t, "pp")
            c.wait(t, "cst")
            c.wait(t, "ld")

        for s in range(nsteps):
            for tc in range(4):
                @blk.vector
                def _(v, s=s, tc=tc, ld_xs=ld_xs):
                    if tc == 0:
                        c.wait(v, "ld", ld_xs)
                        if s == 0:
                            c.wait(v, "cst")
                    c.drain(v, v.tensor_reduce(colw[:, 0:1], xs[:, tc, :], AX, Alu.add))
                    c.drain(v, v.tensor_scalar(colw[:, 1:2], colw[:, 0:1], -1.0 / D, None, Alu.mult))
                    i = v.tensor_scalar(scr_b[:, 0:D], xs[:, tc, :], colw[:, 1:2], None, Alu.add)
                    c.inc(i, "vv", 1)

                @blk.scalar
                def _(sc, s=s, tc=tc):
                    if s == 0 and tc == 0:
                        c.wait(sc, "cst")
                    c.wait(sc, "vv")
                    c.inc(sc.activation(a_f[:, :, :].rearrange("p a b -> p (a b)"),
                                        scr_b[:, 0:D], AF.Square), "ev", 1)

                @blk.vector
                def _(v):
                    c.wait(v, "ev")
                    c.drain(v, v.tensor_reduce(colw[:, 2:3], a_f[:, :, :].rearrange("p a b -> p (a b)"), AX, Alu.add))
                    i = v.tensor_scalar(colw[:, 2:3], colw[:, 2:3], 1.0 / D, EPS, Alu.mult, Alu.add)
                    c.inc(i, "vv", 1)

                @blk.scalar
                def _(sc):
                    c.wait(sc, "vv")
                    c.inc(sc.activation(colw[:, 3:4], colw[:, 2:3], AF.Sqrt), "ev", 1)

                @blk.vector
                def _(v, tc=tc):
                    c.wait(v, "ev")
                    c.drain(v, v.reciprocal(colw[:, 3:4], colw[:, 3:4]))
                    i = v.tensor_scalar(xn_tm[:, tc, :], scr_b[:, 0:D], colw[:, 3:4], None, Alu.mult)
                    c.inc(i, "vv", 1)

                transpose_group(xnT[:, 0:8, tc * 128:(tc + 1) * 128],
                                [xn_tm[:, tc, kc * 128:(kc + 1) * 128] for kc in range(8)],
                                par=tc % 2)
            xnT_vv = c.val("vv")

            ev0 = c.val("ev")
            for pi_, (wT, dst) in enumerate(((tkT_b, z_fm), (tvT_b, y_fm), (tqT_b, q_fm))):
                for oc in range(8):
                    n = pi_ * 8 + oc

                    @blk.tensor
                    def _(t, wT=wT, oc=oc, n=n, ev0=ev0, xnT_vv=xnT_vv):
                        if n == 0:
                            c.wait(t, "vv", xnT_vv)
                        if n >= 4:
                            c.wait(t, "ev", ev0 + n - 3)
                        bank = pb[n % 4]
                        i = None
                        for kc in range(8):
                            i = t.matmul(bank[:, :], wT[:, kc, oc * 128:(oc + 1) * 128],
                                         xnT[:, kc, :], start=(kc == 0), stop=(kc == 7))
                        c.inc(i, "mm", 1)

                    @blk.scalar
                    def _(sc, dst=dst, oc=oc, n=n):
                        c.wait(sc, "mm")
                        c.inc(sc.activation(dst[:, oc, :], pb[n % 4][:, :], AF.Copy), "ev", 1)
            proj_ev = c.val("ev")

            for oc in range(8):
                transpose_group(z_tm[:, 0:4, oc * 128:(oc + 1) * 128],
                                [z_fm[:, oc, ts * 128:(ts + 1) * 128] for ts in range(4)],
                                par=oc % 2, pre_ev=(proj_ev - 16 if oc == 0 else None))

            @blk.vector
            def _(v, proj_ev=proj_ev):
                c.wait(v, "ev", proj_ev)
                i = v.tensor_tensor(y_fm[:, :, :], z_fm[:, :, :], y_fm[:, :, :], Alu.subtract)
                c.inc(i, "vv", 1)
            zy_vv = c.val("vv")

            for ic in range(2):
                @blk.tensor
                def _(t, ic=ic, proj_ev=proj_ev):
                    if ic == 0:
                        c.wait(t, "ev", proj_ev - 16)
                    bank = pb[ic]
                    for kc in range(8):
                        t.matmul(bank[:, :], w1T_b[:, kc, ic * 128:(ic + 1) * 128],
                                 z_fm[:, kc, :], start=(kc == 0), stop=False)
                    i = t.matmul(bank[:, :], b1_b[0:1, ic * 128:(ic + 1) * 128],
                                 ones_b[0:1, :], start=False, stop=True)
                    c.inc(i, "mm", 1)

                @blk.scalar
                def _(sc, ic=ic):
                    c.wait(sc, "mm")
                    c.inc(sc.activation(a_f[:, ic, :], pb[ic][:, :], AF.Copy), "ev", 1)

            @blk.scalar
            def _(sc):
                sc.activation(h_fm[:, :, :], a_f[:, :, :], AF.Gelu)
                c.inc(sc.activation(gp_b[:, :, :], a_f[:, :, :], AF.Derivative_Gelu), "ev", 1)
            h_ev = c.val("ev")

            for ic in range(2):
                transpose_group(h_tm[:, 0:4, ic * 128:(ic + 1) * 128],
                                [h_fm[:, ic, ts * 128:(ts + 1) * 128] for ts in range(4)],
                                par=ic, pre_ev=(h_ev if ic == 0 else None))

            vv_r0 = c.val("vv")
            for oc in range(8):
                @blk.tensor
                def _(t, oc=oc, h_ev=h_ev, vv_r0=vv_r0):
                    if oc == 0:
                        c.wait(t, "ev", h_ev)
                    if oc >= 4:
                        c.wait(t, "vv", vv_r0 + 2 * (oc - 4) + 2)
                    bank = pb[oc % 4]
                    for kc in range(2):
                        t.matmul(bank[:, :], w2T_b[:, kc, oc * 128:(oc + 1) * 128],
                                 h_fm[:, kc, :], start=(kc == 0), stop=False)
                    i = t.matmul(bank[:, :], b2_b[0:1, oc * 128:(oc + 1) * 128],
                                 ones_b[0:1, :], start=False, stop=True)
                    c.inc(i, "mm", 1)

                @blk.vector
                def _(v, oc=oc, zy_vv=zy_vv):
                    c.wait(v, "mm")
                    if oc == 0:
                        c.wait(v, "vv", zy_vv)
                    v.tensor_tensor(scr_b[:, 0:TL], pb[oc % 4][:, :], y_fm[:, oc, :], Alu.add)
                    i = v.tensor_scalar(G_fm[:, oc, :], scr_b[:, 0:TL], CG, None, Alu.mult)
                    c.inc(i, "vv", 2)

            for oc in range(8):
                transpose_group(G_tm[:, 0:4, oc * 128:(oc + 1) * 128],
                                [G_fm[:, oc, ts * 128:(ts + 1) * 128] for ts in range(4)],
                                par=oc % 2)
            G_vv = c.val("vv")

            for ic in range(2):
                @blk.tensor
                def _(t, ic=ic, G_vv=G_vv):
                    if ic == 0:
                        c.wait(t, "vv", G_vv)
                    bank = pb[ic]
                    i = None
                    for kc in range(8):
                        i = t.matmul(bank[:, :], w2n_b[:, kc, ic * 128:(ic + 1) * 128],
                                     G_fm[:, kc, :], start=(kc == 0), stop=(kc == 7))
                    c.inc(i, "mm", 1)

                @blk.vector
                def _(v, ic=ic):
                    c.wait(v, "mm")
                    i = v.tensor_tensor(da_fm[:, ic, :], pb[ic][:, :], gp_b[:, ic, :], Alu.mult)
                    c.inc(i, "vv", 1)

            for ic in range(2):
                transpose_group(da_tm[:, 0:4, ic * 128:(ic + 1) * 128],
                                [da_fm[:, ic, ts * 128:(ts + 1) * 128] for ts in range(4)],
                                par=ic)
            da_vv = c.val("vv")

            gev0 = c.val("ev")
            for os_ in range(8):
                @blk.tensor
                def _(t, os_=os_, da_vv=da_vv, gev0=gev0):
                    if os_ == 0:
                        c.wait(t, "vv", da_vv)
                    if os_ >= 4:
                        c.wait(t, "ev", gev0 + os_ - 3)
                    bank = pb[os_ % 4]
                    i = None
                    for ts in range(4):
                        i = t.matmul(bank[:, 0:DI], z_tm[:, ts, os_ * 128:(os_ + 1) * 128],
                                     da_tm[:, ts, :], start=(ts == 0), stop=(ts == 3))
                    c.inc(i, "mm", 1)

                @blk.scalar
                def _(sc, os_=os_):
                    c.wait(sc, "mm")
                    c.inc(sc.activation(gst1[:, os_, :], pb[os_ % 4][:, 0:DI], AF.Copy), "ev", 1)

            gev1 = c.val("ev")
            for u in range(4):
                is_, cc_ = u % 2, u // 2

                @blk.tensor
                def _(t, is_=is_, cc_=cc_, u=u, gev1=gev1):
                    c.wait(t, "ev", gev1 - 3 + u)
                    bank = pb[u % 4]
                    i = None
                    for ts in range(4):
                        i = t.matmul(bank[:, :], h_tm[:, ts, is_ * 128:(is_ + 1) * 128],
                                     G_tm[:, ts, cc_ * 512:(cc_ + 1) * 512],
                                     start=(ts == 0), stop=(ts == 3))
                    c.inc(i, "mm", 1)

                @blk.scalar
                def _(sc, is_=is_, cc_=cc_, u=u):
                    c.wait(sc, "mm")
                    c.inc(sc.activation(gst2[:, is_, cc_ * 512:(cc_ + 1) * 512],
                                        pb[u % 4][:, :], AF.Copy), "ev", 1)

            @blk.tensor
            def _(t):
                i = None
                for ts in range(4):
                    i = t.matmul(pm[0:1, 0:DI], ones_b[:, 0:1], da_tm[:, ts, :],
                                 start=(ts == 0), stop=(ts == 3))
                c.inc(i, "mm", 1)

            @blk.scalar
            def _(sc):
                c.wait(sc, "mm")
                c.inc(sc.activation(gst3[0:1, 0:DI], pm[0:1, 0:DI], AF.Copy), "ev", 1)

            for cc_ in range(2):
                @blk.tensor
                def _(t, cc_=cc_):
                    c.wait(t, "ev", c.val("ev"))
                    i = None
                    for ts in range(4):
                        i = t.matmul(pm2[0:1, :], ones_b[:, 0:1], G_tm[:, ts, cc_ * 512:(cc_ + 1) * 512],
                                     start=(ts == 0), stop=(ts == 3))
                    c.inc(i, "mm", 1)

                @blk.scalar
                def _(sc, cc_=cc_):
                    c.wait(sc, "mm")
                    c.inc(sc.activation(gst3[0:1, 1024 + cc_ * 512:1024 + (cc_ + 1) * 512],
                                        pm2[0:1, :], AF.Copy), "ev", 1)
            grads_ev = c.val("ev")

            @blk.gpsimd
            def _(g, grads_ev=grads_ev):
                c.wait(g, "ev", grads_ev)
                c.inc(g.dma_start(
                    out=gbuf_in[0:512, :].rearrange("(os ph) (pl i) -> (ph pl) os i", os=8, ph=64, pl=2),
                    in_=gst1[:, :, :]), "st", 16)
                c.inc(g.dma_start(
                    out=gbuf_in[512:1024, :].rearrange("(i p two) cc -> p i (two cc)", i=2, p=128, two=2),
                    in_=gst2[:, :, :]), "st", 16)
                c.inc(g.dma_start(out=gbuf_in[1024:1028, :], in_=gst3[0:1, :]), "st", 16)
                c.wait(g, "st")
                i = g.collective_compute("AllReduce", Alu.add, replica_groups=[list(range(NCORE))],
                                         ins=[gbuf_in.opt()], outs=[gbuf_out.opt()])
                c.inc(i, "cc", 1)

            @blk.sync
            def _(sy):
                c.wait(sy, "cc")
                c.inc(sy.dma_start(
                    out=gr1[:, :, :],
                    in_=gbuf_out[0:512, :].rearrange("(os ph) (pl i) -> (ph pl) os i", os=8, ph=64, pl=2)), "ld", 16)
                c.inc(sy.dma_start(
                    out=gr2[:, :, :],
                    in_=gbuf_out[512:1024, :].rearrange("(i p two) cc -> p i (two cc)", i=2, p=128, two=2)), "ld", 16)
                c.inc(sy.dma_start(out=gr3[0:1, :], in_=gbuf_out[1024:1028, :]), "ld", 16)
            gr_ld = c.val("ld")

            @blk.vector
            def _(v, gr_ld=gr_ld):
                c.wait(v, "ld", gr_ld)
                v.scalar_tensor_tensor(w1T_f[:, :, :], gr1[:, :, :], -LR, w1T_f[:, :, :], Alu.mult, Alu.add)
                v.scalar_tensor_tensor(w2T_f[:, :, :], gr2[:, :, :], -LR, w2T_f[:, :, :], Alu.mult, Alu.add)
                v.scalar_tensor_tensor(b1_f[0:1, :], gr3[0:1, 0:DI], -LR, b1_f[0:1, :], Alu.mult, Alu.add)
                i = v.scalar_tensor_tensor(b2_f[0:1, :], gr3[0:1, 1024:2048], -LR, b2_f[0:1, :], Alu.mult, Alu.add)
                c.inc(i, "vv", 1)

            @blk.scalar
            def _(sc, gr_ld=gr_ld):
                c.wait(sc, "ld", gr_ld)
                c.inc(sc.activation(gr2b[:, :, :], gr2[:, :, :], AF.Copy), "ev", 1)
            g2b_ev = c.val("ev")

            for is_ in range(2):
                @blk.tensor
                def _(t, is_=is_, g2b_ev=g2b_ev):
                    if is_ == 0:
                        c.wait(t, "ev", g2b_ev)
                    else:
                        c.wait(t, "vv", c.val("vv"))
                    i = None
                    for u in range(8):
                        i = t.transpose(pt[is_][:, u, :],
                                        gr2b[:, is_, u * 128:(u + 1) * 128], ident_b[:, :])
                    c.inc(i, "mm", 1)

                @blk.vector
                def _(v, is_=is_):
                    c.wait(v, "mm")
                    i = v.tensor_copy(gst4[:, 0:8, is_ * 128:(is_ + 1) * 128], pt[is_][:, :, :])
                    c.inc(i, "vv", 1)

            @blk.vector
            def _(v):
                i = v.scalar_tensor_tensor(w2n_f[:, :, :], gst4[:, :, :], -LR, w2n_f[:, :, :], Alu.mult, Alu.add)
                c.inc(i, "vv", 1)
            updv2 = c.val("vv")

            @blk.scalar
            def _(sc, updv2=updv2):
                c.wait(sc, "vv", updv2)
                sc.activation(w1T_b[:, :, :], w1T_f[:, :, :], AF.Copy)
                sc.activation(w2T_b[:, :, :], w2T_f[:, :, :], AF.Copy)
                sc.activation(b1_b[0:1, :], b1_f[0:1, :], AF.Copy)
                sc.activation(b2_b[0:1, :], b2_f[0:1, :], AF.Copy)
                c.inc(sc.activation(w2n_b[:, :, :], w2n_f[:, :, :], AF.Copy), "ev", 1)
            params_ev = c.val("ev")

            for ic in range(2):
                @blk.tensor
                def _(t, ic=ic, params_ev=params_ev):
                    if ic == 0:
                        c.wait(t, "ev", params_ev)
                    bank = pb[ic]
                    for kc in range(8):
                        t.matmul(bank[:, :], w1T_b[:, kc, ic * 128:(ic + 1) * 128],
                                 q_fm[:, kc, :], start=(kc == 0), stop=False)
                    i = t.matmul(bank[:, :], b1_b[0:1, ic * 128:(ic + 1) * 128],
                                 ones_b[0:1, :], start=False, stop=True)
                    c.inc(i, "mm", 1)

                @blk.scalar
                def _(sc, ic=ic):
                    c.wait(sc, "mm")
                    c.inc(sc.activation(h2_fm[:, ic, :], pb[ic][:, :], AF.Gelu), "ev", 1)
            h2_ev = c.val("ev")

            ovv0 = c.val("vv")
            for n in range(8):
                tc, occ = divmod(n, 2)

                @blk.tensor
                def _(t, tc=tc, occ=occ, n=n, h2_ev=h2_ev, ovv0=ovv0):
                    if n == 0:
                        c.wait(t, "ev", h2_ev)
                    if n >= 4:
                        c.wait(t, "vv", ovv0 + n - 3)
                    bank = pb[n % 4]
                    for ic in range(2):
                        t.matmul(bank[:, :], h2_fm[:, ic, tc * 128:(tc + 1) * 128],
                                 w2T_b[:, ic, occ * 512:(occ + 1) * 512],
                                 start=(ic == 0), stop=False)
                    t.matmul(bank[:, :], ones_b[0:1, 0:128],
                             b2_b[0:1, occ * 512:(occ + 1) * 512], start=False, stop=False)
                    i = None
                    for u in range(4):
                        i = t.matmul(bank[:, u * 128:(u + 1) * 128],
                                     q_fm[:, occ * 4 + u, tc * 128:(tc + 1) * 128],
                                     ident_b[:, :], start=False, stop=(u == 3))
                    c.inc(i, "mm", 1)

                @blk.vector
                def _(v, tc=tc, occ=occ, n=n):
                    c.wait(v, "mm")
                    i = v.tensor_tensor(xs[:, tc, occ * 512:(occ + 1) * 512],
                                        pb[n % 4][:, :],
                                        xs[:, tc, occ * 512:(occ + 1) * 512], Alu.add)
                    c.inc(i, "vv", 1)

            @blk.gpsimd
            def _(g, s=s):
                c.wait(g, "vv")
                c.inc(g.dma_start(
                    out=xres[s * TL:(s + 1) * TL, :].rearrange("(t p) o -> p t o", p=128),
                    in_=xs[:, :, :]), "st", 16)

            barrier()

            if s + 1 < nsteps:
                @blk.sync
                def _(sy, s=s):
                    c.inc(sy.dma_start(
                        out=xs[:, :, :],
                        in_=x_in[(s + 1) * TL:(s + 2) * TL, :].rearrange("(t p) o -> p t o", p=128)),
                        "ld", 16)
                ld_xs = c.val("ld")

    def phase_dequant():
        arf.reset()
        arb.reset()
        wchunk = arf.take(2, FF)
        sblk = arf.take(96)
        isblk = arf.take(96)
        sfull = arf.take(FF)
        isfull = arf.take(FF)
        tq_ = arf.take(FF)
        assert arf.off <= AF_ELE, arf.off
        dqo = arb.take(2, FF)
        dqst = arb.take(2, 8, 128)

        jobs = ([("g", oc, wgate[oc * 128:(oc + 1) * 128, :], D) for oc in range(24)]
                + [("u", oc, wup[oc * 128:(oc + 1) * 128, :], D) for oc in range(24)]
                + [("d", oc, wdown[oc * 128:(oc + 1) * 128, :], FF) for oc in range(8)])
        dq_vv = {}
        st_done = {}

        for j, (kind, oc, src_, w) in enumerate(jobs):
            nblk = w // 32
            nct = w // 128

            @blk.sync
            def _(sy, j=j, src_=src_, w=w):
                if j >= 2:
                    c.wait(sy, "vv", dq_vv[j - 2])
                c.inc(sy.dma_start(out=wchunk[:, j % 2, 0:w], in_=src_), "ld", 16)
            ldt = c.val("ld")

            @blk.vector
            def _(v, j=j, w=w, nblk=nblk, ldt=ldt):
                c.wait(v, "ld", ldt)
                c.drain(v, v.tensor_reduce(
                    sblk[:, 0:nblk].rearrange("p (b o) -> p b o", o=1),
                    wchunk[:, j % 2, 0:w].rearrange("p (b q) -> p b q", q=32),
                    AX, Alu.max, apply_absolute_value=True))
                c.drain(v, v.tensor_scalar(sblk[:, 0:nblk], sblk[:, 0:nblk],
                                           1.0 / 7.0, 1e-12, Alu.mult, Alu.max))
                c.drain(v, v.reciprocal(isblk[:, 0:nblk], sblk[:, 0:nblk]))
                v.tensor_copy(sfull[:, 0:w].rearrange("p (b q) -> p b q", q=32),
                              sblk[:, 0:nblk].rearrange("p (b o) -> p b o", o=1).broadcast_to([128, nblk, 32]))
                c.drain(v, v.tensor_copy(isfull[:, 0:w].rearrange("p (b q) -> p b q", q=32),
                                         isblk[:, 0:nblk].rearrange("p (b o) -> p b o", o=1).broadcast_to([128, nblk, 32])))
                v.tensor_tensor(tq_[:, 0:w], wchunk[:, j % 2, 0:w], isfull[:, 0:w], Alu.mult)
                v.tensor_scalar(tq_[:, 0:w], tq_[:, 0:w], MAGIC, MAGIC, Alu.add, Alu.subtract)
                i = v.tensor_tensor(dqo[:, j % 2, 0:w], tq_[:, 0:w], sfull[:, 0:w], Alu.mult)
                dq_vv[j] = c.inc(i, "vv", 1)

            for grp in range((nct + 7) // 8):
                n = min(8, nct - grp * 8)

                @blk.tensor
                def _(t, j=j, grp=grp, n=n):
                    c.wait(t, "vv", c.val("vv"))
                    if j >= 2 and grp == 0:
                        c.wait(t, "st", st_done[j - 2])
                    i = None
                    for u in range(n):
                        i = t.transpose(pt[j % 2][:, u, :],
                                        dqo[:, j % 2, (grp * 8 + u) * 128:(grp * 8 + u + 1) * 128],
                                        ident_b[:, :])
                    c.inc(i, "mm", 1)

                @blk.vector
                def _(v, j=j, grp=grp, n=n):
                    c.wait(v, "mm")
                    if grp >= 1:
                        c.wait(v, "st", st_done[j])
                    i = v.tensor_copy(dqst[:, j % 2, 0:n, :], pt[j % 2][:, 0:n, :])
                    c.inc(i, "vv", 1)

                @blk.gpsimd
                def _(g, j=j, kind=kind, oc=oc, grp=grp, n=n):
                    c.wait(g, "vv")
                    if kind == "g":
                        dst = wgT_dq[oc].rearrange("k p f -> p k f")
                    elif kind == "u":
                        dst = wuT_dq[oc].rearrange("k p f -> p k f")
                    else:
                        ic0 = grp * 8
                        dst = wdT_dq[ic0:ic0 + n, :, oc * 128:(oc + 1) * 128].rearrange("i p f -> p i f")
                    st_done[j] = c.inc(g.dma_start(out=dst, in_=dqst[:, j % 2, 0:n, :]), "st", 16)

        barrier()

    def phase_ffn(src, dst, mxb):
        arf.reset()
        arb.reset()
        p3_vv, p4_vv, p5_vv, p5_st, p5_mm, p1_mm = [], [], [], [], [], []
        xtile = arf.take(2, D)
        xc_s = arf.take(D)
        sq_s = arf.take(D)
        t3a = arf.take(4096)
        t1q = t3a
        t3b = arf.take(4096)
        ost = arf.take(2, 2, D)
        xr4 = arf.at(0, 2, 2, D)
        assert arf.off <= AF_ELE, arf.off
        xnT_f = arb.take(8, 4096)
        wslab = arb.take(2, 8, 128)
        evslab = arb.take(2, 4096)
        xnstg = arb.take(D)
        mcastb = arb.take(128)[:, 0:1]
        assert arb.off <= AB_ELE, arb.off
        ldg = arb.at(0, 2, 4096)
        ldu = arb.at(8192, 2, 4096)
        g2sl = arb.at(16384, 2, 4096)
        t4sl = arb.at(24576, 2, 4096)
        wdT_all = arb.at(0, 24, D)
        oqin = arb.at(24576, 2, 24, 256)
        oqb = arb.at(36864, 24, 256)

        @blk.vector
        def _(v):
            v.memset(colw[:, 24:27], 0.0)
            i = v.memset(colw[:, 27:28], 1.0)
            c.inc(i, "vv", 1)

        p1_vv = []
        for tc in range(32):
            @blk.sync
            def _(sy, tc=tc):
                if tc >= 2:
                    c.wait(sy, "vv", p1_vv[tc - 2])
                c.inc(sy.dma_start(out=xtile[:, tc % 2, :],
                                   in_=src[tc * 128:(tc + 1) * 128, :]), "ld", 16)
            ldt = c.val("ld")

            @blk.vector
            def _(v, tc=tc, ldt=ldt):
                c.wait(v, "ld", ldt)
                c.drain(v, v.tensor_reduce(colw[:, 0:1], xtile[:, tc % 2, :], AX, Alu.add))
                c.drain(v, v.tensor_scalar(colw[:, 1:2], colw[:, 0:1], -1.0 / D, None, Alu.mult))
                i = v.tensor_scalar(xc_s[:, :], xtile[:, tc % 2, :], colw[:, 1:2], None, Alu.add)
                c.inc(i, "vv", 1)

            @blk.scalar
            def _(sc):
                c.wait(sc, "vv")
                c.inc(sc.activation(sq_s[:, :], xc_s[:, :], AF.Square), "ev", 1)

            @blk.vector
            def _(v):
                c.wait(v, "ev")
                c.drain(v, v.tensor_reduce(colw[:, 2:3], sq_s[:, :], AX, Alu.add))
                i = v.tensor_scalar(colw[:, 2:3], colw[:, 2:3], 1.0 / D, EPS, Alu.mult, Alu.add)
                c.inc(i, "vv", 1)

            @blk.scalar
            def _(sc):
                c.wait(sc, "vv")
                c.inc(sc.activation(colw[:, 3:4], colw[:, 2:3], AF.Sqrt), "ev", 1)

            @blk.vector
            def _(v, tc=tc):
                c.wait(v, "ev")
                if tc >= 1:
                    c.wait(v, "mm", p1_mm[tc - 1])
                c.drain(v, v.reciprocal(colw[:, 3:4], colw[:, 3:4]))
                c.drain(v, v.tensor_scalar(xnstg[:, :], xc_s[:, :], colw[:, 3:4], None, Alu.mult))
                c.drain(v, v.reduce_max(colw[:, 8:9], xnstg[:, :], AX, apply_absolute_value=True))
                i = v.tensor_tensor(colw[:, 24:25], colw[:, 24:25], colw[:, 8:9], Alu.max)
                p1_vv.append(c.inc(i, "vv", 1))

            @blk.tensor
            def _(t, tc=tc):
                c.wait(t, "vv", c.val("vv"))
                i = None
                for kc in range(8):
                    i = t.transpose(pt[tc % 2][:, kc, :], xnstg[:, kc * 128:(kc + 1) * 128], ident_b[:, :])
                p1_mm.append(c.inc(i, "mm", 1))

            @blk.vector
            def _(v, tc=tc):
                c.wait(v, "mm")
                i = v.tensor_copy(xnT_f[:, 0:8, tc * 128:(tc + 1) * 128], pt[tc % 2][:, :, :])
                c.inc(i, "vv", 1)

        def gmax(acc_col, k, out_col):
            @blk.vector
            def _(v):
                c.wait(v, "vv", c.val("vv"))
                i = v.tensor_copy(mcastb, colw[:, acc_col:acc_col + 1])
                c.inc(i, "vv", 1)

            @blk.tensor
            def _(t):
                c.wait(t, "vv", c.val("vv"))
                c.wait(t, "ev", c.val("ev"))
                i = t.matmul(pm[0:1, 0:128], mcastb, ident_b[:, :], start=True, stop=True)
                c.inc(i, "mm", 1)

            @blk.vector
            def _(v):
                c.wait(v, "mm")
                i = v.reduce_max(colw[0:1, 8:9], pm[0:1, 0:128], AX)
                c.inc(i, "vv", 1)

            @blk.gpsimd
            def _(g, k=k):
                c.wait(g, "vv")
                c.inc(g.dma_start(out=mxi[k], in_=colw[0:1, 8:9]), "st", 16)
                c.wait(g, "st")
                i = g.collective_compute("AllReduce", Alu.max, replica_groups=[list(range(NCORE))],
                                         ins=[mxi[k].opt()], outs=[mxo[k].opt()])
                c.inc(i, "cc", 1)

            @blk.sync
            def _(sy, k=k, out_col=out_col):
                c.wait(sy, "cc")
                c.inc(sy.dma_start(out=colw[:, out_col:out_col + 1],
                                   in_=mxo[k][0:1, 0:1].broadcast_to([128, 1])), "ld", 16)
            return c.val("ld")

        def scale_ops(ldt, s_col, inv_col=None, extra=None):
            @blk.vector
            def _(v):
                c.wait(v, "ld", ldt)
                c.drain(v, v.tensor_scalar(colw[:, s_col:s_col + 1], colw[:, s_col:s_col + 1],
                                           1.0 / 32767.0, 1e-12, Alu.mult, Alu.max))
                if inv_col is not None:
                    c.drain(v, v.reciprocal(colw[:, inv_col:inv_col + 1], colw[:, s_col:s_col + 1]))
                if extra is not None:
                    extra(v)
                i = v.engine_nop()
                c.inc(i, "pp", 1)
            return c.val("pp")

        pp_s0 = scale_ops(gmax(24, mxb + 0, 10), 10, 11)
        barrier()

        for kc in range(8):
            @blk.scalar
            def _(sc, kc=kc):
                if kc > 0:
                    c.wait(sc, "vv", c.val("vv"))
                c.inc(sc.activation(t1q[:, :], xnT_f[:, kc, :], AF.Identity,
                                    bias=consts[:, 0:1], scale=colw[:, 11:12]), "ev", 1)

            @blk.vector
            def _(v, kc=kc):
                c.wait(v, "ev")
                i = v.tensor_scalar(xnT_f[:, kc, :], t1q[:, :], MAGIC, colw[:, 10:11],
                                    Alu.subtract, Alu.mult)
                c.inc(i, "vv", 1)
        barrier()

        @blk.vector
        def _(v):
            i = v.memset(colw[:, 24:26], 0.0)
            c.inc(i, "vv", 1)
        barrier()

        for widx, (cache, spill, acc_col) in enumerate(((wgT_dq, gate_s, 24), (wuT_dq, up_s, 25))):
            mm_sp = {}
            ev_b0 = c.val("ev")
            st_b0 = c.val("st")
            for i_ in range(24):
                @blk.sync
                def _(sy, i_=i_, cache=cache):
                    if i_ >= 2:
                        c.wait(sy, "mm", mm_sp[i_ - 2])
                    c.inc(sy.dma_start(out=wslab[:, i_ % 2, :, :],
                                       in_=cache[i_].rearrange("k p f -> p k f")), "ld", 16)
                ldt = c.val("ld")

                @blk.tensor
                def _(t, i_=i_, ldt=ldt, ev_b0=ev_b0):
                    c.wait(t, "ld", ldt)
                    i2 = None
                    for tch in range(8):
                        n = i_ * 8 + tch
                        if n >= 4:
                            c.wait(t, "ev", ev_b0 + n - 3)
                        bank = pb[n % 4]
                        for kc in range(8):
                            i2 = t.matmul(bank[:, :], wslab[:, i_ % 2, kc, :],
                                          xnT_f[:, kc, tch * 512:(tch + 1) * 512],
                                          start=(kc == 0), stop=(kc == 7))
                        c.inc(i2, "mm", 1)
                mm_sp[i_] = c.val("mm")

                @blk.scalar
                def _(sc, i_=i_, st_b0=st_b0):
                    for tch in range(8):
                        c.wait(sc, "mm", mm_sp[i_] - 7 + tch)
                        if i_ >= 2 and tch == 0:
                            c.wait(sc, "st", st_b0 + 16 * (i_ - 1))
                        n = i_ * 8 + tch
                        c.inc(sc.activation(evslab[:, i_ % 2, tch * 512:(tch + 1) * 512],
                                            pb[n % 4][:, :], AF.Copy), "ev", 1)

                @blk.vector
                def _(v, i_=i_, acc_col=acc_col):
                    for tch in range(8):
                        c.wait(v, "ev")
                        c.drain(v, v.reduce_max(colw[:, 8:9],
                                                evslab[:, i_ % 2, tch * 512:(tch + 1) * 512],
                                                AX, apply_absolute_value=True))
                        i2 = v.tensor_tensor(colw[:, acc_col:acc_col + 1],
                                             colw[:, acc_col:acc_col + 1], colw[:, 8:9], Alu.max)
                        c.inc(i2, "vv", 1)

                @blk.gpsimd
                def _(g, i_=i_, spill=spill):
                    c.wait(g, "ev", c.val("ev"))
                    c.inc(g.dma_start(out=spill[i_], in_=evslab[:, i_ % 2, :]), "st", 16)
            barrier()
            if widx == 0:
                def ex2(v):
                    v.tensor_scalar(colw[:, 20:21], colw[:, 14:15], -MAGIC, None, Alu.mult)
                    v.tensor_scalar(colw[:, 21:22], colw[:, 14:15], 1.0 / 65536.0, None, Alu.mult)
                scale_ops(gmax(24, mxb + 1, 14), 14, 15, extra=ex2)
            else:
                def ex1(g):
                    pass
                scale_ops(gmax(25, mxb + 2, 12), 12, 13)
        barrier()

        @blk.vector
        def _(v):
            i = v.memset(colw[:, 24:25], 0.0)
            c.inc(i, "vv", 1)
        for i_ in range(24):
            @blk.sync
            def _(sy, i_=i_):
                if i_ >= 2:
                    c.wait(sy, "vv", p3_vv[i_ - 2])
                c.inc(sy.dma_start(out=ldg[:, i_ % 2, :], in_=gate_s[i_]), "ld", 16)
            ldt = c.val("ld")

            @blk.scalar
            def _(sc, i_=i_, ldt=ldt):
                c.wait(sc, "ld", ldt)
                if i_ >= 1:
                    c.wait(sc, "vv", c.val("vv"))
                sc.activation(t3a[:, :], ldg[:, i_ % 2, :], AF.Identity,
                              bias=consts[:, 0:1], scale=colw[:, 15:16])
                sc.activation(t3b[:, :], t3a[:, :], AF.Sigmoid,
                              bias=colw[:, 20:21], scale=colw[:, 14:15])
                i2 = sc.activation(t3b[:, :], t3b[:, :], AF.Identity,
                                   bias=consts[:, 0:1], scale=consts[:, 2:3])
                c.inc(i2, "ev", 1)

            @blk.vector
            def _(v, i_=i_):
                c.wait(v, "ev")
                v.tensor_scalar(t3b[:, :], t3b[:, :], MAGIC, 65535.0, Alu.subtract, Alu.min)
                v.tensor_scalar(t3b[:, :], t3b[:, :], colw[:, 21:22], None, Alu.mult)
                v.scalar_tensor_tensor(g2sl[:, i_ % 2, :], t3a[:, :], MAGIC, t3b[:, :],
                                       Alu.subtract, Alu.mult)
                c.drain(v, v.reduce_max(colw[:, 8:9], g2sl[:, i_ % 2, :], AX,
                                        apply_absolute_value=True))
                i2 = v.tensor_tensor(colw[:, 24:25], colw[:, 24:25], colw[:, 8:9], Alu.max)
                p3_vv.append(c.inc(i2, "vv", 1))

            @blk.gpsimd
            def _(g, i_=i_):
                c.wait(g, "vv")
                c.inc(g.dma_start(out=g2_s[i_], in_=g2sl[:, i_ % 2, :]), "st", 16)
        barrier()

        def ex3(v):
            v.tensor_tensor(colw[:, 17:18], colw[:, 12:13], colw[:, 16:17], Alu.mult)
        scale_ops(gmax(24, mxb + 3, 16), 16, None, extra=ex3)
        @blk.vector
        def _(v):
            c.wait(v, "pp", c.val("pp"))
            c.drain(v, v.reciprocal(colw[:, 22:23], colw[:, 16:17]))
            i = v.engine_nop()
            c.inc(i, "pp", 1)
        barrier()

        @blk.vector
        def _(v):
            i = v.memset(colw[:, 24:25], 0.0)
            c.inc(i, "vv", 1)
        for i_ in range(24):
            @blk.sync
            def _(sy, i_=i_):
                if i_ >= 2:
                    c.wait(sy, "vv", p4_vv[i_ - 2])
                c.inc(sy.dma_start(out=ldg[:, i_ % 2, :], in_=g2_s[i_]), "ld", 16)
                c.inc(sy.dma_start(out=ldu[:, i_ % 2, :], in_=up_s[i_]), "ld", 16)
            ldt = c.val("ld")

            @blk.scalar
            def _(sc, i_=i_, ldt=ldt):
                c.wait(sc, "ld", ldt)
                if i_ >= 1:
                    c.wait(sc, "vv", c.val("vv"))
                sc.activation(t3a[:, :], ldu[:, i_ % 2, :], AF.Identity,
                              bias=consts[:, 0:1], scale=colw[:, 13:14])
                i2 = sc.activation(t3b[:, :], ldg[:, i_ % 2, :], AF.Identity,
                                   bias=consts[:, 0:1], scale=colw[:, 22:23])
                c.inc(i2, "ev", 1)

            @blk.vector
            def _(v, i_=i_):
                c.wait(v, "ev")
                v.tensor_scalar(t3a[:, :], t3a[:, :], MAGIC, colw[:, 17:18],
                                Alu.subtract, Alu.mult)
                v.scalar_tensor_tensor(t4sl[:, i_ % 2, :], t3b[:, :], MAGIC, t3a[:, :],
                                       Alu.subtract, Alu.mult)
                c.drain(v, v.reduce_max(colw[:, 8:9], t4sl[:, i_ % 2, :], AX,
                                        apply_absolute_value=True))
                i2 = v.tensor_tensor(colw[:, 24:25], colw[:, 24:25], colw[:, 8:9], Alu.max)
                p4_vv.append(c.inc(i2, "vv", 1))

            @blk.gpsimd
            def _(g, i_=i_):
                c.wait(g, "vv")
                c.inc(g.dma_start(out=t4_s[i_], in_=t4sl[:, i_ % 2, :]), "st", 16)
        barrier()
        scale_ops(gmax(24, mxb + 4, 18), 18, 19)
        barrier()

        @blk.sync
        def _(sy):
            c.inc(sy.dma_start(out=wdT_all[:, :, :],
                               in_=wdT_dq.rearrange("i p o -> p i o")), "ld", 16)
        wd_ld = c.val("ld")

        for tb in range(16):
            @blk.sync
            def _(sy, tb=tb):
                if tb >= 2:
                    c.wait(sy, "vv", p5_vv[tb - 2])
                c.inc(sy.dma_start(out=oqin[:, tb % 2, :, :],
                                   in_=t4_s[:, :, tb * 256:(tb + 1) * 256].rearrange("i p t -> p i t")), "ld", 16)
                c.inc(sy.dma_start(out=xr4[:, tb % 2, :, :],
                                   in_=src[tb * 256:(tb + 1) * 256, :].rearrange("(t p) o -> p t o", p=128)), "ld", 16)
            ldt = c.val("ld")

            @blk.scalar
            def _(sc, tb=tb, ldt=ldt):
                c.wait(sc, "ld", ldt)
                if tb >= 1:
                    c.wait(sc, "vv", c.val("vv"))
                i2 = sc.activation(t3a[:, 0:4096].rearrange("p (a b) -> p a b", a=16)
                                   if False else t3a[:, :], oqin[:, tb % 2, :, :].rearrange("p a b -> p (a b)")[:, 0:4096],
                                   AF.Identity, bias=consts[:, 0:1], scale=colw[:, 19:20])
                i3 = sc.activation(t3b[:, 0:2048], oqin[:, tb % 2, :, :].rearrange("p a b -> p (a b)")[:, 4096:6144],
                                   AF.Identity, bias=consts[:, 0:1], scale=colw[:, 19:20])
                c.inc(i3, "ev", 1)

            @blk.vector
            def _(v, tb=tb):
                c.wait(v, "ev")
                if tb >= 1:
                    c.wait(v, "mm", p5_mm[tb - 1])
                if tb >= 2:
                    c.wait(v, "st", p5_st[tb - 2])
                v.tensor_scalar(oqb[:, :, :].rearrange("p a b -> p (a b)")[:, 0:4096],
                                t3a[:, :], MAGIC, None, Alu.subtract)
                i2 = v.tensor_scalar(oqb[:, :, :].rearrange("p a b -> p (a b)")[:, 4096:6144],
                                     t3b[:, 0:2048], MAGIC, None, Alu.subtract)
                c.inc(i2, "vv", 1)
            oq_vv = c.val("vv")

            for u in range(4):
                ts2, occ = divmod(u, 2)

                @blk.tensor
                def _(t, tb=tb, ts2=ts2, occ=occ, u=u, oq_vv=oq_vv, wd_ld=wd_ld):
                    if tb == 0 and u == 0:
                        c.wait(t, "ld", wd_ld)
                    c.wait(t, "vv", oq_vv)
                    if tb >= 1:
                        c.wait(t, "vv", c.val("vv"))
                    bank = pb[u]
                    i2 = None
                    for ic in range(24):
                        i2 = t.matmul(bank[:, :], oqb[:, ic, ts2 * 128:(ts2 + 1) * 128],
                                      wdT_all[:, ic, occ * 512:(occ + 1) * 512],
                                      start=(ic == 0), stop=(ic == 23))
                    mv = c.inc(i2, "mm", 1)
                    if u == 3:
                        p5_mm.append(mv)

                @blk.vector
                def _(v, tb=tb, ts2=ts2, occ=occ, u=u):
                    c.wait(v, "mm")
                    i2 = v.scalar_tensor_tensor(ost[:, tb % 2, ts2, occ * 512:(occ + 1) * 512],
                                                pb[u][:, :], colw[:, 18:19],
                                                xr4[:, tb % 2, ts2, occ * 512:(occ + 1) * 512],
                                                Alu.mult, Alu.add)
                    if u == 3:
                        p5_vv.append(c.inc(i2, "vv", 1))
                    else:
                        c.inc(i2, "vv", 1)

            @blk.gpsimd
            def _(g, tb=tb):
                c.wait(g, "vv")
                p5_st.append(c.inc(g.dma_start(
                    out=dst[tb * 256:(tb + 1) * 256, :].rearrange("(t p) o -> p t o", p=128),
                    in_=ost[:, tb % 2, :, :]), "st", 16))
        barrier()


    def phase_attn_b():
        TWO_PI = 2.0 * PI

        def emit_ln2(xsrc_tile, xnstg_ap, xc_s2, sq_s2, tc_guard_mm=None):
            @blk.vector
            def _(v):
                c.drain(v, v.tensor_reduce(colw[:, 0:1], xsrc_tile, AX, Alu.add))
                c.drain(v, v.tensor_scalar(colw[:, 1:2], colw[:, 0:1], -1.0 / D, None, Alu.mult))
                i = v.tensor_scalar(xc_s2[:, :], xsrc_tile, colw[:, 1:2], None, Alu.add)
                c.inc(i, "vv", 1)

            @blk.scalar
            def _(sc):
                c.wait(sc, "vv")
                c.inc(sc.activation(sq_s2[:, :], xc_s2[:, :], AF.Square), "ev", 1)

            @blk.vector
            def _(v):
                c.wait(v, "ev")
                c.drain(v, v.tensor_reduce(colw[:, 2:3], sq_s2[:, :], AX, Alu.add))
                i = v.tensor_scalar(colw[:, 2:3], colw[:, 2:3], 1.0 / D, EPS, Alu.mult, Alu.add)
                c.inc(i, "vv", 1)

            @blk.scalar
            def _(sc):
                c.wait(sc, "vv")
                c.inc(sc.activation(colw[:, 3:4], colw[:, 2:3], AF.Sqrt), "ev", 1)

            @blk.vector
            def _(v, tc_guard_mm=tc_guard_mm):
                c.wait(v, "ev")
                if tc_guard_mm is not None:
                    c.wait(v, "mm", tc_guard_mm)
                c.drain(v, v.reciprocal(colw[:, 3:4], colw[:, 3:4]))
                i = v.tensor_scalar(xnstg_ap, xc_s2[:, :], colw[:, 3:4], None, Alu.mult)
                c.inc(i, "vv", 1)

        def build_rfp(rfp, tag):
            @blk.gpsimd
            def _(g):
                i = g.memset(rfp[:, :], 0.0)
                c.inc(i, "pp", 1)
                g.wait_ge(c.sem["pp"], c.cnt["pp"])
                c.inc(g.dma_start(out=rfp[0:64, 0:256], in_=rfTi), "cst", 16)
                c.inc(g.dma_start(out=rfp[64:128, 256:512], in_=rfTi), "cst", 16)

        def emit_trig_pair(src_f32, cos_dst, sin_dst, u1, u2, pre_ev, pre_mm):
            @blk.vector
            def _(v):
                c.wait(v, "ev", pre_ev)
                v.tensor_scalar(u1[:, :], src_f32, 1.0 / TWO_PI, MAGIC, Alu.mult, Alu.add)
                v.tensor_scalar(u1[:, :], u1[:, :], MAGIC, TWO_PI, Alu.subtract, Alu.mult)
                v.scalar_tensor_tensor(u1[:, :], u1[:, :], -1.0, src_f32, Alu.mult, Alu.add)
                v.tensor_scalar(u2[:, :], src_f32, PI / 2, 1.0 / TWO_PI, Alu.add, Alu.mult)
                v.tensor_scalar(u2[:, :], u2[:, :], MAGIC, MAGIC, Alu.add, Alu.subtract)
                v.tensor_scalar(u2[:, :], u2[:, :], TWO_PI, None, Alu.mult)
                i = v.scalar_tensor_tensor(u2[:, :], u2[:, :], -1.0, src_f32, Alu.mult, Alu.add)
                c.inc(i, "vv", 1)

            @blk.scalar
            def _(sc):
                c.wait(sc, "vv")
                c.wait(sc, "mm", pre_mm)
                sc.activation(sin_dst, u1[:, :], AF.Sin)
                i = sc.activation(cos_dst, u2[:, :], AF.Sin, bias=consts[:, 1:2])
                c.inc(i, "ev", 1)

        arf.reset()
        arb.reset()
        x2til = arf.take(4, D)
        xc_s2 = arf.take(D)
        sq_s2 = arf.take(D)
        sqs = arf.take(2, 512)
        nsqb = arf.take(64)
        u1_s = arf.take(2048)
        u2_s = arf.take(2048)
        kvstg = arf.take(16, 264)
        assert arf.off <= AF_ELE, arf.off
        wkTb = arb.take(8, D)
        wvTb = arb.take(8, D)
        xnT_ab = arb.take(8, TL)
        K_fm = arb.take(8, TL)
        V_tm = arb.take(4, D)
        xnstg2 = arb.take(D)
        projp = arb.take(8, 4, 512)
        Kfcsp = arb.take(2, 4, 1024)
        Vsbp = arb.take(2, 4, 2, 66)
        rfp = arb.take(512)
        assert arb.off <= AB_ELE, arb.off

        @blk.gpsimd
        def _(g):
            c.inc(g.dma_start(out=wkTb[:, :, :], in_=wkT.rearrange("(k p) o -> p k o", p=128)), "cst", 16)
            c.inc(g.dma_start(out=wvTb[:, :, :], in_=wvT.rearrange("(k p) o -> p k o", p=128)), "cst", 16)
        build_rfp(rfp, "s1")
        barrier()

        ck = {}
        for b in range(NB):
            @blk.sync
            def _(sy, b=b):
                c.inc(sy.dma_start(out=x2til[:, :, :],
                                   in_=xres[b * TL:(b + 1) * TL, :].rearrange("(t p) o -> p t o", p=128)),
                      "ld", 16)
            ldt = c.val("ld")

            @blk.vector
            def _(v, ldt=ldt):
                c.wait(v, "ld", ldt)

            for tc in range(4):
                tgm = None if (b == 0 and tc == 0) else c.val("mm")
                emit_ln2(x2til[:, tc, :], xnstg2[:, :], xc_s2, sq_s2, tc_guard_mm=tgm)

                @blk.tensor
                def _(t, tc=tc):
                    c.wait(t, "vv", c.val("vv"))
                    i = None
                    for kc in range(8):
                        i = t.transpose(pt[tc % 2][:, kc, :], xnstg2[:, kc * 128:(kc + 1) * 128], ident_b[:, :])
                    c.inc(i, "mm", 1)

                @blk.vector
                def _(v, tc=tc):
                    c.wait(v, "mm")
                    i = v.tensor_copy(xnT_ab[:, 0:8, tc * 128:(tc + 1) * 128], pt[tc % 2][:, :, :])
                    c.inc(i, "vv", 1)
            xn_vv = c.val("vv")

            @blk.gpsimd
            def _(g, b=b, xn_vv=xn_vv):
                c.wait(g, "vv", xn_vv)
                c.inc(g.dma_start(out=xnT_sp[b], in_=xnT_ab[:, :, :].rearrange("p a b -> p (a b)")), "st", 16)

            ev_k0 = c.val("ev")
            for oc in range(8):
                @blk.tensor
                def _(t, oc=oc, ev_k0=ev_k0):
                    if oc >= 4:
                        c.wait(t, "ev", ev_k0 + oc - 3)
                    bank = pb[oc % 4]
                    i = None
                    for kc in range(8):
                        i = t.matmul(bank[:, :], wkTb[:, kc, oc * 128:(oc + 1) * 128],
                                     xnT_ab[:, kc, :], start=(kc == 0), stop=(kc == 7))
                    c.inc(i, "mm", 1)

                @blk.scalar
                def _(sc, oc=oc):
                    c.wait(sc, "mm")
                    c.inc(sc.activation(K_fm[:, oc, :], pb[oc % 4][:, :], AF.Copy), "ev", 1)

            ev_v0 = c.val("ev")
            for u in range(8):
                ts_, occ = divmod(u, 2)

                @blk.tensor
                def _(t, ts_=ts_, occ=occ, u=u, ev_v0=ev_v0):
                    if u >= 4:
                        c.wait(t, "ev", ev_v0 + u - 3)
                    bank = pb[u % 4]
                    i = None
                    for kc in range(8):
                        i = t.matmul(bank[:, :], xnT_ab[:, kc, ts_ * 128:(ts_ + 1) * 128],
                                     wvTb[:, kc, occ * 512:(occ + 1) * 512],
                                     start=(kc == 0), stop=(kc == 7))
                    c.inc(i, "mm", 1)

                @blk.scalar
                def _(sc, ts_=ts_, occ=occ):
                    c.wait(sc, "mm")
                    c.inc(sc.activation(V_tm[:, ts_, occ * 512:(occ + 1) * 512],
                                        pb[u % 4][:, :], AF.Copy), "ev", 1)
            prep_ev = c.val("ev")

            for pr in range(8):
                @blk.tensor
                def _(t, pr=pr):
                    if pr == 0:
                        c.wait(t, "ev", prep_ev)
                        c.wait(t, "vv", c.val("vv"))
                    else:
                        c.wait(t, "ev", ck[("prev", b * 8 + pr - 1)])
                    i = None
                    for ts_ in range(4):
                        t.matmul(pb[ts_][:, 0:512], K_fm[:, pr, ts_ * 128:(ts_ + 1) * 128],
                                 rfp[:, :], start=True, stop=True)
                        i = t.matmul(pm[:, ts_ * 128:(ts_ + 1) * 128],
                                     K_fm[:, pr, ts_ * 128:(ts_ + 1) * 128],
                                     ident_b[:, :], start=True, stop=True)
                    ck[("prmm", b * 8 + pr)] = c.inc(i, "mm", 1)

                @blk.scalar
                def _(sc, pr=pr):
                    c.wait(sc, "mm", ck[("prmm", b * 8 + pr)])
                    if pr >= 2:
                        c.wait(sc, "vv", ck[("sqv", b * 8 + pr - 2)])
                    for ts_ in range(4):
                        sc.activation(projp[:, pr, ts_, :], pb[ts_][:, 0:512], AF.Copy)
                    i = sc.activation(sqs[:, pr % 2, :], pm[:, :], AF.Copy)
                    ck[("prev", b * 8 + pr)] = c.inc(i, "ev", 1)

                @blk.vector
                def _(v, pr=pr):
                    c.wait(v, "ev", ck[("prev", b * 8 + pr)])
                    v.tensor_tensor(sqs[:, pr % 2, :], sqs[:, pr % 2, :], sqs[:, pr % 2, :], Alu.mult)
                    i = None
                    for hh in range(2):
                        i = v.tensor_reduce(
                            nsqb[:, pr * 8 + 4 * hh:pr * 8 + 4 * hh + 4].rearrange("p (g o) -> p g o", o=1),
                            sqs[:, pr % 2, :].rearrange("p (a q) -> p a q", a=4)[:, :, hh * 64:(hh + 1) * 64],
                            AX, Alu.add)
                    ck[("sqv", b * 8 + pr)] = c.inc(i, "vv", 1)

            @blk.scalar
            def _(sc):
                c.wait(sc, "vv", ck[("sqv", b * 8 + 7)])
                i = sc.activation(nsqb[:, :], nsqb[:, :], AF.Exp, scale=-0.5)
                c.inc(i, "ev", 1)
            expb_ev = c.val("ev")

            @blk.vector
            def _(v, expb_ev=expb_ev):
                c.wait(v, "ev", expb_ev)
                c.drain(v, v.tensor_scalar(nsqb[:, :], nsqb[:, :], 1.0 / 16.0, None, Alu.mult))
                i = v.engine_nop()
                c.inc(i, "vv", 1)
            ksall_vv = c.val("vv")

            def s1_trig(pr, gi, par):
                @blk.vector
                def _(v):
                    if pr == 0:
                        c.wait(v, "vv", ksall_vv)
                    if gi >= 1:
                        c.wait(v, "ev", ck[("trge", gi - 1)])
                    src_ = projp[:, pr, :, :].rearrange("p a b -> p (a b)")
                    v.tensor_scalar(u1_s[:, :], src_, 1.0 / TWO_PI, MAGIC, Alu.mult, Alu.add)
                    v.tensor_scalar(u1_s[:, :], u1_s[:, :], MAGIC, TWO_PI, Alu.subtract, Alu.mult)
                    v.scalar_tensor_tensor(u1_s[:, :], u1_s[:, :], -1.0, src_, Alu.mult, Alu.add)
                    v.tensor_scalar(u2_s[:, :], src_, PI / 2, 1.0 / TWO_PI, Alu.add, Alu.mult)
                    v.tensor_scalar(u2_s[:, :], u2_s[:, :], MAGIC, MAGIC, Alu.add, Alu.subtract)
                    v.tensor_scalar(u2_s[:, :], u2_s[:, :], TWO_PI, None, Alu.mult)
                    i = v.scalar_tensor_tensor(u2_s[:, :], u2_s[:, :], -1.0, src_, Alu.mult, Alu.add)
                    ck[("trgv", gi)] = c.inc(i, "vv", 1)

            def s1_vs(pr, gi, par):
                @blk.vector
                def _(v):
                    if gi >= 2:
                        c.wait(v, "mm", ck[("kvmm", gi - 2)])
                    v.memset(Vsbp[:, par, :, :, 65:66], 0.0)
                    for ts_ in range(4):
                        for hh in range(2):
                            v.tensor_scalar(Vsbp[:, par, ts_, hh, 0:64],
                                            V_tm[:, ts_, (pr * 2 + hh) * 64:(pr * 2 + hh + 1) * 64],
                                            nsqb[:, pr * 8 + 4 * hh + ts_:pr * 8 + 4 * hh + ts_ + 1],
                                            None, Alu.mult)
                    i = None
                    for hh in range(2):
                        i = v.tensor_copy(Vsbp[:, par, :, hh, 64:65],
                                          nsqb[:, pr * 8 + 4 * hh:pr * 8 + 4 * hh + 4]
                                          .rearrange("p (g o) -> p g o", o=1))
                    ck[("vsv", gi)] = c.inc(i, "vv", 1)

            def s1_sin(pr, gi, par):
                @blk.scalar
                def _(sc):
                    c.wait(sc, "vv", ck[("trgv", gi)])
                    if gi >= 2:
                        c.wait(sc, "mm", ck[("kvmm", gi - 2)])
                    sc.activation(Kfcsp[:, par, :, :].rearrange("p a (h x) -> p a h x", h=2)[:, :, :, 256:512],
                                  u1_s[:, :], AF.Sin)
                    i = sc.activation(Kfcsp[:, par, :, :].rearrange("p a (h x) -> p a h x", h=2)[:, :, :, 0:256],
                                      u2_s[:, :], AF.Sin, bias=consts[:, 1:2])
                    ck[("trge", gi)] = c.inc(i, "ev", 1)

            def s1_kv(pr, gi, par):
                @blk.tensor
                def _(t):
                    c.wait(t, "ev", ck[("trge", gi)])
                    c.wait(t, "vv", ck[("vsv", gi)])
                    if gi >= 1:
                        c.wait(t, "ev", ck[("kvev", gi - 1)])
                    i = None
                    for hh in range(2):
                        bank = pm2 if hh == 0 else pb[0]
                        for fs in range(4):
                            for ts_ in range(4):
                                i = t.matmul(bank[:, fs * 66:(fs + 1) * 66],
                                             Kfcsp[:, par, ts_, hh * 512 + fs * 128:hh * 512 + (fs + 1) * 128],
                                             Vsbp[:, par, ts_, hh, :], start=(ts_ == 0), stop=(ts_ == 3))
                    ck[("kvmm", gi)] = c.inc(i, "mm", 1)

            def s1_kvev(pr, gi, par):
                @blk.scalar
                def _(sc):
                    c.wait(sc, "mm", ck[("kvmm", gi)])
                    sc.activation(kvstg[:, pr * 2, :], pm2[:, 0:264], AF.Copy)
                    i = sc.activation(kvstg[:, pr * 2 + 1, :], pb[0][:, 0:264], AF.Copy)
                    ck[("kvev", gi)] = c.inc(i, "ev", 1)

            late1 = [s1_sin, s1_kv, s1_kvev]
            early1 = [s1_trig, s1_vs]
            for it in range(9):
                if it >= 1:
                    pr = it - 1
                    for f in late1:
                        f(pr, b * 8 + pr, (b * 8 + pr) % 2)
                if it < 8:
                    pr = it
                    for f in early1:
                        f(pr, b * 8 + pr, (b * 8 + pr) % 2)

            @blk.gpsimd
            def _(g, b=b):
                c.wait(g, "ev", c.val("ev"))
                c.inc(g.dma_start(out=kv_in[b * 128:(b + 1) * 128, :], in_=kvstg[:, :, :]), "st", 16)
            barrier()

        @blk.gpsimd
        def _(g):
            c.wait(g, "st", c.val("st"))
            i = g.collective_compute("AllReduce", Alu.add, replica_groups=[list(range(NCORE))],
                                     ins=[kv_in.opt()], outs=[kv_out.opt()])
            c.inc(i, "cc", 1)
        barrier()

        arf.reset()
        arb.reset()
        sqs2 = arf.take(2, 512)
        nsqb2 = arf.take(64)
        u1_s = arf.take(2048)
        u2_s = arf.take(2048)
        xr2 = arf.take(4, D)
        ost2 = arf.take(4, D)
        szz = arf.take(2, 264)
        assert arf.off <= AF_ELE, arf.off
        wqTb = arb.take(8, D)
        woTb = arb.take(8, D)
        xnT_ab = arb.take(8, TL)
        Q_fm = arb.take(8, TL)
        projq_bf = arb.take(8, 4, 512)
        Qfcsp = arb.take(2, 8, 512)
        attT = arb.take(8, TL)
        kvbb = arb.take(16, 264)
        attstg2 = arb.take(4, 128)
        rfp = arb.take(512)
        assert arb.off <= AB_ELE, arb.off

        @blk.gpsimd
        def _(g):
            c.inc(g.dma_start(out=wqTb[:, :, :], in_=wqT.rearrange("(k p) o -> p k o", p=128)), "cst", 16)
            c.inc(g.dma_start(out=woTb[:, :, :], in_=woT.rearrange("(k p) o -> p k o", p=128)), "cst", 16)
        build_rfp(rfp, "s2")
        barrier()

        for b in range(NB):
            @blk.sync
            def _(sy, b=b):
                c.inc(sy.dma_start(out=xnT_ab[:, :, :].rearrange("p a b -> p (a b)"), in_=xnT_sp[b]), "ld", 16)
                c.inc(sy.dma_start(out=xr2[:, :, :],
                                   in_=xres[b * TL:(b + 1) * TL, :].rearrange("(t p) o -> p t o", p=128)), "ld", 16)
            ldt = c.val("ld")

            @blk.gpsimd
            def _(g, b=b):
                c.inc(g.dma_start(out=kvbb[:, :, :].rearrange("p a b -> p (a b)"),
                                  in_=kv_out[b * 128:(b + 1) * 128, :]), "cst", 16)
            kv_ld = c.val("cst")

            ev_q0 = c.val("ev")
            for oc in range(8):
                @blk.tensor
                def _(t, oc=oc, ev_q0=ev_q0, ldt=ldt):
                    if oc == 0:
                        c.wait(t, "ld", ldt)
                    if oc >= 4:
                        c.wait(t, "ev", ev_q0 + oc - 3)
                    bank = pb[oc % 4]
                    i = None
                    for kc in range(8):
                        i = t.matmul(bank[:, :], wqTb[:, kc, oc * 128:(oc + 1) * 128],
                                     xnT_ab[:, kc, :], start=(kc == 0), stop=(kc == 7))
                    c.inc(i, "mm", 1)

                @blk.scalar
                def _(sc, oc=oc):
                    c.wait(sc, "mm")
                    c.inc(sc.activation(Q_fm[:, oc, :], pb[oc % 4][:, :], AF.Copy), "ev", 1)
            prep_ev = c.val("ev")

            if dbg and b == 0:
                dbg7 = nc.dram_tensor("dbg7", [128, 8192], F32, kind="ExternalOutput").ap()

                @blk.gpsimd
                def _(g, prep_ev=prep_ev):
                    c.wait(g, "ev", prep_ev)
                    c.inc(g.dma_start(out=dbg7[:, 0:4096], in_=Q_fm[:, :, :].rearrange("p a b -> p (a b)")), "st", 16)
                    c.inc(g.dma_start(out=dbg7[:, 4096:8192], in_=xnT_ab[:, :, :].rearrange("p a b -> p (a b)")), "st", 16)

            for pr in range(8):
                gi = 64 + b * 8 + pr

                @blk.tensor
                def _(t, pr=pr, gi=gi):
                    if pr == 0:
                        c.wait(t, "ev", prep_ev)
                        c.wait(t, "vv", c.val("vv"))
                    else:
                        c.wait(t, "ev", ck[("prev", gi - 1)])
                    i = None
                    for blk_ in range(4):
                        t.matmul(pb[blk_][:, 0:512], rfp[:, blk_ * 128:(blk_ + 1) * 128],
                                 Q_fm[:, pr, :], start=True, stop=True)
                        i = t.matmul(pm[:, blk_ * 128:(blk_ + 1) * 128],
                                     Q_fm[:, pr, blk_ * 128:(blk_ + 1) * 128],
                                     ident_b[:, :], start=True, stop=True)
                    ck[("prmm", gi)] = c.inc(i, "mm", 1)

                @blk.scalar
                def _(sc, pr=pr, gi=gi):
                    c.wait(sc, "mm", ck[("prmm", gi)])
                    if pr >= 2:
                        c.wait(sc, "vv", ck[("sqv", gi - 2)])
                    for blk_ in range(4):
                        sc.activation(projq_bf[:, pr, blk_, :], pb[blk_][:, 0:512], AF.Copy)
                    i = sc.activation(sqs2[:, pr % 2, :], pm[:, :], AF.Copy)
                    ck[("prev", gi)] = c.inc(i, "ev", 1)

                @blk.vector
                def _(v, pr=pr, gi=gi):
                    c.wait(v, "ev", ck[("prev", gi)])
                    v.tensor_tensor(sqs2[:, pr % 2, :], sqs2[:, pr % 2, :], sqs2[:, pr % 2, :], Alu.mult)
                    i = None
                    for hh in range(2):
                        i = v.tensor_reduce(
                            nsqb2[:, pr * 8 + 4 * hh:pr * 8 + 4 * hh + 4].rearrange("p (g o) -> p g o", o=1),
                            sqs2[:, pr % 2, :].rearrange("p (a q) -> p a q", a=4)[:, :, hh * 64:(hh + 1) * 64],
                            AX, Alu.add)
                    ck[("sqv", gi)] = c.inc(i, "vv", 1)

            @blk.scalar
            def _(sc, b=b):
                c.wait(sc, "vv", ck[("sqv", 64 + b * 8 + 7)])
                i = sc.activation(nsqb2[:, :], nsqb2[:, :], AF.Exp, scale=-0.5)
                c.inc(i, "ev", 1)

            @blk.vector
            def _(v, ev0=c.val("ev")):
                c.wait(v, "ev", ev0)
                c.drain(v, v.tensor_scalar(nsqb2[:, :], nsqb2[:, :], 1.0 / 16.0, None, Alu.mult))
                i = v.engine_nop()
                c.inc(i, "vv", 1)

            def s2_trig(pr, gi, par):
                @blk.vector
                def _(v):
                    if gi >= 65:
                        c.wait(v, "ev", ck[("trge", gi - 1)])
                    src_ = projq_bf[:, pr, :, :].rearrange("p a b -> p (a b)")
                    v.tensor_scalar(u1_s[:, :], src_, 1.0 / TWO_PI, MAGIC, Alu.mult, Alu.add)
                    v.tensor_scalar(u1_s[:, :], u1_s[:, :], MAGIC, TWO_PI, Alu.subtract, Alu.mult)
                    v.scalar_tensor_tensor(u1_s[:, :], u1_s[:, :], -1.0, src_, Alu.mult, Alu.add)
                    v.tensor_scalar(u2_s[:, :], src_, PI / 2, 1.0 / TWO_PI, Alu.add, Alu.mult)
                    v.tensor_scalar(u2_s[:, :], u2_s[:, :], MAGIC, MAGIC, Alu.add, Alu.subtract)
                    v.tensor_scalar(u2_s[:, :], u2_s[:, :], TWO_PI, None, Alu.mult)
                    i = v.scalar_tensor_tensor(u2_s[:, :], u2_s[:, :], -1.0, src_, Alu.mult, Alu.add)
                    ck[("trgv", gi)] = c.inc(i, "vv", 1)

            def s2_sin(pr, gi, par):
                @blk.scalar
                def _(sc):
                    c.wait(sc, "vv", ck[("trgv", gi)])
                    if gi >= 66:
                        c.wait(sc, "mm", ck[("qkmm", gi - 2)])
                    sc.activation(Qfcsp[:, par, :, :].rearrange("p (h f) x -> p h f x", h=2)[:, :, 2:4, :],
                                  u1_s[:, :], AF.Sin)
                    i = sc.activation(Qfcsp[:, par, :, :].rearrange("p (h f) x -> p h f x", h=2)[:, :, 0:2, :],
                                      u2_s[:, :], AF.Sin, bias=consts[:, 1:2])
                    ck[("trge", gi)] = c.inc(i, "ev", 1)

            def s2_qkvz(pr, gi, par):
                @blk.tensor
                def _(t):
                    c.wait(t, "ev", ck[("trge", gi)])
                    if gi == 64 + b * 8:
                        c.wait(t, "cst", kv_ld)
                    if gi >= 65:
                        c.wait(t, "ev", ck.get(("qze", gi - 1), 0))
                    i = None
                    for hh in range(2):
                        bank = pm2 if hh == 0 else pb[0]
                        h = pr * 2 + hh
                        for ts_ in range(4):
                            for fc in range(4):
                                i = t.matmul(bank[:, ts_ * 66:ts_ * 66 + 66],
                                             Qfcsp[:, par, hh * 4 + fc, ts_ * 128:(ts_ + 1) * 128],
                                             kvbb[:, h, fc * 66:(fc + 1) * 66],
                                             start=(fc == 0), stop=(fc == 3))
                    ck[("qkmm", gi)] = c.inc(i, "mm", 1)

            def s2_qze(pr, gi, par):
                @blk.scalar
                def _(sc):
                    c.wait(sc, "mm", ck[("qkmm", gi)])
                    if gi >= 65:
                        c.wait(sc, "vv", ck[("epiv", gi - 1)])
                    sc.activation(szz[:, 0, :], pm2[:, 0:264], AF.Copy)
                    i = sc.activation(szz[:, 1, :], pb[0][:, 0:264], AF.Copy)
                    ck[("qze", gi)] = c.inc(i, "ev", 1)

            def s2_epi(pr, gi, par):
                @blk.vector
                def _(v):
                    c.wait(v, "ev", ck[("qze", gi)])
                    if gi >= 65:
                        c.wait(v, "mm", ck[("attm", gi - 1)])
                    i = None
                    for hh in range(2):
                        qv3 = nsqb2[:, pr * 8 + 4 * hh:pr * 8 + 4 * hh + 4].rearrange("p (g o) -> p g o", o=1)
                        zz = colw[:, 24:28] if par == 0 else colw[:, 28:32]
                        zz3 = zz.rearrange("p (g o) -> p g o", o=1)
                        c.drain(v, v.tensor_tensor(zz3, qv3,
                                        szz[:, hh, :].rearrange("p (w c) -> p w c", c=66)[:, :, 64:65],
                                        Alu.mult))
                        c.drain(v, v.tensor_scalar(zz, zz, 1e-6, None, Alu.max))
                        c.drain(v, v.reciprocal(zz, zz))
                        qsc = colw[:, 32:36] if par == 0 else colw[:, 36:40]
                        c.drain(v, v.tensor_tensor(qsc.rearrange("p (g o) -> p g o", o=1), qv3, zz3, Alu.mult))
                        for ts_ in range(4):
                            i = v.tensor_scalar(attstg2[:, ts_, hh * 64:(hh + 1) * 64],
                                                szz[:, hh, ts_ * 66:ts_ * 66 + 64],
                                                qsc[:, ts_:ts_ + 1], None, Alu.mult)
                    ck[("epiv", gi)] = c.inc(i, "vv", 1)

            def s2_atttr(pr, gi, par):
                @blk.tensor
                def _(t):
                    c.wait(t, "vv", ck[("epiv", gi)])
                    i = None
                    for ts_ in range(4):
                        i = t.transpose(pt[1][:, ts_, :], attstg2[:, ts_, :], ident_b[:, :])
                    ck[("attm", gi)] = c.inc(i, "mm", 1)

            def s2_attcp(pr, gi, par):
                @blk.vector
                def _(v):
                    c.wait(v, "mm", ck[("attm", gi)])
                    i = v.tensor_copy(attT[:, pr, :], pt[1][:, 0:4, :])
                    ck[("attv", gi)] = c.inc(i, "vv", 1)

            late2 = [s2_sin, s2_qkvz, s2_qze, s2_epi, s2_atttr, s2_attcp]
            early2 = [s2_trig]
            for it in range(9):
                if it >= 1:
                    pr = it - 1
                    for f in late2:
                        f(pr, 64 + b * 8 + pr, (64 + b * 8 + pr) % 2)
                if it < 8:
                    pr = it
                    for f in early2:
                        f(pr, 64 + b * 8 + pr, (64 + b * 8 + pr) % 2)

            att_vv = c.val("vv")

            if dbg and b == 0:
                dbg6 = nc.dram_tensor("dbg6", [128, 16384], F32, kind="ExternalOutput").ap()

                @blk.gpsimd
                def _(g):
                    c.wait(g, "vv", att_vv)
                    c.inc(g.dma_start(out=dbg6[:, 0:4096], in_=attT[:, :, :].rearrange("p a b -> p (a b)")), "st", 16)
                    c.inc(g.dma_start(out=dbg6[:, 4096:12288], in_=Qfcsp[:, :, :, :].rearrange("p a b c -> p (a b c)")), "st", 16)

            for u in range(8):
                ts_, occ = divmod(u, 2)

                @blk.tensor
                def _(t, ts_=ts_, occ=occ, u=u, att_vv=att_vv):
                    if u == 0:
                        c.wait(t, "vv", att_vv)
                    if u >= 4:
                        c.wait(t, "vv", c.val("vv"))
                    bank = pb[u % 4]
                    i = None
                    for cc in range(8):
                        i = t.matmul(bank[:, :], attT[:, cc, ts_ * 128:(ts_ + 1) * 128],
                                     woTb[:, cc, occ * 512:(occ + 1) * 512],
                                     start=(cc == 0), stop=(cc == 7))
                    c.inc(i, "mm", 1)

                @blk.vector
                def _(v, ts_=ts_, occ=occ, u=u):
                    c.wait(v, "mm")
                    i = v.tensor_tensor(ost2[:, ts_, occ * 512:(occ + 1) * 512],
                                        pb[u % 4][:, :],
                                        xr2[:, ts_, occ * 512:(occ + 1) * 512], Alu.add)
                    c.inc(i, "vv", 1)

            @blk.gpsimd
            def _(g, b=b):
                c.wait(g, "vv", c.val("vv"))
                c.inc(g.dma_start(
                    out=xres[b * TL:(b + 1) * TL, :].rearrange("(t p) o -> p t o", p=128),
                    in_=ost2[:, :, :]), "st", 16)
            barrier()

    phase_setup()
    phase_attn_a()
    if stage >= 2:
        phase_dequant()
        phase_ffn(xres, y_out if stage == 2 else xres, 0)
    if stage >= 3:
        phase_attn_b()
        if stage == 3:
            @blk.gpsimd
            def _(g):
                c.wait(g, "st", c.val("st"))
                i = g.dma_start(out=y_out, in_=xres)
                c.inc(i, "st", 16)
                c.wait(g, "st")
    if stage >= 4:
        phase_ffn(xres, y_out, 5)

    if dbg and stage >= 3:
        kvtap = arf.at(0, 16, 264)

        @blk.sync
        def _(sy):
            c.wait(sy, "st", c.val("st"))
            c.inc(sy.dma_start(out=kvtap[:, :, :].rearrange("p a b -> p (a b)"),
                               in_=kv_out[0:128, :]), "ld", 16)

        @blk.gpsimd
        def _(g):
            c.wait(g, "ld", c.val("ld"))

    if dbg and stage >= 2:
        global DBG_MAP2
        DBG_MAP2 = {}
        dbg2 = nc.dram_tensor("dbg2", [128, 40960], F32, kind="ExternalOutput").ap()

        @blk.gpsimd
        def _(g):
            off = 0
            taps3 = ([("kv0", arf.at(0, 4224), 4224),
                      ("attT", arb.at(26624, 4096), 4096),
                      ("qfcs", arb.at(24576, 2048), 2048)]
                     if stage >= 3 else [])
            for name, ap_, n in taps3 + [("colw", colw[:, :], 32),
                                 ("gate0", gate_s[0], 4096), ("up0", up_s[0], 4096),
                                 ("g20", g2_s[0], 4096), ("t40", t4_s[0], 4096),
                                 ("wg0", wgT_dq[0, 0], 128), ("wg1", wgT_dq[0, 1], 128),
                                 ("wd0", wdT_dq[0], 1024)]:
                DBG_MAP2[name] = (off, n)
                c.inc(g.dma_start(out=dbg2[:, off:off + n], in_=ap_), "st", 16)
                off += n
            c.wait(g, "st")

    if dbg:
        taps = [("xnT", xnT, BF16, 4096), ("z_fm", z_fm, BF16, 4096),
                ("z_tm", z_tm, BF16, 4096), ("q_fm", q_fm, BF16, 4096),
                ("G_fm", G_fm, BF16, 4096), ("G_tm", G_tm, BF16, 4096),
                ("h_tm", h_tm, BF16, 1024), ("gp_b", gp_b, BF16, 1024),
                ("da_fm", da_fm, BF16, 1024), ("da_tm", da_tm, BF16, 1024),
                ("h2_fm", h_fm, BF16, 1024), ("a_f", a_f, F32, 1024),
                ("gst1", gst1, F32, 2048), ("gst2", gst2, F32, 2048),
                ("zy", y_fm, BF16, 4096), ("xs", xs, F32, 4096),
                ("scrb", scr_b, F32, 2048), ("w1T_b", w1T_b, BF16, 2048),
                ("w2T_b", w2T_b, BF16, 2048), ("w2n_b", w2n_b, BF16, 2048)]
        off = [0]
        global DBG_MAP
        DBG_MAP = {}

        @blk.gpsimd
        def _(g):
            for name, ap, dt_, n in taps:
                flat = ap
                if len(flat.shape) == 3:
                    flat = flat.rearrange("p a b -> p (a b)")
                rows = flat.shape[0]
                DBG_MAP[name] = (off[0], n, rows)
                i = g.dma_start(out=dbg_out[0:rows, off[0]:off[0] + n], in_=flat)
                c.inc(i, "st", 16)
                off[0] += n
            c.wait(g, "st")

    if stage == 1:
        @blk.gpsimd
        def _(g):
            c.wait(g, "st", c.val("st"))
            i = g.dma_start(out=y_out, in_=xres)
            c.inc(i, "st", 16)
            c.wait(g, "st")

    es.close()
    return nc


_CACHE = {}


def _get_prog(stage=None):
    key = STAGE if stage is None else stage
    if key not in _CACHE:
        _CACHE[key] = build_program(key)
    return _CACHE[key]


def make_in_maps(inputs):
    x = np.asarray(inputs["L0_attn_a_in_seq"], dtype=np.float32)
    T = lambda a: np.ascontiguousarray(np.asarray(a, dtype=np.float32).T)
    C = lambda a: np.ascontiguousarray(np.asarray(a, dtype=np.float32))
    shared = {
        "tkT": T(inputs["L0_attn_a_t_k"]), "tvT": T(inputs["L0_attn_a_t_v"]),
        "tqT": T(inputs["L0_attn_a_t_q"]),
        "w1T": T(inputs["L0_attn_a_W1"]), "w2T": T(inputs["L0_attn_a_W2"]),
        "w2n": C(inputs["L0_attn_a_W2"]),
        "b1i": C(inputs["L0_attn_a_b1"]).reshape(1, DI),
        "b2i": C(inputs["L0_attn_a_b2"]).reshape(1, D),
        "wqT": T(inputs["L0_attn_b_wq"]), "wkT": T(inputs["L0_attn_b_wk"]),
        "wvT": T(inputs["L0_attn_b_wv"]), "woT": T(inputs["L0_attn_b_wo"]),
        "rfT": T(inputs["L0_attn_b_rf"]),
        "wgate": C(inputs["L0_ffn_w_gate"]), "wup": C(inputs["L0_ffn_w_up"]),
        "wdown": C(inputs["L0_ffn_w_down"]),
    }
    in_maps = []
    for cid in range(NCORE):
        m = dict(shared)
        m["x_in"] = np.ascontiguousarray(x[:, cid * TL:(cid + 1) * TL, :].reshape(NB * TL, D))
        in_maps.append(m)
    return in_maps


def kernel(**inputs):
    in_maps = make_in_maps(inputs)
    nc = _get_prog()
    res = run_bass_kernel_spmd(nc, in_maps, core_ids=list(range(NCORE)))
    out = np.empty((NB, 4096, D), dtype=np.float32)
    for cid in range(NCORE):
        out[:, cid * TL:(cid + 1) * TL, :] = np.asarray(res.results[cid]["y"]).reshape(NB, TL, D)
    return out


# version 05
# speedup vs baseline: 1.3166x; 1.0846x over previous
import contextlib
import numpy as np
import ml_dtypes
import concourse.bass as bass
import concourse.mybir as mybir
from concourse.bass_utils import run_bass_kernel_spmd
from concourse.alu_op_type import AluOpType as Alu

F32 = mybir.dt.float32
BF16 = mybir.dt.bfloat16
AF = mybir.ActivationFunctionType
AX = mybir.AxisListType.X

MAGIC = 1.5 * 2.0 ** 23
PI = float(np.pi)
LR = 0.001
EPS = 1e-5
NB, TL, D, DI, FF = 8, 512, 1024, 256, 3072
CG = 2.0 / (4096.0 * 1024.0)
NCORE = 8

STAGE = 4
DBG = False
DBG_STEPS = None

AF_ELE = 18688
AB_ELE = 67840


class Ctx:
    def __init__(self, nc, es):
        self.nc, self.es, self.sem, self.cnt = nc, es, {}, {}

    def S(self, name):
        self.sem[name] = self.es.enter_context(self.nc.semaphore(name))
        self.cnt[name] = 0

    def inc(self, instr, name, n=1):
        self.cnt[name] += n
        instr.then_inc(self.sem[name], n)
        return self.cnt[name]

    def val(self, name):
        return self.cnt[name]

    def drain(self, eng, instr, name="vv"):
        v = self.inc(instr, name, 1)
        eng.wait_ge(self.sem[name], v)

    def wait(self, eng, name, target=None):
        if target is None:
            target = self.cnt[name]
        if target > 0:
            eng.wait_ge(self.sem[name], target)


class Arena:
    def __init__(self, t, cap):
        self.t, self.cap, self.off = t, cap, 0

    def take(self, *shape):
        n = int(np.prod(shape))
        ap = self.t[:, self.off:self.off + n]
        assert self.off + n <= self.cap, f"arena overflow {self.off}+{n}>{self.cap}"
        self.off += n
        if len(shape) > 1:
            names = " ".join(f"d{i}" for i in range(len(shape)))
            ap = ap.rearrange(f"p ({names}) -> p {names}",
                              **{f"d{i}": s for i, s in enumerate(shape)})
        return ap

    def at(self, off, *shape):
        n = int(np.prod(shape))
        ap = self.t[:, off:off + n]
        if len(shape) > 1:
            names = " ".join(f"d{i}" for i in range(len(shape)))
            ap = ap.rearrange(f"p ({names}) -> p {names}",
                              **{f"d{i}": s for i, s in enumerate(shape)})
        return ap

    def reset(self, off=0):
        self.off = off


def build_program(stage=STAGE, dbg=None, dbg_steps=None):
    dbg = DBG if dbg is None else dbg
    nsteps = NB if (DBG_STEPS if dbg_steps is None else dbg_steps) is None else (DBG_STEPS if dbg_steps is None else dbg_steps)
    nc = bass.Bass("TRN2", target_bir_lowering=False)

    def din(name, shape):
        return nc.dram_tensor(name, shape, F32, kind="ExternalInput").ap()

    x_in = din("x_in", [NB * TL, D])
    tkT, tvT, tqT = din("tkT", [D, D]), din("tvT", [D, D]), din("tqT", [D, D])
    w1Ti, w2Ti, w2ni = din("w1T", [D, DI]), din("w2T", [DI, D]), din("w2n", [D, DI])
    b1i, b2i = din("b1i", [1, DI]), din("b2i", [1, D])
    wqT, wkT = din("wqT", [D, D]), din("wkT", [D, D])
    wvT, woT = din("wvT", [D, D]), din("woT", [D, D])
    rfTi = din("rfT", [64, 256])
    wgate, wup, wdown = din("wgate", [FF, D]), din("wup", [FF, D]), din("wdown", [D, FF])

    y_out = nc.dram_tensor("y", [NB * TL, D], F32, kind="ExternalOutput").ap()
    dbg_out = nc.dram_tensor("dbg", [128, 65536], F32, kind="ExternalOutput").ap() if dbg else None

    xres = nc.dram_tensor("xres", [NB * TL, D], F32).ap()
    wgT_dq = nc.dram_tensor("wgT_dq", [24, 8, 128, 128], BF16).ap()
    wuT_dq = nc.dram_tensor("wuT_dq", [24, 8, 128, 128], BF16).ap()
    wdT_dq = nc.dram_tensor("wdT_dq", [24, 128, D], BF16).ap()
    gbuf_in = nc.dram_tensor("gbuf_in", [1028, 512], F32).ap()
    gbuf_out = nc.dram_tensor("gbuf_out", [1028, 512], F32).ap()
    up_s = nc.dram_tensor("up_s", [24, 128, NB * TL], BF16).ap()
    gate_s = nc.dram_tensor("gate_s", [24, 128, NB * TL], BF16).ap()
    g2_s = nc.dram_tensor("g2_s", [24, 128, NB * TL], BF16).ap()
    t4_s = nc.dram_tensor("t4_s", [24, 128, NB * TL], BF16).ap()
    mxi = [nc.dram_tensor(f"mxi{k}", [1, 1], F32).ap() for k in range(10)]
    mxo = [nc.dram_tensor(f"mxo{k}", [1, 1], F32).ap() for k in range(10)]
    kv_in = nc.dram_tensor("kv_in", [NB * 128, 16 * 264], F32).ap()
    kv_out = nc.dram_tensor("kv_out", [NB * 128, 16 * 264], F32).ap()
    xnT_sp = nc.dram_tensor("xnT_sp", [NB, 128, 8 * TL], BF16).ap()

    es = contextlib.ExitStack()
    c = Ctx(nc, es)

    def sbuf(name, shape, dtype):
        return es.enter_context(nc.sbuf_tensor(name, shape, dtype))

    ident_b = sbuf("ident_b", [128, 128], BF16)
    ones_b = sbuf("ones_b", [128, 512], BF16)
    consts = sbuf("consts", [128, 8], F32)
    colw = sbuf("colw", [128, 48], F32)

    arena_f_t = sbuf("arena_f", [128, AF_ELE], F32)
    arena_b_t = sbuf("arena_b", [128, AB_ELE], BF16)
    arf = Arena(arena_f_t, AF_ELE)
    arb = Arena(arena_b_t, AB_ELE)

    pb = [es.enter_context(nc.psum_tensor(f"pb{i}", [128, 512], F32)) for i in range(4)]
    pt = [es.enter_context(nc.psum_tensor(f"pt{i}", [128, 8, 128], BF16)) for i in range(2)]
    pm = es.enter_context(nc.psum_tensor("pm", [128, 512], F32))
    pm2 = es.enter_context(nc.psum_tensor("pm2", [128, 512], F32))

    for s in ["ld", "cst", "st", "mm", "ev", "vv", "pp", "cc"]:
        c.S(s)

    blk = es.enter_context(nc.Block())

    def barrier():
        t_ = {k: c.val(k) for k in ("mm", "ev", "vv", "st", "ld", "cst", "pp", "cc")}

        @blk.tensor
        def _(t):
            for k in ("ev", "vv", "pp", "st", "ld", "cst"):
                c.wait(t, k, t_[k])

        @blk.vector
        def _(v):
            for k in ("mm", "ev", "pp", "st", "ld", "cst"):
                c.wait(v, k, t_[k])

        @blk.scalar
        def _(sc):
            for k in ("mm", "vv", "pp", "st", "ld", "cst"):
                c.wait(sc, k, t_[k])

        @blk.gpsimd
        def _(g):
            for k in ("mm", "ev", "vv", "ld", "cc"):
                c.wait(g, k, t_[k])

        @blk.sync
        def _(sy):
            for k in ("mm", "ev", "vv", "pp", "st", "cst", "cc"):
                c.wait(sy, k, t_[k])

    xs = arf.take(4, D)
    scr_b = arf.take(2048)
    gst3 = scr_b[0:1, :]
    a_f = arf.take(2, TL)
    w1T_f = arf.take(8, DI)
    w2T_f = arf.take(2, D)
    w2n_f = arf.take(8, DI)
    b1_f = arf.take(DI)[0:1, :]
    b2_f = arf.take(D)[0:1, :]
    gst1 = arf.take(8, DI)
    gst2 = arf.take(2, D)
    gr1, gr2, gr3 = gst1, gst2, gst3
    assert arf.off <= AF_ELE, arf.off

    tkT_b = arb.take(8, D)
    tvT_b = arb.take(8, D)
    tqT_b = arb.take(8, D)
    w1T_b = arb.take(8, DI)
    w2T_b = arb.take(2, D)
    w2n_b = arb.take(8, DI)
    b1_b = arb.take(DI)[0:1, :]
    b2_b = arb.take(D)[0:1, :]
    xn_tm = arb.take(4, D)
    G_tm = xn_tm
    xnT = arb.take(8, TL)
    z_fm = arb.take(8, TL)
    z_tm = arb.take(4, D)
    y_fm = arb.take(8, TL)
    q_fm = arb.take(8, TL)
    _goff = arb.off
    G_fm = arb.take(8, TL)
    gr2b = arb.at(_goff, 2, 1024)
    h_fm = arb.take(2, TL)
    h2_fm = h_fm
    h_tm = arb.take(4, DI)
    gp_b = arb.take(2, TL)
    da_fm = arb.take(2, TL)
    da_tm = arb.take(4, DI)
    gst4 = arb.take(8, DI)
    G_fm_off = None
    print("attn_a arena use: f32", arf.off, "bf16", arb.off)
    assert arb.off <= AB_ELE, arb.off

    def phase_setup():
        @blk.gpsimd
        def _(g):
            g.memset(ident_b[:, :], 0.0)
            g.affine_select(out=ident_b[:, :], in_=ident_b[:, :], compare_op=Alu.not_equal,
                            fill=1.0, base=0, pattern=[[-1, 128]], channel_multiplier=1)
            g.memset(ones_b[:, :], 1.0)
            g.memset(consts[:, 0:1], MAGIC)
            g.memset(consts[:, 1:2], PI / 2)
            i = g.memset(consts[:, 2:3], 65536.0)
            c.inc(i, "pp", 1)
            for dst, src in ((tkT_b, tkT), (tvT_b, tvT), (tqT_b, tqT),
                             (w1T_b, w1Ti), (w2T_b, w2Ti), (w2n_b, w2ni)):
                c.inc(g.dma_start(out=dst[:, :, :], in_=src.rearrange("(k p) o -> p k o", p=128)),
                      "cst", 16)
            c.inc(g.dma_start(out=b1_b, in_=b1i), "cst", 16)
            c.inc(g.dma_start(out=b2_b, in_=b2i), "cst", 16)

        @blk.sync
        def _(s):
            c.inc(s.dma_start(out=w1T_f[:, :, :], in_=w1Ti.rearrange("(k p) o -> p k o", p=128)), "ld", 16)
            c.inc(s.dma_start(out=w2T_f[:, :, :], in_=w2Ti.rearrange("(k p) o -> p k o", p=128)), "ld", 16)
            c.inc(s.dma_start(out=w2n_f[:, :, :], in_=w2ni.rearrange("(k p) o -> p k o", p=128)), "ld", 16)
            c.inc(s.dma_start(out=b1_f, in_=b1i), "ld", 16)
            c.inc(s.dma_start(out=b2_f, in_=b2i), "ld", 16)
            c.inc(s.dma_start(out=xs[:, :, :],
                              in_=x_in[0:TL, :].rearrange("(t p) o -> p t o", p=128)), "ld", 16)

    def transpose_group(dst, srcs, par, pre_ev=None):
        @blk.tensor
        def _(t):
            if pre_ev is not None:
                c.wait(t, "ev", pre_ev)
            c.wait(t, "vv", c.val("vv"))
            i = None
            for k, s_ap in enumerate(srcs):
                i = t.transpose(pt[par][:, k, :], s_ap, ident_b[:, :])
            c.inc(i, "mm", 1)

        @blk.vector
        def _(v):
            c.wait(v, "mm")
            i = v.tensor_copy(dst, pt[par][:, 0:len(srcs), :])
            c.inc(i, "vv", 1)

    def phase_attn_a():
        ld_xs = c.val("ld")

        @blk.tensor
        def _(t):
            c.wait(t, "pp")
            c.wait(t, "cst")
            c.wait(t, "ld")

        for s in range(nsteps):
            @blk.vector
            def _(v, s=s, ld_xs=ld_xs):
                c.wait(v, "ld", ld_xs)
                if s == 0:
                    c.wait(v, "cst")
                af = a_f[:, :, :].rearrange("p a b -> p (a b)")
                i = None
                for tc in range(4):
                    v.tensor_tensor(af, xs[:, tc, :], xs[:, tc, :], Alu.mult)
                    v.tensor_reduce(colw[:, 40 + tc:41 + tc], xs[:, tc, :], AX, Alu.add)
                    i = v.tensor_reduce(colw[:, 44 + tc:45 + tc], af, AX, Alu.add)
                c.drain(v, i)
                v.tensor_scalar(colw[:, 40:44], colw[:, 40:44], 1.0 / D, None, Alu.mult)
                c.drain(v, v.tensor_scalar(colw[:, 44:48], colw[:, 44:48], 1.0 / D, EPS, Alu.mult, Alu.add))
                c.drain(v, v.tensor_tensor(colw[:, 0:4], colw[:, 40:44], colw[:, 40:44], Alu.mult))
                i = v.tensor_tensor(colw[:, 44:48], colw[:, 44:48], colw[:, 0:4], Alu.subtract)
                c.inc(i, "vv", 1)

            @blk.scalar
            def _(sc, s=s):
                if s == 0:
                    c.wait(sc, "cst")
                c.wait(sc, "vv")
                c.inc(sc.activation(colw[:, 44:48], colw[:, 44:48], AF.Sqrt), "ev", 1)
            lnsq_ev = c.val("ev")

            for tc in range(4):
                @blk.vector
                def _(v, tc=tc, lnsq_ev=lnsq_ev):
                    if tc == 0:
                        c.wait(v, "ev", lnsq_ev)
                        c.drain(v, v.reciprocal(colw[:, 44:48], colw[:, 44:48]))
                        c.drain(v, v.scalar_tensor_tensor(colw[:, 40:44], colw[:, 40:44], -1.0,
                                                          colw[:, 44:48], Alu.mult, Alu.mult))
                    i = v.tensor_scalar(xn_tm[:, tc, :], xs[:, tc, :],
                                        colw[:, 44 + tc:45 + tc], colw[:, 40 + tc:41 + tc],
                                        Alu.mult, Alu.add)
                    c.inc(i, "vv", 1)

                transpose_group(xnT[:, 0:8, tc * 128:(tc + 1) * 128],
                                [xn_tm[:, tc, kc * 128:(kc + 1) * 128] for kc in range(8)],
                                par=tc % 2)
            xnT_vv = c.val("vv")

            ev0 = c.val("ev")
            for pi_, (wT, dst) in enumerate(((tkT_b, z_fm), (tvT_b, y_fm), (tqT_b, q_fm))):
                for oc in range(8):
                    n = pi_ * 8 + oc

                    @blk.tensor
                    def _(t, wT=wT, oc=oc, n=n, ev0=ev0, xnT_vv=xnT_vv):
                        if n == 0:
                            c.wait(t, "vv", xnT_vv)
                        if n >= 4:
                            c.wait(t, "ev", ev0 + n - 3)
                        bank = pb[n % 4]
                        i = None
                        for kc in range(8):
                            i = t.matmul(bank[:, :], wT[:, kc, oc * 128:(oc + 1) * 128],
                                         xnT[:, kc, :], start=(kc == 0), stop=(kc == 7))
                        c.inc(i, "mm", 1)

                    @blk.scalar
                    def _(sc, dst=dst, oc=oc, n=n):
                        c.wait(sc, "mm")
                        c.inc(sc.activation(dst[:, oc, :], pb[n % 4][:, :], AF.Copy), "ev", 1)
            proj_ev = c.val("ev")

            for oc in range(8):
                transpose_group(z_tm[:, 0:4, oc * 128:(oc + 1) * 128],
                                [z_fm[:, oc, ts * 128:(ts + 1) * 128] for ts in range(4)],
                                par=oc % 2, pre_ev=(proj_ev - 16 if oc == 0 else None))

            @blk.vector
            def _(v, proj_ev=proj_ev):
                c.wait(v, "ev", proj_ev)
                i = v.tensor_tensor(y_fm[:, :, :], z_fm[:, :, :], y_fm[:, :, :], Alu.subtract)
                c.inc(i, "vv", 1)
            zy_vv = c.val("vv")

            for ic in range(2):
                @blk.tensor
                def _(t, ic=ic, proj_ev=proj_ev):
                    if ic == 0:
                        c.wait(t, "ev", proj_ev - 16)
                    bank = pb[ic]
                    for kc in range(8):
                        t.matmul(bank[:, :], w1T_b[:, kc, ic * 128:(ic + 1) * 128],
                                 z_fm[:, kc, :], start=(kc == 0), stop=False)
                    i = t.matmul(bank[:, :], b1_b[0:1, ic * 128:(ic + 1) * 128],
                                 ones_b[0:1, :], start=False, stop=True)
                    c.inc(i, "mm", 1)

                @blk.scalar
                def _(sc, ic=ic):
                    c.wait(sc, "mm")
                    c.inc(sc.activation(a_f[:, ic, :], pb[ic][:, :], AF.Copy), "ev", 1)

            @blk.scalar
            def _(sc):
                sc.activation(h_fm[:, :, :], a_f[:, :, :], AF.Gelu)
                c.inc(sc.activation(gp_b[:, :, :], a_f[:, :, :], AF.Derivative_Gelu), "ev", 1)
            h_ev = c.val("ev")

            for ic in range(2):
                transpose_group(h_tm[:, 0:4, ic * 128:(ic + 1) * 128],
                                [h_fm[:, ic, ts * 128:(ts + 1) * 128] for ts in range(4)],
                                par=ic, pre_ev=(h_ev if ic == 0 else None))

            vv_r0 = c.val("vv")
            for oc in range(8):
                @blk.tensor
                def _(t, oc=oc, h_ev=h_ev, vv_r0=vv_r0):
                    if oc == 0:
                        c.wait(t, "ev", h_ev)
                    if oc >= 4:
                        c.wait(t, "vv", vv_r0 + 2 * (oc - 4) + 2)
                    bank = pb[oc % 4]
                    for kc in range(2):
                        t.matmul(bank[:, :], w2T_b[:, kc, oc * 128:(oc + 1) * 128],
                                 h_fm[:, kc, :], start=(kc == 0), stop=False)
                    i = t.matmul(bank[:, :], b2_b[0:1, oc * 128:(oc + 1) * 128],
                                 ones_b[0:1, :], start=False, stop=True)
                    c.inc(i, "mm", 1)

                @blk.vector
                def _(v, oc=oc, zy_vv=zy_vv):
                    c.wait(v, "mm")
                    if oc == 0:
                        c.wait(v, "vv", zy_vv)
                    v.tensor_tensor(scr_b[:, 0:TL], pb[oc % 4][:, :], y_fm[:, oc, :], Alu.add)
                    i = v.tensor_scalar(G_fm[:, oc, :], scr_b[:, 0:TL], CG, None, Alu.mult)
                    c.inc(i, "vv", 2)

            for oc in range(8):
                transpose_group(G_tm[:, 0:4, oc * 128:(oc + 1) * 128],
                                [G_fm[:, oc, ts * 128:(ts + 1) * 128] for ts in range(4)],
                                par=oc % 2)
            G_vv = c.val("vv")

            for ic in range(2):
                @blk.tensor
                def _(t, ic=ic, G_vv=G_vv):
                    if ic == 0:
                        c.wait(t, "vv", G_vv)
                    bank = pb[ic]
                    i = None
                    for kc in range(8):
                        i = t.matmul(bank[:, :], w2n_b[:, kc, ic * 128:(ic + 1) * 128],
                                     G_fm[:, kc, :], start=(kc == 0), stop=(kc == 7))
                    c.inc(i, "mm", 1)

                @blk.vector
                def _(v, ic=ic):
                    c.wait(v, "mm")
                    i = v.tensor_tensor(da_fm[:, ic, :], pb[ic][:, :], gp_b[:, ic, :], Alu.mult)
                    c.inc(i, "vv", 1)

            for ic in range(2):
                transpose_group(da_tm[:, 0:4, ic * 128:(ic + 1) * 128],
                                [da_fm[:, ic, ts * 128:(ts + 1) * 128] for ts in range(4)],
                                par=ic)
            da_vv = c.val("vv")

            gev0 = c.val("ev")
            for os_ in range(8):
                @blk.tensor
                def _(t, os_=os_, da_vv=da_vv, gev0=gev0):
                    if os_ == 0:
                        c.wait(t, "vv", da_vv)
                    if os_ >= 4:
                        c.wait(t, "ev", gev0 + os_ - 3)
                    bank = pb[os_ % 4]
                    i = None
                    for ts in range(4):
                        i = t.matmul(bank[:, 0:DI], z_tm[:, ts, os_ * 128:(os_ + 1) * 128],
                                     da_tm[:, ts, :], start=(ts == 0), stop=(ts == 3))
                    c.inc(i, "mm", 1)

                @blk.scalar
                def _(sc, os_=os_):
                    c.wait(sc, "mm")
                    c.inc(sc.activation(gst1[:, os_, :], pb[os_ % 4][:, 0:DI], AF.Copy), "ev", 1)

            gev1 = c.val("ev")
            for u in range(4):
                is_, cc_ = u % 2, u // 2

                @blk.tensor
                def _(t, is_=is_, cc_=cc_, u=u, gev1=gev1):
                    c.wait(t, "ev", gev1 - 3 + u)
                    bank = pb[u % 4]
                    i = None
                    for ts in range(4):
                        i = t.matmul(bank[:, :], h_tm[:, ts, is_ * 128:(is_ + 1) * 128],
                                     G_tm[:, ts, cc_ * 512:(cc_ + 1) * 512],
                                     start=(ts == 0), stop=(ts == 3))
                    c.inc(i, "mm", 1)

                @blk.scalar
                def _(sc, is_=is_, cc_=cc_, u=u):
                    c.wait(sc, "mm")
                    c.inc(sc.activation(gst2[:, is_, cc_ * 512:(cc_ + 1) * 512],
                                        pb[u % 4][:, :], AF.Copy), "ev", 1)

            @blk.tensor
            def _(t):
                i = None
                for ts in range(4):
                    i = t.matmul(pm[0:1, 0:DI], ones_b[:, 0:1], da_tm[:, ts, :],
                                 start=(ts == 0), stop=(ts == 3))
                c.inc(i, "mm", 1)

            @blk.scalar
            def _(sc):
                c.wait(sc, "mm")
                c.inc(sc.activation(gst3[0:1, 0:DI], pm[0:1, 0:DI], AF.Copy), "ev", 1)

            for cc_ in range(2):
                @blk.tensor
                def _(t, cc_=cc_):
                    c.wait(t, "ev", c.val("ev"))
                    i = None
                    for ts in range(4):
                        i = t.matmul(pm2[0:1, :], ones_b[:, 0:1], G_tm[:, ts, cc_ * 512:(cc_ + 1) * 512],
                                     start=(ts == 0), stop=(ts == 3))
                    c.inc(i, "mm", 1)

                @blk.scalar
                def _(sc, cc_=cc_):
                    c.wait(sc, "mm")
                    c.inc(sc.activation(gst3[0:1, 1024 + cc_ * 512:1024 + (cc_ + 1) * 512],
                                        pm2[0:1, :], AF.Copy), "ev", 1)
            grads_ev = c.val("ev")

            @blk.gpsimd
            def _(g, grads_ev=grads_ev):
                c.wait(g, "ev", grads_ev)
                c.inc(g.dma_start(
                    out=gbuf_in[0:512, :].rearrange("(os ph) (pl i) -> (ph pl) os i", os=8, ph=64, pl=2),
                    in_=gst1[:, :, :]), "st", 16)
                c.inc(g.dma_start(
                    out=gbuf_in[512:1024, :].rearrange("(i p two) cc -> p i (two cc)", i=2, p=128, two=2),
                    in_=gst2[:, :, :]), "st", 16)
                c.inc(g.dma_start(out=gbuf_in[1024:1028, :], in_=gst3[0:1, :]), "st", 16)
                c.wait(g, "st")
                i = g.collective_compute("AllReduce", Alu.add, replica_groups=[list(range(NCORE))],
                                         ins=[gbuf_in.opt()], outs=[gbuf_out.opt()])
                c.inc(i, "cc", 1)

            @blk.sync
            def _(sy):
                c.wait(sy, "cc")
                c.inc(sy.dma_start(
                    out=gr1[:, :, :],
                    in_=gbuf_out[0:512, :].rearrange("(os ph) (pl i) -> (ph pl) os i", os=8, ph=64, pl=2)), "ld", 16)
                c.inc(sy.dma_start(
                    out=gr2[:, :, :],
                    in_=gbuf_out[512:1024, :].rearrange("(i p two) cc -> p i (two cc)", i=2, p=128, two=2)), "ld", 16)
                c.inc(sy.dma_start(out=gr3[0:1, :], in_=gbuf_out[1024:1028, :]), "ld", 16)
            gr_ld = c.val("ld")

            @blk.vector
            def _(v, gr_ld=gr_ld):
                c.wait(v, "ld", gr_ld)
                v.scalar_tensor_tensor(w1T_f[:, :, :], gr1[:, :, :], -LR, w1T_f[:, :, :], Alu.mult, Alu.add)
                v.scalar_tensor_tensor(w2T_f[:, :, :], gr2[:, :, :], -LR, w2T_f[:, :, :], Alu.mult, Alu.add)
                v.scalar_tensor_tensor(b1_f[0:1, :], gr3[0:1, 0:DI], -LR, b1_f[0:1, :], Alu.mult, Alu.add)
                i = v.scalar_tensor_tensor(b2_f[0:1, :], gr3[0:1, 1024:2048], -LR, b2_f[0:1, :], Alu.mult, Alu.add)
                c.inc(i, "vv", 1)

            @blk.scalar
            def _(sc, gr_ld=gr_ld):
                c.wait(sc, "ld", gr_ld)
                c.inc(sc.activation(gr2b[:, :, :], gr2[:, :, :], AF.Copy), "ev", 1)
            g2b_ev = c.val("ev")

            for is_ in range(2):
                @blk.tensor
                def _(t, is_=is_, g2b_ev=g2b_ev):
                    if is_ == 0:
                        c.wait(t, "ev", g2b_ev)
                    else:
                        c.wait(t, "vv", c.val("vv"))
                    i = None
                    for u in range(8):
                        i = t.transpose(pt[is_][:, u, :],
                                        gr2b[:, is_, u * 128:(u + 1) * 128], ident_b[:, :])
                    c.inc(i, "mm", 1)

                @blk.vector
                def _(v, is_=is_):
                    c.wait(v, "mm")
                    i = v.tensor_copy(gst4[:, 0:8, is_ * 128:(is_ + 1) * 128], pt[is_][:, :, :])
                    c.inc(i, "vv", 1)

            @blk.vector
            def _(v):
                i = v.scalar_tensor_tensor(w2n_f[:, :, :], gst4[:, :, :], -LR, w2n_f[:, :, :], Alu.mult, Alu.add)
                c.inc(i, "vv", 1)
            updv2 = c.val("vv")

            @blk.scalar
            def _(sc, updv2=updv2):
                c.wait(sc, "vv", updv2)
                sc.activation(w1T_b[:, :, :], w1T_f[:, :, :], AF.Copy)
                sc.activation(w2T_b[:, :, :], w2T_f[:, :, :], AF.Copy)
                sc.activation(b1_b[0:1, :], b1_f[0:1, :], AF.Copy)
                sc.activation(b2_b[0:1, :], b2_f[0:1, :], AF.Copy)
                c.inc(sc.activation(w2n_b[:, :, :], w2n_f[:, :, :], AF.Copy), "ev", 1)
            params_ev = c.val("ev")

            for ic in range(2):
                @blk.tensor
                def _(t, ic=ic, params_ev=params_ev):
                    if ic == 0:
                        c.wait(t, "ev", params_ev)
                    bank = pb[ic]
                    for kc in range(8):
                        t.matmul(bank[:, :], w1T_b[:, kc, ic * 128:(ic + 1) * 128],
                                 q_fm[:, kc, :], start=(kc == 0), stop=False)
                    i = t.matmul(bank[:, :], b1_b[0:1, ic * 128:(ic + 1) * 128],
                                 ones_b[0:1, :], start=False, stop=True)
                    c.inc(i, "mm", 1)

                @blk.scalar
                def _(sc, ic=ic):
                    c.wait(sc, "mm")
                    c.inc(sc.activation(h2_fm[:, ic, :], pb[ic][:, :], AF.Gelu), "ev", 1)
            h2_ev = c.val("ev")

            ovv0 = c.val("vv")
            for n in range(8):
                tc, occ = divmod(n, 2)

                @blk.tensor
                def _(t, tc=tc, occ=occ, n=n, h2_ev=h2_ev, ovv0=ovv0):
                    if n == 0:
                        c.wait(t, "ev", h2_ev)
                    if n >= 4:
                        c.wait(t, "vv", ovv0 + n - 3)
                    bank = pb[n % 4]
                    for ic in range(2):
                        t.matmul(bank[:, :], h2_fm[:, ic, tc * 128:(tc + 1) * 128],
                                 w2T_b[:, ic, occ * 512:(occ + 1) * 512],
                                 start=(ic == 0), stop=False)
                    t.matmul(bank[:, :], ones_b[0:1, 0:128],
                             b2_b[0:1, occ * 512:(occ + 1) * 512], start=False, stop=False)
                    i = None
                    for u in range(4):
                        i = t.matmul(bank[:, u * 128:(u + 1) * 128],
                                     q_fm[:, occ * 4 + u, tc * 128:(tc + 1) * 128],
                                     ident_b[:, :], start=False, stop=(u == 3))
                    c.inc(i, "mm", 1)

                @blk.vector
                def _(v, tc=tc, occ=occ, n=n):
                    c.wait(v, "mm")
                    i = v.tensor_tensor(xs[:, tc, occ * 512:(occ + 1) * 512],
                                        pb[n % 4][:, :],
                                        xs[:, tc, occ * 512:(occ + 1) * 512], Alu.add)
                    c.inc(i, "vv", 1)

            @blk.gpsimd
            def _(g, s=s):
                c.wait(g, "vv")
                c.inc(g.dma_start(
                    out=xres[s * TL:(s + 1) * TL, :].rearrange("(t p) o -> p t o", p=128),
                    in_=xs[:, :, :]), "st", 16)

            barrier()

            if s + 1 < nsteps:
                @blk.sync
                def _(sy, s=s):
                    c.inc(sy.dma_start(
                        out=xs[:, :, :],
                        in_=x_in[(s + 1) * TL:(s + 2) * TL, :].rearrange("(t p) o -> p t o", p=128)),
                        "ld", 16)
                ld_xs = c.val("ld")

    def phase_dequant():
        arf.reset()
        arb.reset()
        wchunk = arf.take(2, FF)
        sblk = arf.take(96)
        isblk = arf.take(96)
        sfull = arf.take(FF)
        isfull = arf.take(FF)
        tq_ = arf.take(FF)
        assert arf.off <= AF_ELE, arf.off
        dqo = arb.take(2, FF)
        dqst = arb.take(2, 8, 128)

        jobs = ([("g", oc, wgate[oc * 128:(oc + 1) * 128, :], D) for oc in range(24)]
                + [("u", oc, wup[oc * 128:(oc + 1) * 128, :], D) for oc in range(24)]
                + [("d", oc, wdown[oc * 128:(oc + 1) * 128, :], FF) for oc in range(8)])
        dq_vv = {}
        st_done = {}

        for j, (kind, oc, src_, w) in enumerate(jobs):
            nblk = w // 32
            nct = w // 128

            @blk.sync
            def _(sy, j=j, src_=src_, w=w):
                if j >= 2:
                    c.wait(sy, "vv", dq_vv[j - 2])
                c.inc(sy.dma_start(out=wchunk[:, j % 2, 0:w], in_=src_), "ld", 16)
            ldt = c.val("ld")

            @blk.vector
            def _(v, j=j, w=w, nblk=nblk, ldt=ldt):
                c.wait(v, "ld", ldt)
                c.drain(v, v.tensor_reduce(
                    sblk[:, 0:nblk].rearrange("p (b o) -> p b o", o=1),
                    wchunk[:, j % 2, 0:w].rearrange("p (b q) -> p b q", q=32),
                    AX, Alu.max, apply_absolute_value=True))
                c.drain(v, v.tensor_scalar(sblk[:, 0:nblk], sblk[:, 0:nblk],
                                           1.0 / 7.0, 1e-12, Alu.mult, Alu.max))
                c.drain(v, v.reciprocal(isblk[:, 0:nblk], sblk[:, 0:nblk]))
                v.tensor_copy(sfull[:, 0:w].rearrange("p (b q) -> p b q", q=32),
                              sblk[:, 0:nblk].rearrange("p (b o) -> p b o", o=1).broadcast_to([128, nblk, 32]))
                c.drain(v, v.tensor_copy(isfull[:, 0:w].rearrange("p (b q) -> p b q", q=32),
                                         isblk[:, 0:nblk].rearrange("p (b o) -> p b o", o=1).broadcast_to([128, nblk, 32])))
                v.tensor_tensor(tq_[:, 0:w], wchunk[:, j % 2, 0:w], isfull[:, 0:w], Alu.mult)
                v.tensor_scalar(tq_[:, 0:w], tq_[:, 0:w], MAGIC, MAGIC, Alu.add, Alu.subtract)
                i = v.tensor_tensor(dqo[:, j % 2, 0:w], tq_[:, 0:w], sfull[:, 0:w], Alu.mult)
                dq_vv[j] = c.inc(i, "vv", 1)

            for grp in range((nct + 7) // 8):
                n = min(8, nct - grp * 8)

                @blk.tensor
                def _(t, j=j, grp=grp, n=n):
                    c.wait(t, "vv", c.val("vv"))
                    if j >= 2 and grp == 0:
                        c.wait(t, "st", st_done[j - 2])
                    i = None
                    for u in range(n):
                        i = t.transpose(pt[j % 2][:, u, :],
                                        dqo[:, j % 2, (grp * 8 + u) * 128:(grp * 8 + u + 1) * 128],
                                        ident_b[:, :])
                    c.inc(i, "mm", 1)

                @blk.vector
                def _(v, j=j, grp=grp, n=n):
                    c.wait(v, "mm")
                    if grp >= 1:
                        c.wait(v, "st", st_done[j])
                    i = v.tensor_copy(dqst[:, j % 2, 0:n, :], pt[j % 2][:, 0:n, :])
                    c.inc(i, "vv", 1)

                @blk.gpsimd
                def _(g, j=j, kind=kind, oc=oc, grp=grp, n=n):
                    c.wait(g, "vv")
                    if kind == "g":
                        dst = wgT_dq[oc].rearrange("k p f -> p k f")
                    elif kind == "u":
                        dst = wuT_dq[oc].rearrange("k p f -> p k f")
                    else:
                        ic0 = grp * 8
                        dst = wdT_dq[ic0:ic0 + n, :, oc * 128:(oc + 1) * 128].rearrange("i p f -> p i f")
                    st_done[j] = c.inc(g.dma_start(out=dst, in_=dqst[:, j % 2, 0:n, :]), "st", 16)

        barrier()

    def phase_ffn(src, dst, mxb):
        arf.reset()
        arb.reset()
        p3_vv, p4_vv, p5_vv, p5_st, p5_mm, p1_mm = [], [], [], [], [], []
        xtile = arf.take(2, D)
        xc_s = arf.take(D)
        sq_s = arf.take(D)
        t3a = arf.take(4096)
        t1q = t3a
        t3b = arf.take(4096)
        ost = arf.take(2, 2, D)
        xr4 = arf.at(0, 2, 2, D)
        assert arf.off <= AF_ELE, arf.off
        xnT_f = arb.take(8, 4096)
        wslab = arb.take(2, 8, 128)
        evslab = arb.take(2, 4096)
        xnstg = arb.take(D)
        mcastb = arb.take(128)[:, 0:1]
        assert arb.off <= AB_ELE, arb.off
        ldg = arb.at(0, 2, 4096)
        ldu = arb.at(8192, 2, 4096)
        g2sl = arb.at(16384, 2, 4096)
        t4sl = arb.at(24576, 2, 4096)
        wdT_all = arb.at(0, 24, D)
        oqin = arb.at(24576, 2, 24, 256)
        oqb = arb.at(36864, 24, 256)

        @blk.vector
        def _(v):
            v.memset(colw[:, 24:27], 0.0)
            i = v.memset(colw[:, 27:28], 1.0)
            c.inc(i, "vv", 1)

        p1_vv = []
        for tc in range(32):
            @blk.sync
            def _(sy, tc=tc):
                if tc >= 2:
                    c.wait(sy, "vv", p1_vv[tc - 2])
                c.inc(sy.dma_start(out=xtile[:, tc % 2, :],
                                   in_=src[tc * 128:(tc + 1) * 128, :]), "ld", 16)
            ldt = c.val("ld")

            @blk.vector
            def _(v, tc=tc, ldt=ldt):
                c.wait(v, "ld", ldt)
                v.tensor_tensor(sq_s[:, :], xtile[:, tc % 2, :], xtile[:, tc % 2, :], Alu.mult)
                v.tensor_reduce(colw[:, 0:1], xtile[:, tc % 2, :], AX, Alu.add)
                c.drain(v, v.tensor_reduce(colw[:, 2:3], sq_s[:, :], AX, Alu.add))
                v.tensor_scalar(colw[:, 1:2], colw[:, 0:1], 1.0 / D, None, Alu.mult)
                c.drain(v, v.tensor_scalar(colw[:, 2:3], colw[:, 2:3], 1.0 / D, EPS, Alu.mult, Alu.add))
                c.drain(v, v.tensor_tensor(colw[:, 5:6], colw[:, 1:2], colw[:, 1:2], Alu.mult))
                i = v.tensor_tensor(colw[:, 2:3], colw[:, 2:3], colw[:, 5:6], Alu.subtract)
                c.inc(i, "vv", 1)

            @blk.scalar
            def _(sc):
                c.wait(sc, "vv")
                c.inc(sc.activation(colw[:, 3:4], colw[:, 2:3], AF.Sqrt), "ev", 1)

            @blk.vector
            def _(v, tc=tc):
                c.wait(v, "ev")
                if tc >= 1:
                    c.wait(v, "mm", p1_mm[tc - 1])
                c.drain(v, v.reciprocal(colw[:, 3:4], colw[:, 3:4]))
                c.drain(v, v.scalar_tensor_tensor(colw[:, 4:5], colw[:, 1:2], -1.0,
                                                  colw[:, 3:4], Alu.mult, Alu.mult))
                v.tensor_scalar(xnstg[:, :], xtile[:, tc % 2, :],
                                colw[:, 3:4], colw[:, 4:5], Alu.mult, Alu.add)
                c.drain(v, v.reduce_max(colw[:, 8:9], xnstg[:, :], AX, apply_absolute_value=True))
                i = v.tensor_tensor(colw[:, 24:25], colw[:, 24:25], colw[:, 8:9], Alu.max)
                p1_vv.append(c.inc(i, "vv", 1))

            @blk.tensor
            def _(t, tc=tc):
                c.wait(t, "vv", c.val("vv"))
                i = None
                for kc in range(8):
                    i = t.transpose(pt[tc % 2][:, kc, :], xnstg[:, kc * 128:(kc + 1) * 128], ident_b[:, :])
                p1_mm.append(c.inc(i, "mm", 1))

            @blk.vector
            def _(v, tc=tc):
                c.wait(v, "mm")
                i = v.tensor_copy(xnT_f[:, 0:8, tc * 128:(tc + 1) * 128], pt[tc % 2][:, :, :])
                c.inc(i, "vv", 1)

        def gmax(acc_col, k, out_col):
            @blk.vector
            def _(v):
                c.wait(v, "vv", c.val("vv"))
                i = v.tensor_copy(mcastb, colw[:, acc_col:acc_col + 1])
                c.inc(i, "vv", 1)

            @blk.tensor
            def _(t):
                c.wait(t, "vv", c.val("vv"))
                c.wait(t, "ev", c.val("ev"))
                i = t.matmul(pm[0:1, 0:128], mcastb, ident_b[:, :], start=True, stop=True)
                c.inc(i, "mm", 1)

            @blk.vector
            def _(v):
                c.wait(v, "mm")
                i = v.reduce_max(colw[0:1, 8:9], pm[0:1, 0:128], AX)
                c.inc(i, "vv", 1)

            @blk.gpsimd
            def _(g, k=k):
                c.wait(g, "vv")
                c.inc(g.dma_start(out=mxi[k], in_=colw[0:1, 8:9]), "st", 16)
                c.wait(g, "st")
                i = g.collective_compute("AllReduce", Alu.max, replica_groups=[list(range(NCORE))],
                                         ins=[mxi[k].opt()], outs=[mxo[k].opt()])
                c.inc(i, "cc", 1)

            @blk.sync
            def _(sy, k=k, out_col=out_col):
                c.wait(sy, "cc")
                c.inc(sy.dma_start(out=colw[:, out_col:out_col + 1],
                                   in_=mxo[k][0:1, 0:1].broadcast_to([128, 1])), "ld", 16)
            return c.val("ld")

        def scale_ops(ldt, s_col, inv_col=None, extra=None):
            @blk.vector
            def _(v):
                c.wait(v, "ld", ldt)
                c.drain(v, v.tensor_scalar(colw[:, s_col:s_col + 1], colw[:, s_col:s_col + 1],
                                           1.0 / 32767.0, 1e-12, Alu.mult, Alu.max))
                if inv_col is not None:
                    c.drain(v, v.reciprocal(colw[:, inv_col:inv_col + 1], colw[:, s_col:s_col + 1]))
                if extra is not None:
                    extra(v)
                i = v.engine_nop()
                c.inc(i, "pp", 1)
            return c.val("pp")

        pp_s0 = scale_ops(gmax(24, mxb + 0, 10), 10, 11)
        barrier()

        for kc in range(8):
            @blk.scalar
            def _(sc, kc=kc):
                if kc > 0:
                    c.wait(sc, "vv", c.val("vv"))
                c.inc(sc.activation(t1q[:, :], xnT_f[:, kc, :], AF.Identity,
                                    bias=consts[:, 0:1], scale=colw[:, 11:12]), "ev", 1)

            @blk.vector
            def _(v, kc=kc):
                c.wait(v, "ev")
                i = v.tensor_scalar(xnT_f[:, kc, :], t1q[:, :], MAGIC, colw[:, 10:11],
                                    Alu.subtract, Alu.mult)
                c.inc(i, "vv", 1)
        barrier()

        @blk.vector
        def _(v):
            i = v.memset(colw[:, 24:26], 0.0)
            c.inc(i, "vv", 1)
        barrier()

        for widx, (cache, spill, acc_col) in enumerate(((wgT_dq, gate_s, 24), (wuT_dq, up_s, 25))):
            mm_sp = {}
            ev_b0 = c.val("ev")
            st_b0 = c.val("st")
            for i_ in range(24):
                @blk.sync
                def _(sy, i_=i_, cache=cache):
                    if i_ >= 2:
                        c.wait(sy, "mm", mm_sp[i_ - 2])
                    c.inc(sy.dma_start(out=wslab[:, i_ % 2, :, :],
                                       in_=cache[i_].rearrange("k p f -> p k f")), "ld", 16)
                ldt = c.val("ld")

                @blk.tensor
                def _(t, i_=i_, ldt=ldt, ev_b0=ev_b0):
                    c.wait(t, "ld", ldt)
                    i2 = None
                    for tch in range(8):
                        n = i_ * 8 + tch
                        if n >= 4:
                            c.wait(t, "ev", ev_b0 + n - 3)
                        bank = pb[n % 4]
                        for kc in range(8):
                            i2 = t.matmul(bank[:, :], wslab[:, i_ % 2, kc, :],
                                          xnT_f[:, kc, tch * 512:(tch + 1) * 512],
                                          start=(kc == 0), stop=(kc == 7))
                        c.inc(i2, "mm", 1)
                mm_sp[i_] = c.val("mm")

                @blk.scalar
                def _(sc, i_=i_, st_b0=st_b0):
                    for tch in range(8):
                        c.wait(sc, "mm", mm_sp[i_] - 7 + tch)
                        if i_ >= 2 and tch == 0:
                            c.wait(sc, "st", st_b0 + 16 * (i_ - 1))
                        n = i_ * 8 + tch
                        c.inc(sc.activation(evslab[:, i_ % 2, tch * 512:(tch + 1) * 512],
                                            pb[n % 4][:, :], AF.Copy), "ev", 1)

                @blk.vector
                def _(v, i_=i_, acc_col=acc_col):
                    for tch in range(8):
                        c.wait(v, "ev")
                        c.drain(v, v.reduce_max(colw[:, 8:9],
                                                evslab[:, i_ % 2, tch * 512:(tch + 1) * 512],
                                                AX, apply_absolute_value=True))
                        i2 = v.tensor_tensor(colw[:, acc_col:acc_col + 1],
                                             colw[:, acc_col:acc_col + 1], colw[:, 8:9], Alu.max)
                        c.inc(i2, "vv", 1)

                @blk.gpsimd
                def _(g, i_=i_, spill=spill):
                    c.wait(g, "ev", c.val("ev"))
                    c.inc(g.dma_start(out=spill[i_], in_=evslab[:, i_ % 2, :]), "st", 16)
            barrier()
            if widx == 0:
                def ex2(v):
                    v.tensor_scalar(colw[:, 20:21], colw[:, 14:15], -MAGIC, None, Alu.mult)
                    v.tensor_scalar(colw[:, 21:22], colw[:, 14:15], 1.0 / 65536.0, None, Alu.mult)
                scale_ops(gmax(24, mxb + 1, 14), 14, 15, extra=ex2)
            else:
                def ex1(g):
                    pass
                scale_ops(gmax(25, mxb + 2, 12), 12, 13)
        barrier()

        @blk.vector
        def _(v):
            i = v.memset(colw[:, 24:25], 0.0)
            c.inc(i, "vv", 1)
        for i_ in range(24):
            @blk.sync
            def _(sy, i_=i_):
                if i_ >= 2:
                    c.wait(sy, "vv", p3_vv[i_ - 2])
                c.inc(sy.dma_start(out=ldg[:, i_ % 2, :], in_=gate_s[i_]), "ld", 16)
            ldt = c.val("ld")

            @blk.scalar
            def _(sc, i_=i_, ldt=ldt):
                c.wait(sc, "ld", ldt)
                if i_ >= 1:
                    c.wait(sc, "vv", c.val("vv"))
                sc.activation(t3a[:, :], ldg[:, i_ % 2, :], AF.Identity,
                              bias=consts[:, 0:1], scale=colw[:, 15:16])
                sc.activation(t3b[:, :], t3a[:, :], AF.Sigmoid,
                              bias=colw[:, 20:21], scale=colw[:, 14:15])
                i2 = sc.activation(t3b[:, :], t3b[:, :], AF.Identity,
                                   bias=consts[:, 0:1], scale=consts[:, 2:3])
                c.inc(i2, "ev", 1)

            @blk.vector
            def _(v, i_=i_):
                c.wait(v, "ev")
                v.tensor_scalar(t3b[:, :], t3b[:, :], MAGIC, 65535.0, Alu.subtract, Alu.min)
                v.tensor_scalar(t3b[:, :], t3b[:, :], colw[:, 21:22], None, Alu.mult)
                v.scalar_tensor_tensor(g2sl[:, i_ % 2, :], t3a[:, :], MAGIC, t3b[:, :],
                                       Alu.subtract, Alu.mult)
                c.drain(v, v.reduce_max(colw[:, 8:9], g2sl[:, i_ % 2, :], AX,
                                        apply_absolute_value=True))
                i2 = v.tensor_tensor(colw[:, 24:25], colw[:, 24:25], colw[:, 8:9], Alu.max)
                p3_vv.append(c.inc(i2, "vv", 1))

            @blk.gpsimd
            def _(g, i_=i_):
                c.wait(g, "vv")
                c.inc(g.dma_start(out=g2_s[i_], in_=g2sl[:, i_ % 2, :]), "st", 16)
        barrier()

        def ex3(v):
            v.tensor_tensor(colw[:, 17:18], colw[:, 12:13], colw[:, 16:17], Alu.mult)
        scale_ops(gmax(24, mxb + 3, 16), 16, None, extra=ex3)
        @blk.vector
        def _(v):
            c.wait(v, "pp", c.val("pp"))
            c.drain(v, v.reciprocal(colw[:, 22:23], colw[:, 16:17]))
            i = v.engine_nop()
            c.inc(i, "pp", 1)
        barrier()

        @blk.vector
        def _(v):
            i = v.memset(colw[:, 24:25], 0.0)
            c.inc(i, "vv", 1)
        for i_ in range(24):
            @blk.sync
            def _(sy, i_=i_):
                if i_ >= 2:
                    c.wait(sy, "vv", p4_vv[i_ - 2])
                c.inc(sy.dma_start(out=ldg[:, i_ % 2, :], in_=g2_s[i_]), "ld", 16)
                c.inc(sy.dma_start(out=ldu[:, i_ % 2, :], in_=up_s[i_]), "ld", 16)
            ldt = c.val("ld")

            @blk.scalar
            def _(sc, i_=i_, ldt=ldt):
                c.wait(sc, "ld", ldt)
                if i_ >= 1:
                    c.wait(sc, "vv", c.val("vv"))
                sc.activation(t3a[:, :], ldu[:, i_ % 2, :], AF.Identity,
                              bias=consts[:, 0:1], scale=colw[:, 13:14])
                i2 = sc.activation(t3b[:, :], ldg[:, i_ % 2, :], AF.Identity,
                                   bias=consts[:, 0:1], scale=colw[:, 22:23])
                c.inc(i2, "ev", 1)

            @blk.vector
            def _(v, i_=i_):
                c.wait(v, "ev")
                v.tensor_scalar(t3a[:, :], t3a[:, :], MAGIC, colw[:, 17:18],
                                Alu.subtract, Alu.mult)
                v.scalar_tensor_tensor(t4sl[:, i_ % 2, :], t3b[:, :], MAGIC, t3a[:, :],
                                       Alu.subtract, Alu.mult)
                c.drain(v, v.reduce_max(colw[:, 8:9], t4sl[:, i_ % 2, :], AX,
                                        apply_absolute_value=True))
                i2 = v.tensor_tensor(colw[:, 24:25], colw[:, 24:25], colw[:, 8:9], Alu.max)
                p4_vv.append(c.inc(i2, "vv", 1))

            @blk.gpsimd
            def _(g, i_=i_):
                c.wait(g, "vv")
                c.inc(g.dma_start(out=t4_s[i_], in_=t4sl[:, i_ % 2, :]), "st", 16)
        barrier()
        scale_ops(gmax(24, mxb + 4, 18), 18, 19)
        barrier()

        @blk.sync
        def _(sy):
            c.inc(sy.dma_start(out=wdT_all[:, :, :],
                               in_=wdT_dq.rearrange("i p o -> p i o")), "ld", 16)
        wd_ld = c.val("ld")

        for tb in range(16):
            @blk.sync
            def _(sy, tb=tb):
                if tb >= 2:
                    c.wait(sy, "vv", p5_vv[tb - 2])
                c.inc(sy.dma_start(out=oqin[:, tb % 2, :, :],
                                   in_=t4_s[:, :, tb * 256:(tb + 1) * 256].rearrange("i p t -> p i t")), "ld", 16)
                c.inc(sy.dma_start(out=xr4[:, tb % 2, :, :],
                                   in_=src[tb * 256:(tb + 1) * 256, :].rearrange("(t p) o -> p t o", p=128)), "ld", 16)
            ldt = c.val("ld")

            @blk.scalar
            def _(sc, tb=tb, ldt=ldt):
                c.wait(sc, "ld", ldt)
                if tb >= 1:
                    c.wait(sc, "vv", c.val("vv"))
                i2 = sc.activation(t3a[:, 0:4096].rearrange("p (a b) -> p a b", a=16)
                                   if False else t3a[:, :], oqin[:, tb % 2, :, :].rearrange("p a b -> p (a b)")[:, 0:4096],
                                   AF.Identity, bias=consts[:, 0:1], scale=colw[:, 19:20])
                i3 = sc.activation(t3b[:, 0:2048], oqin[:, tb % 2, :, :].rearrange("p a b -> p (a b)")[:, 4096:6144],
                                   AF.Identity, bias=consts[:, 0:1], scale=colw[:, 19:20])
                c.inc(i3, "ev", 1)

            @blk.vector
            def _(v, tb=tb):
                c.wait(v, "ev")
                if tb >= 1:
                    c.wait(v, "mm", p5_mm[tb - 1])
                if tb >= 2:
                    c.wait(v, "st", p5_st[tb - 2])
                v.tensor_scalar(oqb[:, :, :].rearrange("p a b -> p (a b)")[:, 0:4096],
                                t3a[:, :], MAGIC, None, Alu.subtract)
                i2 = v.tensor_scalar(oqb[:, :, :].rearrange("p a b -> p (a b)")[:, 4096:6144],
                                     t3b[:, 0:2048], MAGIC, None, Alu.subtract)
                c.inc(i2, "vv", 1)
            oq_vv = c.val("vv")

            for u in range(4):
                ts2, occ = divmod(u, 2)

                @blk.tensor
                def _(t, tb=tb, ts2=ts2, occ=occ, u=u, oq_vv=oq_vv, wd_ld=wd_ld):
                    if tb == 0 and u == 0:
                        c.wait(t, "ld", wd_ld)
                    c.wait(t, "vv", oq_vv)
                    if tb >= 1:
                        c.wait(t, "vv", c.val("vv"))
                    bank = pb[u]
                    i2 = None
                    for ic in range(24):
                        i2 = t.matmul(bank[:, :], oqb[:, ic, ts2 * 128:(ts2 + 1) * 128],
                                      wdT_all[:, ic, occ * 512:(occ + 1) * 512],
                                      start=(ic == 0), stop=(ic == 23))
                    mv = c.inc(i2, "mm", 1)
                    if u == 3:
                        p5_mm.append(mv)

                @blk.vector
                def _(v, tb=tb, ts2=ts2, occ=occ, u=u):
                    c.wait(v, "mm")
                    i2 = v.scalar_tensor_tensor(ost[:, tb % 2, ts2, occ * 512:(occ + 1) * 512],
                                                pb[u][:, :], colw[:, 18:19],
                                                xr4[:, tb % 2, ts2, occ * 512:(occ + 1) * 512],
                                                Alu.mult, Alu.add)
                    if u == 3:
                        p5_vv.append(c.inc(i2, "vv", 1))
                    else:
                        c.inc(i2, "vv", 1)

            @blk.gpsimd
            def _(g, tb=tb):
                c.wait(g, "vv")
                p5_st.append(c.inc(g.dma_start(
                    out=dst[tb * 256:(tb + 1) * 256, :].rearrange("(t p) o -> p t o", p=128),
                    in_=ost[:, tb % 2, :, :]), "st", 16))
        barrier()


    def phase_attn_b():
        TWO_PI = 2.0 * PI

        def emit_ln2(xsrc_tile, xnstg_ap, xc_s2, sq_s2, tc_guard_mm=None):
            @blk.vector
            def _(v):
                c.drain(v, v.tensor_reduce(colw[:, 0:1], xsrc_tile, AX, Alu.add))
                c.drain(v, v.tensor_scalar(colw[:, 1:2], colw[:, 0:1], -1.0 / D, None, Alu.mult))
                i = v.tensor_scalar(xc_s2[:, :], xsrc_tile, colw[:, 1:2], None, Alu.add)
                c.inc(i, "vv", 1)

            @blk.scalar
            def _(sc):
                c.wait(sc, "vv")
                c.inc(sc.activation(sq_s2[:, :], xc_s2[:, :], AF.Square), "ev", 1)

            @blk.vector
            def _(v):
                c.wait(v, "ev")
                c.drain(v, v.tensor_reduce(colw[:, 2:3], sq_s2[:, :], AX, Alu.add))
                i = v.tensor_scalar(colw[:, 2:3], colw[:, 2:3], 1.0 / D, EPS, Alu.mult, Alu.add)
                c.inc(i, "vv", 1)

            @blk.scalar
            def _(sc):
                c.wait(sc, "vv")
                c.inc(sc.activation(colw[:, 3:4], colw[:, 2:3], AF.Sqrt), "ev", 1)

            @blk.vector
            def _(v, tc_guard_mm=tc_guard_mm):
                c.wait(v, "ev")
                if tc_guard_mm is not None:
                    c.wait(v, "mm", tc_guard_mm)
                c.drain(v, v.reciprocal(colw[:, 3:4], colw[:, 3:4]))
                i = v.tensor_scalar(xnstg_ap, xc_s2[:, :], colw[:, 3:4], None, Alu.mult)
                c.inc(i, "vv", 1)

        def build_rfp(rfp, tag):
            @blk.gpsimd
            def _(g):
                i = g.memset(rfp[:, :], 0.0)
                c.inc(i, "pp", 1)
                g.wait_ge(c.sem["pp"], c.cnt["pp"])
                c.inc(g.dma_start(out=rfp[0:64, 0:256], in_=rfTi), "cst", 16)
                c.inc(g.dma_start(out=rfp[64:128, 256:512], in_=rfTi), "cst", 16)

        def emit_trig_pair(src_f32, cos_dst, sin_dst, u1, u2, pre_ev, pre_mm):
            @blk.vector
            def _(v):
                c.wait(v, "ev", pre_ev)
                v.tensor_scalar(u1[:, :], src_f32, 1.0 / TWO_PI, MAGIC, Alu.mult, Alu.add)
                v.tensor_scalar(u1[:, :], u1[:, :], MAGIC, TWO_PI, Alu.subtract, Alu.mult)
                v.scalar_tensor_tensor(u1[:, :], u1[:, :], -1.0, src_f32, Alu.mult, Alu.add)
                v.tensor_scalar(u2[:, :], src_f32, PI / 2, 1.0 / TWO_PI, Alu.add, Alu.mult)
                v.tensor_scalar(u2[:, :], u2[:, :], MAGIC, MAGIC, Alu.add, Alu.subtract)
                v.tensor_scalar(u2[:, :], u2[:, :], TWO_PI, None, Alu.mult)
                i = v.scalar_tensor_tensor(u2[:, :], u2[:, :], -1.0, src_f32, Alu.mult, Alu.add)
                c.inc(i, "vv", 1)

            @blk.scalar
            def _(sc):
                c.wait(sc, "vv")
                c.wait(sc, "mm", pre_mm)
                sc.activation(sin_dst, u1[:, :], AF.Sin)
                i = sc.activation(cos_dst, u2[:, :], AF.Sin, bias=consts[:, 1:2])
                c.inc(i, "ev", 1)

        arf.reset()
        arb.reset()
        x2til = arf.take(4, D)
        xc_s2 = arf.take(D)
        sq_s2 = arf.take(D)
        sqs = arf.take(2, 512)
        nsqb = arf.take(64)
        u1_s = arf.take(2048)
        u2_s = arf.take(2048)
        kvstg = arf.take(16, 264)
        assert arf.off <= AF_ELE, arf.off
        wkTb = arb.take(8, D)
        wvTb = arb.take(8, D)
        xnT_ab = arb.take(8, TL)
        K_fm = arb.take(8, TL)
        V_tm = arb.take(4, D)
        xnstg2 = arb.take(D)
        projp = arb.take(8, 4, 512)
        Kfcsp = arb.take(2, 4, 1024)
        Vsbp = arb.take(2, 4, 2, 66)
        rfp = arb.take(512)
        assert arb.off <= AB_ELE, arb.off

        @blk.gpsimd
        def _(g):
            c.inc(g.dma_start(out=wkTb[:, :, :], in_=wkT.rearrange("(k p) o -> p k o", p=128)), "cst", 16)
            c.inc(g.dma_start(out=wvTb[:, :, :], in_=wvT.rearrange("(k p) o -> p k o", p=128)), "cst", 16)
        build_rfp(rfp, "s1")
        barrier()

        ck = {}
        for b in range(NB):
            @blk.sync
            def _(sy, b=b):
                c.inc(sy.dma_start(out=x2til[:, :, :],
                                   in_=xres[b * TL:(b + 1) * TL, :].rearrange("(t p) o -> p t o", p=128)),
                      "ld", 16)
            ldt = c.val("ld")

            @blk.vector
            def _(v, ldt=ldt):
                c.wait(v, "ld", ldt)
                i = None
                for tc in range(4):
                    v.tensor_tensor(sq_s2[:, :], x2til[:, tc, :], x2til[:, tc, :], Alu.mult)
                    v.tensor_reduce(colw[:, 40 + tc:41 + tc], x2til[:, tc, :], AX, Alu.add)
                    i = v.tensor_reduce(colw[:, 44 + tc:45 + tc], sq_s2[:, :], AX, Alu.add)
                c.drain(v, i)
                v.tensor_scalar(colw[:, 40:44], colw[:, 40:44], 1.0 / D, None, Alu.mult)
                c.drain(v, v.tensor_scalar(colw[:, 44:48], colw[:, 44:48], 1.0 / D, EPS, Alu.mult, Alu.add))
                c.drain(v, v.tensor_tensor(colw[:, 0:4], colw[:, 40:44], colw[:, 40:44], Alu.mult))
                i = v.tensor_tensor(colw[:, 44:48], colw[:, 44:48], colw[:, 0:4], Alu.subtract)
                c.inc(i, "vv", 1)

            @blk.scalar
            def _(sc):
                c.wait(sc, "vv")
                c.inc(sc.activation(colw[:, 44:48], colw[:, 44:48], AF.Sqrt), "ev", 1)
            lnsq2_ev = c.val("ev")

            for tc in range(4):
                @blk.vector
                def _(v, tc=tc, lnsq2_ev=lnsq2_ev):
                    if tc == 0:
                        c.wait(v, "ev", lnsq2_ev)
                        c.drain(v, v.reciprocal(colw[:, 44:48], colw[:, 44:48]))
                        c.drain(v, v.scalar_tensor_tensor(colw[:, 40:44], colw[:, 40:44], -1.0,
                                                          colw[:, 44:48], Alu.mult, Alu.mult))
                    else:
                        c.wait(v, "mm")
                    i = v.tensor_scalar(xnstg2[:, :], x2til[:, tc, :],
                                        colw[:, 44 + tc:45 + tc], colw[:, 40 + tc:41 + tc],
                                        Alu.mult, Alu.add)
                    c.inc(i, "vv", 1)

                @blk.tensor
                def _(t, tc=tc):
                    c.wait(t, "vv", c.val("vv"))
                    i = None
                    for kc in range(8):
                        i = t.transpose(pt[tc % 2][:, kc, :], xnstg2[:, kc * 128:(kc + 1) * 128], ident_b[:, :])
                    c.inc(i, "mm", 1)

                @blk.vector
                def _(v, tc=tc):
                    c.wait(v, "mm")
                    i = v.tensor_copy(xnT_ab[:, 0:8, tc * 128:(tc + 1) * 128], pt[tc % 2][:, :, :])
                    c.inc(i, "vv", 1)
            xn_vv = c.val("vv")

            @blk.gpsimd
            def _(g, b=b, xn_vv=xn_vv):
                c.wait(g, "vv", xn_vv)
                c.inc(g.dma_start(out=xnT_sp[b], in_=xnT_ab[:, :, :].rearrange("p a b -> p (a b)")), "st", 16)

            ev_k0 = c.val("ev")
            for oc in range(8):
                @blk.tensor
                def _(t, oc=oc, ev_k0=ev_k0):
                    if oc >= 4:
                        c.wait(t, "ev", ev_k0 + oc - 3)
                    bank = pb[oc % 4]
                    i = None
                    for kc in range(8):
                        i = t.matmul(bank[:, :], wkTb[:, kc, oc * 128:(oc + 1) * 128],
                                     xnT_ab[:, kc, :], start=(kc == 0), stop=(kc == 7))
                    c.inc(i, "mm", 1)

                @blk.scalar
                def _(sc, oc=oc):
                    c.wait(sc, "mm")
                    c.inc(sc.activation(K_fm[:, oc, :], pb[oc % 4][:, :], AF.Copy), "ev", 1)

            ev_v0 = c.val("ev")
            for u in range(8):
                ts_, occ = divmod(u, 2)

                @blk.tensor
                def _(t, ts_=ts_, occ=occ, u=u, ev_v0=ev_v0):
                    if u >= 4:
                        c.wait(t, "ev", ev_v0 + u - 3)
                    bank = pb[u % 4]
                    i = None
                    for kc in range(8):
                        i = t.matmul(bank[:, :], xnT_ab[:, kc, ts_ * 128:(ts_ + 1) * 128],
                                     wvTb[:, kc, occ * 512:(occ + 1) * 512],
                                     start=(kc == 0), stop=(kc == 7))
                    c.inc(i, "mm", 1)

                @blk.scalar
                def _(sc, ts_=ts_, occ=occ):
                    c.wait(sc, "mm")
                    c.inc(sc.activation(V_tm[:, ts_, occ * 512:(occ + 1) * 512],
                                        pb[u % 4][:, :], AF.Copy), "ev", 1)
            prep_ev = c.val("ev")

            for pr in range(8):
                @blk.tensor
                def _(t, pr=pr):
                    if pr == 0:
                        c.wait(t, "ev", prep_ev)
                        c.wait(t, "vv", c.val("vv"))
                    else:
                        c.wait(t, "ev", ck[("prev", b * 8 + pr - 1)])
                    i = None
                    for ts_ in range(4):
                        t.matmul(pb[ts_][:, 0:512], K_fm[:, pr, ts_ * 128:(ts_ + 1) * 128],
                                 rfp[:, :], start=True, stop=True)
                        i = t.matmul(pm[:, ts_ * 128:(ts_ + 1) * 128],
                                     K_fm[:, pr, ts_ * 128:(ts_ + 1) * 128],
                                     ident_b[:, :], start=True, stop=True)
                    ck[("prmm", b * 8 + pr)] = c.inc(i, "mm", 1)

                @blk.scalar
                def _(sc, pr=pr):
                    c.wait(sc, "mm", ck[("prmm", b * 8 + pr)])
                    if pr >= 2:
                        c.wait(sc, "vv", ck[("sqv", b * 8 + pr - 2)])
                    for ts_ in range(4):
                        sc.activation(projp[:, pr, ts_, :], pb[ts_][:, 0:512], AF.Copy)
                    i = sc.activation(sqs[:, pr % 2, :], pm[:, :], AF.Copy)
                    ck[("prev", b * 8 + pr)] = c.inc(i, "ev", 1)

                @blk.vector
                def _(v, pr=pr):
                    c.wait(v, "ev", ck[("prev", b * 8 + pr)])
                    v.tensor_tensor(sqs[:, pr % 2, :], sqs[:, pr % 2, :], sqs[:, pr % 2, :], Alu.mult)
                    i = None
                    for hh in range(2):
                        i = v.tensor_reduce(
                            nsqb[:, pr * 8 + 4 * hh:pr * 8 + 4 * hh + 4].rearrange("p (g o) -> p g o", o=1),
                            sqs[:, pr % 2, :].rearrange("p (a q) -> p a q", a=4)[:, :, hh * 64:(hh + 1) * 64],
                            AX, Alu.add)
                    ck[("sqv", b * 8 + pr)] = c.inc(i, "vv", 1)

            @blk.scalar
            def _(sc):
                c.wait(sc, "vv", ck[("sqv", b * 8 + 7)])
                i = sc.activation(nsqb[:, :], nsqb[:, :], AF.Exp, scale=-0.5)
                c.inc(i, "ev", 1)
            expb_ev = c.val("ev")

            @blk.vector
            def _(v, expb_ev=expb_ev):
                c.wait(v, "ev", expb_ev)
                c.drain(v, v.tensor_scalar(nsqb[:, :], nsqb[:, :], 1.0 / 16.0, None, Alu.mult))
                i = v.engine_nop()
                c.inc(i, "vv", 1)
            ksall_vv = c.val("vv")

            def s1_trig(pr, gi, par):
                @blk.vector
                def _(v):
                    if pr == 0:
                        c.wait(v, "vv", ksall_vv)
                    if gi >= 1:
                        c.wait(v, "ev", ck[("trge", gi - 1)])
                    src_ = projp[:, pr, :, :].rearrange("p a b -> p (a b)")
                    v.tensor_scalar(u1_s[:, :], src_, 1.0 / TWO_PI, MAGIC, Alu.mult, Alu.add)
                    v.tensor_scalar(u1_s[:, :], u1_s[:, :], MAGIC, TWO_PI, Alu.subtract, Alu.mult)
                    v.scalar_tensor_tensor(u1_s[:, :], u1_s[:, :], -1.0, src_, Alu.mult, Alu.add)
                    v.tensor_scalar(u2_s[:, :], src_, PI / 2, 1.0 / TWO_PI, Alu.add, Alu.mult)
                    v.tensor_scalar(u2_s[:, :], u2_s[:, :], MAGIC, MAGIC, Alu.add, Alu.subtract)
                    i = v.scalar_tensor_tensor(u2_s[:, :], u2_s[:, :], -TWO_PI, src_, Alu.mult, Alu.add)
                    ck[("trgv", gi)] = c.inc(i, "vv", 1)

            def s1_vs(pr, gi, par):
                @blk.vector
                def _(v):
                    if gi >= 2:
                        c.wait(v, "mm", ck[("kvmm", gi - 2)])
                    v.memset(Vsbp[:, par, :, :, 65:66], 0.0)
                    for ts_ in range(4):
                        for hh in range(2):
                            v.tensor_scalar(Vsbp[:, par, ts_, hh, 0:64],
                                            V_tm[:, ts_, (pr * 2 + hh) * 64:(pr * 2 + hh + 1) * 64],
                                            nsqb[:, pr * 8 + 4 * hh + ts_:pr * 8 + 4 * hh + ts_ + 1],
                                            None, Alu.mult)
                    i = None
                    for hh in range(2):
                        i = v.tensor_copy(Vsbp[:, par, :, hh, 64:65],
                                          nsqb[:, pr * 8 + 4 * hh:pr * 8 + 4 * hh + 4]
                                          .rearrange("p (g o) -> p g o", o=1))
                    ck[("vsv", gi)] = c.inc(i, "vv", 1)

            def s1_sin(pr, gi, par):
                @blk.scalar
                def _(sc):
                    c.wait(sc, "vv", ck[("trgv", gi)])
                    if gi >= 2:
                        c.wait(sc, "mm", ck[("kvmm", gi - 2)])
                    sc.activation(Kfcsp[:, par, :, :].rearrange("p a (h x) -> p a h x", h=2)[:, :, :, 256:512],
                                  u1_s[:, :], AF.Sin)
                    i = sc.activation(Kfcsp[:, par, :, :].rearrange("p a (h x) -> p a h x", h=2)[:, :, :, 0:256],
                                      u2_s[:, :], AF.Sin, bias=consts[:, 1:2])
                    ck[("trge", gi)] = c.inc(i, "ev", 1)

            def s1_kv(pr, gi, par):
                @blk.tensor
                def _(t):
                    c.wait(t, "ev", ck[("trge", gi)])
                    c.wait(t, "vv", ck[("vsv", gi)])
                    if gi >= 1:
                        c.wait(t, "vv", ck[("kvev", gi - 1)])
                    i = None
                    for hh in range(2):
                        bank = pm2 if hh == 0 else pb[0]
                        for fs in range(4):
                            for ts_ in range(4):
                                i = t.matmul(bank[:, fs * 66:(fs + 1) * 66],
                                             Kfcsp[:, par, ts_, hh * 512 + fs * 128:hh * 512 + (fs + 1) * 128],
                                             Vsbp[:, par, ts_, hh, :], start=(ts_ == 0), stop=(ts_ == 3))
                    ck[("kvmm", gi)] = c.inc(i, "mm", 1)

            def s1_kvev(pr, gi, par):
                @blk.vector
                def _(v):
                    c.wait(v, "mm", ck[("kvmm", gi)])
                    v.tensor_copy(kvstg[:, pr * 2, :], pm2[:, 0:264])
                    i = v.tensor_copy(kvstg[:, pr * 2 + 1, :], pb[0][:, 0:264])
                    ck[("kvev", gi)] = c.inc(i, "vv", 1)

            late1a = [s1_sin, s1_kv]
            early1 = [s1_trig, s1_vs]
            late1b = [s1_kvev]
            for it in range(9):
                if it >= 1:
                    pr = it - 1
                    for f in late1a:
                        f(pr, b * 8 + pr, (b * 8 + pr) % 2)
                if it < 8:
                    pr = it
                    for f in early1:
                        f(pr, b * 8 + pr, (b * 8 + pr) % 2)
                if it >= 1:
                    pr = it - 1
                    for f in late1b:
                        f(pr, b * 8 + pr, (b * 8 + pr) % 2)

            @blk.gpsimd
            def _(g, b=b):
                c.wait(g, "ev", c.val("ev"))
                c.wait(g, "vv", c.val("vv"))
                c.inc(g.dma_start(out=kv_in[b * 128:(b + 1) * 128, :], in_=kvstg[:, :, :]), "st", 16)
            barrier()

        @blk.gpsimd
        def _(g):
            c.wait(g, "st", c.val("st"))
            i = g.collective_compute("AllReduce", Alu.add, replica_groups=[list(range(NCORE))],
                                     ins=[kv_in.opt()], outs=[kv_out.opt()])
            c.inc(i, "cc", 1)
        barrier()

        arf.reset()
        arb.reset()
        sqs2 = arf.take(2, 512)
        nsqb2 = arf.take(64)
        u1_s = arf.take(2048)
        u2_s = arf.take(2048)
        xr2 = arf.take(4, D)
        ost2 = arf.take(4, D)
        szz = arf.take(2, 264)
        assert arf.off <= AF_ELE, arf.off
        wqTb = arb.take(8, D)
        woTb = arb.take(8, D)
        xnT_ab = arb.take(8, TL)
        Q_fm = arb.take(8, TL)
        projq_bf = arb.take(8, 4, 512)
        Qfcsp = arb.take(2, 8, 512)
        attT = arb.take(8, TL)
        kvbb = arb.take(16, 264)
        attstg2 = arb.take(4, 128)
        rfp = arb.take(512)
        assert arb.off <= AB_ELE, arb.off

        @blk.gpsimd
        def _(g):
            c.inc(g.dma_start(out=wqTb[:, :, :], in_=wqT.rearrange("(k p) o -> p k o", p=128)), "cst", 16)
            c.inc(g.dma_start(out=woTb[:, :, :], in_=woT.rearrange("(k p) o -> p k o", p=128)), "cst", 16)
        build_rfp(rfp, "s2")
        barrier()

        for b in range(NB):
            @blk.sync
            def _(sy, b=b):
                c.inc(sy.dma_start(out=xnT_ab[:, :, :].rearrange("p a b -> p (a b)"), in_=xnT_sp[b]), "ld", 16)
                c.inc(sy.dma_start(out=xr2[:, :, :],
                                   in_=xres[b * TL:(b + 1) * TL, :].rearrange("(t p) o -> p t o", p=128)), "ld", 16)
            ldt = c.val("ld")

            @blk.gpsimd
            def _(g, b=b):
                c.inc(g.dma_start(out=kvbb[:, :, :].rearrange("p a b -> p (a b)"),
                                  in_=kv_out[b * 128:(b + 1) * 128, :]), "cst", 16)
            kv_ld = c.val("cst")

            ev_q0 = c.val("ev")
            for oc in range(8):
                @blk.tensor
                def _(t, oc=oc, ev_q0=ev_q0, ldt=ldt):
                    if oc == 0:
                        c.wait(t, "ld", ldt)
                    if oc >= 4:
                        c.wait(t, "ev", ev_q0 + oc - 3)
                    bank = pb[oc % 4]
                    i = None
                    for kc in range(8):
                        i = t.matmul(bank[:, :], wqTb[:, kc, oc * 128:(oc + 1) * 128],
                                     xnT_ab[:, kc, :], start=(kc == 0), stop=(kc == 7))
                    c.inc(i, "mm", 1)

                @blk.scalar
                def _(sc, oc=oc):
                    c.wait(sc, "mm")
                    c.inc(sc.activation(Q_fm[:, oc, :], pb[oc % 4][:, :], AF.Copy), "ev", 1)
            prep_ev = c.val("ev")

            if dbg and b == 0:
                dbg7 = nc.dram_tensor("dbg7", [128, 8192], F32, kind="ExternalOutput").ap()

                @blk.gpsimd
                def _(g, prep_ev=prep_ev):
                    c.wait(g, "ev", prep_ev)
                    c.inc(g.dma_start(out=dbg7[:, 0:4096], in_=Q_fm[:, :, :].rearrange("p a b -> p (a b)")), "st", 16)
                    c.inc(g.dma_start(out=dbg7[:, 4096:8192], in_=xnT_ab[:, :, :].rearrange("p a b -> p (a b)")), "st", 16)

            for pr in range(8):
                gi = 64 + b * 8 + pr

                @blk.tensor
                def _(t, pr=pr, gi=gi):
                    if pr == 0:
                        c.wait(t, "ev", prep_ev)
                        c.wait(t, "vv", c.val("vv"))
                    else:
                        c.wait(t, "ev", ck[("prev", gi - 1)])
                    i = None
                    for blk_ in range(4):
                        t.matmul(pb[blk_][:, 0:512], rfp[:, blk_ * 128:(blk_ + 1) * 128],
                                 Q_fm[:, pr, :], start=True, stop=True)
                        i = t.matmul(pm[:, blk_ * 128:(blk_ + 1) * 128],
                                     Q_fm[:, pr, blk_ * 128:(blk_ + 1) * 128],
                                     ident_b[:, :], start=True, stop=True)
                    ck[("prmm", gi)] = c.inc(i, "mm", 1)

                @blk.scalar
                def _(sc, pr=pr, gi=gi):
                    c.wait(sc, "mm", ck[("prmm", gi)])
                    if pr >= 2:
                        c.wait(sc, "vv", ck[("sqv", gi - 2)])
                    for blk_ in range(4):
                        sc.activation(projq_bf[:, pr, blk_, :], pb[blk_][:, 0:512], AF.Copy)
                    i = sc.activation(sqs2[:, pr % 2, :], pm[:, :], AF.Copy)
                    ck[("prev", gi)] = c.inc(i, "ev", 1)

                @blk.vector
                def _(v, pr=pr, gi=gi):
                    c.wait(v, "ev", ck[("prev", gi)])
                    v.tensor_tensor(sqs2[:, pr % 2, :], sqs2[:, pr % 2, :], sqs2[:, pr % 2, :], Alu.mult)
                    i = None
                    for hh in range(2):
                        i = v.tensor_reduce(
                            nsqb2[:, pr * 8 + 4 * hh:pr * 8 + 4 * hh + 4].rearrange("p (g o) -> p g o", o=1),
                            sqs2[:, pr % 2, :].rearrange("p (a q) -> p a q", a=4)[:, :, hh * 64:(hh + 1) * 64],
                            AX, Alu.add)
                    ck[("sqv", gi)] = c.inc(i, "vv", 1)

            @blk.scalar
            def _(sc, b=b):
                c.wait(sc, "vv", ck[("sqv", 64 + b * 8 + 7)])
                i = sc.activation(nsqb2[:, :], nsqb2[:, :], AF.Exp, scale=-0.5)
                c.inc(i, "ev", 1)

            @blk.vector
            def _(v, ev0=c.val("ev")):
                c.wait(v, "ev", ev0)
                c.drain(v, v.tensor_scalar(nsqb2[:, :], nsqb2[:, :], 1.0 / 16.0, None, Alu.mult))
                i = v.engine_nop()
                c.inc(i, "vv", 1)

            def s2_trig(pr, gi, par):
                @blk.vector
                def _(v):
                    if gi >= 65:
                        c.wait(v, "ev", ck[("trge", gi - 1)])
                    src_ = projq_bf[:, pr, :, :].rearrange("p a b -> p (a b)")
                    v.tensor_scalar(u1_s[:, :], src_, 1.0 / TWO_PI, MAGIC, Alu.mult, Alu.add)
                    v.tensor_scalar(u1_s[:, :], u1_s[:, :], MAGIC, TWO_PI, Alu.subtract, Alu.mult)
                    v.scalar_tensor_tensor(u1_s[:, :], u1_s[:, :], -1.0, src_, Alu.mult, Alu.add)
                    v.tensor_scalar(u2_s[:, :], src_, PI / 2, 1.0 / TWO_PI, Alu.add, Alu.mult)
                    v.tensor_scalar(u2_s[:, :], u2_s[:, :], MAGIC, MAGIC, Alu.add, Alu.subtract)
                    i = v.scalar_tensor_tensor(u2_s[:, :], u2_s[:, :], -TWO_PI, src_, Alu.mult, Alu.add)
                    ck[("trgv", gi)] = c.inc(i, "vv", 1)

            def s2_sin(pr, gi, par):
                @blk.scalar
                def _(sc):
                    c.wait(sc, "vv", ck[("trgv", gi)])
                    if gi >= 66:
                        c.wait(sc, "mm", ck[("qkmm", gi - 2)])
                    sc.activation(Qfcsp[:, par, :, :].rearrange("p (h f) x -> p h f x", h=2)[:, :, 2:4, :],
                                  u1_s[:, :], AF.Sin)
                    i = sc.activation(Qfcsp[:, par, :, :].rearrange("p (h f) x -> p h f x", h=2)[:, :, 0:2, :],
                                      u2_s[:, :], AF.Sin, bias=consts[:, 1:2])
                    ck[("trge", gi)] = c.inc(i, "ev", 1)

            def s2_qkvz(pr, gi, par):
                @blk.tensor
                def _(t):
                    c.wait(t, "ev", ck[("trge", gi)])
                    if gi == 64 + b * 8:
                        c.wait(t, "cst", kv_ld)
                    if gi >= 65:
                        c.wait(t, "vv", ck.get(("qze", gi - 1), 0))
                    i = None
                    for hh in range(2):
                        bank = pm2 if hh == 0 else pb[0]
                        h = pr * 2 + hh
                        for ts_ in range(4):
                            for fc in range(4):
                                i = t.matmul(bank[:, ts_ * 66:ts_ * 66 + 66],
                                             Qfcsp[:, par, hh * 4 + fc, ts_ * 128:(ts_ + 1) * 128],
                                             kvbb[:, h, fc * 66:(fc + 1) * 66],
                                             start=(fc == 0), stop=(fc == 3))
                    ck[("qkmm", gi)] = c.inc(i, "mm", 1)

            def s2_qze(pr, gi, par):
                @blk.vector
                def _(v):
                    c.wait(v, "mm", ck[("qkmm", gi)])
                    v.tensor_copy(szz[:, 0, :], pm2[:, 0:264])
                    i = v.tensor_copy(szz[:, 1, :], pb[0][:, 0:264])
                    ck[("qze", gi)] = c.inc(i, "vv", 1)

            def s2_epi(pr, gi, par):
                @blk.vector
                def _(v):
                    if gi >= 65:
                        c.wait(v, "mm", ck[("attm", gi - 1)])
                    i = None
                    for hh in range(2):
                        qv3 = nsqb2[:, pr * 8 + 4 * hh:pr * 8 + 4 * hh + 4].rearrange("p (g o) -> p g o", o=1)
                        zz = colw[:, 24:28] if par == 0 else colw[:, 28:32]
                        zz3 = zz.rearrange("p (g o) -> p g o", o=1)
                        c.drain(v, v.tensor_tensor(zz3, qv3,
                                        szz[:, hh, :].rearrange("p (w c) -> p w c", c=66)[:, :, 64:65],
                                        Alu.mult))
                        c.drain(v, v.tensor_scalar(zz, zz, 1e-6, None, Alu.max))
                        c.drain(v, v.reciprocal(zz, zz))
                        qsc = colw[:, 32:36] if par == 0 else colw[:, 36:40]
                        c.drain(v, v.tensor_tensor(qsc.rearrange("p (g o) -> p g o", o=1), qv3, zz3, Alu.mult))
                        for ts_ in range(4):
                            i = v.tensor_scalar(attstg2[:, ts_, hh * 64:(hh + 1) * 64],
                                                szz[:, hh, ts_ * 66:ts_ * 66 + 64],
                                                qsc[:, ts_:ts_ + 1], None, Alu.mult)
                    ck[("epiv", gi)] = c.inc(i, "vv", 1)

            def s2_atttr(pr, gi, par):
                @blk.tensor
                def _(t):
                    c.wait(t, "vv", ck[("epiv", gi)])
                    i = None
                    for ts_ in range(4):
                        i = t.transpose(pt[1][:, par * 4 + ts_, :], attstg2[:, ts_, :], ident_b[:, :])
                    ck[("attm", gi)] = c.inc(i, "mm", 1)

            def s2_attcp(pr, gi, par):
                @blk.vector
                def _(v):
                    c.wait(v, "mm", ck[("attm", gi)])
                    i = v.tensor_copy(attT[:, pr, :], pt[1][:, par * 4:par * 4 + 4, :])
                    ck[("attv", gi)] = c.inc(i, "vv", 1)

            late2a = [s2_sin, s2_qkvz]
            early2 = [s2_trig]
            late2b = [s2_qze, s2_epi, s2_atttr]
            for it in range(10):
                if 1 <= it <= 8:
                    pr = it - 1
                    for f in late2a:
                        f(pr, 64 + b * 8 + pr, (64 + b * 8 + pr) % 2)
                if it < 8:
                    pr = it
                    for f in early2:
                        f(pr, 64 + b * 8 + pr, (64 + b * 8 + pr) % 2)
                if 1 <= it <= 8:
                    pr = it - 1
                    for f in late2b:
                        f(pr, 64 + b * 8 + pr, (64 + b * 8 + pr) % 2)
                if it >= 2:
                    pr = it - 2
                    s2_attcp(pr, 64 + b * 8 + pr, (64 + b * 8 + pr) % 2)

            att_vv = c.val("vv")

            if dbg and b == 0:
                dbg6 = nc.dram_tensor("dbg6", [128, 16384], F32, kind="ExternalOutput").ap()

                @blk.gpsimd
                def _(g):
                    c.wait(g, "vv", att_vv)
                    c.inc(g.dma_start(out=dbg6[:, 0:4096], in_=attT[:, :, :].rearrange("p a b -> p (a b)")), "st", 16)
                    c.inc(g.dma_start(out=dbg6[:, 4096:12288], in_=Qfcsp[:, :, :, :].rearrange("p a b c -> p (a b c)")), "st", 16)

            for u in range(8):
                ts_, occ = divmod(u, 2)

                @blk.tensor
                def _(t, ts_=ts_, occ=occ, u=u, att_vv=att_vv):
                    if u == 0:
                        c.wait(t, "vv", att_vv)
                    if u >= 4:
                        c.wait(t, "vv", c.val("vv"))
                    bank = pb[u % 4]
                    i = None
                    for cc in range(8):
                        i = t.matmul(bank[:, :], attT[:, cc, ts_ * 128:(ts_ + 1) * 128],
                                     woTb[:, cc, occ * 512:(occ + 1) * 512],
                                     start=(cc == 0), stop=(cc == 7))
                    c.inc(i, "mm", 1)

                @blk.vector
                def _(v, ts_=ts_, occ=occ, u=u):
                    c.wait(v, "mm")
                    i = v.tensor_tensor(ost2[:, ts_, occ * 512:(occ + 1) * 512],
                                        pb[u % 4][:, :],
                                        xr2[:, ts_, occ * 512:(occ + 1) * 512], Alu.add)
                    c.inc(i, "vv", 1)

            @blk.gpsimd
            def _(g, b=b):
                c.wait(g, "vv", c.val("vv"))
                c.inc(g.dma_start(
                    out=xres[b * TL:(b + 1) * TL, :].rearrange("(t p) o -> p t o", p=128),
                    in_=ost2[:, :, :]), "st", 16)
            barrier()

    phase_setup()
    phase_attn_a()
    if stage >= 2:
        phase_dequant()
        phase_ffn(xres, y_out if stage == 2 else xres, 0)
    if stage >= 3:
        phase_attn_b()
        if stage == 3:
            @blk.gpsimd
            def _(g):
                c.wait(g, "st", c.val("st"))
                i = g.dma_start(out=y_out, in_=xres)
                c.inc(i, "st", 16)
                c.wait(g, "st")
    if stage >= 4:
        phase_ffn(xres, y_out, 5)

    if dbg and stage >= 3:
        kvtap = arf.at(0, 16, 264)

        @blk.sync
        def _(sy):
            c.wait(sy, "st", c.val("st"))
            c.inc(sy.dma_start(out=kvtap[:, :, :].rearrange("p a b -> p (a b)"),
                               in_=kv_out[0:128, :]), "ld", 16)

        @blk.gpsimd
        def _(g):
            c.wait(g, "ld", c.val("ld"))

    if dbg and stage >= 2:
        global DBG_MAP2
        DBG_MAP2 = {}
        dbg2 = nc.dram_tensor("dbg2", [128, 40960], F32, kind="ExternalOutput").ap()

        @blk.gpsimd
        def _(g):
            off = 0
            taps3 = ([("kv0", arf.at(0, 4224), 4224),
                      ("attT", arb.at(26624, 4096), 4096),
                      ("qfcs", arb.at(24576, 2048), 2048)]
                     if stage >= 3 else [])
            for name, ap_, n in taps3 + [("colw", colw[:, :], 32),
                                 ("gate0", gate_s[0], 4096), ("up0", up_s[0], 4096),
                                 ("g20", g2_s[0], 4096), ("t40", t4_s[0], 4096),
                                 ("wg0", wgT_dq[0, 0], 128), ("wg1", wgT_dq[0, 1], 128),
                                 ("wd0", wdT_dq[0], 1024)]:
                DBG_MAP2[name] = (off, n)
                c.inc(g.dma_start(out=dbg2[:, off:off + n], in_=ap_), "st", 16)
                off += n
            c.wait(g, "st")

    if dbg:
        taps = [("xnT", xnT, BF16, 4096), ("z_fm", z_fm, BF16, 4096),
                ("z_tm", z_tm, BF16, 4096), ("q_fm", q_fm, BF16, 4096),
                ("G_fm", G_fm, BF16, 4096), ("G_tm", G_tm, BF16, 4096),
                ("h_tm", h_tm, BF16, 1024), ("gp_b", gp_b, BF16, 1024),
                ("da_fm", da_fm, BF16, 1024), ("da_tm", da_tm, BF16, 1024),
                ("h2_fm", h_fm, BF16, 1024), ("a_f", a_f, F32, 1024),
                ("gst1", gst1, F32, 2048), ("gst2", gst2, F32, 2048),
                ("zy", y_fm, BF16, 4096), ("xs", xs, F32, 4096),
                ("scrb", scr_b, F32, 2048), ("w1T_b", w1T_b, BF16, 2048),
                ("w2T_b", w2T_b, BF16, 2048), ("w2n_b", w2n_b, BF16, 2048)]
        off = [0]
        global DBG_MAP
        DBG_MAP = {}

        @blk.gpsimd
        def _(g):
            for name, ap, dt_, n in taps:
                flat = ap
                if len(flat.shape) == 3:
                    flat = flat.rearrange("p a b -> p (a b)")
                rows = flat.shape[0]
                DBG_MAP[name] = (off[0], n, rows)
                i = g.dma_start(out=dbg_out[0:rows, off[0]:off[0] + n], in_=flat)
                c.inc(i, "st", 16)
                off[0] += n
            c.wait(g, "st")

    if stage == 1:
        @blk.gpsimd
        def _(g):
            c.wait(g, "st", c.val("st"))
            i = g.dma_start(out=y_out, in_=xres)
            c.inc(i, "st", 16)
            c.wait(g, "st")

    es.close()
    return nc


_CACHE = {}


def _get_prog(stage=None):
    key = STAGE if stage is None else stage
    if key not in _CACHE:
        _CACHE[key] = build_program(key)
    return _CACHE[key]


def make_in_maps(inputs):
    x = np.asarray(inputs["L0_attn_a_in_seq"], dtype=np.float32)
    T = lambda a: np.ascontiguousarray(np.asarray(a, dtype=np.float32).T)
    C = lambda a: np.ascontiguousarray(np.asarray(a, dtype=np.float32))
    shared = {
        "tkT": T(inputs["L0_attn_a_t_k"]), "tvT": T(inputs["L0_attn_a_t_v"]),
        "tqT": T(inputs["L0_attn_a_t_q"]),
        "w1T": T(inputs["L0_attn_a_W1"]), "w2T": T(inputs["L0_attn_a_W2"]),
        "w2n": C(inputs["L0_attn_a_W2"]),
        "b1i": C(inputs["L0_attn_a_b1"]).reshape(1, DI),
        "b2i": C(inputs["L0_attn_a_b2"]).reshape(1, D),
        "wqT": T(inputs["L0_attn_b_wq"]), "wkT": T(inputs["L0_attn_b_wk"]),
        "wvT": T(inputs["L0_attn_b_wv"]), "woT": T(inputs["L0_attn_b_wo"]),
        "rfT": T(inputs["L0_attn_b_rf"]),
        "wgate": C(inputs["L0_ffn_w_gate"]), "wup": C(inputs["L0_ffn_w_up"]),
        "wdown": C(inputs["L0_ffn_w_down"]),
    }
    in_maps = []
    for cid in range(NCORE):
        m = dict(shared)
        m["x_in"] = np.ascontiguousarray(x[:, cid * TL:(cid + 1) * TL, :].reshape(NB * TL, D))
        in_maps.append(m)
    return in_maps


def kernel(**inputs):
    in_maps = make_in_maps(inputs)
    nc = _get_prog()
    res = run_bass_kernel_spmd(nc, in_maps, core_ids=list(range(NCORE)))
    out = np.empty((NB, 4096, D), dtype=np.float32)
    for cid in range(NCORE):
        out[:, cid * TL:(cid + 1) * TL, :] = np.asarray(res.results[cid]["y"]).reshape(NB, TL, D)
    return out
